# version 08
# baseline (speedup 1.0000x reference)
import math
from collections import defaultdict
from contextlib import ExitStack

import numpy as np
import ml_dtypes

import concourse.bass as bass
import concourse.mybir as mybir
from concourse.bass_utils import run_bass_kernel_spmd

F32 = mybir.dt.float32
BF16 = mybir.dt.bfloat16
AF = mybir.ActivationFunctionType
ALU = mybir.AluOpType
AX = mybir.AxisListType
BF = ml_dtypes.bfloat16

NCORES = 8
B, S, D, H, Dh = 2, 8192, 1024, 16, 64
T = S // 4
NDC = D // 128
ENGINES = ("sync", "scalar", "vector", "tensor", "gpsimd")

_LAUNCHES = []


class P:

    SERIAL = {"vector": "dv", "scalar": "ac", "gpsimd": "gp"}

    def __init__(self, name):
        self.nc = bass.Bass("TRN2", target_bir_lowering=False)
        self.name = name
        self.prefix = ""
        self.alias = {}
        self.stack = ExitStack()
        self.ops = {e: [] for e in ENGINES}
        self.cnt = defaultdict(int)
        self.dmas = [self.sem(f"g_dma{i}") for i in range(8)]
        self.dma_rr = 0
        self.sdmas = [self.sem(f"g_sdma{i}") for i in range(6)]
        self.sdma_rr = 0
        self.mm = self.sem("g_mm")
        self.dv = self.sem("g_dv")
        self.ac = self.sem("g_ac")
        self.gp = self.sem("g_gp")
        self._esem = {"vector": self.dv, "scalar": self.ac, "gpsimd": self.gp}
        self._consts = {}
        self._const_tile = self.sb("constcols", [128, 16], F32)
        self.ccsem = None

    def collective(self, kind, op, ins, outs, groups=((0, 1, 2, 3), (4, 5, 6, 7))):
        if self.ccsem is None:
            self.ccsem = self.sem("g_cc")
        return self.op("gpsimd", lambda g: g.collective_compute(
            kind, op, replica_groups=[list(g_) for g_ in groups],
            ins=[ins.ap().opt()], outs=[outs.ap().opt()]), inc=self.gp)

    def barrier(self, tag=""):
        pairs = [(self.mm, self.at(self.mm)), (self.dv, self.at(self.dv)),
                 (self.ac, self.at(self.ac)), (self.gp, self.at(self.gp))]
        pairs += self.dma_all()
        for e in ENGINES:
            self.op(e, lambda eng: None, wait=pairs)

    def const_col(self, value, dtype=F32):
        assert dtype == F32
        key = float(value)
        if key not in self._consts:
            idx = len(self._consts)
            assert idx < 16
            ap = self._const_tile[:, idx:idx + 1]
            post = self.op("gpsimd", lambda g, ap=ap, value=value: g.memset(ap, value))
            self._consts[key] = (ap, (self.gp, post))
        return self._consts[key]

    def sem(self, name):
        return self.stack.enter_context(self.nc.semaphore(name))

    def sb(self, name, shape, dtype, stack=None):
        return (stack or self.stack).enter_context(
            self.nc.sbuf_tensor(self.prefix + name, shape, dtype))

    def ps(self, name, shape, dtype=F32, stack=None):
        return (stack or self.stack).enter_context(
            self.nc.psum_tensor(self.prefix + name, shape, dtype))

    def dram(self, name, shape, dtype, kind=None):
        full = self.prefix + name
        if full in self.alias:
            return self.alias[full]
        if self.prefix and kind == "ExternalOutput" and \
                full not in getattr(self, "keep_outputs", ()):
            kind = None
        if kind is None:
            t = self.nc.dram_tensor(full, shape, dtype)
        else:
            t = self.nc.dram_tensor(full, shape, dtype, kind=kind)
        self.alias[full] = t
        return t

    def op(self, eng, fn, wait=(), inc=None, inc_n=1, _dma=False):
        wait = [w for w in wait if w is not None]
        if not _dma and eng in self._esem:
            es = self._esem[eng]
            assert inc is None or inc is es, (self.name, eng, inc and inc.name)
            wait.append((es, self.cnt[es.name]))
            inc = es
            inc_n = 1
        for s, v in wait:
            assert self.cnt[s.name] >= v, (self.name, eng, s.name, v, self.cnt[s.name])
        post = None
        if inc is not None:
            self.cnt[inc.name] += inc_n
            post = self.cnt[inc.name]

        def run(engobj, wait=wait, inc=inc, inc_n=inc_n, fn=fn):
            for s, v in wait:
                engobj.wait_ge(s, v)
            inst = fn(engobj)
            if inc is not None:
                if inst is None:
                    inst = engobj.nop()
                inst.then_inc(inc, inc_n)

        self.ops[eng].append(run)
        return post

    def dma(self, eng, out, in_, wait=(), inc=None, **kw):
        if eng == "gpsimd":
            s = self.sdmas[self.sdma_rr % len(self.sdmas)]
            self.sdma_rr += 1
        else:
            s = self.dmas[self.dma_rr % len(self.dmas)]
            self.dma_rr += 1
        wait = list(wait) + [(s, self.cnt[s.name])]
        post = self.op(eng, lambda e: e.dma_start(out=out, in_=in_, **kw), wait=wait,
                       inc=s, inc_n=16, _dma=True)
        return (s, post)

    def dma_all(self):
        return [(s, self.cnt[s.name]) for s in self.dmas + self.sdmas]

    def at(self, s):
        return self.cnt[s.name]

    def finalize(self):
        nc = self.nc
        with nc.Block() as block:
            for e in ENGINES:
                def mk(e):
                    def body(engobj):
                        for fn in self.ops[e]:
                            fn(engobj)
                    return body
                getattr(block, e)(mk(e))
        self.stack.close()
        return nc


def launch(nc, in_maps, outs):
    res = run_bass_kernel_spmd(nc, in_maps, core_ids=list(range(NCORES)))
    return [{k: r[k] for k in outs} for r in res.results]



def host_consts():
    c = {}
    c["ones_col_bf"] = np.ones((128, 1), BF)
    c["ones_row_f32"] = np.ones((1, 128), np.float32)
    c["eye_bf"] = np.eye(128, dtype=BF)
    return c



def emit_ln_fm(p, src, dst_f32, dst_bf=None, Tn=T, tag="ln", start_wait=()):
    nc = p.nc
    st = ExitStack()
    x = [p.sb(f"{tag}_x{i}", [128, Tn], F32, st) for i in range(NDC)]
    xb = [p.sb(f"{tag}_xb{i}", [128, Tn], BF16, st) for i in range(NDC)]
    ones_col = p.sb(f"{tag}_ones", [128, 1], BF16, st)
    ones_row = p.sb(f"{tag}_onesr", [1, 128], F32, st)
    mean_r = p.sb(f"{tag}_mean", [1, Tn], F32, st)
    msq_r = p.sb(f"{tag}_msq", [1, Tn], F32, st)
    var_r = p.sb(f"{tag}_var", [1, Tn], F32, st)
    rstd_r = p.sb(f"{tag}_rstd", [1, Tn], F32, st)
    mb_r = p.sb(f"{tag}_mb", [1, Tn], F32, st)
    cv, mm, ev = p.dv, p.mm, p.dv
    gp = p.gp

    p.op("gpsimd", lambda g: g.memset(ones_col[:, :], 1.0))
    p.op("gpsimd", lambda g: g.memset(ones_row[:, :], 1.0))
    gp_ones = p.at(gp)
    for i in range(NDC):
        p.dma("sync", x[i][:, :], src[i * 128:(i + 1) * 128, :],
              wait=start_wait if i == 0 else ())
        p.dma("gpsimd", xb[i][:, :], src[i * 128:(i + 1) * 128, :],
              wait=start_wait if i == 0 else ())
    ld_pairs = p.dma_all()

    with st:
        st2 = ExitStack()
        sq = [p.sb(f"{tag}_sq{i}", [128, Tn], BF16, st2) for i in range(NDC)]
        ps_sum = p.ps(f"{tag}_psum", [1, Tn], F32, st2)
        ps_sq = p.ps(f"{tag}_psq", [1, Tn], F32, st2)
        with st2:
            for i in range(NDC):
                p.op("scalar", lambda s, i=i: s.activation(sq[i][:, :], x[i][:, :], AF.Square),
                     wait=ld_pairs if i == 0 else ())
            sq_ac = p.at(p.ac)
            NT = Tn // 512
            for i in range(NDC):
                for n in range(NT):
                    cs = bass.ts(n, 512)
                    p.op("tensor", lambda t, i=i, n=n, cs=cs: t.matmul(
                        ps_sum[:, cs], ones_col[:, :], xb[i][:, cs],
                        start=(i == 0), stop=(i == NDC - 1)),
                        wait=(ld_pairs + [(gp, gp_ones)]) if i == 0 and n == 0 else ())
                    p.op("tensor", lambda t, i=i, n=n, cs=cs: t.matmul(
                        ps_sq[:, cs], ones_col[:, :], sq[i][:, cs],
                        start=(i == 0), stop=(i == NDC - 1)),
                        wait=[(p.ac, sq_ac)] if i == 0 and n == 0 else (),
                        inc=mm if i == NDC - 1 and n == NT - 1 else None)
            mm_done = p.at(mm)
            p.op("vector", lambda v: v.tensor_scalar_mul(mean_r[:, :], ps_sum[:, :], 1.0 / D),
                 wait=[(mm, mm_done)])
            p.op("vector", lambda v: v.tensor_mul(msq_r[:, :], mean_r[:, :], mean_r[:, :]))
            p.op("vector", lambda v: v.scalar_tensor_tensor(
                var_r[:, :], ps_sq[:, :], 1.0 / D, msq_r[:, :], ALU.mult, ALU.subtract),
                inc=ev)
            ev_rows = p.at(ev)
            epsc, epsw = p.const_col(1e-5)
            p.op("scalar", lambda s: s.activation(var_r[:, :], var_r[:, :], AF.Ln,
                                                  bias=epsc[0:1]),
                 wait=[(ev, ev_rows), epsw])
            sq_post = p.op("scalar", lambda s: s.activation(rstd_r[:, :], var_r[:, :],
                                                            AF.Exp, scale=-0.5))
            p.op("vector", lambda v: v.tensor_mul(mb_r[:, :], mean_r[:, :], rstd_r[:, :]),
                 wait=[(p.ac, sq_post)], inc=ev)
            ev_done = p.at(ev)
        ps_rstd = p.ps(f"{tag}_prstd", [128, Tn], F32, st)
        ps_mb = p.ps(f"{tag}_pmb", [128, Tn], F32, st)
        NT = Tn // 512
        for n in range(NT):
            cs = bass.ts(n, 512)
            p.op("tensor", lambda t, cs=cs: t.matmul(
                ps_rstd[:, cs], ones_row[:, :], rstd_r[:, cs], start=True, stop=True),
                wait=[(ev, ev_done)] if n == 0 else ())
            p.op("tensor", lambda t, cs=cs: t.matmul(
                ps_mb[:, cs], ones_row[:, :], mb_r[:, cs], start=True, stop=True),
                inc=mm if n == NT - 1 else None)
        rep_done = p.at(mm)
        for i in range(NDC):
            p.op("vector", lambda v, i=i: v.tensor_mul(x[i][:, :], x[i][:, :], ps_rstd[:, :]),
                 wait=[(mm, rep_done)] if i == 0 else ())
            p.op("vector", lambda v, i=i: v.tensor_sub(x[i][:, :], x[i][:, :], ps_mb[:, :]),
                 inc=cv)
            if dst_bf is not None:
                p.op("scalar", lambda s, i=i: s.copy(xb[i][:, :], x[i][:, :]), wait=[(cv, p.at(cv))])
        wr_done = p.at(cv)
        ac_done = p.at(p.ac)
        for i in range(NDC):
            p.dma("sync", dst_f32[i * 128:(i + 1) * 128, :], x[i][:, :],
                  wait=[(cv, wr_done)] if i == 0 else ())
            if dst_bf is not None:
                p.dma("scalar", dst_bf[i * 128:(i + 1) * 128, :], xb[i][:, :],
                      wait=[(p.ac, ac_done)] if i == 0 else ())
        st_pairs = p.dma_all()
        p.op("sync", lambda e: None, wait=st_pairs)
    return st_pairs


def build_s1():
    p = P("s1")
    xT = p.dram("xT", [D, T], F32, kind="ExternalInput")
    xn = p.dram("xn", [D, T], F32, kind="ExternalOutput")
    xnb = p.dram("xnb", [D, T], BF16, kind="ExternalOutput")
    emit_ln_fm(p, xT.ap(), xn.ap(), xnb.ap())
    return p.finalize()


def mm_accum(p, ps, pairs, wait_first=(), inc=None):
    n = len(pairs)
    post = None
    for idx, (l, r) in enumerate(pairs):
        post = p.op("tensor", lambda t, l=l, r=r, idx=idx, n=n: t.matmul(
            ps, l, r, start=(idx == 0), stop=(idx == n - 1)),
            wait=wait_first if idx == 0 else (),
            inc=inc if idx == n - 1 else None)
    return post



def emit_qkv0(p):
    nc = p.nc
    xnb_d = p.dram("xnb", [D, T], BF16, kind="ExternalInput")
    wqT_d = p.dram("wqT", [D, D], F32, kind="ExternalInput")
    wkT_d = p.dram("wkT", [D, D], F32, kind="ExternalInput")
    wvT_d = p.dram("wvT", [D, D], F32, kind="ExternalInput")
    qnw_d = p.dram("qnw", [128, 1], F32, kind="ExternalInput")
    knw_d = p.dram("knw", [128, 1], F32, kind="ExternalInput")
    cfm_d = p.dram("cfm", [128, T], F32, kind="ExternalInput")
    sfm_d = p.dram("sfm", [128, T], F32, kind="ExternalInput")
    hs2_d = p.dram("hs2", [128, 2], BF16, kind="ExternalInput")
    rep2_d = p.dram("rep2", [2, 128], F32, kind="ExternalInput")
    psign_d = p.dram("psign", [128, 128], BF16, kind="ExternalInput")
    Qp_d = p.dram("Qp", [D, T], BF16, kind="ExternalOutput")
    Kp_d = p.dram("Kp", [D, T], BF16, kind="ExternalOutput")
    V_d = p.dram("V", [128, (T // 128) * D], BF16, kind="ExternalOutput")

    st = ExitStack()
    xnb = [p.sb(f"q0_x{i}", [128, T], BF16, st) for i in range(NDC)]
    w_sb = [p.sb(f"q0_w{i}", [128, D], BF16, st) for i in range(NDC)]
    cfm = p.sb("q0_cfm", [128, T], F32, st)
    sfm = p.sb("q0_sfm", [128, T], F32, st)
    hs2 = p.sb("q0_hs2", [128, 2], BF16, st)
    rep2 = p.sb("q0_rep2", [2, 128], F32, st)
    psign = p.sb("q0_psign", [128, 128], BF16, st)
    nw = p.sb("q0_nw", [128, 1], F32, st)
    qraw = p.sb("q0_qraw", [128, T], F32, st)
    sqb = p.sb("q0_sqb", [128, T], BF16, st)
    ssrow = p.sb("q0_ss", [2, T], F32, st)
    qhat = p.sb("q0_qhat", [128, T], F32, st)
    qhatb = p.sb("q0_qhatb", [128, T], BF16, st)
    arow = p.sb("q0_arow", [128, T], F32, st)
    qfin = p.sb("q0_qfin", [128, T], BF16, st)
    ps_acc = [p.ps(f"q0_pa{i}", [128, 512], F32, st) for i in range(2)]
    ps_ss = p.ps("q0_pss", [2, 512], F32, st)
    ps_rep = [p.ps(f"q0_pr{i}", [128, 512], F32, st) for i in range(2)]
    ps_p = [p.ps(f"q0_pp{i}", [128, 512], F32, st) for i in range(2)]

    mm, dv, ac = p.mm, p.dv, p.ac

    eps64, w64 = p.const_col(64e-6)
    eps1, w1 = p.const_col(1e-6)
    sc64, wsc = p.const_col(1.0 / 64.0)

    for i in range(NDC):
        p.dma("sync", xnb[i][:, :], xnb_d.ap()[i * 128:(i + 1) * 128, :])
    p.dma("sync", cfm[:, :], cfm_d.ap()[:, :])
    p.dma("sync", sfm[:, :], sfm_d.ap()[:, :])
    p.dma("gpsimd", hs2[:, :], hs2_d.ap()[:, :])
    p.dma("sync", rep2[:, :], rep2_d.ap()[:, :])
    p.dma("gpsimd", psign[:, :], psign_d.ap()[:, :])
    base_pairs = p.dma_all()

    NT = T // 512
    last = {}

    def acq(key):
        v = last.get(key)
        if v is None:
            return []
        return list(v) if isinstance(v, list) else [v]

    for which, wT_d, nw_d, out_d in (("q", wqT_d, qnw_d, Qp_d), ("k", wkT_d, knw_d, Kp_d)):
        for i in range(NDC):
            p.dma("gpsimd", w_sb[i][:, :], wT_d.ap()[i * 128:(i + 1) * 128, :],
                  wait=acq(("w", i)))
        p.dma("sync", nw[:, :], nw_d.ap()[:, :], wait=acq(("nw",)))
        w_pairs = p.dma_all()

        for i in range(NDC):
            ochunk = slice(i * 128, (i + 1) * 128)
            for n in range(NT):
                cs = bass.ts(n, 512)
                par = n % 2
                mm_accum(p, ps_acc[par][:, :],
                         [(w_sb[k][:, ochunk], xnb[k][:, cs]) for k in range(NDC)],
                         wait_first=w_pairs + acq(("pa", par)), inc=mm)
                v = p.op("vector", lambda vv, n=n, par=par, cs=cs: vv.tensor_copy(
                    qraw[:, cs], ps_acc[par][:, :]), wait=[(mm, p.at(mm))], inc=dv)
                last[("pa", par)] = (dv, v)
            a = p.op("scalar", lambda s: s.activation(sqb[:, :], qraw[:, :], AF.Square),
                     wait=[(dv, p.at(dv))], inc=ac)
            for n in range(NT):
                cs = bass.ts(n, 512)
                m = mm_accum(p, ps_ss[:, :], [(hs2[:, :], sqb[:, cs])],
                             wait_first=[(ac, a)] + acq(("pss",)), inc=mm)
                v = p.op("vector", lambda vv, cs=cs: vv.tensor_copy(ssrow[:, cs], ps_ss[:, :]),
                         wait=[(mm, m)], inc=dv)
                last[("pss",)] = (dv, v)
            if which == "q":
                a = p.op("scalar", lambda s: s.activation(
                    ssrow[:, :], ssrow[:, :], AF.Ln, bias=eps64[0:2]),
                    wait=[(dv, p.at(dv)), w64], inc=ac)
            else:
                a = p.op("scalar", lambda s: s.activation(
                    ssrow[:, :], ssrow[:, :], AF.Ln, bias=eps1[0:2], scale=sc64[0:2]),
                    wait=[(dv, p.at(dv)), w1, wsc], inc=ac)
            rst_done = p.op("scalar", lambda s: s.activation(ssrow[:, :], ssrow[:, :],
                                                              AF.Exp, scale=-0.5), inc=ac)
            for n in range(NT):
                cs = bass.ts(n, 512)
                par = n % 2
                m = mm_accum(p, ps_rep[par][:, :], [(rep2[:, :], ssrow[:, cs])],
                             wait_first=[(ac, rst_done)] + acq(("pr", par)), inc=mm)
                v = p.op("vector", lambda vv, cs=cs, par=par: vv.scalar_tensor_tensor(
                    qhat[:, cs], qraw[:, cs], nw[:, :], ps_rep[par][:, :],
                    ALU.mult, ALU.mult), wait=[(mm, m)], inc=dv)
                last[("pr", par)] = (dv, v)
            cb = p.op("scalar", lambda s: s.copy(qhatb[:, :], qhat[:, :]),
                      wait=[(dv, p.at(dv))], inc=ac)
            p.op("vector", lambda vv: vv.tensor_mul(arow[:, :], qhat[:, :], cfm[:, :]),
                 wait=base_pairs, inc=dv)
            for n in range(NT):
                cs = bass.ts(n, 512)
                par = n % 2
                m = mm_accum(p, ps_p[par][:, :], [(psign[:, :], qhatb[:, cs])],
                             wait_first=[(ac, cb)] + acq(("pp", par)), inc=mm)
                v = p.op("vector", lambda vv, cs=cs, par=par: vv.tensor_mul(
                    qfin[:, cs], ps_p[par][:, :], sfm[:, cs]), wait=[(mm, m)], inc=dv)
                last[("pp", par)] = (dv, v)
            v = p.op("vector", lambda vv: vv.tensor_add(qfin[:, :], qfin[:, :], arow[:, :]),
                     inc=dv)
            sv = p.dma("scalar", out_d.ap()[ochunk, :], qfin[:, :], wait=[(dv, v)])
            last[("pa", 0)] = sv
            last[("pa", 1)] = sv
            last[("nw",)] = (dv, v)
        last[("w", 0)] = p.dma_all()
        for i in range(1, NDC):
            last[("w", i)] = last[("w", 0)]

    for i in range(NDC):
        p.dma("gpsimd", w_sb[i][:, :], wvT_d.ap()[i * 128:(i + 1) * 128, :],
              wait=acq(("w", i)))
    for tt in range(T // 128):
        trows = slice(tt * 128, (tt + 1) * 128)
        for o in range(2):
            cs = bass.ts(o, 512)
            par = (tt * 2 + o) % 2
            m = mm_accum(p, ps_acc[par][:, :],
                         [(xnb[k][:, trows], w_sb[k][:, cs]) for k in range(NDC)],
                         wait_first=p.dma_all() + acq(("pa", par)), inc=mm)
            v = p.op("vector", lambda vv, par=par: vv.tensor_copy(sqb[:, 0:512], ps_acc[par][:, :]),
                     wait=[(mm, m)] + acq(("vout",)), inc=dv)
            last[("pa", par)] = (dv, v)
            sv = p.dma("scalar", V_d.ap()[0:128, tt * D + o * 512:tt * D + o * 512 + 512],
                       sqb[:, 0:512], wait=[(dv, v)])
            last[("vout",)] = sv
    p.op("sync", lambda e: None, wait=p.dma_all())
    st.close()


def build_s2():
    p = P("s2")
    emit_qkv0(p)
    return p.finalize()



def emit_attn0(p):
    NKC0 = S // 128
    Qp_d = p.dram("Qp2", [D, T], BF16, kind="ExternalInput")
    Kag_d = p.dram("Kag", [4 * D, T], BF16, kind="ExternalInput")
    Vag_d = p.dram("Vag", [4 * 128, (T // 128) * D], BF16, kind="ExternalInput")
    avt_d = p.dram("AVt", [D, T], BF16, kind="ExternalOutput")
    mm, dv, ac = p.mm, p.dv, p.ac
    st = ExitStack()
    qsb = [p.sb(f"a0_q{i}", [64, T], BF16, st) for i in range(2)]
    ksb = [p.sb(f"a0_k{i}", [64, S], BF16, st) for i in range(2)]
    vraw = [p.sb(f"a0_vr{i}", [128, NKC0 * 64], BF16, st) for i in range(2)]
    vah = [p.sb(f"a0_va{i}", [128, NKC0 * 65], BF16, st) for i in range(2)]
    pex = p.sb("a0_p", [128, 2048], BF16, st)
    onesr = p.sb("a0_ones", [1, 64], F32, st)
    denr = p.sb("a0_den", [1, 512], F32, st)
    avf = p.sb("a0_avf", [64, 512], F32, st)
    avn = p.sb("a0_avn", [64, 512], BF16, st)
    ps_s = p.ps("a0_ps", [128, 2048], F32, st)
    ps_av = [p.ps(f"a0_pav{i}", [65, 512], F32, st) for i in range(2)]
    ps_rep = p.ps("a0_prep", [64, 512], F32, st)
    last = {}

    def acq(key):
        v = last.get(key)
        return [] if v is None else (list(v) if isinstance(v, list) else [v])

    p.op("gpsimd", lambda g: g.memset(onesr[:, :], 1.0))
    gp0 = p.at(p.gp)

    def load_head(h, par):
        p.dma("sync", qsb[par][:, :], Qp_d.ap()[h * 64:(h + 1) * 64, :],
              wait=acq(("hd", par)))
        for r in range(4):
            p.dma("sync", ksb[par][:, r * T:(r + 1) * T],
                  Kag_d.ap()[r * D + h * 64:r * D + h * 64 + 64, :])
        for r in range(4):
            p.dma("sync",
                  bass.AP(vraw[par], r * 16 * 64, [[NKC0 * 64, 128], [64, 16], [1, 64]]),
                  bass.AP(Vag_d.handle if hasattr(Vag_d, "handle") else Vag_d,
                          r * 128 * 16 * D + h * 64, [[16 * D, 128], [D, 16], [1, 64]]))
        prs = p.dma_all()
        g = p.op("gpsimd", lambda gg: gg.memset(vah[par][:, :], 1.0),
                 wait=acq(("vah", par)))
        v = p.op("vector", lambda vv: vv.tensor_copy(
            bass.AP(vah[par], 0, [[NKC0 * 65, 128], [65, NKC0], [1, 64]]),
            bass.AP(vraw[par], 0, [[NKC0 * 64, 128], [64, NKC0], [1, 64]])),
            wait=prs + [(p.gp, p.at(p.gp))], inc=dv)
        return prs, (dv, v)

    ld = {}
    ld[0] = load_head(0, 0)
    for h in range(16):
        par = h % 2
        if h + 1 < 16:
            ld[h + 1] = load_head(h + 1, (h + 1) % 2)
        prs, vdone = ld[h]
        for qg in range(4):
            qs = slice(qg * 512, (qg + 1) * 512)
            avm = None
            exp_of = {}

            def emit_av(kc):
                sp = kc % 4
                half = sp // 2
                a_ = p.op("tensor", lambda t, sp=sp, kc=kc, par=par, qg=qg: t.matmul(
                    ps_av[qg % 2][:, :], vah[par][:, kc * 65:(kc + 1) * 65],
                    pex[:, sp * 512:(sp + 1) * 512], start=(kc == 0), stop=(kc == NKC0 - 1)),
                    wait=[(ac, exp_of[kc]), vdone] + (acq(("pav", qg % 2)) if kc == 0 else []),
                    inc=mm)
                last[("pex", half)] = (mm, a_)
                return a_

            for kc in range(NKC0):
                sp = kc % 4
                half = sp // 2
                hs = slice(half * 1024, half * 1024 + 1024)
                sm = p.op("tensor", lambda t, sp=sp, kc=kc, qs=qs, par=par: t.matmul(
                    ps_s[:, sp * 512:(sp + 1) * 512], ksb[par][:, kc * 128:(kc + 1) * 128],
                    qsb[par][:, qs], start=True, stop=True),
                    wait=(prs if qg == 0 and kc == 0 else []) +
                         (acq(("ps", half)) if sp % 2 == 0 else []),
                    inc=mm)
                if sp % 2 == 1:
                    a = p.op("scalar", lambda s, hs=hs: s.activation(
                        pex[:, hs], ps_s[:, hs], AF.Exp),
                        wait=[(mm, sm)] + acq(("pex", half)), inc=ac)
                    last[("ps", half)] = (ac, a)
                    exp_of[kc - 1] = a
                    exp_of[kc] = a
                if kc >= 2:
                    avm = emit_av(kc - 2)
            avm = emit_av(NKC0 - 2)
            avm = emit_av(NKC0 - 1)
            d1 = p.op("vector", lambda v, qg=qg: v.tensor_copy(denr[:, :], ps_av[qg % 2][64:65, :]),
                      wait=[(mm, avm)] + acq(("denr",)), inc=dv)
            p.op("scalar", lambda s: s.activation(denr[:, :], denr[:, :], AF.Ln),
                 wait=[(dv, d1)], inc=ac)
            d2 = p.op("scalar", lambda s: s.activation(denr[:, :], denr[:, :],
                                                       AF.Exp, scale=-1.0), inc=ac)
            rm = p.op("tensor", lambda t: t.matmul(
                ps_rep[:, :], onesr[:, :], denr[:, :], start=True, stop=True),
                wait=[(ac, d2), (p.gp, gp0)] + acq(("prep",)), inc=mm)
            e1 = p.op("vector", lambda v, qg=qg: v.tensor_copy(avf[:, :], ps_av[qg % 2][0:64, :]),
                      wait=acq(("avf",)), inc=dv)
            last[("pav", qg % 2)] = (dv, e1)
            e2 = p.op("vector", lambda v: v.tensor_mul(avn[:, :], avf[:, :], ps_rep[:, :]),
                      wait=[(mm, rm)] + acq(("avn",)), inc=dv)
            last[("prep",)] = (dv, e2)
            last[("avf",)] = (dv, e2)
            last[("denr",)] = (dv, e2)
            sv = p.dma("scalar", avt_d.ap()[h * 64:(h + 1) * 64, qs], avn[:, :],
                       wait=[(dv, e2)])
            last[("avn",)] = sv
        last[("hd", par)] = [(mm, p.at(mm))]
        last[("vah", par)] = (mm, p.at(mm))
    p.op("sync", lambda e: None, wait=p.dma_all())
    st.close()


def build_s3():
    p = P("s3")
    emit_attn0(p)
    return p.finalize()



def emit_linear_fm(p, tag, xb_d, wT_d, out_d, Din, Dout, Tn, out_f32=True, start_wait=()):
    st = ExitStack()
    nin = Din // 128
    xb = [p.sb(f"{tag}_x{i}", [128, Tn], BF16, st) for i in range(nin)]
    w_sb = [p.sb(f"{tag}_w{i}", [128, Dout], BF16, st) for i in range(nin)]
    ev = p.sb(f"{tag}_ev", [128, 512], F32 if out_f32 else BF16, st)
    ps = [p.ps(f"{tag}_ps{i}", [128, 512], F32, st) for i in range(2)]
    mm, dv = p.mm, p.dv
    for i in range(nin):
        p.dma("sync", xb[i][:, :], xb_d[i * 128:(i + 1) * 128, :],
              wait=start_wait if i == 0 else ())
        p.dma("gpsimd", w_sb[i][:, :], wT_d[i * 128:(i + 1) * 128, :],
              wait=start_wait if i == 0 else ())
    ld_pairs = p.dma_all()
    last = {}

    def acq(key):
        v = last.get(key)
        if v is None:
            return []
        return list(v) if isinstance(v, list) else [v]

    NT = Tn // 512
    for o in range(Dout // 128):
        for n in range(NT):
            cs = bass.ts(n, 512)
            par = (o * NT + n) % 2
            m = mm_accum(p, ps[par][:, :],
                         [(w_sb[k][:, o * 128:(o + 1) * 128], xb[k][:, cs]) for k in range(nin)],
                         wait_first=ld_pairs + acq(("ps", par)), inc=mm)
            v = p.op("vector", lambda vv, par=par: vv.tensor_copy(ev[:, :], ps[par][:, :]),
                     wait=[(mm, m)] + acq(("ev",)), inc=dv)
            last[("ps", par)] = (dv, v)
            sv = p.dma("scalar", out_d[o * 128:(o + 1) * 128, cs], ev[:, :],
                       wait=[(dv, v)])
            last[("ev",)] = sv
    p.op("sync", lambda e: None, wait=p.dma_all())
    st.close()


def build_s4():
    p = P("s4")
    avt = p.dram("AVt", [D, T], BF16, kind="ExternalInput")
    woT = p.dram("woT", [D, D], F32, kind="ExternalInput")
    s1 = p.dram("s1", [D, T], F32, kind="ExternalOutput")
    emit_linear_fm(p, "wo0", avt.ap(), woT.ap(), s1.ap(), D, D, T)
    return p.finalize()



YLO, KLO, QLO = 258, 384, 128
NYC = T + YLO
NKC_F = T + KLO
NQC_F = T + QLO


def tiles(total, pref=512):
    out = []
    off = 0
    while off < total:
        sz = min(pref, total - off)
        out.append((off, sz))
        off += sz
    return out


class View:

    def __init__(self, t, c0, w, rows=None):
        self.t, self.c0, self.w, self.rows = t, c0, w, rows

    def __getitem__(self, idx):
        r, c = idx
        if self.rows is not None:
            assert r == slice(None, None, None)
            r = self.rows
        start = 0 if c.start is None else c.start
        stop = self.w if c.stop is None else c.stop
        return self.t[r, self.c0 + start:self.c0 + stop]


def emit_ffn_layer(p, debug=False, stop=None, hc_count=NDC, no_attn=False, no_v=False, no_proj=False, proj_upto=99):
    nc = p.nc
    yh_d = p.dram("yh", [D, YLO], F32, kind="ExternalInput")
    if debug:
        dbg_q = p.dram("dbg_q", [128, NQC_F], BF16, kind="ExternalOutput")
        dbg_k = p.dram("dbg_k", [128, NKC_F], BF16, kind="ExternalOutput")
        dbg_v = p.dram("dbg_v", [128, 19 * 130], BF16, kind="ExternalOutput")
    y_d = p.dram("y", [D, T], F32, kind="ExternalInput")
    Wqkv_d = p.dram("Wqkv", [D, 3 * D], F32, kind="ExternalInput")
    qs_d = p.dram("qsc", [128, 1], F32, kind="ExternalInput")
    ks_d = p.dram("ksc", [128, 1], F32, kind="ExternalInput")
    Wo_d = p.dram("Wo", [D, D], F32, kind="ExternalInput")
    lngb_d = p.dram("lngb", [128, 16], F32, kind="ExternalInput")
    W1_d = p.dram("W1", [D, 5460], F32, kind="ExternalInput")
    b1a_d = p.dram("b1a", [128, 22], F32, kind="ExternalInput")
    b1g_d = p.dram("b1g", [128, 22], F32, kind="ExternalInput")
    W2_d = p.dram("W2", [2730, D], F32, kind="ExternalInput")
    b2_d = p.dram("b2c", [128, 8], F32, kind="ExternalInput")
    cq_d = p.dram("cq", [128, NQC_F], F32, kind="ExternalInput")
    sq_d = p.dram("sq", [128, NQC_F], F32, kind="ExternalInput")
    ck_d = p.dram("ck", [128, NKC_F], F32, kind="ExternalInput")
    sk_d = p.dram("sk", [128, NKC_F], F32, kind="ExternalInput")
    B2_d = p.dram("B2", [128, 1], F32, kind="ExternalInput")
    prh_d = p.dram("prh", [128, 128], BF16, kind="ExternalInput")
    hs2_d = p.dram("hs2", [128, 2], BF16, kind="ExternalInput")
    rep2_d = p.dram("rep2", [2, 128], F32, kind="ExternalInput")
    eye_d = p.dram("eye", [128, 128], BF16, kind="ExternalInput")
    mA_d = p.dram("maskA", [128, 1024], BF16, kind="ExternalInput")
    mB_d = p.dram("maskB", [128, 384], BF16, kind="ExternalInput")
    mA0_d = p.dram("maskA0", [128, 1024], BF16, kind="ExternalInput")
    avn_d = p.dram("avn_s", [D, NQC_F], BF16)
    yout_d = p.dram("y_out", [D, T], F32, kind="ExternalOutput")

    mm, dv, ac, gp = p.mm, p.dv, p.ac, p.gp
    last = {}

    def acq(key):
        v = last.get(key)
        if v is None:
            return []
        return list(v) if isinstance(v, list) else [v]

    st = ExitStack()
    YB = 64
    ybuf = [p.sb(f"f_y{i}", [128, T + YB], F32, st) for i in range(NDC)]
    onesr = p.sb("f_onesr", [1, 128], F32, st)

    for i in range(NDC):
        p.dma("sync", ybuf[i][:, 0:YB], yh_d.ap()[i * 128:(i + 1) * 128, YLO - YB:])
        p.dma("sync", ybuf[i][:, YB:], y_d.ap()[i * 128:(i + 1) * 128, :])
    ones_gp = p.op("gpsimd", lambda g: g.memset(onesr[:, :], 1.0))

    def coly(t):
        return t + YB

    def colk(t):
        return t + KLO

    def colq(t):
        return t + QLO

    blocks = [(-128, 128, 3)] + [(k * 256, 256, 4) for k in range(8)]
    c8, w8 = p.const_col(8.0)

    st_a = ExitStack()
    ybf = [p.sb(f"f_yb{i}", [128, NKC_F + 1], BF16, st_a) for i in range(NDC)]
    hs2 = p.sb("f_hs2", [128, 2], BF16, st_a)
    rep2 = p.sb("f_rep2", [2, 128], F32, st_a)
    eye = p.sb("f_eye", [128, 128], BF16, st_a)
    prh = p.sb("f_prh", [128, 128], BF16, st_a)
    qs_c = p.sb("f_qs", [128, 1], F32, st_a)
    ks_c = p.sb("f_ks", [128, 1], F32, st_a)
    B2c = p.sb("f_B2", [128, 1], F32, st_a)
    mA = p.sb("f_mA", [128, 1024], BF16, st_a)
    mA0 = p.sb("f_mA0", [128, 1024], BF16, st_a)
    mB = p.sb("f_mB", [128, 384], BF16, st_a)
    cq = p.sb("f_cq", [128, NQC_F], F32, st_a)
    sq = p.sb("f_sq", [128, NQC_F], F32, st_a)
    ck = p.sb("f_ck", [128, NKC_F], F32, st_a)
    sk = p.sb("f_sk", [128, NKC_F], F32, st_a)
    wq_all3 = [p.sb(f"f_wqa{t}", [128, NDC * 128], BF16, st_a) for t in range(3)]

    class _WqK:
        def __init__(self, k):
            self.k = k

        def __getitem__(self, idx):
            r, c = idx
            start = 0 if c.start is None else c.start
            part = start // 128
            off = start % 128
            w = c.stop - start
            return wq_all3[part][:, self.k * 128 + off:self.k * 128 + off + w]

    class _WqView:
        def __getitem__(self, k):
            return _WqK(k)
    wq_all = _WqView()
    scr = p.sb("f_scr", [128, NKC_F], BF16, st_a)
    sqb2 = [p.sb(f"f_sqb{i}", [128, 512], BF16, st_a) for i in range(2)]
    rows = p.sb("f_rows", [2, 2 * NQC_F], F32, st_a)
    qhat = p.sb("f_qhat", [128, NKC_F], BF16, st_a)
    tsc0 = p.sb("f_tsc", [128, 512], F32, st_a)
    qfin = p.sb("f_qfin", [128, NQC_F], BF16, st_a)
    qfinL = p.sb("f_qfinL", [128, NQC_F], BF16, st_a)
    kfin = p.sb("f_kfin", [128, NKC_F], BF16, st_a)
    vfm = [p.sb(f"f_vfm{i}", [128, 512], BF16, st_a) for i in range(2)]
    vsb = p.sb("f_vsb", [128, 19 * 130], BF16, st_a)
    pmk1 = p.sb("f_pmk1", [128, 1024], BF16, st_a)
    pex_t2 = p.sb("f_pex2", [128, 1024], BF16, st_a)
    qraw = scr
    pex_t = View(scr, 0, 1024)
    pmk = [View(scr, 1024, 2048), pmk1]
    ssrow = View(rows, 0, 2 * NQC_F, rows=slice(0, 2))
    den2 = [View(rows, 0, NQC_F, rows=slice(0, 1)),
            View(rows, NQC_F, NQC_F, rows=slice(0, 1))]
    tsc = [tsc0, tsc0]
    avasm = qhat
    avnb = qfinL

    for dst, src in ((hs2, hs2_d), (rep2, rep2_d), (eye, eye_d), (prh, prh_d),
                     (qs_c, qs_d), (ks_c, ks_d), (B2c, B2_d),
                     (mA, mA_d), (mA0, mA0_d), (mB, mB_d),
                     (cq, cq_d), (sq, sq_d), (ck, ck_d), (sk, sk_d)):
        p.dma("scalar", dst[:, :], src.ap()[:, :])
    for i in range(NDC):
        p.op("gpsimd", lambda g, i=i: g.memset(ybf[i][:, 0:KLO - YLO + 1], 0.0), inc=gp)
        p.dma("gpsimd", ybf[i][:, KLO - YLO + 1:KLO - YB + 1],
              yh_d.ap()[i * 128:(i + 1) * 128, 0:YLO - YB])
        p.op("vector", lambda v, i=i: v.tensor_copy(ybf[i][:, KLO - YB + 1:], ybuf[i][:, :]),
             wait=p.dma_all() if i == 0 else (), inc=dv)
    ybf_done = p.at(dv)
    gp0 = p.at(gp)
    pb = [p.ps(f"f_pb{i}", [128, 512], F32, st_a) for i in range(4)]
    ps_a = [pb[0], pb[1]]
    ps_ss = lambda sz: pb[2][0:2, 0:sz]
    ps_rep = [pb[2], pb[3]]
    ps_pp = [pb[0], pb[1]]
    ps_s = [p.ps(f"f_pscr{i}", [128, 512], F32, st_a) for i in range(2)]
    ps_av = p.ps("f_pav", [65, 512], F32, st_a)
    ps_t = p.ps("f_pst", [128, 1024], BF16, st_a)

    QT = tiles(NQC_F)
    KT = tiles(NKC_F)

    def proj_norm_rope(hc, which, upto=99):
        isq = which == "q"
        rng = QT if isq else KT
        base_col = (KLO - QLO) if isq else 0
        wcol = slice(0, 128) if isq else slice(128, 256)
        nw_c = qs_c if isq else ks_c
        ctab, stab = (cq, sq) if isq else (ck, sk)
        dst = qfin if isq else kfin
        w = NQC_F if isq else NKC_F
        for ti, (off, sz) in enumerate(rng):
            par_i = ti % 2
            src_off = off + base_col
            m = mm_accum(p, ps_a[par_i][:, 0:sz],
                         [(wq_all[k][:, wcol],
                           ybf[k][:, src_off + 1:src_off + 1 + sz] if k < 4 else
                           ybf[k][:, src_off:src_off + sz])
                          for k in range(NDC)],
                         wait_first=[(dv, ybf_done), (gp, gp0)]
                         + p.dma_all() + acq(("pb", par_i)) + acq(("pb2", par_i)), inc=mm)
            v = p.op("vector", lambda vv, off=off, sz=sz, par_i=par_i: vv.tensor_copy(
                qraw[:, off:off + sz], ps_a[par_i][:, 0:sz]), wait=[(mm, m)], inc=dv)
            last[("pb", par_i)] = (dv, v)
            last[("pb2", par_i)] = (dv, v)
            if upto <= 0:
                continue
            a = p.op("scalar", lambda s, off=off, sz=sz, par_i=par_i: s.activation(
                sqb2[par_i][:, 0:sz], qraw[:, off:off + sz], AF.Square),
                wait=[(dv, v)] + acq(("sqb", par_i)), inc=ac)
            m2 = mm_accum(p, ps_ss(sz), [(hs2[:, :], sqb2[par_i][:, 0:sz])],
                          wait_first=[(ac, a)] + acq(("pb", 2)) + acq(("pb2", 2)), inc=mm)
            last[("sqb", par_i)] = (mm, m2)
            v2 = p.op("vector", lambda vv, off=off, sz=sz: vv.tensor_copy(
                ssrow[:, off:off + sz], ps_ss(sz)), wait=[(mm, m2)], inc=dv)
            last[("pb", 2)] = (dv, v2)
            last[("pb2", 2)] = (dv, v2)
        if upto <= 1:
            return p.at(dv)
        e24, w24 = p.const_col(1e-24)
        a = p.op("scalar", lambda s: s.activation(ssrow[:, 0:w], ssrow[:, 0:w], AF.Ln,
                                                  bias=e24[0:2]),
                 wait=[(dv, p.at(dv)), w24], inc=ac)
        rst = p.op("scalar", lambda s: s.activation(ssrow[:, 0:w], ssrow[:, 0:w], AF.Exp,
                                                    scale=-0.5), inc=ac)
        for ti, (off, sz) in enumerate(rng):
            par_i = ti % 2
            m = mm_accum(p, ps_rep[par_i][:, 0:sz], [(rep2[:, :], ssrow[:, off:off + sz])],
                         wait_first=[(ac, rst)] + acq(("pb", 2 + par_i)) + acq(("pb2", 2 + par_i)), inc=mm)
            v = p.op("vector", lambda vv, off=off, sz=sz, par_i=par_i: vv.scalar_tensor_tensor(
                qhat[:, off:off + sz], qraw[:, off:off + sz], nw_c[:, :],
                ps_rep[par_i][:, 0:sz], ALU.mult, ALU.mult), wait=[(mm, m)], inc=dv)
            last[("pb", 2 + par_i)] = (dv, v)
            last[("pb2", 2 + par_i)] = (dv, v)
        qhat_done = p.at(dv)
        if upto <= 2:
            return qhat_done
        for ti, (off, sz) in enumerate(rng):
            par_i = ti % 2
            m = mm_accum(p, ps_pp[par_i][:, 0:sz], [(prh[:, :], qhat[:, off:off + sz])],
                         wait_first=[(dv, qhat_done)] + acq(("pb", par_i)) + acq(("pb2", par_i)), inc=mm)
            v = p.op("vector", lambda vv, off=off, sz=sz, par_i=par_i: vv.tensor_mul(
                tsc[par_i][:, 0:sz], qhat[:, off:off + sz], ctab[:, off:off + sz]),
                wait=acq(("tsc", par_i)), inc=dv)
            v = p.op("vector", lambda vv, off=off, sz=sz, par_i=par_i: vv.tensor_mul(
                dst[:, off:off + sz], ps_pp[par_i][:, 0:sz], stab[:, off:off + sz]),
                wait=[(mm, m)], inc=dv)
            v = p.op("vector", lambda vv, off=off, sz=sz, par_i=par_i: vv.tensor_add(
                dst[:, off:off + sz], dst[:, off:off + sz], tsc[par_i][:, 0:sz]), inc=dv)
            last[("pb", par_i)] = (dv, v)
            last[("pb2", par_i)] = (dv, v)
            last[("tsc", par_i)] = (dv, v)
            if isq:
                v = p.op("vector", lambda vv, off=off, sz=sz: vv.tensor_scalar_mul(
                    qfinL[:, off:off + sz], dst[:, off:off + sz], B2c[:, :]),
                    wait=acq(("avnb",)) if ti == 0 else (), inc=dv)
        return p.at(dv)

    for hc in range(hc_count):
        Wqv = Wqkv_d.ap().rearrange("(k p) c -> p k c", p=128)
        wl = None
        for part, c0 in enumerate((hc * 128, D + hc * 128, 2 * D + hc * 128)):
            wl = p.dma("gpsimd",
                       bass.AP(wq_all3[part], 0, [[NDC * 128, 128], [128, NDC], [1, 128]]),
                       Wqv[:, :, c0:c0 + 128], wait=acq(("wq", 0, part)))
        if not no_proj:
            qv = proj_norm_rope(hc, "q", upto=proj_upto)
            kv = proj_norm_rope(hc, "k", upto=proj_upto)
        if no_v:
            continue
        gm = p.op("gpsimd", lambda g: g.memset(vsb[:, :], 1.0),
                  wait=acq(("vsb",)) + acq(("vsbdbg",)))
        gm_v = p.at(gp)
        for ti, (off, sz) in enumerate(KT):
            par_i = ti % 2
            m = mm_accum(p, ps_a[par_i][:, 0:sz],
                         [(wq_all[k][:, 256:384],
                           ybf[k][:, off + 1:off + 1 + sz] if k < 4 else
                           ybf[k][:, off:off + sz])
                          for k in range(NDC)],
                         wait_first=acq(("pb", par_i)) + acq(("pb2", par_i)), inc=mm)
            v = p.op("vector", lambda vv, sz=sz, par_i=par_i: vv.tensor_copy(
                vfm[par_i][:, 0:sz], ps_a[par_i][:, 0:sz]),
                wait=[(mm, m)] + acq(("vfm", par_i)), inc=dv)
            last[("pb2", par_i)] = (dv, v)
            last[("pb", par_i)] = (dv, v)
            for tl in range(sz // 128):
                tt = (off + tl * 128) // 128
                tm = p.op("tensor", lambda t, tl=tl, par_i=par_i: t.transpose(
                    ps_t[:, 0:128], vfm[par_i][:, tl * 128:(tl + 1) * 128], eye[:, :]),
                    wait=[(dv, v)] + acq(("pst",)), inc=mm)
                dst_ap = bass.AP(vsb, tt * 130, [[vsb.shape[1], 128], [65, 2], [1, 64]])
                src_ap = bass.AP(ps_t, 0, [[1024, 128], [64, 2], [1, 64]])
                v2 = p.op("vector", lambda vv, dst_ap=dst_ap, src_ap=src_ap: vv.tensor_copy(
                    dst_ap, src_ap), wait=[(mm, tm), (gp, gm_v)], inc=dv)
                last[("pst",)] = (dv, v2)
                last[("vfm", par_i)] = (mm, tm)
        vsb_done = p.at(dv)

        if debug and hc == 0:
            p.dma("scalar", dbg_q.ap()[:, :], qfin[:, :], wait=[(dv, p.at(dv))])
            p.dma("scalar", dbg_k.ap()[:, :], kfin[:, :])
            dvv = p.dma("scalar", dbg_v.ap()[:, :], vsb[:, :], wait=[(dv, vsb_done)])
            last[("vsbdbg",)] = dvv
        if no_attn:
            continue
        if no_attn:
            continue
        sset = [ps_s, [pb[2], pb[3]]]
        pexs = [pex_t, pex_t2]
        for hl in range(2):
            hb = slice(hl * 64, hl * 64 + 64)
            den = den2[hl]
            pend = None
            for bi in range(len(blocks) + 1):
                if bi < len(blocks):
                    tb, nq, nks = blocks[bi]
                    par = bi % 2
                    qcol = colq(tb)
                    banks = sset[par]
                    mypex = pexs[par]
                    a = None
                    for ks in range(nks):
                        bank = ks // 2
                        coloff = (ks % 2) * nq
                        kcol = colk(tb - 256 + ks * 128)
                        if bi == 0:
                            qsrc = qfinL if ks == 0 else qfin
                        else:
                            qsrc = qfinL if ks < 2 else qfin
                        bkey = ("pss2", par, bank)
                        sm = p.op("tensor", lambda t, banks=banks, bank=bank, coloff=coloff, kcol=kcol, qcol=qcol, nq=nq, qsrc=qsrc, hb=hb: t.matmul(
                            banks[bank][:, coloff:coloff + nq],
                            kfin[hb, kcol:kcol + 128], qsrc[hb, qcol:qcol + nq],
                            start=True, stop=True),
                            wait=(acq(bkey) if ks % 2 == 0 else []) +
                                 ([(dv, vsb_done)] if bi == 0 and hl == 0 and ks == 0 else []),
                            inc=mm if (ks % 2 == 1 or ks == nks - 1) else None)
                        if ks % 2 == 1 or ks == nks - 1:
                            wexp = (ks % 2 + 1) * nq
                            a = p.op("scalar", lambda s, banks=banks, bank=bank, wexp=wexp, nq=nq, mypex=mypex: s.activation(
                                mypex[:, bank * 2 * nq:bank * 2 * nq + wexp],
                                banks[bank][:, 0:wexp], AF.Exp, scale=c8[:, :]),
                                wait=[(mm, sm), w8] + (acq(("pexk", par)) if bank == 0 else []), inc=ac)
                            last[bkey] = (ac, a)
                    msk = mB if bi == 0 else (mA0 if bi == 1 else mA)
                    g = p.op("gpsimd", lambda gg, par=par, nq=nq, nks=nks, msk=msk, mypex=mypex: gg.tensor_mul(
                        pmk[par][:, 0:nq * nks], mypex[:, 0:nq * nks], msk[:, 0:nq * nks]),
                        wait=[(ac, a)] + acq(("pmk", par)), inc=gp)
                    last[("pexk", par)] = (gp, g)
                    cur = (bi, g, nq, nks, qcol, par)
                else:
                    cur = None
                if pend is not None:
                    bi0, g0, nq0, nks0, qcol0, par0 = pend
                    tb0 = blocks[bi0][0]
                    avm = None
                    for ks in range(nks0):
                        tt = (colk(tb0 - 256) + ks * 128) // 128
                        avm = p.op("tensor", lambda t, ks=ks, tt=tt, nq0=nq0, hl=hl, par0=par0, nks0=nks0: t.matmul(
                            ps_av[:, 0:nq0], vsb[:, tt * 130 + hl * 65: tt * 130 + hl * 65 + 65],
                            pmk[par0][:, ks * nq0:(ks + 1) * nq0],
                            start=(ks == 0), stop=(ks == nks0 - 1)),
                            wait=[(gp, g0)] + (acq(("pav",)) if ks == 0 else []),
                            inc=mm if ks == nks0 - 1 else None)
                    last[("pmk", par0)] = (mm, avm)
                    v1 = p.op("vector", lambda vv, qcol0=qcol0, nq0=nq0, hb=hb: vv.tensor_copy(
                        avasm[hb, qcol0:qcol0 + nq0], ps_av[0:64, 0:nq0]),
                        wait=[(mm, avm)] + acq(("avasm", hl)), inc=dv)
                    v2 = p.op("vector", lambda vv, qcol0=qcol0, nq0=nq0, den=den: vv.tensor_copy(
                        den[:, qcol0:qcol0 + nq0], ps_av[64:65, 0:nq0]),
                        wait=acq(("den", hl)) if bi0 == 0 else (), inc=dv)
                    last[("pav",)] = (dv, v2)
                pend = cur
        for hl in range(2):
            hb = slice(hl * 64, hl * 64 + 64)
            den = den2[hl]
            rv0 = p.op("scalar", lambda s, den=den: s.activation(den[:, :], den[:, :], AF.Ln),
                       wait=[(dv, p.at(dv))], inc=ac)
            rv = p.op("scalar", lambda s, den=den: s.activation(den[:, :], den[:, :],
                                                                AF.Exp, scale=-1.0), inc=ac)
            for (off, sz) in QT:
                par_i = (off // 512) % 2
                m = p.op("tensor", lambda t, off=off, sz=sz, par_i=par_i, den=den, hl=hl: t.matmul(
                    ps_rep[par_i][hl * 64:hl * 64 + 64, 0:sz], onesr[:, 0:64],
                    den[:, off:off + sz], start=True, stop=True),
                    wait=[(ac, rv), (gp, ones_gp)] + acq(("pb", 2 + par_i)) + acq(("pb2", 2 + par_i)), inc=mm)
                v = p.op("vector", lambda vv, off=off, sz=sz, par_i=par_i, hb=hb, hl=hl: vv.tensor_mul(
                    avnb[hb, off:off + sz], avasm[hb, off:off + sz],
                    ps_rep[par_i][hl * 64:hl * 64 + 64, 0:sz]),
                    wait=[(mm, m)], inc=dv)
                last[("pb", 2 + par_i)] = (dv, v)
                last[("pb2", 2 + par_i)] = (dv, v)
            last[("avasm", hl)] = (dv, p.at(dv))
            last[("den", hl)] = (mm, p.at(mm))
        sv = p.dma("scalar", avn_d.ap()[hc * 128:(hc + 1) * 128, :], avnb[:, :],
                   wait=[(dv, p.at(dv))])
        last[("avnb",)] = sv
        last[("vsb",)] = (mm, p.at(mm))
        for part in range(3):
            last[("wq", 0, part)] = (mm, p.at(mm))
    st_a.close()

    if stop == "A":
        fin = []
        for oc in range(NDC):
            fin.append(p.dma("sync", yout_d.ap()[oc * 128:(oc + 1) * 128, :],
                             ybuf[oc][:, YB:], wait=p.dma_all() if oc == 0 else ()))
        p.op("sync", lambda e: None, wait=p.dma_all())
        st_a.close()
        st.close()
        return
    st_b = ExitStack()
    avn_sb = [p.sb(f"f_av{i}", [128, NQC_F], BF16, st_b) for i in range(NDC)]
    wo_sb = [p.sb(f"f_wo{i}", [128, D], BF16, st_b) for i in range(NDC)]
    ps_o = [p.ps(f"f_pso{i}", [128, 512], F32, st_b) for i in range(2)]
    aw = p.dma_all()
    for i in range(NDC):
        p.dma("sync", avn_sb[i][:, :], avn_d.ap()[i * 128:(i + 1) * 128, :],
              wait=aw if i == 0 else ())
        p.dma("gpsimd", wo_sb[i][:, :], Wo_d.ap()[i * 128:(i + 1) * 128, :],
              wait=aw if i == 0 else ())
    a2_pairs = p.dma_all()
    MT = tiles(NQC_F - 64)
    for oc in range(NDC):
        for (off, sz) in MT:
            qcol = 64 + off
            ycol = coly(-64) + off
            par = (oc * len(MT) + (off // 512)) % 2
            m = mm_accum(p, ps_o[par][:, 0:sz],
                         [(wo_sb[k][:, oc * 128:(oc + 1) * 128], avn_sb[k][:, qcol:qcol + sz])
                          for k in range(NDC)],
                         wait_first=a2_pairs + acq(("pso", par)), inc=mm)
            v = p.op("vector", lambda vv, oc=oc, ycol=ycol, sz=sz, par=par: vv.scalar_tensor_tensor(
                ybuf[oc][:, ycol:ycol + sz], ps_o[par][:, 0:sz], 1.0,
                ybuf[oc][:, ycol:ycol + sz], ALU.mult, ALU.add),
                wait=[(mm, m)], inc=dv)
            last[("pso", par)] = (dv, v)
    ymid_done = p.at(dv)
    st_b.close()

    if stop == "A2":
        for oc in range(NDC):
            p.dma("sync", yout_d.ap()[oc * 128:(oc + 1) * 128, :], ybuf[oc][:, YB:],
                  wait=[(dv, ymid_done)] if oc == 0 else ())
        p.op("sync", lambda e: None, wait=p.dma_all())
        st.close()
        return
    st_c = ExitStack()
    xnb = [p.sb(f"f_xnb{i}", [128, T], BF16, st_c) for i in range(NDC)]
    lngb = p.sb("f_lngb", [128, 16], F32, st_c)
    ones_cb = p.sb("f_onescb", [128, 1], BF16, st_c)
    mean_r = p.sb("f_mean", [1, T], F32, st_c)
    msq_r = p.sb("f_msq", [1, T], F32, st_c)
    var_r = p.sb("f_var", [1, T], F32, st_c)
    mb_r = p.sb("f_mb", [1, T], F32, st_c)
    lngb_pair = p.dma("scalar", lngb[:, :], lngb_d.ap()[:, :], wait=[(dv, ymid_done)])
    p.op("gpsimd", lambda g: g.memset(ones_cb[:, :], 1.0), wait=[(dv, ymid_done)])
    ln_gp = p.at(gp)

    def xin_ap(buf_list, i, off, sz, lead):
        o = lead + off if i < 4 else lead + off - 1
        return buf_list[i][:, o:o + sz]

    NT = T // 512
    st_stats = ExitStack()
    ymb = [p.sb(f"f_ymb{i}", [128, T + 4], BF16, st_stats) for i in range(NDC)]
    sqf = [p.sb(f"f_sqf{i}", [128, T + 4], BF16, st_stats) for i in range(NDC)]
    ps_sum = p.ps("f_psum", [1, T], F32, st_stats)
    ps_sq = p.ps("f_psq", [1, T], F32, st_stats)
    for i in range(NDC):
        p.op("vector", lambda v, i=i: v.tensor_copy(ymb[i][:, :], ybuf[i][:, coly(-4):]),
             wait=[(dv, ymid_done)] if i == 0 else (), inc=dv)
        p.op("scalar", lambda s, i=i: s.activation(sqf[i][:, :], ybuf[i][:, coly(-4):],
                                                   AF.Square),
             wait=[(dv, ymid_done)], inc=ac)
    cast_done = p.at(dv)
    sq_done = p.at(ac)
    for i in range(NDC):
        for n in range(NT):
            cs = bass.ts(n, 512)
            p.op("tensor", lambda t, i=i, n=n, cs=cs: t.matmul(
                ps_sum[:, cs], ones_cb[:, :], xin_ap(ymb, i, n * 512, 512, 4),
                start=(i == 0), stop=(i == NDC - 1)),
                wait=[(dv, cast_done), (gp, ln_gp)] if i == 0 and n == 0 else ())
            p.op("tensor", lambda t, i=i, n=n, cs=cs: t.matmul(
                ps_sq[:, cs], ones_cb[:, :], xin_ap(sqf, i, n * 512, 512, 4),
                start=(i == 0), stop=(i == NDC - 1)),
                wait=[(ac, sq_done)] if i == 0 and n == 0 else (),
                inc=mm if i == NDC - 1 and n == NT - 1 else None)
    stats_mm = p.at(mm)
    p.op("vector", lambda v: v.tensor_scalar_mul(mean_r[:, :], ps_sum[:, :], 1.0 / D),
         wait=[(mm, stats_mm)])
    p.op("vector", lambda v: v.tensor_mul(msq_r[:, :], mean_r[:, :], mean_r[:, :]))
    p.op("vector", lambda v: v.scalar_tensor_tensor(
        var_r[:, :], ps_sq[:, :], 1.0 / D, msq_r[:, :], ALU.mult, ALU.subtract), inc=dv)
    epsc, epsw = p.const_col(1e-5)
    p.op("scalar", lambda s: s.activation(var_r[:, :], var_r[:, :], AF.Ln,
                                          bias=epsc[0:1]),
         wait=[(dv, p.at(dv)), epsw], inc=ac)
    a = p.op("scalar", lambda s: s.activation(var_r[:, :], var_r[:, :], AF.Exp,
                                              scale=-0.5), inc=ac)
    p.op("vector", lambda v: v.tensor_mul(mb_r[:, :], mean_r[:, :], var_r[:, :]),
         wait=[(ac, a)], inc=dv)
    rows_done = p.at(dv)
    st_stats.close()
    st_rep = ExitStack()
    ps_mrep = p.ps("f_pmrep", [128, T], F32, st_rep)
    ps_rrep = p.ps("f_prrep", [128, T], F32, st_rep)
    for n in range(NT):
        cs = bass.ts(n, 512)
        p.op("tensor", lambda t, cs=cs: t.matmul(
            ps_mrep[:, cs], onesr[:, :], mean_r[:, cs], start=True, stop=True),
            wait=[(dv, rows_done)] if n == 0 else ())
        p.op("tensor", lambda t, cs=cs: t.matmul(
            ps_rrep[:, cs], onesr[:, :], var_r[:, cs], start=True, stop=True),
            inc=mm if n == NT - 1 else None)
    rep_mm = p.at(mm)
    tmpf = p.sb("f_tmpf", [128, T], F32, st_rep)
    for i in range(NDC):
        p.op("vector", lambda v, i=i: v.tensor_sub(
            tmpf[:, :], xin_ap(ybuf, i, 0, T, YB), ps_mrep[:, :]),
            wait=[(mm, rep_mm), lngb_pair] if i == 0 else ())
        p.op("vector", lambda v, i=i: v.tensor_mul(tmpf[:, :], tmpf[:, :], ps_rrep[:, :]))
        p.op("vector", lambda v, i=i: v.tensor_scalar(
            xnb[i][:, :], tmpf[:, :], lngb[:, i:i + 1], lngb[:, 8 + i:9 + i],
            ALU.mult, ALU.add), inc=dv)
    xnb_done = p.at(dv)
    st_rep.close()

    ICS = tiles(2730, 128)
    st_f = ExitStack()
    w1a = [p.sb(f"f_w1a{i}", [128, 8 * 128], BF16, st_f) for i in range(2)]
    w1g = [p.sb(f"f_w1g{i}", [128, 8 * 128], BF16, st_f) for i in range(2)]
    w2sb2 = [p.sb(f"f_w2sb{i}", [128, 22 * 128], BF16, st_f) for i in range(2)]
    b1a = p.sb("f_b1a", [128, 22], F32, st_f)
    b1g = p.sb("f_b1g", [128, 22], F32, st_f)
    b2c = p.sb("f_b2c", [128, 8], F32, st_f)
    gel = p.sb("f_gel", [128, 512], BF16, st_f)
    prod = [p.sb(f"f_prod{i}", [128, 1024], BF16, st_f) for i in range(22)]
    yout_sb = p.sb("f_yout", [128, 512], F32, st_f)
    ps_1 = [p.ps(f"f_ps1{i}", [128, 512], F32, st_f) for i in range(4)]
    ps_o2 = [p.ps(f"f_pso2{i}", [128, 512], F32, st_f) for i in range(2)]
    p.dma("scalar", b1a[:, :], b1a_d.ap()[:, :], wait=[(dv, xnb_done)])
    p.dma("scalar", b1g[:, :], b1g_d.ap()[:, :])
    p.dma("scalar", b2c[:, :], b2_d.ap()[:, :])
    fb_pairs = p.dma_all()
    W1v = W1_d.ap().rearrange("(k p) c -> p k c", p=128)
    W2v = W2_d.ap()
    for nh in range(2):
        t0 = nh * 1024
        for ici, (ic0, sz) in enumerate(ICS):
            par_w = ici % 2
            wla = p.dma("gpsimd", bass.AP(w1a[par_w], 0, [[8 * 128, 128], [128, NDC], [1, sz]]),
                        W1v[:, :, ic0:ic0 + sz],
                        wait=acq(("w1a", par_w)) + ([(dv, xnb_done)] if nh == 0 and ici == 0 else []))
            wlg = p.dma("gpsimd", bass.AP(w1g[par_w], 0, [[8 * 128, 128], [128, NDC], [1, sz]]),
                        W1v[:, :, 2730 + ic0:2730 + ic0 + sz], wait=acq(("w1g", par_w)))
            pr1 = 2 * (ici % 2)
            for n2 in range(2):
                cs = slice(t0 + n2 * 512, t0 + n2 * 512 + 512)
                ma = mm_accum(p, ps_1[pr1][0:sz, :],
                              [(w1a[par_w][:, k * 128:k * 128 + sz], xnb[k][:, cs])
                               for k in range(NDC)],
                              wait_first=[wla, wlg, (dv, xnb_done)] + acq(("ps1", pr1)), inc=mm)
                mg = mm_accum(p, ps_1[pr1 + 1][0:sz, :],
                              [(w1g[par_w][:, k * 128:k * 128 + sz], xnb[k][:, cs])
                               for k in range(NDC)],
                              wait_first=acq(("ps1", pr1 + 1)), inc=mm)
                ga = p.op("scalar", lambda s, sz=sz, ici=ici, pr1=pr1: s.activation(
                    gel[0:sz, :], ps_1[pr1 + 1][0:sz, :], AF.Gelu, bias=b1g[0:sz, ici:ici + 1]),
                    wait=[(mm, mg)] + fb_pairs + acq(("gel",)), inc=ac)
                last[("ps1", pr1 + 1)] = (ac, ga)
                pv = p.op("vector", lambda v, sz=sz, ici=ici, n2=n2, pr1=pr1: v.scalar_tensor_tensor(
                    prod[ici][0:sz, n2 * 512:(n2 + 1) * 512], ps_1[pr1][0:sz, :],
                    b1a[0:sz, ici:ici + 1], gel[0:sz, :], ALU.add, ALU.mult),
                    wait=[(mm, ma), (ac, ga)] + acq(("prod", ici)), inc=dv)
                last[("ps1", pr1)] = (dv, pv)
                last[("gel",)] = (dv, pv)
            last[("w1a", par_w)] = (mm, p.at(mm))
            last[("w1g", par_w)] = (mm, p.at(mm))
        prod_done = p.at(dv)
        for oc in range(NDC):
            po = oc % 2
            w2sb = w2sb2[po]
            W2r = W2_d.ap()[0:2688, :].rearrange("(ic p) c -> p ic c", p=128)
            wl2a = p.dma("gpsimd",
                         bass.AP(w2sb, 0, [[22 * 128, 128], [128, 11], [1, 128]]),
                         W2r[:, 0:11, oc * 128:(oc + 1) * 128],
                         wait=acq(("w2", po)) + ([(dv, xnb_done)] if nh == 0 and oc == 0 else []))
            wl2a2 = p.dma("gpsimd",
                          bass.AP(w2sb, 11 * 128, [[22 * 128, 128], [128, 10], [1, 128]]),
                          W2r[:, 11:21, oc * 128:(oc + 1) * 128],
                          wait=acq(("w2", po)))
            wl2b = p.dma("gpsimd", w2sb[0:42, 21 * 128:21 * 128 + 128],
                         W2_d.ap()[2688:2730, oc * 128:(oc + 1) * 128],
                         wait=acq(("w2", po)))
            w2pairs = [wl2a, wl2a2, wl2b]
            for n2 in range(2):
                cs = slice(t0 + n2 * 512, t0 + n2 * 512 + 512)
                par = n2
                m = mm_accum(p, ps_o2[par][:, :],
                             [(w2sb[0:sz, ici * 128:ici * 128 + 128],
                               prod[ici][0:sz, n2 * 512:(n2 + 1) * 512])
                              for ici, (ic0, sz) in enumerate(ICS)],
                             wait_first=w2pairs + [(dv, prod_done)] + acq(("pso2", par)),
                             inc=mm)
                v = p.op("vector", lambda vv, oc=oc, cs=cs, par=par: vv.scalar_tensor_tensor(
                    yout_sb[:, :], ps_o2[par][:, :], b2c[:, oc:oc + 1],
                    ybuf[oc][:, YB + cs.start:YB + cs.stop], ALU.add, ALU.add),
                    wait=[(mm, m)] + acq(("yout",)), inc=dv)
                last[("pso2", par)] = (dv, v)
                sv = p.dma("sync", yout_d.ap()[oc * 128:(oc + 1) * 128, cs], yout_sb[:, :],
                           wait=[(dv, v)])
                last[("yout",)] = sv
            last[("w2", po)] = (mm, p.at(mm))
        for ici in range(22):
            last[("prod", ici)] = (mm, p.at(mm))
    p.op("sync", lambda e: None, wait=p.dma_all())
    st_f.close()
    st_c.close()
    st.close()


def build_ffn_layer(debug=False, stop=None, **kw):
    p = P("ffn")
    emit_ffn_layer(p, debug=debug, stop=stop, **kw)
    return p.finalize()



def shard_tokens(x):
    out = []
    for c in range(NCORES):
        b, j = divmod(c, 4)
        sl = x[b, j * T:(j + 1) * T, :]
        out.append(np.ascontiguousarray(sl.T))
    return out




def emit_comb1(p):
    xT_d = p.dram("xT", [D, T], F32, kind="ExternalInput")
    s1_d = p.dram("s1", [D, T], F32, kind="ExternalInput")
    y_d = p.dram("yf", [D, T], F32, kind="ExternalInput")
    x2_d = p.dram("x2", [D, T], F32, kind="ExternalOutput")
    xn2_d = p.dram("xn2", [D, T], F32, kind="ExternalOutput")
    xn2b_d = p.dram("xn2b", [D, T], BF16, kind="ExternalOutput")
    mm, dv, ac = p.mm, p.dv, p.ac
    st = ExitStack()
    xa = p.sb("c1_x", [128, T], F32, st)
    sa = p.sb("c1_s", [128, T], F32, st)
    ya = p.sb("c1_y", [128, T], F32, st)
    ga = p.sb("c1_g", [128, T], F32, st)
    last = {}

    def acq(key):
        v = last.get(key)
        return [] if v is None else (list(v) if isinstance(v, list) else [v])

    for i in range(NDC):
        lx = p.dma("sync", xa[:, :], xT_d.ap()[i * 128:(i + 1) * 128, :],
                   wait=acq(("bufs",)))
        ls = p.dma("sync", sa[:, :], s1_d.ap()[i * 128:(i + 1) * 128, :])
        ly = p.dma("sync", ya[:, :], y_d.ap()[i * 128:(i + 1) * 128, :])
        a = p.op("scalar", lambda s: s.activation(ga[:, :], ya[:, :], AF.Silu),
                 wait=[ly] + acq(("ga",)), inc=ac)
        v = p.op("vector", lambda vv: vv.tensor_mul(ga[:, :], ga[:, :], sa[:, :]),
                 wait=[(ac, a), ls], inc=dv)
        v = p.op("vector", lambda vv: vv.tensor_add(xa[:, :], xa[:, :], ga[:, :]),
                 wait=[lx], inc=dv)
        sv = p.dma("scalar", x2_d.ap()[i * 128:(i + 1) * 128, :], xa[:, :],
                   wait=[(dv, v)])
        last[("bufs",)] = sv
        last[("ga",)] = (dv, v)
    x2_done = p.dma_all()
    st.close()
    emit_ln_fm(p, x2_d.ap(), xn2_d.ap(), xn2b_d.ap(), tag="ln2", start_wait=x2_done)


def build_s11():
    p = P("s11")
    emit_comb1(p)
    return p.finalize()



NCH = T // 128


def emit_qkv1(p):
    xnb_d = p.dram("xn2b", [D, T], BF16, kind="ExternalInput")
    wqT_d = p.dram("wqT", [D, D], F32, kind="ExternalInput")
    wkT_d = p.dram("wkT", [D, D], F32, kind="ExternalInput")
    wvT_d = p.dram("wvT", [D, D], F32, kind="ExternalInput")
    eye_d = p.dram("eye", [128, 128], BF16, kind="ExternalInput")
    sq_d = p.dram("sq", [D, T], BF16, kind="ExternalOutput")
    sk_d = p.dram("sk", [D, T], BF16, kind="ExternalOutput")
    v_d = p.dram("v", [T, D], BF16, kind="ExternalOutput")
    snap_d = p.dram("snap", [128, 8 * NCH * 65], BF16, kind="ExternalOutput")
    sqs_d = p.dram("sqs", [128, 8], F32, kind="ExternalOutput")
    mm, dv, ac = p.mm, p.dv, p.ac
    st = ExitStack()
    xnb = [p.sb(f"q1_x{i}", [128, T], BF16, st) for i in range(NDC)]
    w_sb = [p.sb(f"q1_w{i}", [128, 128], BF16, st) for i in range(NDC)]
    eye1 = p.sb("q1_eye", [128, 128], BF16, st)
    raw = p.sb("q1_raw", [128, T], F32, st)
    e1b = p.sb("q1_e1", [128, T], F32, st)
    fout = p.sb("q1_f", [128, T], BF16, st)
    ktm = p.sb("q1_ktm", [128, T], BF16, st)
    vtm = p.sb("q1_vtm", [128, T], BF16, st)
    ones_c = p.sb("q1_ones", [128, 1], BF16, st)
    sqs = p.sb("q1_sqs", [128, 8], F32, st)
    snapb = p.sb("q1_snap", [128, NCH * 65], BF16, st)
    ps_a = [p.ps(f"q1_pa{i}", [128, 512], F32, st) for i in range(2)]
    ps_t = p.ps("q1_pt", [128, 1024], BF16, st)
    ps_st = [p.ps(f"q1_ps{i}", [128, 512], F32, st) for i in range(2)]
    last = {}

    def acq(key):
        v = last.get(key)
        return [] if v is None else (list(v) if isinstance(v, list) else [v])

    p.op("gpsimd", lambda g: g.memset(ones_c[:, :], 1.0))
    gp_ones = p.at(p.gp)
    for i in range(NDC):
        p.dma("sync", xnb[i][:, :], xnb_d.ap()[i * 128:(i + 1) * 128, :])
    p.dma("scalar", eye1[:, :], eye_d.ap()[:, :])
    x_pairs = p.dma_all()
    NT = T // 512

    def proj(hp, which, wT_d, store_d):
        for i in range(NDC):
            p.dma("gpsimd", w_sb[i][:, :],
                  wT_d.ap()[i * 128:(i + 1) * 128, hp * 128:(hp + 1) * 128],
                  wait=acq(("w", i)))
        wp = p.dma_all()
        for n in range(NT):
            cs = bass.ts(n, 512)
            par = n % 2
            m = mm_accum(p, ps_a[par][:, :],
                         [(w_sb[k][:, :], xnb[k][:, cs]) for k in range(NDC)],
                         wait_first=x_pairs + wp + acq(("pa", par)), inc=mm)
            v = p.op("vector", lambda vv, cs=cs, par=par: vv.tensor_copy(
                raw[:, cs], ps_a[par][:, :]), wait=[(mm, m)] + acq(("raw",)), inc=dv)
            last[("pa", par)] = (dv, v)
        raw_done = p.at(dv)
        if which == "v":
            cb = p.op("scalar", lambda s: s.copy(fout[:, :], raw[:, :]),
                      wait=[(dv, raw_done)] + acq(("fout",)), inc=ac)
            fin = (ac, cb)
        else:
            v = p.op("vector", lambda vv: vv.tensor_scalar_min(e1b[:, :], raw[:, :], 0.0),
                     wait=[(dv, raw_done)] + acq(("e1",)), inc=dv)
            a = p.op("scalar", lambda s: s.activation(e1b[:, :], e1b[:, :], AF.Exp),
                     wait=[(dv, v)], inc=ac)
            v = p.op("vector", lambda vv: vv.scalar_tensor_tensor(
                fout[:, :], raw[:, :], 1.0, e1b[:, :], ALU.add, ALU.max),
                wait=[(ac, a)] + acq(("fout",)), inc=dv)
            last[("e1",)] = (ac, a) if False else (dv, v)
            fin = (dv, v)
        sv = p.dma("scalar", store_d.ap()[hp * 128:(hp + 1) * 128, :], fout[:, :],
                   wait=[fin])
        last[("raw",)] = fin
        return fin, sv

    for hp in range(NDC):
        fin, sv = proj(hp, "q", wqT_d, sq_d)
        r = p.op("vector", lambda vv, hp=hp: vv.reduce_sum(
            sqs[:, hp:hp + 1], fout[:, :], axis=AX.X), inc=dv)
        last[("fout",)] = [sv, (dv, r)]
        for i in range(NDC):
            last[("w", i)] = (mm, p.at(mm))
    p.dma("scalar", sqs_d.ap()[:, :], sqs[:, :], wait=[(dv, p.at(dv))])

    for hp in range(NDC):
        fin, sv = proj(hp, "k", wkT_d, sk_d)
        last[("fout",)] = [sv]
        for c in range(NCH):
            tmv = p.op("tensor", lambda t, c=c: t.transpose(
                ps_t[:, 0:128], fout[:, c * 128:(c + 1) * 128], eye1[:, :]),
                wait=acq(("pt",)) + ([fin] if c == 0 else []), inc=mm)
            v = p.op("vector", lambda vv, c=c: vv.tensor_copy(
                ktm[:, c * 128:(c + 1) * 128], ps_t[:, 0:128]),
                wait=[(mm, tmv)] + (acq(("ktm",)) if c == 0 else []), inc=dv)
            last[("pt",)] = (dv, v)
        last[("fout",)] = [(mm, p.at(mm))]
        for i in range(NDC):
            last[("w", i)] = (mm, p.at(mm))
        ktm_done = p.at(dv)
        for c in range(NCH):
            par = c % 2
            m = mm_accum(p, ps_a[par][:, 0:128],
                         [(xnb[k][:, c * 128:(c + 1) * 128],
                           w_sb2_cols(w_sb, k)) for k in range(NDC)],
                         wait_first=acq(("pa", par)) + (vw_pairs(p, wvT_d, w_sb, hp, acq) if c == 0 else []),
                         inc=mm)
            v = p.op("vector", lambda vv, c=c, par=par: vv.tensor_copy(
                vtm[:, c * 128:(c + 1) * 128], ps_a[par][:, 0:128]),
                wait=[(mm, m)] + (acq(("vtm",)) if c == 0 else []), inc=dv)
            last[("pa", par)] = (dv, v)
            sv = p.dma("scalar", v_d.ap()[c * 128:(c + 1) * 128, hp * 128:(hp + 1) * 128],
                       vtm[:, c * 128:(c + 1) * 128], wait=[(dv, v)])
            last[("vst", c)] = sv
        vtm_done = p.at(dv)
        for hl in range(2):
            hb = slice(hl * 64, hl * 64 + 64)
            for c in range(NCH):
                kv = p.op("tensor", lambda t, c=c, hl=hl, hb=hb: t.matmul(
                    ps_st[hl][0:64, 0:64], ktm[:, c * 128 + hl * 64:c * 128 + hl * 64 + 64],
                    vtm[:, c * 128 + hl * 64:c * 128 + hl * 64 + 64],
                    start=(c == 0), stop=False),
                    wait=([(dv, vtm_done), (p.gp, gp_ones)] if c == 0 else []) +
                         acq(("pst", hl)))
                ks = p.op("tensor", lambda t, c=c, hl=hl: t.matmul(
                    ps_st[hl][0:64, 64:65], ktm[:, c * 128 + hl * 64:c * 128 + hl * 64 + 64],
                    ones_c[:, :], start=False, stop=False), inc=mm)
                v = p.op("vector", lambda vv, c=c, hl=hl, hb=hb: vv.tensor_copy(
                    snapb[hb, c * 65:(c + 1) * 65], ps_st[hl][0:64, 0:65]),
                    wait=[(mm, p.at(mm))] + (acq(("snapb",)) if c == 0 and hl == 0 else []),
                    inc=dv)
                last[("pst", hl)] = (dv, v)
        sv = p.dma("scalar", snap_d.ap()[:, hp * NCH * 65:(hp + 1) * NCH * 65],
                   snapb[:, :], wait=[(dv, p.at(dv))])
        last[("snapb",)] = sv
        last[("ktm",)] = (mm, p.at(mm))
        last[("vtm",)] = (mm, p.at(mm))
        for i in range(NDC):
            last[("w", i)] = (mm, p.at(mm))
    p.op("sync", lambda e: None, wait=p.dma_all())
    st.close()


def w_sb2_cols(w_sb, k):
    return w_sb[k][:, :]


def vw_pairs(p, wvT_d, w_sb, hp, acq):
    for i in range(NDC):
        p.dma("gpsimd", w_sb[i][:, :],
              wvT_d.ap()[i * 128:(i + 1) * 128, hp * 128:(hp + 1) * 128],
              wait=acq(("w", i)))
    return p.dma_all()


def build_s12():
    p = P("s12")
    emit_qkv1(p)
    return p.finalize()




def emit_attn1(p):
    sq_dd = p.dram("sq", [D, T], BF16, kind="ExternalInput")
    sk_dd = p.dram("sk", [D, T], BF16, kind="ExternalInput")
    v_dd = p.dram("v", [T, D], BF16, kind="ExternalInput")
    snap_dd = p.dram("snap", [128, 8 * NCH * 65], BF16, kind="ExternalInput")
    totA_d = p.dram("totA", [4 * 128, 8 * 65], BF16, kind="ExternalInput")
    sqsA_d = p.dram("sqsA", [4 * 128, 8], F32, kind="ExternalInput")
    pm_d = p.dram("pmask", [128, 4], F32, kind="ExternalInput")
    mnT_d = p.dram("mnT", [128, 64], F32, kind="ExternalInput")
    memd_d = p.dram("memd", [128, 2048], F32, kind="ExternalInput")
    sel4_d = p.dram("sel4", [4, 512], F32, kind="ExternalInput")
    ltri_d = p.dram("ltri", [128, 128], BF16, kind="ExternalInput")
    gate_d = p.dram("gate", [1, 16], F32, kind="ExternalInput")
    woT_d = p.dram("woT", [D, D], F32, kind="ExternalInput")
    comb_d = p.dram("comb", [D, T], BF16)
    s2_d = p.dram("s2", [D, T], F32, kind="ExternalOutput")
    mm, dv, ac = p.mm, p.dv, p.ac
    st = ExitStack()
    sqc = p.sb("a1_sq", [128, T], BF16, st)
    skc = p.sb("a1_sk", [128, T], BF16, st)
    vtm = p.sb("a1_v", [128, T], BF16, st)
    snapP = p.sb("a1_snapP", [128, (NCH + 1) * 65], BF16, st)
    snapo = p.sb("a1_snapo", [128, NCH * 65], F32, st)
    pref = p.sb("a1_pref", [128, 8 * 65], F32, st)
    totb = p.sb("a1_tot", [128, 8 * 65], F32, st)
    pmc = p.sb("a1_pm", [128, 4], F32, st)
    mnT = p.sb("a1_mnT", [128, 64], BF16, st)
    memd = p.sb("a1_memd", [128, 2048], BF16, st)
    sel4 = p.sb("a1_sel4", [4, 512], F32, st)
    ltri = p.sb("a1_ltri", [128, 128], BF16, st)
    grow = p.sb("a1_grow", [1, 16], F32, st)
    gcols = p.sb("a1_gcols", [64, 16], F32, st)
    onesr = p.sb("a1_onesr", [1, 128], F32, st)
    ones_c = p.sb("a1_ones", [128, 1], BF16, st)
    tsqs = p.sb("a1_tsqs", [128, 8], F32, st)
    tsqsb = p.sb("a1_tsqsb", [128, 8], BF16, st)
    rel16 = p.sb("a1_rel16", [4, 16], F32, st)
    w16 = p.sb("a1_w16", [16, 4], F32, st)
    w4 = p.sb("a1_w4", [4, 16], F32, st)
    Asb = p.sb("a1_A", [128, 128], BF16, st)
    vaug = p.sb("a1_vaug", [128, NCH * 130], BF16, st)
    locasm = [p.sb(f"a1_loc{i}", [64, T], BF16, st) for i in range(2)]
    locn = [p.sb(f"a1_locn{i}", [64, T], BF16, st) for i in range(2)]
    rows = p.sb("a1_rows", [2, 2 * T], F32, st)
    den2 = [View(rows, 0, T, rows=slice(0, 1)), View(rows, T, T, rows=slice(0, 1))]
    srows = p.sb("a1_srows", [4, 512], F32, st)
    sqm = p.sb("a1_sqm", [128, 512], BF16, st)
    tmpc = p.sb("a1_tmpc", [64, 512], F32, st)
    combt = p.sb("a1_combt", [64, 512], BF16, st)
    ps_A = [p.ps(f"a1_pA{i}", [128, 512], F32, st) for i in range(2)]
    ps_o = [p.ps(f"a1_po{i}", [65, 512], F32, st) for i in range(2)]
    ps_r = p.ps("a1_pr", [128, 512], F32, st)
    ps_m = p.ps("a1_pm2", [128, 512], F32, st)
    ps_t4 = p.ps("a1_pt4", [16, 512], F32, st)
    last = {}

    def acq(key):
        v = last.get(key)
        return [] if v is None else (list(v) if isinstance(v, list) else [v])

    p.op("gpsimd", lambda g: g.memset(onesr[:, :], 1.0))
    p.op("gpsimd", lambda g: g.memset(ones_c[:, :], 1.0))
    gp0 = p.at(p.gp)
    for dst, sd in ((pmc, pm_d), (mnT, mnT_d), (memd, memd_d), (sel4, sel4_d),
                    (ltri, ltri_d), (grow, gate_d)):
        p.dma("scalar" if dst.dtype == F32 else "gpsimd", dst[:, :], sd.ap()[:, :])
    c_pairs = p.dma_all()

    scr_d = p.dram("a1scr", [16, 4], F32)
    l0 = p.dma("sync", tsqs[:, :], sqsA_d.ap()[0:128, :])
    for j in range(1, 4):
        lj = p.dma("sync", totb[:, 0:8], sqsA_d.ap()[j * 128:(j + 1) * 128, :], wait=acq(("totb",)))
        acc = p.op("vector", lambda vv: vv.tensor_add(tsqs[:, :], tsqs[:, :], totb[:, 0:8]),
                   wait=[l0, lj], inc=dv)
        last[("totb",)] = (dv, acc)
    vtb = p.op("vector", lambda vv: vv.tensor_copy(tsqsb[:, :], tsqs[:, :]),
               wait=[(dv, p.at(dv))], inc=dv)
    relmm = None
    for h in range(16):
        hl, hp = h % 2, h // 2
        hb = slice(hl * 64, hl * 64 + 64)
        relmm = p.op("tensor", lambda t, h=h, hb=hb, hp=hp: t.matmul(
            ps_t4[0:4, h:h + 1], mnT[hb, 4 * h:4 * h + 4], tsqsb[hb, hp:hp + 1],
            start=True, stop=True),
            wait=([(dv, p.at(dv))] + c_pairs) if h == 0 else (), inc=mm)
    v1 = p.op("vector", lambda vv: vv.tensor_copy(rel16[:, :], ps_t4[0:4, 0:16]),
              wait=[(mm, relmm)], inc=dv)
    d1 = p.dma("scalar", scr_d.ap().rearrange("h m -> m h"), rel16[:, :], wait=[(dv, v1)],
               allow_slow_non_contiguous=True)
    d2 = p.dma("sync", w16[:, :], scr_d.ap()[:, :], wait=[d1])
    csc, wsc = p.const_col(1.0 / 8192.0)
    vmx = p.op("vector", lambda vv: vv.reduce_max(w4[0:1, 0:1] if False else rows[0:1, 0:16].t if False else tsqs[0:16, 0:1] if False else w16[:, 0:4].t if False else w16[:, :], w16[:, :], axis=AX.X), wait=[d2]) if False else None
    mxc = p.sb("a1_mx", [16, 1], F32, st)
    vmx = p.op("vector", lambda vv: vv.reduce_max(mxc[:, :], w16[:, :], axis=AX.X),
               wait=[d2], inc=dv)
    vsb_ = p.op("vector", lambda vv: vv.tensor_scalar(
        w16[:, :], w16[:, :], mxc[:, :], None, ALU.subtract), inc=dv)
    aex = p.op("scalar", lambda s: s.activation(w16[:, :], w16[:, :], AF.Exp,
                                                scale=csc[0:16]),
               wait=[(dv, vsb_), wsc], inc=ac)
    vsm = p.op("vector", lambda vv: vv.reduce_sum(mxc[:, :], w16[:, :], axis=AX.X),
               wait=[(ac, aex)], inc=dv)
    vrc = p.op("vector", lambda vv: vv.reciprocal(mxc[:, :], mxc[:, :]), inc=dv)
    vw = p.op("vector", lambda vv: vv.tensor_scalar(
        w16[:, :], w16[:, :], mxc[:, :], None, ALU.mult), inc=dv)
    d3 = p.dma("scalar", scr_d.ap()[:, :], w16[:, :], wait=[(dv, vw)])
    d4 = p.dma("sync", w4[:, :], scr_d.ap().rearrange("h m -> m h"), wait=[d3],
               allow_slow_non_contiguous=True)
    ag = p.op("scalar", lambda s: s.activation(grow[:, :], grow[:, :], AF.Sigmoid),
              wait=c_pairs, inc=ac)
    gmm = None
    for h in range(16):
        gmm = p.op("tensor", lambda t, h=h: t.matmul(
            ps_m[0:64, h:h + 1], onesr[:, 0:64], grow[:, h:h + 1],
            start=True, stop=True),
            wait=[(ac, ag), (p.gp, gp0)] if h == 0 else (), inc=mm)
    vg = p.op("vector", lambda vv: vv.tensor_copy(gcols[:, :], ps_m[0:64, 0:16]),
              wait=[(mm, gmm)], inc=dv)
    gc_done = (dv, vg)

    for hp in range(NDC):
        lq = p.dma("sync", sqc[:, :], sq_dd.ap()[hp * 128:(hp + 1) * 128, :],
                   wait=acq(("bufs",)))
        lk = p.dma("sync", skc[:, :], sk_dd.ap()[hp * 128:(hp + 1) * 128, :])
        lvs = [p.dma("sync", vtm[:, c * 128:(c + 1) * 128],
                     v_dd.ap()[c * 128:(c + 1) * 128, hp * 128:(hp + 1) * 128])
               for c in range(NCH)]
        lsn = p.dma("gpsimd", snapo[:, :],
                    snap_dd.ap()[:, hp * NCH * 65:(hp + 1) * NCH * 65],
                    wait=acq(("snapo",)))
        vpre = None
        for j in range(4):
            lj = p.dma("gpsimd", totb[:, :], totA_d.ap()[j * 128:(j + 1) * 128, :],
                       wait=acq(("totb",)))
            if j == 0:
                vpre = p.op("vector", lambda vv: vv.tensor_scalar(
                    pref[:, :], totb[:, :], pmc[:, 0:1], None, ALU.mult),
                    wait=[lj] + c_pairs + acq(("pref",)), inc=dv)
            else:
                vpre = p.op("vector", lambda vv, j=j: vv.scalar_tensor_tensor(
                    pref[:, :], totb[:, :], pmc[:, j:j + 1], pref[:, :],
                    ALU.mult, ALU.add), wait=[lj], inc=dv)
            last[("totb",)] = (dv, vpre)
        hp_pairs = p.dma_all()
        gva = p.op("gpsimd", lambda g: g.memset(vaug[:, :], 1.0),
                   wait=acq(("vaug",)))
        vva = p.op("vector", lambda vv: vv.tensor_copy(
            bass.AP(vaug, 0, [[NCH * 130, 128], [130, NCH], [65, 2], [1, 64]]),
            bass.AP(vtm.handle if hasattr(vtm, 'handle') else vtm, 0,
                    [[T, 128], [128, NCH], [64, 2], [1, 64]])),
            wait=hp_pairs + [(p.gp, p.at(p.gp))], inc=dv)
        vsp = p.op("vector", lambda vv, hp=hp: vv.tensor_copy(
            snapP[:, 0:65], pref[:, hp * 65:(hp + 1) * 65]),
            wait=[(dv, vpre)] + acq(("snapP",)), inc=dv)
        for c in range(NCH):
            vsp = p.op("vector", lambda vv, c=c, hp=hp: vv.scalar_tensor_tensor(
                snapP[:, (c + 1) * 65:(c + 2) * 65], snapo[:, c * 65:(c + 1) * 65],
                1.0, pref[:, hp * 65:(hp + 1) * 65], ALU.mult, ALU.add),
                wait=[lsn] if c == 0 else (), inc=dv)
        snp_done = p.at(dv)
        last[("pref",)] = (dv, snp_done)
        last[("snapo",)] = (dv, snp_done)
        for hl in range(2):
            h = 2 * hp + hl
            hb = slice(hl * 64, hl * 64 + 64)
            den = den2[hl]
            for c in range(NCH):
                parA = c % 2
                am = p.op("tensor", lambda t, c=c, hb=hb, parA=parA: t.matmul(
                    ps_A[parA][:, 0:128], skc[hb, c * 128:(c + 1) * 128],
                    sqc[hb, c * 128:(c + 1) * 128], start=True, stop=True),
                    wait=(hp_pairs + [(dv, snp_done)] if c == 0 and hl == 0 else []) +
                         acq(("pA", parA)), inc=mm)
                va = p.op("vector", lambda vv, parA=parA: vv.tensor_mul(
                    Asb[:, :], ps_A[parA][:, 0:128], ltri[:, :]),
                    wait=[(mm, am)] + acq(("Asb",)), inc=dv)
                last[("pA", parA)] = (dv, va)
                paro = c % 2
                m1 = p.op("tensor", lambda t, c=c, hl=hl, paro=paro: t.matmul(
                    ps_o[paro][0:65, 0:128], vaug[:, c * 130 + hl * 65:c * 130 + hl * 65 + 65],
                    Asb[:, :], start=True, stop=False),
                    wait=[(dv, va)] + acq(("po", paro)))
                m3 = p.op("tensor", lambda t, c=c, hb=hb, paro=paro: t.matmul(
                    ps_o[paro][0:65, 0:128], snapP[hb, c * 65:(c + 1) * 65],
                    sqc[hb, c * 128:(c + 1) * 128], start=False, stop=True), inc=mm)
                last[("Asb",)] = (mm, m3)
                vo = p.op("vector", lambda vv, c=c, hl=hl, paro=paro: vv.tensor_copy(
                    locasm[hl][:, c * 128:(c + 1) * 128], ps_o[paro][0:64, 0:128]),
                    wait=[(mm, m3)] + (acq(("locasm", hl)) if c == 0 else []), inc=dv)
                vd = p.op("vector", lambda vv, c=c, den=den, paro=paro: vv.tensor_copy(
                    den[:, c * 128:(c + 1) * 128], ps_o[paro][64:65, 0:128]),
                    wait=acq(("den", hl)) if c == 0 else (), inc=dv)
                last[("po", paro)] = (dv, vd)
            vc1 = p.op("vector", lambda vv, den=den: vv.tensor_scalar_max(
                den[:, :], den[:, :], 1e-6), inc=dv)
            p.op("scalar", lambda s, den=den: s.activation(den[:, :], den[:, :], AF.Ln),
                 wait=[(dv, vc1)], inc=ac)
            vc2 = p.op("scalar", lambda s, den=den: s.activation(den[:, :], den[:, :],
                                                                 AF.Exp, scale=-1.0), inc=ac)
            for nt in range(4):
                cs = slice(nt * 512, nt * 512 + 512)
                rm = p.op("tensor", lambda t, cs=cs, den=den: t.matmul(
                    ps_r[0:64, :], onesr[:, 0:64], den[:, cs], start=True, stop=True),
                    wait=[(ac, vc2)] + acq(("pr",)), inc=mm)
                vn = p.op("vector", lambda vv, cs=cs, hl=hl: vv.tensor_mul(
                    locn[hl][:, cs], locasm[hl][:, cs], ps_r[0:64, :]),
                    wait=[(mm, rm)] + (acq(("locn", hl)) if nt == 0 else []), inc=dv)
                last[("pr",)] = (dv, vn)
            last[("locasm", hl)] = (dv, p.at(dv))
            last[("den", hl)] = (mm, p.at(mm))
            for nt in range(4):
                cs = slice(nt * 512, nt * 512 + 512)
                rm = p.op("tensor", lambda t, cs=cs, hb=hb, h=h: t.matmul(
                    ps_t4[0:4, 0:512], mnT[hb, 4 * h:4 * h + 4], sqc[hb, cs],
                    start=True, stop=True),
                    wait=acq(("pt4",)) + [(dv, p.at(dv))] if nt == 0 else acq(("pt4",)),
                    inc=mm)
                vs1 = p.op("vector", lambda vv: vv.tensor_scalar_max(
                    srows[:, :], ps_t4[0:4, 0:512], 1e-6),
                    wait=[(mm, rm)] + acq(("srows",)), inc=dv)
                last[("pt4",)] = (dv, vs1)
                p.op("scalar", lambda s: s.activation(srows[:, :], srows[:, :], AF.Ln),
                     wait=[(dv, vs1)], inc=ac)
                vs2a = p.op("scalar", lambda s: s.activation(srows[:, :], srows[:, :],
                                                             AF.Exp, scale=-1.0), inc=ac)
                vs3 = p.op("vector", lambda vv, h=h: vv.tensor_scalar(
                    srows[:, :], srows[:, :], w4[:, h:h + 1], None, ALU.mult),
                    wait=[d4, (ac, vs2a)], inc=dv)
                mlast = None
                for m in range(4):
                    sm = p.op("tensor", lambda t, m=m: t.matmul(
                        ps_r[:, :], sel4[:, m * 128:(m + 1) * 128], srows[:, :],
                        start=True, stop=True),
                        wait=[(dv, vs3)] + acq(("pr",)), inc=mm)
                    vq = p.op("vector", lambda vv, cs=cs, hb=hb: vv.tensor_mul(
                        sqm[hb, :], sqc[hb, cs], ps_r[hb, :]),
                        wait=[(mm, sm)] + acq(("sqm",)), inc=dv)
                    last[("pr",)] = (dv, vq)
                    mcol = (hp * 4 + m) * 64
                    mlast = p.op("tensor", lambda t, m=m, hb=hb, mcol=mcol: t.matmul(
                        ps_m[hb, :], memd[hb, mcol:mcol + 64],
                        sqm[hb, :], start=(m == 0), stop=(m == 3)),
                        wait=[(dv, vq)] + (acq(("pm2",)) if m == 0 else []), inc=mm)
                    last[("sqm",)] = (mm, mlast)
                last[("srows",)] = (mm, p.at(mm))
                vt = p.op("vector", lambda vv, cs=cs, hl=hl, hb=hb: vv.tensor_sub(
                    tmpc[:, :], ps_m[hb, :], locn[hl][:, cs]),
                    wait=[(mm, mlast), (dv, gc_done[1])] + acq(("tmpc",)), inc=dv)
                last[("pm2",)] = (dv, vt)
                vcb = p.op("vector", lambda vv, h=h, hl=hl, cs=cs: vv.scalar_tensor_tensor(
                    combt[:, :], tmpc[:, :], gcols[:, h:h + 1], locn[hl][:, cs],
                    ALU.mult, ALU.add), wait=acq(("combt",)), inc=dv)
                last[("tmpc",)] = (dv, vcb)
                sv = p.dma("scalar", comb_d.ap()[h * 64:(h + 1) * 64, cs], combt[:, :],
                           wait=[(dv, vcb)])
                last[("combt",)] = sv
            last[("locn", hl)] = (dv, p.at(dv))
        last[("bufs",)] = [(mm, p.at(mm)), (dv, p.at(dv))]
        last[("snapP",)] = (mm, p.at(mm))
        last[("vaug",)] = (mm, p.at(mm))
    comb_done = p.dma_all()
    st.close()
    emit_linear_fm(p, "wo1", comb_d.ap(), woT_d.ap(), s2_d.ap(), D, D, T,
                   start_wait=comb_done)


def build_s13():
    p = P("s13")
    emit_attn1(p)
    return p.finalize()



def emit_moe(p):
    xnb_d = p.dram("xn2b", [D, T], BF16, kind="ExternalInput")
    x2_d = p.dram("x2", [D, T], F32, kind="ExternalInput")
    s2_d = p.dram("s2", [D, T], F32, kind="ExternalInput")
    gWT_d = p.dram("gWT", [D, 8], F32, kind="ExternalInput")
    W0T_d = p.dram("W0T", [D, D], F32, kind="ExternalInput")
    W1T_d = p.dram("W1T", [D, D], F32, kind="ExternalInput")
    oWT_d = p.dram("oWT", [D, D], F32, kind="ExternalInput")
    eye_d = p.dram("eyeF", [128, 128], F32, kind="ExternalInput")
    sel2_d = p.dram("sel2", [2, 256], F32, kind="ExternalInput")
    x3_d = p.dram("x3", [D, T], F32, kind="ExternalOutput")
    f_d = p.dram("fscr", [D, T], BF16)
    g2_d = p.dram("g2scr", [D, T], BF16)
    mm, dv, ac = p.mm, p.dv, p.ac
    st = ExitStack()
    xnb = [p.sb(f"m_x{i}", [128, T], BF16, st) for i in range(NDC)]
    gwt = p.sb("m_gwt", [128, 8 * 8], BF16, st)
    eye1 = p.sb("m_eye", [128, 128], F32, st)
    lgt = p.sb("m_lgt", [128, 8], F32, st)
    col = p.sb("m_col", [128, 4], F32, st)
    s01 = p.sb("m_s01", [128, 2], F32, st)
    s01row = p.sb("m_s01r", [2, T], F32, st)
    cnt = p.sb("m_cnt", [128, 8], F32, st)
    w0 = [p.sb(f"m_w0{i}", [128, 128], BF16, st) for i in range(NDC)]
    w1 = [p.sb(f"m_w1{i}", [128, 128], BF16, st) for i in range(NDC)]
    h0b = p.sb("m_h0", [128, 512], BF16, st)
    fb = p.sb("m_f", [128, 512], F32, st)
    fbb = p.sb("m_fb", [128, 512], BF16, st)
    onesr = p.sb("m_onesr", [1, 128], F32, st)
    sel2 = p.sb("m_sel2", [2, 256], F32, st)
    ps_g = [p.ps(f"m_pg{i}", [128, 512], F32, st) for i in range(2)]
    ps_t = p.ps("m_pt", [128, 512], F32, st)
    ps_r = p.ps("m_pr", [128, 512], F32, st)
    last = {}

    def acq(key):
        v = last.get(key)
        return [] if v is None else (list(v) if isinstance(v, list) else [v])

    p.op("gpsimd", lambda g: g.memset(onesr[:, :], 1.0))
    gp0 = p.at(p.gp)
    for i in range(NDC):
        p.dma("sync", xnb[i][:, :], xnb_d.ap()[i * 128:(i + 1) * 128, :])
        p.dma("gpsimd", gwt[:, i * 8:(i + 1) * 8], gWT_d.ap()[i * 128:(i + 1) * 128, :])
    p.dma("scalar", eye1[:, :], eye_d.ap()[:, :])
    p.dma("scalar", sel2[:, :], sel2_d.ap()[:, :])
    ld0 = p.dma_all()
    for tt in range(NCH):
        trows = slice(tt * 128, (tt + 1) * 128)
        m = mm_accum(p, ps_g[tt % 2][:, 0:8],
                     [(xnb[k][:, trows], gwt[:, k * 8:(k + 1) * 8]) for k in range(NDC)],
                     wait_first=ld0 + acq(("pg", tt % 2)), inc=mm)
        v = p.op("vector", lambda vv, par=tt % 2: vv.tensor_copy(lgt[:, :], ps_g[par][:, 0:8]),
                 wait=[(mm, m)] + acq(("lgt",)), inc=dv)
        last[("pg", tt % 2)] = (dv, v)
        v = p.op("vector", lambda vv: vv.reduce_max(col[:, 0:1], lgt[:, :], axis=AX.X),
                 inc=dv)
        v = p.op("vector", lambda vv: vv.tensor_scalar_mul(col[:, 0:1], col[:, 0:1], -1.0),
                 inc=dv)
        a = p.op("scalar", lambda s: s.activation(lgt[:, :], lgt[:, :], AF.Exp,
                                                  bias=col[:, 0:1]),
                 wait=[(dv, v)], inc=ac)
        v = p.op("vector", lambda vv: vv.reduce_sum(col[:, 0:1], lgt[:, :], axis=AX.X),
                 wait=[(ac, a)], inc=dv)
        v = p.op("vector", lambda vv: vv.reciprocal(col[:, 0:1], col[:, 0:1]), inc=dv)
        v = p.op("vector", lambda vv: vv.tensor_scalar(
            lgt[:, :], lgt[:, :], col[:, 0:1], None, ALU.mult), inc=dv)
        v = p.op("vector", lambda vv: vv.tensor_single_scalar(
            cnt[:, :], lgt[:, :], 0.3, ALU.is_gt), inc=dv)
        v = p.op("vector", lambda vv: vv.reduce_sum(col[:, 1:2], cnt[:, :], axis=AX.X),
                 inc=dv)
        v = p.op("vector", lambda vv: vv.tensor_scalar(
            col[:, 2:3], col[:, 1:2], -1.0, 8.0, ALU.mult, ALU.add), inc=dv)
        v = p.op("vector", lambda vv: vv.tensor_mul(s01[:, 0:1], col[:, 2:3], lgt[:, 0:1]),
                 inc=dv)
        v = p.op("vector", lambda vv: vv.tensor_mul(s01[:, 1:2], col[:, 1:2], lgt[:, 1:2]),
                 inc=dv)
        tmv = p.op("tensor", lambda t: t.transpose(
            ps_t[0:2, 0:128], s01[:, :], eye1[:, :]),
            wait=[(dv, v)] + acq(("pt",)), inc=mm)
        v = p.op("vector", lambda vv, tt=tt: vv.tensor_copy(
            s01row[:, tt * 128:(tt + 1) * 128], ps_t[0:2, 0:128]),
            wait=[(mm, tmv)], inc=dv)
        last[("pt",)] = (dv, v)
        last[("lgt",)] = (dv, v)
    s01_done = p.at(dv)
    for oc in range(NDC):
        for i in range(NDC):
            p.dma("gpsimd", w0[i][:, :], W0T_d.ap()[i * 128:(i + 1) * 128, oc * 128:(oc + 1) * 128],
                  wait=acq(("w0", i)))
            p.dma("gpsimd", w1[i][:, :], W1T_d.ap()[i * 128:(i + 1) * 128, oc * 128:(oc + 1) * 128],
                  wait=acq(("w1", i)))
        wp = p.dma_all()
        for nt in range(4):
            cs = bass.ts(nt, 512)
            m0 = mm_accum(p, ps_g[0][:, :],
                          [(w0[k][:, :], xnb[k][:, cs]) for k in range(NDC)],
                          wait_first=wp + acq(("pg", 0)), inc=mm)
            m1 = mm_accum(p, ps_g[1][:, :],
                          [(w1[k][:, :], xnb[k][:, cs]) for k in range(NDC)],
                          wait_first=acq(("pg", 1)), inc=mm)
            r0 = p.op("tensor", lambda t, cs=cs: t.matmul(
                ps_r[:, :], sel2[:, 0:128], s01row[0:2, cs], start=True, stop=True),
                wait=[(dv, s01_done), (p.gp, gp0)] + acq(("prr",)), inc=mm)
            v = p.op("vector", lambda vv: vv.tensor_copy(h0b[:, :], ps_g[0][:, :]),
                     wait=[(mm, m0)] + acq(("h0b",)), inc=dv)
            last[("pg", 0)] = (dv, v)
            v2 = p.op("vector", lambda vv: vv.tensor_mul(fb[:, :], h0b[:, :], ps_r[:, :]),
                      wait=[(mm, r0)] + acq(("fb",)), inc=dv)
            last[("h0b",)] = (dv, v2)
            r1 = p.op("tensor", lambda t, cs=cs: t.matmul(
                ps_r[:, :], sel2[:, 128:256], s01row[0:2, cs], start=True, stop=True),
                wait=[(dv, v2)], inc=mm)
            last[("prr",)] = (mm, r1)
            v3 = p.op("vector", lambda vv: vv.tensor_copy(h0b[:, :], ps_g[1][:, :]),
                      wait=[(mm, m1)], inc=dv)
            last[("pg", 1)] = (dv, v3)
            v4 = p.op("vector", lambda vv: vv.tensor_mul(h0b[:, :], h0b[:, :], ps_r[:, :]),
                      wait=[(mm, r1)], inc=dv)
            last[("prr",)] = (dv, v4)
            v5 = p.op("vector", lambda vv: vv.tensor_add(fbb[:, :], fb[:, :], h0b[:, :]),
                      wait=acq(("fbb",)), inc=dv)
            last[("fb",)] = (dv, v5)
            sv = p.dma("scalar", f_d.ap()[oc * 128:(oc + 1) * 128, cs], fbb[:, :],
                       wait=[(dv, v5)])
            last[("fbb",)] = sv
        for i in range(NDC):
            last[("w0", i)] = (mm, p.at(mm))
            last[("w1", i)] = (mm, p.at(mm))
    f_done = p.dma_all()
    st.close()
    st2 = ExitStack()
    fsb = [p.sb(f"m_f2{i}", [128, T], BF16, st2) for i in range(NDC)]
    wsb = [p.sb(f"m_wo{i}", [128, D], BF16, st2) for i in range(NDC)]
    g2s = p.sb("m_g2", [128, 512], BF16, st2)
    x2s = p.sb("m_x2s", [128, T], F32, st2)
    s2s = p.sb("m_s2", [128, T], F32, st2)
    g2f = p.sb("m_g2f", [128, T], BF16, st2)
    x3s = p.sb("m_x3s", [128, 512], F32, st2)
    ps_o = [p.ps(f"m_po{i}", [128, 512], F32, st2) for i in range(2)]
    for i in range(NDC):
        p.dma("sync", fsb[i][:, :], f_d.ap()[i * 128:(i + 1) * 128, :],
              wait=f_done if i == 0 else ())
        p.dma("gpsimd", wsb[i][:, :], oWT_d.ap()[i * 128:(i + 1) * 128, :],
              wait=f_done if i == 0 else ())
    f2_pairs = p.dma_all()
    for oc in range(NDC):
        for nt in range(4):
            cs = bass.ts(nt, 512)
            par = (oc * 4 + nt) % 2
            m = mm_accum(p, ps_o[par][:, :],
                         [(wsb[k][:, oc * 128:(oc + 1) * 128], fsb[k][:, cs])
                          for k in range(NDC)],
                         wait_first=f2_pairs + acq(("po2", par)), inc=mm)
            a = p.op("scalar", lambda s, par=par: s.activation(
                g2s[:, :], ps_o[par][:, :], AF.Silu),
                wait=[(mm, m)] + acq(("g2s",)), inc=ac)
            last[("po2", par)] = (ac, a)
            sv = p.dma("scalar", g2_d.ap()[oc * 128:(oc + 1) * 128, cs], g2s[:, :],
                       wait=[(ac, a)])
            last[("g2s",)] = sv
    g2_done = p.dma_all()
    for oc in range(NDC):
        l1 = p.dma("sync", x2s[:, :], x2_d.ap()[oc * 128:(oc + 1) * 128, :],
                   wait=acq(("x2s",)) + (g2_done if oc == 0 else []))
        l2 = p.dma("sync", s2s[:, :], s2_d.ap()[oc * 128:(oc + 1) * 128, :])
        l3 = p.dma("sync", g2f[:, :], g2_d.ap()[oc * 128:(oc + 1) * 128, :])
        for nt in range(4):
            cs = bass.ts(nt, 512)
            v = p.op("vector", lambda vv, cs=cs: vv.tensor_mul(
                x3s[:, :], s2s[:, cs], g2f[:, cs]),
                wait=([l1, l2, l3] if nt == 0 else []) + acq(("x3s",)), inc=dv)
            v = p.op("vector", lambda vv, cs=cs: vv.tensor_add(
                x3s[:, :], x3s[:, :], x2s[:, cs]), inc=dv)
            sv = p.dma("scalar", x3_d.ap()[oc * 128:(oc + 1) * 128, cs], x3s[:, :],
                       wait=[(dv, v)])
            last[("x3s",)] = sv
        last[("x2s",)] = [sv]
    p.op("sync", lambda e: None, wait=p.dma_all())
    st2.close()


def build_s14():
    p = P("s14")
    emit_moe(p)
    return p.finalize()




def _rope_tables(rope_cos, rope_sin, tok0):
    cf = np.ones((128, T), np.float32)
    sf = np.zeros((128, T), np.float32)
    P = rope_cos.shape[0]
    r0 = S - P
    g = np.arange(tok0, tok0 + T)
    sel = g >= r0
    if sel.any():
        rows = g[sel] - r0
        pi = (np.arange(128) % 64) // 2
        cf[:, sel] = rope_cos[rows][:, pi].T
        sf[:, sel] = rope_sin[rows][:, pi].T
    return cf, sf


def _qkv0_consts():
    hs2 = np.zeros((128, 2), BF)
    hs2[:64, 0] = 1; hs2[64:, 1] = 1
    rep2 = np.zeros((2, 128), np.float32)
    rep2[0, :64] = 1; rep2[1, 64:] = 1
    psign = np.zeros((128, 128), BF)
    i = np.arange(0, 128, 2)
    psign[i + 1, i] = -1
    psign[i, i + 1] = 1
    return hs2, rep2, psign


def run_stage1(x):
    xTs = shard_tokens(x)
    nc1 = build_s1()
    return xTs, launch(nc1, [{"xT": xTs[c]} for c in range(NCORES)], ["xn", "xnb"])


def run_stage2(inputs, r1):
    wqT = np.ascontiguousarray(np.asarray(inputs["L0_attn_wq"], np.float32).T)
    wkT = np.ascontiguousarray(np.asarray(inputs["L0_attn_wk"], np.float32).T)
    wvT = np.ascontiguousarray(np.asarray(inputs["L0_attn_wv"], np.float32).T)
    qnw = np.ascontiguousarray(np.tile(np.asarray(inputs["L0_attn_q_norm_w"], np.float32), 2)[:, None])
    knw = np.ascontiguousarray(np.tile(np.asarray(inputs["L0_attn_k_norm_w"], np.float32), 2)[:, None])
    hs2, rep2, psign = _qkv0_consts()
    nc2 = build_s2()
    maps = []
    for c in range(NCORES):
        j = c % 4
        cf, sf = _rope_tables(np.asarray(inputs["L0_attn_rope_cos"], np.float32),
                              np.asarray(inputs["L0_attn_rope_sin"], np.float32), j * T)
        maps.append({"xnb": r1[c]["xnb"], "wqT": wqT, "wkT": wkT, "wvT": wvT,
                     "qnw": qnw, "knw": knw, "cfm": cf, "sfm": sf,
                     "hs2": hs2, "rep2": rep2, "psign": psign})
    return launch(nc2, maps, ["Qp", "Kp", "V"])


def run_stage3(r2):
    nc3 = build_s3()
    maps = []
    for c in range(NCORES):
        b, j = divmod(c, 4)
        Kag = np.ascontiguousarray(np.concatenate([r2[4 * b + r]["Kp"] for r in range(4)], axis=0))
        Vag = np.ascontiguousarray(np.concatenate([r2[4 * b + r]["V"] for r in range(4)], axis=0))
        maps.append({"Qp2": r2[c]["Qp"], "Kag": Kag, "Vag": Vag})
    return launch(nc3, maps, ["AVt"])


def run_stage4(inputs, r3):
    nc4 = build_s4()
    woT = np.ascontiguousarray(np.asarray(inputs["L0_attn_wo"], np.float32).T)
    maps = [{"AVt": r3[c]["AVt"], "woT": woT} for c in range(NCORES)]
    return launch(nc4, maps, ["s1"])


def _ffn_tables(tok0):
    f = (np.arange(128) % 64) % 32
    inv_freq = (1.0 / 10000.0 ** (2.0 * f / 64.0)).astype(np.float64)
    base = ((2.0 * f + 25.6) / 89.6).astype(np.float64)

    def tab(lo):
        g = np.arange(tok0 + lo, tok0 + T, dtype=np.float64)
        ang = g[None, :] * inv_freq[:, None]
        ex = np.mod(g, 256.0) / 128.0
        sc_q = base[:, None] ** ex[None, :]
        return (np.cos(ang) * sc_q).astype(np.float32), \
               (np.sin(ang) * sc_q).astype(np.float32), \
               (np.cos(ang) / sc_q).astype(np.float32), \
               (np.sin(ang) / sc_q).astype(np.float32)

    cq, sq, _, _ = tab(-QLO)
    _, _, ck, sk = tab(-KLO)
    B2 = (base ** 2).astype(np.float32)[:, None]
    return cq, sq, ck, sk, B2


def _ffn_consts():
    prh = np.zeros((128, 128), BF)
    for blk in (0, 64):
        i = np.arange(32)
        prh[blk + i + 32, blk + i] = -1
        prh[blk + i, blk + i + 32] = 1
    eye = np.eye(128, dtype=BF)

    def mk_mask(nq, nks):
        m = np.zeros((128, nq * nks), BF)
        for sub in range(nks):
            kk = np.arange(128)[:, None]
            qq = np.arange(nq)[None, :]
            ko = sub * 128 + kk
            m[:, sub * nq:(sub + 1) * nq] = ((ko - 256 <= qq) & (qq <= ko)).astype(BF)
        return m

    maskA = mk_mask(256, 4)
    maskB = mk_mask(128, 3)
    return prh, eye, maskA, maskB


def run_ffn_layers(inputs, r1):
    hs2, rep2, _ = _qkv0_consts()
    prh, eye, maskA, maskB = _ffn_consts()
    maskA0_core0 = maskA.copy()
    maskA0_core0[:, 0:512] = 0
    ICS = tiles(2730, 128)
    nc = build_ffn_layer()
    ys = [r1[c]["xn"] for c in range(NCORES)]
    Wq = np.asarray(inputs["L0_ffn_Wqkv"], np.float32)
    QS = np.asarray(inputs["L0_ffn_q_scale"], np.float32)
    KS = np.asarray(inputs["L0_ffn_k_scale"], np.float32)
    WO = np.asarray(inputs["L0_ffn_Wo"], np.float32)
    LG = np.asarray(inputs["L0_ffn_ln_g"], np.float32)
    LB = np.asarray(inputs["L0_ffn_ln_b"], np.float32)
    W1 = np.asarray(inputs["L0_ffn_W1"], np.float32)
    B1 = np.asarray(inputs["L0_ffn_b1"], np.float32)
    W2 = np.asarray(inputs["L0_ffn_W2"], np.float32)
    B2b = np.asarray(inputs["L0_ffn_b2"], np.float32)
    ylist = None
    for l in range(6):
        maps = []
        for c in range(NCORES):
            b, j = divmod(c, 4)
            if j == 0:
                halo = np.zeros((D, YLO), np.float32)
            else:
                halo = np.ascontiguousarray(ys[c - 1][:, T - YLO:])
            mp = ffn_layer_map(inputs, l, c)
            mp["y"] = ys[c]
            mp["yh"] = halo
            maps.append(mp)
        res = launch(nc, maps, ["y_out"])
        ys = [res[c]["y_out"] for c in range(NCORES)]
        ylist = ys
    return ylist


def ffn_layer_map(inputs, l, c):
    hs2, rep2, _ = _qkv0_consts()
    prh, eye, maskA, maskB = _ffn_consts()
    maskA0_core0 = maskA.copy()
    maskA0_core0[:, 0:512] = 0
    ICS = tiles(2730, 128)
    b, j = divmod(c, 4)
    Wq = np.asarray(inputs["L0_ffn_Wqkv"], np.float32)
    QS = np.asarray(inputs["L0_ffn_q_scale"], np.float32)
    KS = np.asarray(inputs["L0_ffn_k_scale"], np.float32)
    WO = np.asarray(inputs["L0_ffn_Wo"], np.float32)
    LG = np.asarray(inputs["L0_ffn_ln_g"], np.float32)
    LB = np.asarray(inputs["L0_ffn_ln_b"], np.float32)
    W1 = np.asarray(inputs["L0_ffn_W1"], np.float32)
    B1 = np.asarray(inputs["L0_ffn_b1"], np.float32)
    W2 = np.asarray(inputs["L0_ffn_W2"], np.float32)
    B2b = np.asarray(inputs["L0_ffn_b2"], np.float32)
    lngb = np.zeros((128, 16), np.float32)
    lngb[:, 0:8] = LG[l].reshape(8, 128).T
    lngb[:, 8:16] = LB[l].reshape(8, 128).T
    b1a = np.zeros((128, 22), np.float32)
    b1g = np.zeros((128, 22), np.float32)
    for ici, (ic0, sz) in enumerate(ICS):
        b1a[0:sz, ici] = B1[l][ic0:ic0 + sz]
        b1g[0:sz, ici] = B1[l][2730 + ic0:2730 + ic0 + sz]
    b2c = np.ascontiguousarray(B2b[l].reshape(8, 128).T)
    cq, sq, ck, sk, B2col = _ffn_tables(j * T)
    return {"Wqkv": Wq[l], "qsc": np.tile(QS[l], 2)[:, None],
            "ksc": np.tile(KS[l], 2)[:, None], "Wo": WO[l], "lngb": lngb,
            "W1": W1[l], "b1a": b1a, "b1g": b1g, "W2": W2[l], "b2c": b2c,
            "cq": cq, "sq": sq, "ck": ck, "sk": sk, "B2": B2col,
            "prh": prh, "hs2": hs2, "rep2": rep2, "eye": eye,
            "maskA": maskA, "maskB": maskB,
            "maskA0": maskA0_core0 if j == 0 else maskA}



def run_stage11(inputs, xTs, r4, ys):
    nc = build_s11()
    maps = [{"xT": xTs[c], "s1": r4[c]["s1"], "yf": ys[c]} for c in range(NCORES)]
    return launch(nc, maps, ["x2", "xn2", "xn2b"])


def run_stage12(inputs, r11):
    nc = build_s12()
    wqT = np.ascontiguousarray(np.asarray(inputs["L1_attn_w_q"], np.float32).T)
    wkT = np.ascontiguousarray(np.asarray(inputs["L1_attn_w_k"], np.float32).T)
    wvT = np.ascontiguousarray(np.asarray(inputs["L1_attn_w_v"], np.float32).T)
    eye = np.eye(128, dtype=BF)
    maps = [{"xn2b": r11[c]["xn2b"], "wqT": wqT, "wkT": wkT, "wvT": wvT, "eye": eye}
            for c in range(NCORES)]
    return launch(nc, maps, ["sq", "sk", "v", "snap", "sqs"])


def run_stage13(inputs, r12):
    nc = build_s13()
    mem = np.asarray(inputs["L1_attn_memories"], np.float32)
    mn = np.asarray(inputs["L1_attn_memory_norms"], np.float32)
    mnT = np.zeros((128, 64), np.float32)
    memd = np.zeros((128, 2048), np.float32)
    for h in range(16):
        hl = h % 2
        hp = h // 2
        for m in range(4):
            mnT[hl * 64:(hl + 1) * 64, 4 * h + m] = mn[m, h]
            memd[hl * 64:(hl + 1) * 64, (hp * 4 + m) * 64:(hp * 4 + m) * 64 + 64] = mem[m, h]
    sel4 = np.zeros((4, 512), np.float32)
    for m in range(4):
        sel4[m, m * 128:(m + 1) * 128] = 1
    ltri = (np.arange(128)[:, None] <= np.arange(128)[None, :]).astype(BF)
    gate = np.asarray(inputs["L1_attn_gate"], np.float32)[None, :]
    woT = np.ascontiguousarray(np.asarray(inputs["L1_attn_w_o"], np.float32).T)
    tots = []
    for c in range(NCORES):
        s = r12[c]["snap"].reshape(128, 8, NCH, 65)
        tots.append(np.ascontiguousarray(s[:, :, NCH - 1, :].reshape(128, 8 * 65)))
    maps = []
    for c in range(NCORES):
        b, j = divmod(c, 4)
        pm = np.zeros((128, 4), np.float32)
        pm[:, :j] = 1.0
        mp = {"sq": r12[c]["sq"], "sk": r12[c]["sk"], "v": r12[c]["v"],
              "snap": r12[c]["snap"], "pmask": pm, "mnT": mnT, "memd": memd,
              "sel4": sel4, "ltri": ltri, "gate": gate, "woT": woT,
              "totA": np.ascontiguousarray(np.concatenate([tots[4 * b + jj] for jj in range(4)], axis=0)),
              "sqsA": np.ascontiguousarray(np.concatenate([r12[4 * b + jj]["sqs"] for jj in range(4)], axis=0))}
        maps.append(mp)
    return launch(nc, maps, ["s2"])


def run_stage14(inputs, r11, r13):
    nc = build_s14()
    gWT = np.ascontiguousarray(np.asarray(inputs["L1_ffn_gate_W"], np.float32).T)
    W0T = np.ascontiguousarray(np.asarray(inputs["L1_ffn_expert_W"], np.float32)[0].T)
    W1T = np.ascontiguousarray(np.asarray(inputs["L1_ffn_expert_W"], np.float32)[1].T)
    oWT = np.ascontiguousarray(np.asarray(inputs["L1_ffn_out_W"], np.float32).T)
    eyeF = np.eye(128, dtype=np.float32)
    sel2 = np.zeros((2, 256), np.float32)
    sel2[0, 0:128] = 1
    sel2[1, 128:256] = 1
    maps = [{"xn2b": r11[c]["xn2b"], "x2": r11[c]["x2"], "s2": r13[c]["s2"],
             "gWT": gWT, "W0T": W0T, "W1T": W1T, "oWT": oWT, "eyeF": eyeF,
             "sel2": sel2} for c in range(NCORES)]
    return launch(nc, maps, ["x3"])




def emit_halo_prep(p, tlag_d, yh_d, hsel_d):
    dv = p.dv
    st = ExitStack()
    hb = p.sb("hp_b", [128, YLO], F32, st)
    acc = p.sb("hp_acc", [128, YLO], F32, st)
    hsel = p.sb("hp_sel", [128, 4], F32, st)
    ls = p.dma("sync", hsel[:, :], hsel_d.ap()[:, :])
    last_st = None
    for i in range(NDC):
        vv_last = None
        for r in range(4):
            lb = p.dma("sync", hb[:, :], tlag_d.ap()[r * D + i * 128:r * D + (i + 1) * 128, :],
                       wait=([last_st] if last_st and r == 0 else []))
            if r == 0:
                vv_last = p.op("vector", lambda v, r=r: v.tensor_scalar(
                    acc[:, :], hb[:, :], hsel[:, r:r + 1], None, ALU.mult),
                    wait=[lb, ls], inc=dv)
            else:
                vv_last = p.op("vector", lambda v, r=r: v.scalar_tensor_tensor(
                    acc[:, :], hb[:, :], hsel[:, r:r + 1], acc[:, :],
                    ALU.mult, ALU.add), wait=[lb], inc=dv)
        last_st = p.dma("scalar", yh_d.ap()[i * 128:(i + 1) * 128, :], acc[:, :],
                        wait=[(dv, vv_last)])
    st.close()


def build_fused():
    p = P("fused")
    ccs = None
    p.prefix = "s1."
    xT = p.dram("xT", [D, T], F32, kind="ExternalInput")
    xn = p.dram("xn", [D, T], F32)
    xnb = p.dram("xnb", [D, T], BF16)
    emit_ln_fm(p, xT.ap(), xn.ap(), xnb.ap(), tag="ln1f")
    p.barrier()
    p.prefix = "q0."
    p.alias["q0.xnb"] = xnb
    emit_qkv0(p)
    p.barrier()
    p.prefix = "x0."
    kag = p.dram("kag", [4 * D, T], BF16)
    vag = p.dram("vag", [4 * 128, (T // 128) * D], BF16)
    p.collective("AllGather", ALU.bypass, p.alias["q0.Kp"], kag)
    p.collective("AllGather", ALU.bypass, p.alias["q0.V"], vag)
    p.barrier()
    p.prefix = "a0."
    p.alias["a0.Qp2"] = p.alias["q0.Qp"]
    p.alias["a0.Kag"] = kag
    p.alias["a0.Vag"] = vag
    emit_attn0(p)
    p.barrier()
    p.prefix = "s4."
    woT0 = p.dram("woT", [D, D], F32, kind="ExternalInput")
    s1b = p.dram("s1b", [D, T], F32)
    emit_linear_fm(p, "wo0f", p.alias["a0.AVt"].ap(), woT0.ap(), s1b.ap(), D, D, T)
    p.barrier()
    p.prefix = "g."
    hsel = p.dram("hsel", [128, 4], F32, kind="ExternalInput")
    yprev = xn
    for l in range(6):
        p.prefix = f"f{l}."
        tl = p.dram("tl", [D, YLO], F32)
        tlag = p.dram("tlag", [4 * D, YLO], F32)
        yh = p.dram("yhp", [D, YLO], F32)
        for i in range(NDC):
            p.dma("sync", tl.ap()[i * 128:(i + 1) * 128, :],
                  yprev.ap()[i * 128:(i + 1) * 128, T - YLO:])
        p.barrier()
        p.collective("AllGather", ALU.bypass, tl, tlag)
        p.barrier()
        emit_halo_prep(p, tlag, yh, hsel)
        p.barrier()
        p.alias[f"f{l}.y"] = yprev
        p.alias[f"f{l}.yh"] = yh
        emit_ffn_layer(p)
        p.barrier()
        yprev = p.alias[f"f{l}.y_out"]
    p.prefix = "c1."
    p.alias["c1.xT"] = xT
    p.alias["c1.s1"] = s1b
    p.alias["c1.yf"] = yprev
    emit_comb1(p)
    p.barrier()
    p.prefix = "q1."
    p.alias["q1.xn2b"] = p.alias["c1.xn2b"]
    emit_qkv1(p)
    p.barrier()
    p.prefix = "x1."
    snap = p.alias["q1.snap"]
    totl = p.dram("totl", [128, 8 * 65], BF16)
    totA = p.dram("totA", [4 * 128, 8 * 65], BF16)
    sqsA = p.dram("sqsA", [4 * 128, 8], F32)
    p.dma("sync", totl.ap()[:, :],
          bass.AP(snap, NCH * 65 - 65, [[8 * NCH * 65, 128], [NCH * 65, 8], [1, 65]]))
    p.barrier()
    p.collective("AllGather", ALU.bypass, totl, totA)
    p.collective("AllGather", ALU.bypass, p.alias["q1.sqs"], sqsA)
    p.barrier()
    p.prefix = "a1."
    for nm in ("sq", "sk", "v", "snap"):
        p.alias["a1." + nm] = p.alias["q1." + nm]
    p.alias["a1.totA"] = totA
    p.alias["a1.sqsA"] = sqsA
    emit_attn1(p)
    p.barrier()
    p.prefix = "m."
    p.keep_outputs = {"m.x3"}
    p.alias["m.xn2b"] = p.alias["c1.xn2b"]
    p.alias["m.x2"] = p.alias["c1.x2"]
    p.alias["m.s2"] = p.alias["a1.s2"]
    emit_moe(p)
    return p.finalize()


def fused_maps(inputs):
    x = np.asarray(inputs["L0_attn_x"], np.float32)
    xTs = shard_tokens(x)
    wqT = np.ascontiguousarray(np.asarray(inputs["L0_attn_wq"], np.float32).T)
    wkT = np.ascontiguousarray(np.asarray(inputs["L0_attn_wk"], np.float32).T)
    wvT = np.ascontiguousarray(np.asarray(inputs["L0_attn_wv"], np.float32).T)
    qnw = np.ascontiguousarray(np.tile(np.asarray(inputs["L0_attn_q_norm_w"], np.float32), 2)[:, None])
    knw = np.ascontiguousarray(np.tile(np.asarray(inputs["L0_attn_k_norm_w"], np.float32), 2)[:, None])
    hs2, rep2, psign = _qkv0_consts()
    woT0 = np.ascontiguousarray(np.asarray(inputs["L0_attn_wo"], np.float32).T)
    wqT1 = np.ascontiguousarray(np.asarray(inputs["L1_attn_w_q"], np.float32).T)
    wkT1 = np.ascontiguousarray(np.asarray(inputs["L1_attn_w_k"], np.float32).T)
    wvT1 = np.ascontiguousarray(np.asarray(inputs["L1_attn_w_v"], np.float32).T)
    eye = np.eye(128, dtype=BF)
    mem = np.asarray(inputs["L1_attn_memories"], np.float32)
    mn = np.asarray(inputs["L1_attn_memory_norms"], np.float32)
    mnT = np.zeros((128, 64), np.float32)
    memd = np.zeros((128, 2048), np.float32)
    for h in range(16):
        hl = h % 2
        hp = h // 2
        for m in range(4):
            mnT[hl * 64:(hl + 1) * 64, 4 * h + m] = mn[m, h]
            memd[hl * 64:(hl + 1) * 64, (hp * 4 + m) * 64:(hp * 4 + m) * 64 + 64] = mem[m, h]
    sel4 = np.zeros((4, 512), np.float32)
    for m in range(4):
        sel4[m, m * 128:(m + 1) * 128] = 1
    ltri = (np.arange(128)[:, None] <= np.arange(128)[None, :]).astype(BF)
    gate = np.asarray(inputs["L1_attn_gate"], np.float32)[None, :]
    woT1 = np.ascontiguousarray(np.asarray(inputs["L1_attn_w_o"], np.float32).T)
    gWT = np.ascontiguousarray(np.asarray(inputs["L1_ffn_gate_W"], np.float32).T)
    W0T = np.ascontiguousarray(np.asarray(inputs["L1_ffn_expert_W"], np.float32)[0].T)
    W1T = np.ascontiguousarray(np.asarray(inputs["L1_ffn_expert_W"], np.float32)[1].T)
    oWT = np.ascontiguousarray(np.asarray(inputs["L1_ffn_out_W"], np.float32).T)
    eyeF = np.eye(128, dtype=np.float32)
    sel2 = np.zeros((2, 256), np.float32)
    sel2[0, 0:128] = 1
    sel2[1, 128:256] = 1
    maps = []
    for c in range(NCORES):
        b, j = divmod(c, 4)
        cf, sf = _rope_tables(np.asarray(inputs["L0_attn_rope_cos"], np.float32),
                              np.asarray(inputs["L0_attn_rope_sin"], np.float32), j * T)
        hsel = np.zeros((128, 4), np.float32)
        if j > 0:
            hsel[:, j - 1] = 1.0
        pm = np.zeros((128, 4), np.float32)
        pm[:, :j] = 1.0
        mp = {"s1.xT": xTs[c],
              "q0.wqT": wqT, "q0.wkT": wkT, "q0.wvT": wvT, "q0.qnw": qnw,
              "q0.knw": knw, "q0.cfm": cf, "q0.sfm": sf, "q0.hs2": hs2,
              "q0.rep2": rep2, "q0.psign": psign,
              "s4.woT": woT0, "g.hsel": hsel,
              "q1.wqT": wqT1, "q1.wkT": wkT1, "q1.wvT": wvT1, "q1.eye": eye,
              "a1.pmask": pm, "a1.mnT": mnT, "a1.memd": memd, "a1.sel4": sel4,
              "a1.ltri": ltri, "a1.gate": gate, "a1.woT": woT1,
              "m.gWT": gWT, "m.W0T": W0T, "m.W1T": W1T, "m.oWT": oWT,
              "m.eyeF": eyeF, "m.sel2": sel2}
        for l in range(6):
            for k, v in ffn_layer_map(inputs, l, c).items():
                mp[f"f{l}.{k}"] = v
        maps.append(mp)
    return maps


def kernel_fused(**inputs):
    nc = build_fused()
    maps = fused_maps(inputs)
    res = launch(nc, maps, ["m.x3"])
    out = np.zeros((B, S, D), np.float32)
    for c in range(NCORES):
        b, j = divmod(c, 4)
        out[b, j * T:(j + 1) * T, :] = res[c]["m.x3"].T
    return out


def kernel(**inputs):
    return kernel_fused(**inputs)


def kernel_multilaunch(**inputs):
    x = np.asarray(inputs["L0_attn_x"], np.float32)
    xTs, r1 = run_stage1(x)
    r2 = run_stage2(inputs, r1)
    r3 = run_stage3(r2)
    r4 = run_stage4(inputs, r3)
    ys = run_ffn_layers(inputs, r1)
    r11 = run_stage11(inputs, xTs, r4, ys)
    r12 = run_stage12(inputs, r11)
    r13 = run_stage13(inputs, r12)
    r14 = run_stage14(inputs, r11, r13)
    out = np.zeros((B, S, D), np.float32)
    for c in range(NCORES):
        b, j = divmod(c, 4)
        out[b, j * T:(j + 1) * T, :] = r14[c]["x3"].T
    return out


# version 10 (final)
# speedup vs baseline: 1.1766x; 1.1766x over previous
import math
from collections import defaultdict
from contextlib import ExitStack

import numpy as np
import ml_dtypes

import concourse.bass as bass
import concourse.mybir as mybir
from concourse.bass_utils import run_bass_kernel_spmd

F32 = mybir.dt.float32
BF16 = mybir.dt.bfloat16
AF = mybir.ActivationFunctionType
ALU = mybir.AluOpType
AX = mybir.AxisListType
BF = ml_dtypes.bfloat16

NCORES = 8
B, S, D, H, Dh = 2, 8192, 1024, 16, 64
T = S // 4
NDC = D // 128
ENGINES = ("sync", "scalar", "vector", "tensor", "gpsimd")

_LAUNCHES = []


class P:

    SERIAL = {"vector": "dv", "scalar": "ac", "gpsimd": "gp"}

    def __init__(self, name):
        self.nc = bass.Bass("TRN2", target_bir_lowering=False)
        self.name = name
        self.prefix = ""
        self.alias = {}
        self.stack = ExitStack()
        self.ops = {e: [] for e in ENGINES}
        self.cnt = defaultdict(int)
        self.dmas = [self.sem(f"g_dma{i}") for i in range(8)]
        self.dma_rr = 0
        self.sdmas = [self.sem(f"g_sdma{i}") for i in range(6)]
        self.sdma_rr = 0
        self.mm = self.sem("g_mm")
        self.dv = self.sem("g_dv")
        self.ac = self.sem("g_ac")
        self.gp = self.sem("g_gp")
        self._esem = {"vector": self.dv, "scalar": self.ac, "gpsimd": self.gp}
        self._consts = {}
        self._const_tile = self.sb("constcols", [128, 16], F32)
        self.ccsem = None

    def collective(self, kind, op, ins, outs, groups=((0, 1, 2, 3), (4, 5, 6, 7)), wait=()):
        if self.ccsem is None:
            self.ccsem = self.sem("g_cc")
        return self.op("gpsimd", lambda g: g.collective_compute(
            kind, op, replica_groups=[list(g_) for g_ in groups],
            ins=[ins.ap().opt()], outs=[outs.ap().opt()]), wait=list(wait), inc=self.gp)

    def barrier(self, tag=""):
        pairs = [(self.mm, self.at(self.mm)), (self.dv, self.at(self.dv)),
                 (self.ac, self.at(self.ac)), (self.gp, self.at(self.gp))]
        pairs += self.dma_all()
        for e in ENGINES:
            self.op(e, lambda eng: None, wait=pairs)

    def const_col(self, value, dtype=F32):
        assert dtype == F32
        key = float(value)
        if key not in self._consts:
            idx = len(self._consts)
            assert idx < 16
            ap = self._const_tile[:, idx:idx + 1]
            post = self.op("gpsimd", lambda g, ap=ap, value=value: g.memset(ap, value))
            self._consts[key] = (ap, (self.gp, post))
        return self._consts[key]

    def sem(self, name):
        return self.stack.enter_context(self.nc.semaphore(name))

    def sb(self, name, shape, dtype, stack=None):
        return (stack or self.stack).enter_context(
            self.nc.sbuf_tensor(self.prefix + name, shape, dtype))

    def ps(self, name, shape, dtype=F32, stack=None):
        return (stack or self.stack).enter_context(
            self.nc.psum_tensor(self.prefix + name, shape, dtype))

    def dram(self, name, shape, dtype, kind=None):
        full = self.prefix + name
        if full in self.alias:
            return self.alias[full]
        if self.prefix and kind == "ExternalOutput" and \
                full not in getattr(self, "keep_outputs", ()):
            kind = None
        if kind is None:
            t = self.nc.dram_tensor(full, shape, dtype)
        else:
            t = self.nc.dram_tensor(full, shape, dtype, kind=kind)
        self.alias[full] = t
        return t

    def op(self, eng, fn, wait=(), inc=None, inc_n=1, _dma=False):
        wait = [w for w in wait if w is not None]
        if not _dma and eng in self._esem:
            es = self._esem[eng]
            assert inc is None or inc is es, (self.name, eng, inc and inc.name)
            wait.append((es, self.cnt[es.name]))
            inc = es
            inc_n = 1
        for s, v in wait:
            assert self.cnt[s.name] >= v, (self.name, eng, s.name, v, self.cnt[s.name])
        post = None
        if inc is not None:
            self.cnt[inc.name] += inc_n
            post = self.cnt[inc.name]

        def run(engobj, wait=wait, inc=inc, inc_n=inc_n, fn=fn):
            for s, v in wait:
                engobj.wait_ge(s, v)
            inst = fn(engobj)
            if inc is not None:
                if inst is None:
                    inst = engobj.nop()
                inst.then_inc(inc, inc_n)

        self.ops[eng].append(run)
        return post

    def dma(self, eng, out, in_, wait=(), inc=None, **kw):
        if eng == "gpsimd":
            s = self.sdmas[self.sdma_rr % len(self.sdmas)]
            self.sdma_rr += 1
        else:
            s = self.dmas[self.dma_rr % len(self.dmas)]
            self.dma_rr += 1
        wait = list(wait) + [(s, self.cnt[s.name])]
        post = self.op(eng, lambda e: e.dma_start(out=out, in_=in_, **kw), wait=wait,
                       inc=s, inc_n=16, _dma=True)
        return (s, post)

    def dma_all(self):
        return [(s, self.cnt[s.name]) for s in self.dmas + self.sdmas]

    def at(self, s):
        return self.cnt[s.name]

    def finalize(self):
        nc = self.nc
        with nc.Block() as block:
            for e in ENGINES:
                def mk(e):
                    def body(engobj):
                        for fn in self.ops[e]:
                            fn(engobj)
                    return body
                getattr(block, e)(mk(e))
        self.stack.close()
        return nc


def launch(nc, in_maps, outs):
    res = run_bass_kernel_spmd(nc, in_maps, core_ids=list(range(NCORES)))
    return [{k: r[k] for k in outs} for r in res.results]



def host_consts():
    c = {}
    c["ones_col_bf"] = np.ones((128, 1), BF)
    c["ones_row_f32"] = np.ones((1, 128), np.float32)
    c["eye_bf"] = np.eye(128, dtype=BF)
    return c



def emit_ln_fm(p, src, dst_f32, dst_bf=None, Tn=T, tag="ln", start_wait=()):
    nc = p.nc
    st = ExitStack()
    x = [p.sb(f"{tag}_x{i}", [128, Tn], F32, st) for i in range(NDC)]
    xb = [p.sb(f"{tag}_xb{i}", [128, Tn], BF16, st) for i in range(NDC)]
    ones_col = p.sb(f"{tag}_ones", [128, 1], BF16, st)
    ones_row = p.sb(f"{tag}_onesr", [1, 128], F32, st)
    mean_r = p.sb(f"{tag}_mean", [1, Tn], F32, st)
    msq_r = p.sb(f"{tag}_msq", [1, Tn], F32, st)
    var_r = p.sb(f"{tag}_var", [1, Tn], F32, st)
    rstd_r = p.sb(f"{tag}_rstd", [1, Tn], F32, st)
    mb_r = p.sb(f"{tag}_mb", [1, Tn], F32, st)
    cv, mm, ev = p.dv, p.mm, p.dv
    gp = p.gp

    p.op("gpsimd", lambda g: g.memset(ones_col[:, :], 1.0))
    p.op("gpsimd", lambda g: g.memset(ones_row[:, :], 1.0))
    gp_ones = p.at(gp)
    for i in range(NDC):
        p.dma("sync", x[i][:, :], src[i * 128:(i + 1) * 128, :],
              wait=start_wait if i == 0 else ())
        p.dma("gpsimd", xb[i][:, :], src[i * 128:(i + 1) * 128, :],
              wait=start_wait if i == 0 else ())
    ld_pairs = p.dma_all()

    with st:
        st2 = ExitStack()
        sq = [p.sb(f"{tag}_sq{i}", [128, Tn], BF16, st2) for i in range(NDC)]
        ps_sum = p.ps(f"{tag}_psum", [1, Tn], F32, st2)
        ps_sq = p.ps(f"{tag}_psq", [1, Tn], F32, st2)
        with st2:
            for i in range(NDC):
                p.op("scalar", lambda s, i=i: s.activation(sq[i][:, :], x[i][:, :], AF.Square),
                     wait=ld_pairs if i == 0 else ())
            sq_ac = p.at(p.ac)
            NT = Tn // 512
            for i in range(NDC):
                for n in range(NT):
                    cs = bass.ts(n, 512)
                    p.op("tensor", lambda t, i=i, n=n, cs=cs: t.matmul(
                        ps_sum[:, cs], ones_col[:, :], xb[i][:, cs],
                        start=(i == 0), stop=(i == NDC - 1)),
                        wait=(ld_pairs + [(gp, gp_ones)]) if i == 0 and n == 0 else ())
                    p.op("tensor", lambda t, i=i, n=n, cs=cs: t.matmul(
                        ps_sq[:, cs], ones_col[:, :], sq[i][:, cs],
                        start=(i == 0), stop=(i == NDC - 1)),
                        wait=[(p.ac, sq_ac)] if i == 0 and n == 0 else (),
                        inc=mm if i == NDC - 1 and n == NT - 1 else None)
            mm_done = p.at(mm)
            p.op("vector", lambda v: v.tensor_scalar_mul(mean_r[:, :], ps_sum[:, :], 1.0 / D),
                 wait=[(mm, mm_done)])
            p.op("vector", lambda v: v.tensor_mul(msq_r[:, :], mean_r[:, :], mean_r[:, :]))
            p.op("vector", lambda v: v.scalar_tensor_tensor(
                var_r[:, :], ps_sq[:, :], 1.0 / D, msq_r[:, :], ALU.mult, ALU.subtract),
                inc=ev)
            ev_rows = p.at(ev)
            epsc, epsw = p.const_col(1e-5)
            p.op("scalar", lambda s: s.activation(var_r[:, :], var_r[:, :], AF.Ln,
                                                  bias=epsc[0:1]),
                 wait=[(ev, ev_rows), epsw])
            sq_post = p.op("scalar", lambda s: s.activation(rstd_r[:, :], var_r[:, :],
                                                            AF.Exp, scale=-0.5))
            p.op("vector", lambda v: v.tensor_mul(mb_r[:, :], mean_r[:, :], rstd_r[:, :]),
                 wait=[(p.ac, sq_post)], inc=ev)
            ev_done = p.at(ev)
        ps_rstd = p.ps(f"{tag}_prstd", [128, Tn], F32, st)
        ps_mb = p.ps(f"{tag}_pmb", [128, Tn], F32, st)
        NT = Tn // 512
        for n in range(NT):
            cs = bass.ts(n, 512)
            p.op("tensor", lambda t, cs=cs: t.matmul(
                ps_rstd[:, cs], ones_row[:, :], rstd_r[:, cs], start=True, stop=True),
                wait=[(ev, ev_done)] if n == 0 else ())
            p.op("tensor", lambda t, cs=cs: t.matmul(
                ps_mb[:, cs], ones_row[:, :], mb_r[:, cs], start=True, stop=True),
                inc=mm if n == NT - 1 else None)
        rep_done = p.at(mm)
        for i in range(NDC):
            p.op("vector", lambda v, i=i: v.tensor_mul(x[i][:, :], x[i][:, :], ps_rstd[:, :]),
                 wait=[(mm, rep_done)] if i == 0 else ())
            p.op("vector", lambda v, i=i: v.tensor_sub(x[i][:, :], x[i][:, :], ps_mb[:, :]),
                 inc=cv)
            if dst_bf is not None:
                p.op("scalar", lambda s, i=i: s.copy(xb[i][:, :], x[i][:, :]), wait=[(cv, p.at(cv))])
        wr_done = p.at(cv)
        ac_done = p.at(p.ac)
        for i in range(NDC):
            p.dma("sync", dst_f32[i * 128:(i + 1) * 128, :], x[i][:, :],
                  wait=[(cv, wr_done)] if i == 0 else ())
            if dst_bf is not None:
                p.dma("scalar", dst_bf[i * 128:(i + 1) * 128, :], xb[i][:, :],
                      wait=[(p.ac, ac_done)] if i == 0 else ())
        st_pairs = p.dma_all()
        p.op("sync", lambda e: None, wait=st_pairs)
    return st_pairs


def build_s1():
    p = P("s1")
    xT = p.dram("xT", [D, T], F32, kind="ExternalInput")
    xn = p.dram("xn", [D, T], F32, kind="ExternalOutput")
    xnb = p.dram("xnb", [D, T], BF16, kind="ExternalOutput")
    emit_ln_fm(p, xT.ap(), xn.ap(), xnb.ap())
    return p.finalize()


def mm_accum(p, ps, pairs, wait_first=(), inc=None):
    n = len(pairs)
    post = None
    for idx, (l, r) in enumerate(pairs):
        post = p.op("tensor", lambda t, l=l, r=r, idx=idx, n=n: t.matmul(
            ps, l, r, start=(idx == 0), stop=(idx == n - 1)),
            wait=wait_first if idx == 0 else (),
            inc=inc if idx == n - 1 else None)
    return post



def emit_qkv0(p):
    nc = p.nc
    xnb_d = p.dram("xnb", [D, T], BF16, kind="ExternalInput")
    wqT_d = p.dram("wqT", [D, D], F32, kind="ExternalInput")
    wkT_d = p.dram("wkT", [D, D], F32, kind="ExternalInput")
    wvT_d = p.dram("wvT", [D, D], F32, kind="ExternalInput")
    qnw_d = p.dram("qnw", [128, 1], F32, kind="ExternalInput")
    knw_d = p.dram("knw", [128, 1], F32, kind="ExternalInput")
    cfm_d = p.dram("cfm", [128, T], F32, kind="ExternalInput")
    sfm_d = p.dram("sfm", [128, T], F32, kind="ExternalInput")
    hs2_d = p.dram("hs2", [128, 2], BF16, kind="ExternalInput")
    rep2_d = p.dram("rep2", [2, 128], F32, kind="ExternalInput")
    psign_d = p.dram("psign", [128, 128], BF16, kind="ExternalInput")
    Qp_d = p.dram("Qp", [D, T], BF16, kind="ExternalOutput")
    Kp_d = p.dram("Kp", [D, T], BF16, kind="ExternalOutput")
    V_d = p.dram("V", [128, (T // 128) * D], BF16, kind="ExternalOutput")

    st = ExitStack()
    xnb = [p.sb(f"q0_x{i}", [128, T], BF16, st) for i in range(NDC)]
    w_sb = [p.sb(f"q0_w{i}", [128, D], BF16, st) for i in range(NDC)]
    cfm = p.sb("q0_cfm", [128, T], F32, st)
    sfm = p.sb("q0_sfm", [128, T], F32, st)
    hs2 = p.sb("q0_hs2", [128, 2], BF16, st)
    rep2 = p.sb("q0_rep2", [2, 128], F32, st)
    psign = p.sb("q0_psign", [128, 128], BF16, st)
    nw = p.sb("q0_nw", [128, 1], F32, st)
    qraw = p.sb("q0_qraw", [128, T], F32, st)
    sqb = p.sb("q0_sqb", [128, T], BF16, st)
    ssrow = p.sb("q0_ss", [2, T], F32, st)
    qhat = p.sb("q0_qhat", [128, T], F32, st)
    qhatb = p.sb("q0_qhatb", [128, T], BF16, st)
    arow = p.sb("q0_arow", [128, T], F32, st)
    qfin = p.sb("q0_qfin", [128, T], BF16, st)
    ps_acc = [p.ps(f"q0_pa{i}", [128, 512], F32, st) for i in range(2)]
    ps_ss = p.ps("q0_pss", [2, 512], F32, st)
    ps_rep = [p.ps(f"q0_pr{i}", [128, 512], F32, st) for i in range(2)]
    ps_p = [p.ps(f"q0_pp{i}", [128, 512], F32, st) for i in range(2)]

    mm, dv, ac = p.mm, p.dv, p.ac

    eps64, w64 = p.const_col(64e-6)
    eps1, w1 = p.const_col(1e-6)
    sc64, wsc = p.const_col(1.0 / 64.0)

    for i in range(NDC):
        p.dma("sync", xnb[i][:, :], xnb_d.ap()[i * 128:(i + 1) * 128, :])
    p.dma("sync", cfm[:, :], cfm_d.ap()[:, :])
    p.dma("sync", sfm[:, :], sfm_d.ap()[:, :])
    p.dma("gpsimd", hs2[:, :], hs2_d.ap()[:, :])
    p.dma("sync", rep2[:, :], rep2_d.ap()[:, :])
    p.dma("gpsimd", psign[:, :], psign_d.ap()[:, :])
    base_pairs = p.dma_all()

    NT = T // 512
    last = {}

    def acq(key):
        v = last.get(key)
        if v is None:
            return []
        return list(v) if isinstance(v, list) else [v]

    for which, wT_d, nw_d, out_d in (("q", wqT_d, qnw_d, Qp_d), ("k", wkT_d, knw_d, Kp_d)):
        for i in range(NDC):
            p.dma("gpsimd", w_sb[i][:, :], wT_d.ap()[i * 128:(i + 1) * 128, :],
                  wait=acq(("w", i)))
        p.dma("sync", nw[:, :], nw_d.ap()[:, :], wait=acq(("nw",)))
        w_pairs = p.dma_all()

        for i in range(NDC):
            ochunk = slice(i * 128, (i + 1) * 128)
            for n in range(NT):
                cs = bass.ts(n, 512)
                par = n % 2
                mm_accum(p, ps_acc[par][:, :],
                         [(w_sb[k][:, ochunk], xnb[k][:, cs]) for k in range(NDC)],
                         wait_first=w_pairs + acq(("pa", par)), inc=mm)
                v = p.op("vector", lambda vv, n=n, par=par, cs=cs: vv.tensor_copy(
                    qraw[:, cs], ps_acc[par][:, :]), wait=[(mm, p.at(mm))], inc=dv)
                last[("pa", par)] = (dv, v)
            a = p.op("scalar", lambda s: s.activation(sqb[:, :], qraw[:, :], AF.Square),
                     wait=[(dv, p.at(dv))], inc=ac)
            for n in range(NT):
                cs = bass.ts(n, 512)
                m = mm_accum(p, ps_ss[:, :], [(hs2[:, :], sqb[:, cs])],
                             wait_first=[(ac, a)] + acq(("pss",)), inc=mm)
                v = p.op("vector", lambda vv, cs=cs: vv.tensor_copy(ssrow[:, cs], ps_ss[:, :]),
                         wait=[(mm, m)], inc=dv)
                last[("pss",)] = (dv, v)
            if which == "q":
                a = p.op("scalar", lambda s: s.activation(
                    ssrow[:, :], ssrow[:, :], AF.Ln, bias=eps64[0:2]),
                    wait=[(dv, p.at(dv)), w64], inc=ac)
            else:
                a = p.op("scalar", lambda s: s.activation(
                    ssrow[:, :], ssrow[:, :], AF.Ln, bias=eps1[0:2], scale=sc64[0:2]),
                    wait=[(dv, p.at(dv)), w1, wsc], inc=ac)
            rst_done = p.op("scalar", lambda s: s.activation(ssrow[:, :], ssrow[:, :],
                                                              AF.Exp, scale=-0.5), inc=ac)
            for n in range(NT):
                cs = bass.ts(n, 512)
                par = n % 2
                m = mm_accum(p, ps_rep[par][:, :], [(rep2[:, :], ssrow[:, cs])],
                             wait_first=[(ac, rst_done)] + acq(("pr", par)), inc=mm)
                v = p.op("vector", lambda vv, cs=cs, par=par: vv.scalar_tensor_tensor(
                    qhat[:, cs], qraw[:, cs], nw[:, :], ps_rep[par][:, :],
                    ALU.mult, ALU.mult), wait=[(mm, m)], inc=dv)
                last[("pr", par)] = (dv, v)
            cb = p.op("scalar", lambda s: s.copy(qhatb[:, :], qhat[:, :]),
                      wait=[(dv, p.at(dv))], inc=ac)
            p.op("vector", lambda vv: vv.tensor_mul(arow[:, :], qhat[:, :], cfm[:, :]),
                 wait=base_pairs, inc=dv)
            for n in range(NT):
                cs = bass.ts(n, 512)
                par = n % 2
                m = mm_accum(p, ps_p[par][:, :], [(psign[:, :], qhatb[:, cs])],
                             wait_first=[(ac, cb)] + acq(("pp", par)), inc=mm)
                v = p.op("vector", lambda vv, cs=cs, par=par: vv.tensor_mul(
                    qfin[:, cs], ps_p[par][:, :], sfm[:, cs]), wait=[(mm, m)], inc=dv)
                last[("pp", par)] = (dv, v)
            v = p.op("vector", lambda vv: vv.tensor_add(qfin[:, :], qfin[:, :], arow[:, :]),
                     inc=dv)
            sv = p.dma("scalar", out_d.ap()[ochunk, :], qfin[:, :], wait=[(dv, v)])
            last[("pa", 0)] = sv
            last[("pa", 1)] = sv
            last[("nw",)] = (dv, v)
        last[("w", 0)] = p.dma_all()
        for i in range(1, NDC):
            last[("w", i)] = last[("w", 0)]

    for i in range(NDC):
        p.dma("gpsimd", w_sb[i][:, :], wvT_d.ap()[i * 128:(i + 1) * 128, :],
              wait=acq(("w", i)))
    for tt in range(T // 128):
        trows = slice(tt * 128, (tt + 1) * 128)
        for o in range(2):
            cs = bass.ts(o, 512)
            par = (tt * 2 + o) % 2
            m = mm_accum(p, ps_acc[par][:, :],
                         [(xnb[k][:, trows], w_sb[k][:, cs]) for k in range(NDC)],
                         wait_first=p.dma_all() + acq(("pa", par)), inc=mm)
            v = p.op("vector", lambda vv, par=par: vv.tensor_copy(sqb[:, 0:512], ps_acc[par][:, :]),
                     wait=[(mm, m)] + acq(("vout",)), inc=dv)
            last[("pa", par)] = (dv, v)
            sv = p.dma("scalar", V_d.ap()[0:128, tt * D + o * 512:tt * D + o * 512 + 512],
                       sqb[:, 0:512], wait=[(dv, v)])
            last[("vout",)] = sv
    p.op("sync", lambda e: None, wait=p.dma_all())
    st.close()


def build_s2():
    p = P("s2")
    emit_qkv0(p)
    return p.finalize()



def emit_attn0(p):
    NKC0 = S // 128
    Qp_d = p.dram("Qp2", [D, T], BF16, kind="ExternalInput")
    Kag_d = p.dram("Kag", [4 * D, T], BF16, kind="ExternalInput")
    Vag_d = p.dram("Vag", [4 * 128, (T // 128) * D], BF16, kind="ExternalInput")
    avt_d = p.dram("AVt", [D, T], BF16, kind="ExternalOutput")
    mm, dv, ac = p.mm, p.dv, p.ac
    st = ExitStack()
    qsb = [p.sb(f"a0_q{i}", [64, T], BF16, st) for i in range(2)]
    ksb = [p.sb(f"a0_k{i}", [64, S], BF16, st) for i in range(2)]
    vraw = [p.sb(f"a0_vr{i}", [128, NKC0 * 64], BF16, st) for i in range(2)]
    vah = [p.sb(f"a0_va{i}", [128, NKC0 * 65], BF16, st) for i in range(2)]
    pex = p.sb("a0_p", [128, 2048], BF16, st)
    onesr = p.sb("a0_ones", [1, 64], F32, st)
    denr = p.sb("a0_den", [1, 512], F32, st)
    avf = p.sb("a0_avf", [64, 512], F32, st)
    avn = p.sb("a0_avn", [64, 512], BF16, st)
    ps_s = p.ps("a0_ps", [128, 2048], F32, st)
    ps_av = [p.ps(f"a0_pav{i}", [65, 512], F32, st) for i in range(2)]
    ps_rep = p.ps("a0_prep", [64, 512], F32, st)
    last = {}

    def acq(key):
        v = last.get(key)
        return [] if v is None else (list(v) if isinstance(v, list) else [v])

    p.op("gpsimd", lambda g: g.memset(onesr[:, :], 1.0))
    gp0 = p.at(p.gp)

    def load_head(h, par):
        p.dma("sync", qsb[par][:, :], Qp_d.ap()[h * 64:(h + 1) * 64, :],
              wait=acq(("hd", par)))
        for r in range(4):
            p.dma("sync", ksb[par][:, r * T:(r + 1) * T],
                  Kag_d.ap()[r * D + h * 64:r * D + h * 64 + 64, :])
        for r in range(4):
            p.dma("sync",
                  bass.AP(vraw[par], r * 16 * 64, [[NKC0 * 64, 128], [64, 16], [1, 64]]),
                  bass.AP(Vag_d.handle if hasattr(Vag_d, "handle") else Vag_d,
                          r * 128 * 16 * D + h * 64, [[16 * D, 128], [D, 16], [1, 64]]))
        prs = p.dma_all()
        g = p.op("gpsimd", lambda gg: gg.memset(vah[par][:, :], 1.0),
                 wait=acq(("vah", par)))
        v = p.op("vector", lambda vv: vv.tensor_copy(
            bass.AP(vah[par], 0, [[NKC0 * 65, 128], [65, NKC0], [1, 64]]),
            bass.AP(vraw[par], 0, [[NKC0 * 64, 128], [64, NKC0], [1, 64]])),
            wait=prs + [(p.gp, p.at(p.gp))], inc=dv)
        return prs, (dv, v)

    ld = {}
    ld[0] = load_head(0, 0)
    for h in range(16):
        par = h % 2
        if h + 1 < 16:
            ld[h + 1] = load_head(h + 1, (h + 1) % 2)
        prs, vdone = ld[h]
        for qg in range(4):
            qs = slice(qg * 512, (qg + 1) * 512)
            avm = None
            exp_of = {}

            def emit_av(kc):
                sp = kc % 4
                half = sp // 2
                a_ = p.op("tensor", lambda t, sp=sp, kc=kc, par=par, qg=qg: t.matmul(
                    ps_av[qg % 2][:, :], vah[par][:, kc * 65:(kc + 1) * 65],
                    pex[:, sp * 512:(sp + 1) * 512], start=(kc == 0), stop=(kc == NKC0 - 1)),
                    wait=[(ac, exp_of[kc]), vdone] + (acq(("pav", qg % 2)) if kc == 0 else []),
                    inc=mm)
                last[("pex", half)] = (mm, a_)
                return a_

            for kc in range(NKC0):
                sp = kc % 4
                half = sp // 2
                hs = slice(half * 1024, half * 1024 + 1024)
                sm = p.op("tensor", lambda t, sp=sp, kc=kc, qs=qs, par=par: t.matmul(
                    ps_s[:, sp * 512:(sp + 1) * 512], ksb[par][:, kc * 128:(kc + 1) * 128],
                    qsb[par][:, qs], start=True, stop=True),
                    wait=(prs if qg == 0 and kc == 0 else []) +
                         (acq(("ps", half)) if sp % 2 == 0 else []),
                    inc=mm)
                if sp % 2 == 1:
                    a = p.op("scalar", lambda s, hs=hs: s.activation(
                        pex[:, hs], ps_s[:, hs], AF.Exp),
                        wait=[(mm, sm)] + acq(("pex", half)), inc=ac)
                    last[("ps", half)] = (ac, a)
                    exp_of[kc - 1] = a
                    exp_of[kc] = a
                if kc >= 2:
                    avm = emit_av(kc - 2)
            avm = emit_av(NKC0 - 2)
            avm = emit_av(NKC0 - 1)
            d1 = p.op("vector", lambda v, qg=qg: v.tensor_copy(denr[:, :], ps_av[qg % 2][64:65, :]),
                      wait=[(mm, avm)] + acq(("denr",)), inc=dv)
            p.op("scalar", lambda s: s.activation(denr[:, :], denr[:, :], AF.Ln),
                 wait=[(dv, d1)], inc=ac)
            d2 = p.op("scalar", lambda s: s.activation(denr[:, :], denr[:, :],
                                                       AF.Exp, scale=-1.0), inc=ac)
            rm = p.op("tensor", lambda t: t.matmul(
                ps_rep[:, :], onesr[:, :], denr[:, :], start=True, stop=True),
                wait=[(ac, d2), (p.gp, gp0)] + acq(("prep",)), inc=mm)
            e1 = p.op("vector", lambda v, qg=qg: v.tensor_copy(avf[:, :], ps_av[qg % 2][0:64, :]),
                      wait=acq(("avf",)), inc=dv)
            last[("pav", qg % 2)] = (dv, e1)
            e2 = p.op("vector", lambda v: v.tensor_mul(avn[:, :], avf[:, :], ps_rep[:, :]),
                      wait=[(mm, rm)] + acq(("avn",)), inc=dv)
            last[("prep",)] = (dv, e2)
            last[("avf",)] = (dv, e2)
            last[("denr",)] = (dv, e2)
            sv = p.dma("scalar", avt_d.ap()[h * 64:(h + 1) * 64, qs], avn[:, :],
                       wait=[(dv, e2)])
            last[("avn",)] = sv
        last[("hd", par)] = [(mm, p.at(mm))]
        last[("vah", par)] = (mm, p.at(mm))
    p.op("sync", lambda e: None, wait=p.dma_all())
    st.close()


def build_s3():
    p = P("s3")
    emit_attn0(p)
    return p.finalize()



def emit_linear_fm(p, tag, xb_d, wT_d, out_d, Din, Dout, Tn, out_f32=True, start_wait=()):
    st = ExitStack()
    nin = Din // 128
    xb = [p.sb(f"{tag}_x{i}", [128, Tn], BF16, st) for i in range(nin)]
    w_sb = [p.sb(f"{tag}_w{i}", [128, Dout], BF16, st) for i in range(nin)]
    ev = p.sb(f"{tag}_ev", [128, 512], F32 if out_f32 else BF16, st)
    ps = [p.ps(f"{tag}_ps{i}", [128, 512], F32, st) for i in range(2)]
    mm, dv = p.mm, p.dv
    for i in range(nin):
        p.dma("sync", xb[i][:, :], xb_d[i * 128:(i + 1) * 128, :],
              wait=start_wait if i == 0 else ())
        p.dma("gpsimd", w_sb[i][:, :], wT_d[i * 128:(i + 1) * 128, :],
              wait=start_wait if i == 0 else ())
    ld_pairs = p.dma_all()
    last = {}

    def acq(key):
        v = last.get(key)
        if v is None:
            return []
        return list(v) if isinstance(v, list) else [v]

    NT = Tn // 512
    for o in range(Dout // 128):
        for n in range(NT):
            cs = bass.ts(n, 512)
            par = (o * NT + n) % 2
            m = mm_accum(p, ps[par][:, :],
                         [(w_sb[k][:, o * 128:(o + 1) * 128], xb[k][:, cs]) for k in range(nin)],
                         wait_first=ld_pairs + acq(("ps", par)), inc=mm)
            v = p.op("vector", lambda vv, par=par: vv.tensor_copy(ev[:, :], ps[par][:, :]),
                     wait=[(mm, m)] + acq(("ev",)), inc=dv)
            last[("ps", par)] = (dv, v)
            sv = p.dma("scalar", out_d[o * 128:(o + 1) * 128, cs], ev[:, :],
                       wait=[(dv, v)])
            last[("ev",)] = sv
    p.op("sync", lambda e: None, wait=p.dma_all())
    st.close()


def build_s4():
    p = P("s4")
    avt = p.dram("AVt", [D, T], BF16, kind="ExternalInput")
    woT = p.dram("woT", [D, D], F32, kind="ExternalInput")
    s1 = p.dram("s1", [D, T], F32, kind="ExternalOutput")
    emit_linear_fm(p, "wo0", avt.ap(), woT.ap(), s1.ap(), D, D, T)
    return p.finalize()



YLO, KLO, QLO = 258, 384, 128
NYC = T + YLO
NKC_F = T + KLO
NQC_F = T + QLO


def tiles(total, pref=512):
    out = []
    off = 0
    while off < total:
        sz = min(pref, total - off)
        out.append((off, sz))
        off += sz
    return out


class View:

    def __init__(self, t, c0, w, rows=None):
        self.t, self.c0, self.w, self.rows = t, c0, w, rows

    def __getitem__(self, idx):
        r, c = idx
        if self.rows is not None:
            assert r == slice(None, None, None)
            r = self.rows
        start = 0 if c.start is None else c.start
        stop = self.w if c.stop is None else c.stop
        return self.t[r, self.c0 + start:self.c0 + stop]


def emit_ffn_layer(p, debug=False, stop=None, hc_count=NDC, no_attn=False, no_v=False, no_proj=False, proj_upto=99, tail_cb=None):
    nc = p.nc
    yh_d = p.dram("yh", [D, YLO], F32, kind="ExternalInput")
    if debug:
        dbg_q = p.dram("dbg_q", [128, NQC_F], BF16, kind="ExternalOutput")
        dbg_k = p.dram("dbg_k", [128, NKC_F], BF16, kind="ExternalOutput")
        dbg_v = p.dram("dbg_v", [128, 19 * 130], BF16, kind="ExternalOutput")
    y_d = p.dram("y", [D, T], F32, kind="ExternalInput")
    Wqkv_d = p.dram("Wqkv", [D, 3 * D], F32, kind="ExternalInput")
    qs_d = p.dram("qsc", [128, 1], F32, kind="ExternalInput")
    ks_d = p.dram("ksc", [128, 1], F32, kind="ExternalInput")
    Wo_d = p.dram("Wo", [D, D], F32, kind="ExternalInput")
    lngb_d = p.dram("lngb", [128, 16], F32, kind="ExternalInput")
    W1_d = p.dram("W1", [D, 5460], F32, kind="ExternalInput")
    b1a_d = p.dram("b1a", [128, 22], F32, kind="ExternalInput")
    b1g_d = p.dram("b1g", [128, 22], F32, kind="ExternalInput")
    W2_d = p.dram("W2", [2730, D], F32, kind="ExternalInput")
    b2_d = p.dram("b2c", [128, 8], F32, kind="ExternalInput")
    cq_d = p.dram("cq", [128, NQC_F], F32, kind="ExternalInput")
    sq_d = p.dram("sq", [128, NQC_F], F32, kind="ExternalInput")
    ck_d = p.dram("ck", [128, NKC_F], F32, kind="ExternalInput")
    sk_d = p.dram("sk", [128, NKC_F], F32, kind="ExternalInput")
    B2_d = p.dram("B2", [128, 1], F32, kind="ExternalInput")
    prh_d = p.dram("prh", [128, 128], BF16, kind="ExternalInput")
    hs2_d = p.dram("hs2", [128, 2], BF16, kind="ExternalInput")
    rep2_d = p.dram("rep2", [2, 128], F32, kind="ExternalInput")
    eye_d = p.dram("eye", [128, 128], BF16, kind="ExternalInput")
    mA_d = p.dram("maskA", [128, 1024], BF16, kind="ExternalInput")
    mB_d = p.dram("maskB", [128, 384], BF16, kind="ExternalInput")
    mA0_d = p.dram("maskA0", [128, 1024], BF16, kind="ExternalInput")
    avn_d = p.dram("avn_s", [D, NQC_F], BF16)
    yout_d = p.dram("y_out", [D, T], F32, kind="ExternalOutput")

    mm, dv, ac, gp = p.mm, p.dv, p.ac, p.gp
    last = {}

    def acq(key):
        v = last.get(key)
        if v is None:
            return []
        return list(v) if isinstance(v, list) else [v]

    st = ExitStack()
    YB = 64
    ybuf = [p.sb(f"f_y{i}", [128, T + YB], F32, st) for i in range(NDC)]
    onesr = p.sb("f_onesr", [1, 128], F32, st)

    for i in range(NDC):
        p.dma("sync", ybuf[i][:, 0:YB], yh_d.ap()[i * 128:(i + 1) * 128, YLO - YB:])
        p.dma("sync", ybuf[i][:, YB:], y_d.ap()[i * 128:(i + 1) * 128, :])
    ones_gp = p.op("gpsimd", lambda g: g.memset(onesr[:, :], 1.0))

    def coly(t):
        return t + YB

    def colk(t):
        return t + KLO

    def colq(t):
        return t + QLO

    blocks = [(-128, 128, 3)] + [(k * 256, 256, 4) for k in range(8)]
    c8, w8 = p.const_col(8.0)

    st_a = ExitStack()
    ybf = [p.sb(f"f_yb{i}", [128, NKC_F + 1], BF16, st_a) for i in range(NDC)]
    hs2 = p.sb("f_hs2", [128, 2], BF16, st_a)
    rep2 = p.sb("f_rep2", [2, 128], F32, st_a)
    eye = p.sb("f_eye", [128, 128], BF16, st_a)
    prh = p.sb("f_prh", [128, 128], BF16, st_a)
    qs_c = p.sb("f_qs", [128, 1], F32, st_a)
    ks_c = p.sb("f_ks", [128, 1], F32, st_a)
    B2c = p.sb("f_B2", [128, 1], F32, st_a)
    mA = p.sb("f_mA", [128, 1024], BF16, st_a)
    mA0 = p.sb("f_mA0", [128, 1024], BF16, st_a)
    mB = p.sb("f_mB", [128, 384], BF16, st_a)
    cq = p.sb("f_cq", [128, NQC_F], F32, st_a)
    sq = p.sb("f_sq", [128, NQC_F], F32, st_a)
    ck = p.sb("f_ck", [128, NKC_F], F32, st_a)
    sk = p.sb("f_sk", [128, NKC_F], F32, st_a)
    wq_all3 = [p.sb(f"f_wqa{t}", [128, NDC * 128], BF16, st_a) for t in range(3)]

    class _WqK:
        def __init__(self, k):
            self.k = k

        def __getitem__(self, idx):
            r, c = idx
            start = 0 if c.start is None else c.start
            part = start // 128
            off = start % 128
            w = c.stop - start
            return wq_all3[part][:, self.k * 128 + off:self.k * 128 + off + w]

    class _WqView:
        def __getitem__(self, k):
            return _WqK(k)
    wq_all = _WqView()
    scr = p.sb("f_scr", [128, NKC_F], BF16, st_a)
    sqb2 = [p.sb(f"f_sqb{i}", [128, 512], BF16, st_a) for i in range(2)]
    rows = p.sb("f_rows", [2, 2 * NQC_F], F32, st_a)
    qhat = p.sb("f_qhat", [128, NKC_F], BF16, st_a)
    tsc0 = p.sb("f_tsc", [128, 512], F32, st_a)
    qfin = p.sb("f_qfin", [128, NQC_F], BF16, st_a)
    qfinL = p.sb("f_qfinL", [128, NQC_F], BF16, st_a)
    kfin = p.sb("f_kfin", [128, NKC_F], BF16, st_a)
    vfm = [p.sb(f"f_vfm{i}", [128, 512], BF16, st_a) for i in range(2)]
    vsb = p.sb("f_vsb", [128, 19 * 130], BF16, st_a)
    pmk1 = p.sb("f_pmk1", [128, 1024], BF16, st_a)
    pex_t2 = p.sb("f_pex2", [128, 1024], BF16, st_a)
    qraw = scr
    pex_t = View(scr, 0, 1024)
    pmk = [View(scr, 1024, 2048), pmk1]
    ssrow = View(rows, 0, 2 * NQC_F, rows=slice(0, 2))
    den2 = [View(rows, 0, NQC_F, rows=slice(0, 1)),
            View(rows, NQC_F, NQC_F, rows=slice(0, 1))]
    tsc = [tsc0, tsc0]
    avasm = qhat
    avnb = qfinL

    for dst, src in ((hs2, hs2_d), (rep2, rep2_d), (eye, eye_d), (prh, prh_d),
                     (qs_c, qs_d), (ks_c, ks_d), (B2c, B2_d),
                     (mA, mA_d), (mA0, mA0_d), (mB, mB_d),
                     (cq, cq_d), (sq, sq_d), (ck, ck_d), (sk, sk_d)):
        p.dma("scalar", dst[:, :], src.ap()[:, :])
    for i in range(NDC):
        p.op("gpsimd", lambda g, i=i: g.memset(ybf[i][:, 0:KLO - YLO + 1], 0.0), inc=gp)
        p.dma("gpsimd", ybf[i][:, KLO - YLO + 1:KLO - YB + 1],
              yh_d.ap()[i * 128:(i + 1) * 128, 0:YLO - YB])
        p.op("vector", lambda v, i=i: v.tensor_copy(ybf[i][:, KLO - YB + 1:], ybuf[i][:, :]),
             wait=p.dma_all() if i == 0 else (), inc=dv)
    ybf_done = p.at(dv)
    gp0 = p.at(gp)
    pb = [p.ps(f"f_pb{i}", [128, 512], F32, st_a) for i in range(4)]
    ps_a = [pb[0], pb[1]]
    ps_ss = lambda sz: pb[2][0:2, 0:sz]
    ps_rep = [pb[2], pb[3]]
    ps_pp = [pb[0], pb[1]]
    ps_s = [p.ps(f"f_pscr{i}", [128, 512], F32, st_a) for i in range(2)]
    ps_av = p.ps("f_pav", [65, 512], F32, st_a)
    ps_t = p.ps("f_pst", [128, 1024], BF16, st_a)

    QT = tiles(NQC_F)
    KT = tiles(NKC_F)

    def proj_norm_rope(hc, which, upto=99):
        isq = which == "q"
        rng = QT if isq else KT
        base_col = (KLO - QLO) if isq else 0
        wcol = slice(0, 128) if isq else slice(128, 256)
        nw_c = qs_c if isq else ks_c
        ctab, stab = (cq, sq) if isq else (ck, sk)
        dst = qfin if isq else kfin
        w = NQC_F if isq else NKC_F
        for ti, (off, sz) in enumerate(rng):
            par_i = ti % 2
            src_off = off + base_col
            m = mm_accum(p, ps_a[par_i][:, 0:sz],
                         [(wq_all[k][:, wcol],
                           ybf[k][:, src_off + 1:src_off + 1 + sz] if k < 4 else
                           ybf[k][:, src_off:src_off + sz])
                          for k in range(NDC)],
                         wait_first=[(dv, ybf_done), (gp, gp0)]
                         + p.dma_all() + acq(("pb", par_i)) + acq(("pb2", par_i)), inc=mm)
            v = p.op("vector", lambda vv, off=off, sz=sz, par_i=par_i: vv.tensor_copy(
                qraw[:, off:off + sz], ps_a[par_i][:, 0:sz]), wait=[(mm, m)], inc=dv)
            last[("pb", par_i)] = (dv, v)
            last[("pb2", par_i)] = (dv, v)
            if upto <= 0:
                continue
            a = p.op("scalar", lambda s, off=off, sz=sz, par_i=par_i: s.activation(
                sqb2[par_i][:, 0:sz], qraw[:, off:off + sz], AF.Square),
                wait=[(dv, v)] + acq(("sqb", par_i)), inc=ac)
            m2 = mm_accum(p, ps_ss(sz), [(hs2[:, :], sqb2[par_i][:, 0:sz])],
                          wait_first=[(ac, a)] + acq(("pb", 2)) + acq(("pb2", 2)), inc=mm)
            last[("sqb", par_i)] = (mm, m2)
            v2 = p.op("vector", lambda vv, off=off, sz=sz: vv.tensor_copy(
                ssrow[:, off:off + sz], ps_ss(sz)), wait=[(mm, m2)], inc=dv)
            last[("pb", 2)] = (dv, v2)
            last[("pb2", 2)] = (dv, v2)
        if upto <= 1:
            return p.at(dv)
        e24, w24 = p.const_col(1e-24)
        a = p.op("scalar", lambda s: s.activation(ssrow[:, 0:w], ssrow[:, 0:w], AF.Ln,
                                                  bias=e24[0:2]),
                 wait=[(dv, p.at(dv)), w24], inc=ac)
        rst = p.op("scalar", lambda s: s.activation(ssrow[:, 0:w], ssrow[:, 0:w], AF.Exp,
                                                    scale=-0.5), inc=ac)
        for ti, (off, sz) in enumerate(rng):
            par_i = ti % 2
            m = mm_accum(p, ps_rep[par_i][:, 0:sz], [(rep2[:, :], ssrow[:, off:off + sz])],
                         wait_first=[(ac, rst)] + acq(("pb", 2 + par_i)) + acq(("pb2", 2 + par_i)), inc=mm)
            v = p.op("vector", lambda vv, off=off, sz=sz, par_i=par_i: vv.scalar_tensor_tensor(
                qhat[:, off:off + sz], qraw[:, off:off + sz], nw_c[:, :],
                ps_rep[par_i][:, 0:sz], ALU.mult, ALU.mult), wait=[(mm, m)], inc=dv)
            last[("pb", 2 + par_i)] = (dv, v)
            last[("pb2", 2 + par_i)] = (dv, v)
        qhat_done = p.at(dv)
        if upto <= 2:
            return qhat_done
        for ti, (off, sz) in enumerate(rng):
            par_i = ti % 2
            m = mm_accum(p, ps_pp[par_i][:, 0:sz], [(prh[:, :], qhat[:, off:off + sz])],
                         wait_first=[(dv, qhat_done)] + acq(("pb", par_i)) + acq(("pb2", par_i)), inc=mm)
            v = p.op("vector", lambda vv, off=off, sz=sz, par_i=par_i: vv.tensor_mul(
                tsc[par_i][:, 0:sz], qhat[:, off:off + sz], ctab[:, off:off + sz]),
                wait=acq(("tsc", par_i)), inc=dv)
            v = p.op("vector", lambda vv, off=off, sz=sz, par_i=par_i: vv.tensor_mul(
                dst[:, off:off + sz], ps_pp[par_i][:, 0:sz], stab[:, off:off + sz]),
                wait=[(mm, m)], inc=dv)
            v = p.op("vector", lambda vv, off=off, sz=sz, par_i=par_i: vv.tensor_add(
                dst[:, off:off + sz], dst[:, off:off + sz], tsc[par_i][:, 0:sz]), inc=dv)
            last[("pb", par_i)] = (dv, v)
            last[("pb2", par_i)] = (dv, v)
            last[("tsc", par_i)] = (dv, v)
            if isq:
                v = p.op("vector", lambda vv, off=off, sz=sz: vv.tensor_scalar_mul(
                    qfinL[:, off:off + sz], dst[:, off:off + sz], B2c[:, :]),
                    wait=acq(("avnb",)) if ti == 0 else (), inc=dv)
        return p.at(dv)

    for hc in range(hc_count):
        Wqv = Wqkv_d.ap().rearrange("(k p) c -> p k c", p=128)
        wl = None
        for part, c0 in enumerate((hc * 128, D + hc * 128, 2 * D + hc * 128)):
            wl = p.dma("gpsimd",
                       bass.AP(wq_all3[part], 0, [[NDC * 128, 128], [128, NDC], [1, 128]]),
                       Wqv[:, :, c0:c0 + 128], wait=acq(("wq", 0, part)))
        if not no_proj:
            qv = proj_norm_rope(hc, "q", upto=proj_upto)
            kv = proj_norm_rope(hc, "k", upto=proj_upto)
        if no_v:
            continue
        gm = p.op("gpsimd", lambda g: g.memset(vsb[:, :], 1.0),
                  wait=acq(("vsb",)) + acq(("vsbdbg",)))
        gm_v = p.at(gp)
        for ti, (off, sz) in enumerate(KT):
            par_i = ti % 2
            m = mm_accum(p, ps_a[par_i][:, 0:sz],
                         [(wq_all[k][:, 256:384],
                           ybf[k][:, off + 1:off + 1 + sz] if k < 4 else
                           ybf[k][:, off:off + sz])
                          for k in range(NDC)],
                         wait_first=acq(("pb", par_i)) + acq(("pb2", par_i)), inc=mm)
            v = p.op("vector", lambda vv, sz=sz, par_i=par_i: vv.tensor_copy(
                vfm[par_i][:, 0:sz], ps_a[par_i][:, 0:sz]),
                wait=[(mm, m)] + acq(("vfm", par_i)), inc=dv)
            last[("pb2", par_i)] = (dv, v)
            last[("pb", par_i)] = (dv, v)
            for tl in range(sz // 128):
                tt = (off + tl * 128) // 128
                tm = p.op("tensor", lambda t, tl=tl, par_i=par_i: t.transpose(
                    ps_t[:, 0:128], vfm[par_i][:, tl * 128:(tl + 1) * 128], eye[:, :]),
                    wait=[(dv, v)] + acq(("pst",)), inc=mm)
                dst_ap = bass.AP(vsb, tt * 130, [[vsb.shape[1], 128], [65, 2], [1, 64]])
                src_ap = bass.AP(ps_t, 0, [[1024, 128], [64, 2], [1, 64]])
                v2 = p.op("vector", lambda vv, dst_ap=dst_ap, src_ap=src_ap: vv.tensor_copy(
                    dst_ap, src_ap), wait=[(mm, tm), (gp, gm_v)], inc=dv)
                last[("pst",)] = (dv, v2)
                last[("vfm", par_i)] = (mm, tm)
        vsb_done = p.at(dv)

        if debug and hc == 0:
            p.dma("scalar", dbg_q.ap()[:, :], qfin[:, :], wait=[(dv, p.at(dv))])
            p.dma("scalar", dbg_k.ap()[:, :], kfin[:, :])
            dvv = p.dma("scalar", dbg_v.ap()[:, :], vsb[:, :], wait=[(dv, vsb_done)])
            last[("vsbdbg",)] = dvv
        if no_attn:
            continue
        if no_attn:
            continue
        sset = [ps_s, [pb[2], pb[3]]]
        pexs = [pex_t, pex_t2]
        for hl in range(2):
            hb = slice(hl * 64, hl * 64 + 64)
            den = den2[hl]
            pend = None
            for bi in range(len(blocks) + 1):
                if bi < len(blocks):
                    tb, nq, nks = blocks[bi]
                    par = bi % 2
                    qcol = colq(tb)
                    banks = sset[par]
                    mypex = pexs[par]
                    a = None
                    for ks in range(nks):
                        bank = ks // 2
                        coloff = (ks % 2) * nq
                        kcol = colk(tb - 256 + ks * 128)
                        if bi == 0:
                            qsrc = qfinL if ks == 0 else qfin
                        else:
                            qsrc = qfinL if ks < 2 else qfin
                        bkey = ("pss2", par, bank)
                        sm = p.op("tensor", lambda t, banks=banks, bank=bank, coloff=coloff, kcol=kcol, qcol=qcol, nq=nq, qsrc=qsrc, hb=hb: t.matmul(
                            banks[bank][:, coloff:coloff + nq],
                            kfin[hb, kcol:kcol + 128], qsrc[hb, qcol:qcol + nq],
                            start=True, stop=True),
                            wait=(acq(bkey) if ks % 2 == 0 else []) +
                                 ([(dv, vsb_done)] if bi == 0 and hl == 0 and ks == 0 else []),
                            inc=mm if (ks % 2 == 1 or ks == nks - 1) else None)
                        if ks % 2 == 1 or ks == nks - 1:
                            wexp = (ks % 2 + 1) * nq
                            a = p.op("scalar", lambda s, banks=banks, bank=bank, wexp=wexp, nq=nq, mypex=mypex: s.activation(
                                mypex[:, bank * 2 * nq:bank * 2 * nq + wexp],
                                banks[bank][:, 0:wexp], AF.Exp, scale=c8[:, :]),
                                wait=[(mm, sm), w8] + (acq(("pexk", par)) if bank == 0 else []), inc=ac)
                            last[bkey] = (ac, a)
                    msk = mB if bi == 0 else (mA0 if bi == 1 else mA)
                    g = p.op("gpsimd", lambda gg, par=par, nq=nq, nks=nks, msk=msk, mypex=mypex: gg.tensor_mul(
                        pmk[par][:, 0:nq * nks], mypex[:, 0:nq * nks], msk[:, 0:nq * nks]),
                        wait=[(ac, a)] + acq(("pmk", par)), inc=gp)
                    last[("pexk", par)] = (gp, g)
                    cur = (bi, g, nq, nks, qcol, par)
                else:
                    cur = None
                if pend is not None:
                    bi0, g0, nq0, nks0, qcol0, par0 = pend
                    tb0 = blocks[bi0][0]
                    avm = None
                    for ks in range(nks0):
                        tt = (colk(tb0 - 256) + ks * 128) // 128
                        avm = p.op("tensor", lambda t, ks=ks, tt=tt, nq0=nq0, hl=hl, par0=par0, nks0=nks0: t.matmul(
                            ps_av[:, 0:nq0], vsb[:, tt * 130 + hl * 65: tt * 130 + hl * 65 + 65],
                            pmk[par0][:, ks * nq0:(ks + 1) * nq0],
                            start=(ks == 0), stop=(ks == nks0 - 1)),
                            wait=[(gp, g0)] + (acq(("pav",)) if ks == 0 else []),
                            inc=mm if ks == nks0 - 1 else None)
                    last[("pmk", par0)] = (mm, avm)
                    v1 = p.op("vector", lambda vv, qcol0=qcol0, nq0=nq0, hb=hb: vv.tensor_copy(
                        avasm[hb, qcol0:qcol0 + nq0], ps_av[0:64, 0:nq0]),
                        wait=[(mm, avm)] + acq(("avasm", hl)), inc=dv)
                    v2 = p.op("vector", lambda vv, qcol0=qcol0, nq0=nq0, den=den: vv.tensor_copy(
                        den[:, qcol0:qcol0 + nq0], ps_av[64:65, 0:nq0]),
                        wait=acq(("den", hl)) if bi0 == 0 else (), inc=dv)
                    last[("pav",)] = (dv, v2)
                pend = cur
        for hl in range(2):
            hb = slice(hl * 64, hl * 64 + 64)
            den = den2[hl]
            rv0 = p.op("scalar", lambda s, den=den: s.activation(den[:, :], den[:, :], AF.Ln),
                       wait=[(dv, p.at(dv))], inc=ac)
            rv = p.op("scalar", lambda s, den=den: s.activation(den[:, :], den[:, :],
                                                                AF.Exp, scale=-1.0), inc=ac)
            for (off, sz) in QT:
                par_i = (off // 512) % 2
                m = p.op("tensor", lambda t, off=off, sz=sz, par_i=par_i, den=den, hl=hl: t.matmul(
                    ps_rep[par_i][hl * 64:hl * 64 + 64, 0:sz], onesr[:, 0:64],
                    den[:, off:off + sz], start=True, stop=True),
                    wait=[(ac, rv), (gp, ones_gp)] + acq(("pb", 2 + par_i)) + acq(("pb2", 2 + par_i)), inc=mm)
                v = p.op("vector", lambda vv, off=off, sz=sz, par_i=par_i, hb=hb, hl=hl: vv.tensor_mul(
                    avnb[hb, off:off + sz], avasm[hb, off:off + sz],
                    ps_rep[par_i][hl * 64:hl * 64 + 64, 0:sz]),
                    wait=[(mm, m)], inc=dv)
                last[("pb", 2 + par_i)] = (dv, v)
                last[("pb2", 2 + par_i)] = (dv, v)
            last[("avasm", hl)] = (dv, p.at(dv))
            last[("den", hl)] = (mm, p.at(mm))
        sv = p.dma("scalar", avn_d.ap()[hc * 128:(hc + 1) * 128, :], avnb[:, :],
                   wait=[(dv, p.at(dv))])
        last[("avnb",)] = sv
        last[("vsb",)] = (mm, p.at(mm))
        for part in range(3):
            last[("wq", 0, part)] = (mm, p.at(mm))
    st_a.close()

    if stop == "A":
        fin = []
        for oc in range(NDC):
            fin.append(p.dma("sync", yout_d.ap()[oc * 128:(oc + 1) * 128, :],
                             ybuf[oc][:, YB:], wait=p.dma_all() if oc == 0 else ()))
        p.op("sync", lambda e: None, wait=p.dma_all())
        st_a.close()
        st.close()
        return
    st_b = ExitStack()
    avn_sb = [p.sb(f"f_av{i}", [128, NQC_F], BF16, st_b) for i in range(NDC)]
    wo_sb = [p.sb(f"f_wo{i}", [128, D], BF16, st_b) for i in range(NDC)]
    ps_o = [p.ps(f"f_pso{i}", [128, 512], F32, st_b) for i in range(2)]
    aw = p.dma_all()
    for i in range(NDC):
        p.dma("sync", avn_sb[i][:, :], avn_d.ap()[i * 128:(i + 1) * 128, :],
              wait=aw if i == 0 else ())
        p.dma("gpsimd", wo_sb[i][:, :], Wo_d.ap()[i * 128:(i + 1) * 128, :],
              wait=aw if i == 0 else ())
    a2_pairs = p.dma_all()
    MT = tiles(NQC_F - 64)
    for oc in range(NDC):
        for (off, sz) in MT:
            qcol = 64 + off
            ycol = coly(-64) + off
            par = (oc * len(MT) + (off // 512)) % 2
            m = mm_accum(p, ps_o[par][:, 0:sz],
                         [(wo_sb[k][:, oc * 128:(oc + 1) * 128], avn_sb[k][:, qcol:qcol + sz])
                          for k in range(NDC)],
                         wait_first=a2_pairs + acq(("pso", par)), inc=mm)
            v = p.op("vector", lambda vv, oc=oc, ycol=ycol, sz=sz, par=par: vv.scalar_tensor_tensor(
                ybuf[oc][:, ycol:ycol + sz], ps_o[par][:, 0:sz], 1.0,
                ybuf[oc][:, ycol:ycol + sz], ALU.mult, ALU.add),
                wait=[(mm, m)], inc=dv)
            last[("pso", par)] = (dv, v)
    ymid_done = p.at(dv)
    st_b.close()

    if stop == "A2":
        for oc in range(NDC):
            p.dma("sync", yout_d.ap()[oc * 128:(oc + 1) * 128, :], ybuf[oc][:, YB:],
                  wait=[(dv, ymid_done)] if oc == 0 else ())
        p.op("sync", lambda e: None, wait=p.dma_all())
        st.close()
        return
    st_c = ExitStack()
    xnb = [p.sb(f"f_xnb{i}", [128, T], BF16, st_c) for i in range(NDC)]
    lngb = p.sb("f_lngb", [128, 16], F32, st_c)
    ones_cb = p.sb("f_onescb", [128, 1], BF16, st_c)
    mean_r = p.sb("f_mean", [1, T], F32, st_c)
    msq_r = p.sb("f_msq", [1, T], F32, st_c)
    var_r = p.sb("f_var", [1, T], F32, st_c)
    mb_r = p.sb("f_mb", [1, T], F32, st_c)
    lngb_pair = p.dma("scalar", lngb[:, :], lngb_d.ap()[:, :], wait=[(dv, ymid_done)])
    p.op("gpsimd", lambda g: g.memset(ones_cb[:, :], 1.0), wait=[(dv, ymid_done)])
    ln_gp = p.at(gp)

    def xin_ap(buf_list, i, off, sz, lead):
        o = lead + off if i < 4 else lead + off - 1
        return buf_list[i][:, o:o + sz]

    NT = T // 512
    st_stats = ExitStack()
    ymb = [p.sb(f"f_ymb{i}", [128, T + 4], BF16, st_stats) for i in range(NDC)]
    sqf = [p.sb(f"f_sqf{i}", [128, T + 4], BF16, st_stats) for i in range(NDC)]
    ps_sum = p.ps("f_psum", [1, T], F32, st_stats)
    ps_sq = p.ps("f_psq", [1, T], F32, st_stats)
    for i in range(NDC):
        p.op("vector", lambda v, i=i: v.tensor_copy(ymb[i][:, :], ybuf[i][:, coly(-4):]),
             wait=[(dv, ymid_done)] if i == 0 else (), inc=dv)
        p.op("scalar", lambda s, i=i: s.activation(sqf[i][:, :], ybuf[i][:, coly(-4):],
                                                   AF.Square),
             wait=[(dv, ymid_done)], inc=ac)
    cast_done = p.at(dv)
    sq_done = p.at(ac)
    for i in range(NDC):
        for n in range(NT):
            cs = bass.ts(n, 512)
            p.op("tensor", lambda t, i=i, n=n, cs=cs: t.matmul(
                ps_sum[:, cs], ones_cb[:, :], xin_ap(ymb, i, n * 512, 512, 4),
                start=(i == 0), stop=(i == NDC - 1)),
                wait=[(dv, cast_done), (gp, ln_gp)] if i == 0 and n == 0 else ())
            p.op("tensor", lambda t, i=i, n=n, cs=cs: t.matmul(
                ps_sq[:, cs], ones_cb[:, :], xin_ap(sqf, i, n * 512, 512, 4),
                start=(i == 0), stop=(i == NDC - 1)),
                wait=[(ac, sq_done)] if i == 0 and n == 0 else (),
                inc=mm if i == NDC - 1 and n == NT - 1 else None)
    stats_mm = p.at(mm)
    p.op("vector", lambda v: v.tensor_scalar_mul(mean_r[:, :], ps_sum[:, :], 1.0 / D),
         wait=[(mm, stats_mm)])
    p.op("vector", lambda v: v.tensor_mul(msq_r[:, :], mean_r[:, :], mean_r[:, :]))
    p.op("vector", lambda v: v.scalar_tensor_tensor(
        var_r[:, :], ps_sq[:, :], 1.0 / D, msq_r[:, :], ALU.mult, ALU.subtract), inc=dv)
    epsc, epsw = p.const_col(1e-5)
    p.op("scalar", lambda s: s.activation(var_r[:, :], var_r[:, :], AF.Ln,
                                          bias=epsc[0:1]),
         wait=[(dv, p.at(dv)), epsw], inc=ac)
    a = p.op("scalar", lambda s: s.activation(var_r[:, :], var_r[:, :], AF.Exp,
                                              scale=-0.5), inc=ac)
    p.op("vector", lambda v: v.tensor_mul(mb_r[:, :], mean_r[:, :], var_r[:, :]),
         wait=[(ac, a)], inc=dv)
    rows_done = p.at(dv)
    st_stats.close()
    st_rep = ExitStack()
    ps_mrep = p.ps("f_pmrep", [128, T], F32, st_rep)
    ps_rrep = p.ps("f_prrep", [128, T], F32, st_rep)
    for n in range(NT):
        cs = bass.ts(n, 512)
        p.op("tensor", lambda t, cs=cs: t.matmul(
            ps_mrep[:, cs], onesr[:, :], mean_r[:, cs], start=True, stop=True),
            wait=[(dv, rows_done)] if n == 0 else ())
        p.op("tensor", lambda t, cs=cs: t.matmul(
            ps_rrep[:, cs], onesr[:, :], var_r[:, cs], start=True, stop=True),
            inc=mm if n == NT - 1 else None)
    rep_mm = p.at(mm)
    tmpf = p.sb("f_tmpf", [128, T], F32, st_rep)
    for i in range(NDC):
        p.op("vector", lambda v, i=i: v.tensor_sub(
            tmpf[:, :], xin_ap(ybuf, i, 0, T, YB), ps_mrep[:, :]),
            wait=[(mm, rep_mm), lngb_pair] if i == 0 else ())
        p.op("vector", lambda v, i=i: v.tensor_mul(tmpf[:, :], tmpf[:, :], ps_rrep[:, :]))
        p.op("vector", lambda v, i=i: v.tensor_scalar(
            xnb[i][:, :], tmpf[:, :], lngb[:, i:i + 1], lngb[:, 8 + i:9 + i],
            ALU.mult, ALU.add), inc=dv)
    xnb_done = p.at(dv)
    st_rep.close()

    ICS = tiles(2730, 128)
    st_f = ExitStack()
    w1a = [p.sb(f"f_w1a{i}", [128, 8 * 128], BF16, st_f) for i in range(2)]
    w1g = [p.sb(f"f_w1g{i}", [128, 8 * 128], BF16, st_f) for i in range(2)]
    w2sb2 = [p.sb(f"f_w2sb{i}", [128, 22 * 128], BF16, st_f) for i in range(2)]
    b1a = p.sb("f_b1a", [128, 22], F32, st_f)
    b1g = p.sb("f_b1g", [128, 22], F32, st_f)
    b2c = p.sb("f_b2c", [128, 8], F32, st_f)
    gel = p.sb("f_gel", [128, 512], BF16, st_f)
    prod = [p.sb(f"f_prod{i}", [128, 1024], BF16, st_f) for i in range(22)]
    yout_sb = p.sb("f_yout", [128, 512], F32, st_f)
    ps_1 = [p.ps(f"f_ps1{i}", [128, 512], F32, st_f) for i in range(4)]
    ps_o2 = [p.ps(f"f_pso2{i}", [128, 512], F32, st_f) for i in range(2)]
    p.dma("scalar", b1a[:, :], b1a_d.ap()[:, :], wait=[(dv, xnb_done)])
    p.dma("scalar", b1g[:, :], b1g_d.ap()[:, :])
    p.dma("scalar", b2c[:, :], b2_d.ap()[:, :])
    fb_pairs = p.dma_all()
    W1v = W1_d.ap().rearrange("(k p) c -> p k c", p=128)
    W2v = W2_d.ap()
    tail_stores = []
    halo_fin = None
    for nh in (1, 0):
        t0 = nh * 1024
        for ici, (ic0, sz) in enumerate(ICS):
            par_w = ici % 2
            wla = p.dma("gpsimd", bass.AP(w1a[par_w], 0, [[8 * 128, 128], [128, NDC], [1, sz]]),
                        W1v[:, :, ic0:ic0 + sz],
                        wait=acq(("w1a", par_w)) + ([(dv, xnb_done)] if nh == 1 and ici == 0 else []))
            wlg = p.dma("gpsimd", bass.AP(w1g[par_w], 0, [[8 * 128, 128], [128, NDC], [1, sz]]),
                        W1v[:, :, 2730 + ic0:2730 + ic0 + sz], wait=acq(("w1g", par_w)))
            pr1 = 2 * (ici % 2)
            for n2 in range(2):
                cs = slice(t0 + n2 * 512, t0 + n2 * 512 + 512)
                ma = mm_accum(p, ps_1[pr1][0:sz, :],
                              [(w1a[par_w][:, k * 128:k * 128 + sz], xnb[k][:, cs])
                               for k in range(NDC)],
                              wait_first=[wla, wlg, (dv, xnb_done)] + acq(("ps1", pr1)), inc=mm)
                mg = mm_accum(p, ps_1[pr1 + 1][0:sz, :],
                              [(w1g[par_w][:, k * 128:k * 128 + sz], xnb[k][:, cs])
                               for k in range(NDC)],
                              wait_first=acq(("ps1", pr1 + 1)), inc=mm)
                ga = p.op("scalar", lambda s, sz=sz, ici=ici, pr1=pr1: s.activation(
                    gel[0:sz, :], ps_1[pr1 + 1][0:sz, :], AF.Gelu, bias=b1g[0:sz, ici:ici + 1]),
                    wait=[(mm, mg)] + fb_pairs + acq(("gel",)), inc=ac)
                last[("ps1", pr1 + 1)] = (ac, ga)
                pv = p.op("vector", lambda v, sz=sz, ici=ici, n2=n2, pr1=pr1: v.scalar_tensor_tensor(
                    prod[ici][0:sz, n2 * 512:(n2 + 1) * 512], ps_1[pr1][0:sz, :],
                    b1a[0:sz, ici:ici + 1], gel[0:sz, :], ALU.add, ALU.mult),
                    wait=[(mm, ma), (ac, ga)] + acq(("prod", ici)), inc=dv)
                last[("ps1", pr1)] = (dv, pv)
                last[("gel",)] = (dv, pv)
            last[("w1a", par_w)] = (mm, p.at(mm))
            last[("w1g", par_w)] = (mm, p.at(mm))
        prod_done = p.at(dv)
        for oc in range(NDC):
            po = oc % 2
            w2sb = w2sb2[po]
            W2r = W2_d.ap()[0:2688, :].rearrange("(ic p) c -> p ic c", p=128)
            wl2a = p.dma("gpsimd",
                         bass.AP(w2sb, 0, [[22 * 128, 128], [128, 11], [1, 128]]),
                         W2r[:, 0:11, oc * 128:(oc + 1) * 128],
                         wait=acq(("w2", po)) + ([(dv, xnb_done)] if nh == 1 and oc == 0 else []))
            wl2a2 = p.dma("gpsimd",
                          bass.AP(w2sb, 11 * 128, [[22 * 128, 128], [128, 10], [1, 128]]),
                          W2r[:, 11:21, oc * 128:(oc + 1) * 128],
                          wait=acq(("w2", po)))
            wl2b = p.dma("gpsimd", w2sb[0:42, 21 * 128:21 * 128 + 128],
                         W2_d.ap()[2688:2730, oc * 128:(oc + 1) * 128],
                         wait=acq(("w2", po)))
            w2pairs = [wl2a, wl2a2, wl2b]
            for n2 in (1, 0):
                cs = slice(t0 + n2 * 512, t0 + n2 * 512 + 512)
                par = n2
                m = mm_accum(p, ps_o2[par][:, :],
                             [(w2sb[0:sz, ici * 128:ici * 128 + 128],
                               prod[ici][0:sz, n2 * 512:(n2 + 1) * 512])
                              for ici, (ic0, sz) in enumerate(ICS)],
                             wait_first=w2pairs + [(dv, prod_done)] + acq(("pso2", par)),
                             inc=mm)
                v = p.op("vector", lambda vv, oc=oc, cs=cs, par=par: vv.scalar_tensor_tensor(
                    yout_sb[:, :], ps_o2[par][:, :], b2c[:, oc:oc + 1],
                    ybuf[oc][:, YB + cs.start:YB + cs.stop], ALU.add, ALU.add),
                    wait=[(mm, m)] + acq(("yout",)), inc=dv)
                last[("pso2", par)] = (dv, v)
                sv = p.dma("sync", yout_d.ap()[oc * 128:(oc + 1) * 128, cs], yout_sb[:, :],
                           wait=[(dv, v)])
                last[("yout",)] = sv
                if nh == 1 and n2 == 1:
                    tail_stores.append(sv)
            last[("w2", po)] = (mm, p.at(mm))
        for ici in range(22):
            last[("prod", ici)] = (mm, p.at(mm))
        if nh == 1 and tail_cb is not None:
            halo_fin = tail_cb(tail_stores, yout_d)
    if halo_fin is not None:
        halo_fin()
    p.op("sync", lambda e: None, wait=p.dma_all())
    st_f.close()
    st_c.close()
    st.close()


def build_ffn_layer(debug=False, stop=None, **kw):
    p = P("ffn")
    emit_ffn_layer(p, debug=debug, stop=stop, **kw)
    return p.finalize()



def shard_tokens(x):
    out = []
    for c in range(NCORES):
        b, j = divmod(c, 4)
        sl = x[b, j * T:(j + 1) * T, :]
        out.append(np.ascontiguousarray(sl.T))
    return out




def emit_comb1(p):
    xT_d = p.dram("xT", [D, T], F32, kind="ExternalInput")
    s1_d = p.dram("s1", [D, T], F32, kind="ExternalInput")
    y_d = p.dram("yf", [D, T], F32, kind="ExternalInput")
    x2_d = p.dram("x2", [D, T], F32, kind="ExternalOutput")
    xn2_d = p.dram("xn2", [D, T], F32, kind="ExternalOutput")
    xn2b_d = p.dram("xn2b", [D, T], BF16, kind="ExternalOutput")
    mm, dv, ac = p.mm, p.dv, p.ac
    st = ExitStack()
    xa = p.sb("c1_x", [128, T], F32, st)
    sa = p.sb("c1_s", [128, T], F32, st)
    ya = p.sb("c1_y", [128, T], F32, st)
    ga = p.sb("c1_g", [128, T], F32, st)
    last = {}

    def acq(key):
        v = last.get(key)
        return [] if v is None else (list(v) if isinstance(v, list) else [v])

    for i in range(NDC):
        lx = p.dma("sync", xa[:, :], xT_d.ap()[i * 128:(i + 1) * 128, :],
                   wait=acq(("bufs",)))
        ls = p.dma("sync", sa[:, :], s1_d.ap()[i * 128:(i + 1) * 128, :])
        ly = p.dma("sync", ya[:, :], y_d.ap()[i * 128:(i + 1) * 128, :])
        a = p.op("scalar", lambda s: s.activation(ga[:, :], ya[:, :], AF.Silu),
                 wait=[ly] + acq(("ga",)), inc=ac)
        v = p.op("vector", lambda vv: vv.tensor_mul(ga[:, :], ga[:, :], sa[:, :]),
                 wait=[(ac, a), ls], inc=dv)
        v = p.op("vector", lambda vv: vv.tensor_add(xa[:, :], xa[:, :], ga[:, :]),
                 wait=[lx], inc=dv)
        sv = p.dma("scalar", x2_d.ap()[i * 128:(i + 1) * 128, :], xa[:, :],
                   wait=[(dv, v)])
        last[("bufs",)] = sv
        last[("ga",)] = (dv, v)
    x2_done = p.dma_all()
    st.close()
    emit_ln_fm(p, x2_d.ap(), xn2_d.ap(), xn2b_d.ap(), tag="ln2", start_wait=x2_done)


def build_s11():
    p = P("s11")
    emit_comb1(p)
    return p.finalize()



NCH = T // 128


def emit_qkv1(p):
    xnb_d = p.dram("xn2b", [D, T], BF16, kind="ExternalInput")
    wqT_d = p.dram("wqT", [D, D], F32, kind="ExternalInput")
    wkT_d = p.dram("wkT", [D, D], F32, kind="ExternalInput")
    wvT_d = p.dram("wvT", [D, D], F32, kind="ExternalInput")
    eye_d = p.dram("eye", [128, 128], BF16, kind="ExternalInput")
    sq_d = p.dram("sq", [D, T], BF16, kind="ExternalOutput")
    sk_d = p.dram("sk", [D, T], BF16, kind="ExternalOutput")
    v_d = p.dram("v", [T, D], BF16, kind="ExternalOutput")
    snap_d = p.dram("snap", [128, 8 * NCH * 65], BF16, kind="ExternalOutput")
    sqs_d = p.dram("sqs", [128, 8], F32, kind="ExternalOutput")
    mm, dv, ac = p.mm, p.dv, p.ac
    st = ExitStack()
    xnb = [p.sb(f"q1_x{i}", [128, T], BF16, st) for i in range(NDC)]
    w_sb = [p.sb(f"q1_w{i}", [128, 128], BF16, st) for i in range(NDC)]
    eye1 = p.sb("q1_eye", [128, 128], BF16, st)
    raw = p.sb("q1_raw", [128, T], F32, st)
    e1b = p.sb("q1_e1", [128, T], F32, st)
    fout = p.sb("q1_f", [128, T], BF16, st)
    ktm = p.sb("q1_ktm", [128, T], BF16, st)
    vtm = p.sb("q1_vtm", [128, T], BF16, st)
    ones_c = p.sb("q1_ones", [128, 1], BF16, st)
    sqs = p.sb("q1_sqs", [128, 8], F32, st)
    snapb = p.sb("q1_snap", [128, NCH * 65], BF16, st)
    ps_a = [p.ps(f"q1_pa{i}", [128, 512], F32, st) for i in range(2)]
    ps_t = p.ps("q1_pt", [128, 1024], BF16, st)
    ps_st = [p.ps(f"q1_ps{i}", [128, 512], F32, st) for i in range(2)]
    last = {}

    def acq(key):
        v = last.get(key)
        return [] if v is None else (list(v) if isinstance(v, list) else [v])

    p.op("gpsimd", lambda g: g.memset(ones_c[:, :], 1.0))
    gp_ones = p.at(p.gp)
    for i in range(NDC):
        p.dma("sync", xnb[i][:, :], xnb_d.ap()[i * 128:(i + 1) * 128, :])
    p.dma("scalar", eye1[:, :], eye_d.ap()[:, :])
    x_pairs = p.dma_all()
    NT = T // 512

    def proj(hp, which, wT_d, store_d):
        for i in range(NDC):
            p.dma("gpsimd", w_sb[i][:, :],
                  wT_d.ap()[i * 128:(i + 1) * 128, hp * 128:(hp + 1) * 128],
                  wait=acq(("w", i)))
        wp = p.dma_all()
        for n in range(NT):
            cs = bass.ts(n, 512)
            par = n % 2
            m = mm_accum(p, ps_a[par][:, :],
                         [(w_sb[k][:, :], xnb[k][:, cs]) for k in range(NDC)],
                         wait_first=x_pairs + wp + acq(("pa", par)), inc=mm)
            v = p.op("vector", lambda vv, cs=cs, par=par: vv.tensor_copy(
                raw[:, cs], ps_a[par][:, :]), wait=[(mm, m)] + acq(("raw",)), inc=dv)
            last[("pa", par)] = (dv, v)
        raw_done = p.at(dv)
        if which == "v":
            cb = p.op("scalar", lambda s: s.copy(fout[:, :], raw[:, :]),
                      wait=[(dv, raw_done)] + acq(("fout",)), inc=ac)
            fin = (ac, cb)
        else:
            v = p.op("vector", lambda vv: vv.tensor_scalar_min(e1b[:, :], raw[:, :], 0.0),
                     wait=[(dv, raw_done)] + acq(("e1",)), inc=dv)
            a = p.op("scalar", lambda s: s.activation(e1b[:, :], e1b[:, :], AF.Exp),
                     wait=[(dv, v)], inc=ac)
            v = p.op("vector", lambda vv: vv.scalar_tensor_tensor(
                fout[:, :], raw[:, :], 1.0, e1b[:, :], ALU.add, ALU.max),
                wait=[(ac, a)] + acq(("fout",)), inc=dv)
            last[("e1",)] = (ac, a) if False else (dv, v)
            fin = (dv, v)
        sv = p.dma("scalar", store_d.ap()[hp * 128:(hp + 1) * 128, :], fout[:, :],
                   wait=[fin])
        last[("raw",)] = fin
        return fin, sv

    for hp in range(NDC):
        fin, sv = proj(hp, "q", wqT_d, sq_d)
        r = p.op("vector", lambda vv, hp=hp: vv.reduce_sum(
            sqs[:, hp:hp + 1], fout[:, :], axis=AX.X), inc=dv)
        last[("fout",)] = [sv, (dv, r)]
        for i in range(NDC):
            last[("w", i)] = (mm, p.at(mm))
    p.dma("scalar", sqs_d.ap()[:, :], sqs[:, :], wait=[(dv, p.at(dv))])

    for hp in range(NDC):
        fin, sv = proj(hp, "k", wkT_d, sk_d)
        last[("fout",)] = [sv]
        for c in range(NCH):
            tmv = p.op("tensor", lambda t, c=c: t.transpose(
                ps_t[:, 0:128], fout[:, c * 128:(c + 1) * 128], eye1[:, :]),
                wait=acq(("pt",)) + ([fin] if c == 0 else []), inc=mm)
            v = p.op("vector", lambda vv, c=c: vv.tensor_copy(
                ktm[:, c * 128:(c + 1) * 128], ps_t[:, 0:128]),
                wait=[(mm, tmv)] + (acq(("ktm",)) if c == 0 else []), inc=dv)
            last[("pt",)] = (dv, v)
        last[("fout",)] = [(mm, p.at(mm))]
        for i in range(NDC):
            last[("w", i)] = (mm, p.at(mm))
        ktm_done = p.at(dv)
        for c in range(NCH):
            par = c % 2
            m = mm_accum(p, ps_a[par][:, 0:128],
                         [(xnb[k][:, c * 128:(c + 1) * 128],
                           w_sb2_cols(w_sb, k)) for k in range(NDC)],
                         wait_first=acq(("pa", par)) + (vw_pairs(p, wvT_d, w_sb, hp, acq) if c == 0 else []),
                         inc=mm)
            v = p.op("vector", lambda vv, c=c, par=par: vv.tensor_copy(
                vtm[:, c * 128:(c + 1) * 128], ps_a[par][:, 0:128]),
                wait=[(mm, m)] + (acq(("vtm",)) if c == 0 else []), inc=dv)
            last[("pa", par)] = (dv, v)
            sv = p.dma("scalar", v_d.ap()[c * 128:(c + 1) * 128, hp * 128:(hp + 1) * 128],
                       vtm[:, c * 128:(c + 1) * 128], wait=[(dv, v)])
            last[("vst", c)] = sv
        vtm_done = p.at(dv)
        for hl in range(2):
            hb = slice(hl * 64, hl * 64 + 64)
            for c in range(NCH):
                kv = p.op("tensor", lambda t, c=c, hl=hl, hb=hb: t.matmul(
                    ps_st[hl][0:64, 0:64], ktm[:, c * 128 + hl * 64:c * 128 + hl * 64 + 64],
                    vtm[:, c * 128 + hl * 64:c * 128 + hl * 64 + 64],
                    start=(c == 0), stop=False),
                    wait=([(dv, vtm_done), (p.gp, gp_ones)] if c == 0 else []) +
                         acq(("pst", hl)))
                ks = p.op("tensor", lambda t, c=c, hl=hl: t.matmul(
                    ps_st[hl][0:64, 64:65], ktm[:, c * 128 + hl * 64:c * 128 + hl * 64 + 64],
                    ones_c[:, :], start=False, stop=False), inc=mm)
                v = p.op("vector", lambda vv, c=c, hl=hl, hb=hb: vv.tensor_copy(
                    snapb[hb, c * 65:(c + 1) * 65], ps_st[hl][0:64, 0:65]),
                    wait=[(mm, p.at(mm))] + (acq(("snapb",)) if c == 0 and hl == 0 else []),
                    inc=dv)
                last[("pst", hl)] = (dv, v)
        sv = p.dma("scalar", snap_d.ap()[:, hp * NCH * 65:(hp + 1) * NCH * 65],
                   snapb[:, :], wait=[(dv, p.at(dv))])
        last[("snapb",)] = sv
        last[("ktm",)] = (mm, p.at(mm))
        last[("vtm",)] = (mm, p.at(mm))
        for i in range(NDC):
            last[("w", i)] = (mm, p.at(mm))
    p.op("sync", lambda e: None, wait=p.dma_all())
    st.close()


def w_sb2_cols(w_sb, k):
    return w_sb[k][:, :]


def vw_pairs(p, wvT_d, w_sb, hp, acq):
    for i in range(NDC):
        p.dma("gpsimd", w_sb[i][:, :],
              wvT_d.ap()[i * 128:(i + 1) * 128, hp * 128:(hp + 1) * 128],
              wait=acq(("w", i)))
    return p.dma_all()


def build_s12():
    p = P("s12")
    emit_qkv1(p)
    return p.finalize()




def emit_attn1(p):
    sq_dd = p.dram("sq", [D, T], BF16, kind="ExternalInput")
    sk_dd = p.dram("sk", [D, T], BF16, kind="ExternalInput")
    v_dd = p.dram("v", [T, D], BF16, kind="ExternalInput")
    snap_dd = p.dram("snap", [128, 8 * NCH * 65], BF16, kind="ExternalInput")
    totA_d = p.dram("totA", [4 * 128, 8 * 65], BF16, kind="ExternalInput")
    sqsA_d = p.dram("sqsA", [4 * 128, 8], F32, kind="ExternalInput")
    pm_d = p.dram("pmask", [128, 4], F32, kind="ExternalInput")
    mnT_d = p.dram("mnT", [128, 64], F32, kind="ExternalInput")
    memd_d = p.dram("memd", [128, 2048], F32, kind="ExternalInput")
    sel4_d = p.dram("sel4", [4, 512], F32, kind="ExternalInput")
    ltri_d = p.dram("ltri", [128, 128], BF16, kind="ExternalInput")
    gate_d = p.dram("gate", [1, 16], F32, kind="ExternalInput")
    woT_d = p.dram("woT", [D, D], F32, kind="ExternalInput")
    comb_d = p.dram("comb", [D, T], BF16)
    s2_d = p.dram("s2", [D, T], F32, kind="ExternalOutput")
    mm, dv, ac = p.mm, p.dv, p.ac
    st = ExitStack()
    sqc = p.sb("a1_sq", [128, T], BF16, st)
    skc = p.sb("a1_sk", [128, T], BF16, st)
    vtm = p.sb("a1_v", [128, T], BF16, st)
    snapP = p.sb("a1_snapP", [128, (NCH + 1) * 65], BF16, st)
    snapo = p.sb("a1_snapo", [128, NCH * 65], F32, st)
    pref = p.sb("a1_pref", [128, 8 * 65], F32, st)
    totb = p.sb("a1_tot", [128, 8 * 65], F32, st)
    pmc = p.sb("a1_pm", [128, 4], F32, st)
    mnT = p.sb("a1_mnT", [128, 64], BF16, st)
    memd = p.sb("a1_memd", [128, 2048], BF16, st)
    sel4 = p.sb("a1_sel4", [4, 512], F32, st)
    ltri = p.sb("a1_ltri", [128, 128], BF16, st)
    grow = p.sb("a1_grow", [1, 16], F32, st)
    gcols = p.sb("a1_gcols", [64, 16], F32, st)
    onesr = p.sb("a1_onesr", [1, 128], F32, st)
    ones_c = p.sb("a1_ones", [128, 1], BF16, st)
    tsqs = p.sb("a1_tsqs", [128, 8], F32, st)
    tsqsb = p.sb("a1_tsqsb", [128, 8], BF16, st)
    rel16 = p.sb("a1_rel16", [4, 16], F32, st)
    w16 = p.sb("a1_w16", [16, 4], F32, st)
    w4 = p.sb("a1_w4", [4, 16], F32, st)
    Asb = p.sb("a1_A", [128, 128], BF16, st)
    vaug = p.sb("a1_vaug", [128, NCH * 130], BF16, st)
    locasm = [p.sb(f"a1_loc{i}", [64, T], BF16, st) for i in range(2)]
    locn = [p.sb(f"a1_locn{i}", [64, T], BF16, st) for i in range(2)]
    rows = p.sb("a1_rows", [2, 2 * T], F32, st)
    den2 = [View(rows, 0, T, rows=slice(0, 1)), View(rows, T, T, rows=slice(0, 1))]
    srows = p.sb("a1_srows", [4, 512], F32, st)
    sqm = p.sb("a1_sqm", [128, 512], BF16, st)
    tmpc = p.sb("a1_tmpc", [64, 512], F32, st)
    combt = p.sb("a1_combt", [64, 512], BF16, st)
    ps_A = [p.ps(f"a1_pA{i}", [128, 512], F32, st) for i in range(2)]
    ps_o = [p.ps(f"a1_po{i}", [65, 512], F32, st) for i in range(2)]
    ps_r = p.ps("a1_pr", [128, 512], F32, st)
    ps_m = p.ps("a1_pm2", [128, 512], F32, st)
    ps_t4 = p.ps("a1_pt4", [16, 512], F32, st)
    last = {}

    def acq(key):
        v = last.get(key)
        return [] if v is None else (list(v) if isinstance(v, list) else [v])

    p.op("gpsimd", lambda g: g.memset(onesr[:, :], 1.0))
    p.op("gpsimd", lambda g: g.memset(ones_c[:, :], 1.0))
    gp0 = p.at(p.gp)
    for dst, sd in ((pmc, pm_d), (mnT, mnT_d), (memd, memd_d), (sel4, sel4_d),
                    (ltri, ltri_d), (grow, gate_d)):
        p.dma("scalar" if dst.dtype == F32 else "gpsimd", dst[:, :], sd.ap()[:, :])
    c_pairs = p.dma_all()

    scr_d = p.dram("a1scr", [16, 4], F32)
    l0 = p.dma("sync", tsqs[:, :], sqsA_d.ap()[0:128, :])
    for j in range(1, 4):
        lj = p.dma("sync", totb[:, 0:8], sqsA_d.ap()[j * 128:(j + 1) * 128, :], wait=acq(("totb",)))
        acc = p.op("vector", lambda vv: vv.tensor_add(tsqs[:, :], tsqs[:, :], totb[:, 0:8]),
                   wait=[l0, lj], inc=dv)
        last[("totb",)] = (dv, acc)
    vtb = p.op("vector", lambda vv: vv.tensor_copy(tsqsb[:, :], tsqs[:, :]),
               wait=[(dv, p.at(dv))], inc=dv)
    relmm = None
    for h in range(16):
        hl, hp = h % 2, h // 2
        hb = slice(hl * 64, hl * 64 + 64)
        relmm = p.op("tensor", lambda t, h=h, hb=hb, hp=hp: t.matmul(
            ps_t4[0:4, h:h + 1], mnT[hb, 4 * h:4 * h + 4], tsqsb[hb, hp:hp + 1],
            start=True, stop=True),
            wait=([(dv, p.at(dv))] + c_pairs) if h == 0 else (), inc=mm)
    v1 = p.op("vector", lambda vv: vv.tensor_copy(rel16[:, :], ps_t4[0:4, 0:16]),
              wait=[(mm, relmm)], inc=dv)
    d1 = p.dma("scalar", scr_d.ap().rearrange("h m -> m h"), rel16[:, :], wait=[(dv, v1)],
               allow_slow_non_contiguous=True)
    d2 = p.dma("sync", w16[:, :], scr_d.ap()[:, :], wait=[d1])
    csc, wsc = p.const_col(1.0 / 8192.0)
    vmx = p.op("vector", lambda vv: vv.reduce_max(w4[0:1, 0:1] if False else rows[0:1, 0:16].t if False else tsqs[0:16, 0:1] if False else w16[:, 0:4].t if False else w16[:, :], w16[:, :], axis=AX.X), wait=[d2]) if False else None
    mxc = p.sb("a1_mx", [16, 1], F32, st)
    vmx = p.op("vector", lambda vv: vv.reduce_max(mxc[:, :], w16[:, :], axis=AX.X),
               wait=[d2], inc=dv)
    vsb_ = p.op("vector", lambda vv: vv.tensor_scalar(
        w16[:, :], w16[:, :], mxc[:, :], None, ALU.subtract), inc=dv)
    aex = p.op("scalar", lambda s: s.activation(w16[:, :], w16[:, :], AF.Exp,
                                                scale=csc[0:16]),
               wait=[(dv, vsb_), wsc], inc=ac)
    vsm = p.op("vector", lambda vv: vv.reduce_sum(mxc[:, :], w16[:, :], axis=AX.X),
               wait=[(ac, aex)], inc=dv)
    vrc = p.op("vector", lambda vv: vv.reciprocal(mxc[:, :], mxc[:, :]), inc=dv)
    vw = p.op("vector", lambda vv: vv.tensor_scalar(
        w16[:, :], w16[:, :], mxc[:, :], None, ALU.mult), inc=dv)
    d3 = p.dma("scalar", scr_d.ap()[:, :], w16[:, :], wait=[(dv, vw)])
    d4 = p.dma("sync", w4[:, :], scr_d.ap().rearrange("h m -> m h"), wait=[d3],
               allow_slow_non_contiguous=True)
    ag = p.op("scalar", lambda s: s.activation(grow[:, :], grow[:, :], AF.Sigmoid),
              wait=c_pairs, inc=ac)
    gmm = None
    for h in range(16):
        gmm = p.op("tensor", lambda t, h=h: t.matmul(
            ps_m[0:64, h:h + 1], onesr[:, 0:64], grow[:, h:h + 1],
            start=True, stop=True),
            wait=[(ac, ag), (p.gp, gp0)] if h == 0 else (), inc=mm)
    vg = p.op("vector", lambda vv: vv.tensor_copy(gcols[:, :], ps_m[0:64, 0:16]),
              wait=[(mm, gmm)], inc=dv)
    gc_done = (dv, vg)

    for hp in range(NDC):
        lq = p.dma("sync", sqc[:, :], sq_dd.ap()[hp * 128:(hp + 1) * 128, :],
                   wait=acq(("bufs",)))
        lk = p.dma("sync", skc[:, :], sk_dd.ap()[hp * 128:(hp + 1) * 128, :])
        lvs = [p.dma("sync", vtm[:, c * 128:(c + 1) * 128],
                     v_dd.ap()[c * 128:(c + 1) * 128, hp * 128:(hp + 1) * 128])
               for c in range(NCH)]
        lsn = p.dma("gpsimd", snapo[:, :],
                    snap_dd.ap()[:, hp * NCH * 65:(hp + 1) * NCH * 65],
                    wait=acq(("snapo",)))
        vpre = None
        for j in range(4):
            lj = p.dma("gpsimd", totb[:, :], totA_d.ap()[j * 128:(j + 1) * 128, :],
                       wait=acq(("totb",)))
            if j == 0:
                vpre = p.op("vector", lambda vv: vv.tensor_scalar(
                    pref[:, :], totb[:, :], pmc[:, 0:1], None, ALU.mult),
                    wait=[lj] + c_pairs + acq(("pref",)), inc=dv)
            else:
                vpre = p.op("vector", lambda vv, j=j: vv.scalar_tensor_tensor(
                    pref[:, :], totb[:, :], pmc[:, j:j + 1], pref[:, :],
                    ALU.mult, ALU.add), wait=[lj], inc=dv)
            last[("totb",)] = (dv, vpre)
        hp_pairs = p.dma_all()
        gva = p.op("gpsimd", lambda g: g.memset(vaug[:, :], 1.0),
                   wait=acq(("vaug",)))
        vva = p.op("vector", lambda vv: vv.tensor_copy(
            bass.AP(vaug, 0, [[NCH * 130, 128], [130, NCH], [65, 2], [1, 64]]),
            bass.AP(vtm.handle if hasattr(vtm, 'handle') else vtm, 0,
                    [[T, 128], [128, NCH], [64, 2], [1, 64]])),
            wait=hp_pairs + [(p.gp, p.at(p.gp))], inc=dv)
        vsp = p.op("vector", lambda vv, hp=hp: vv.tensor_copy(
            snapP[:, 0:65], pref[:, hp * 65:(hp + 1) * 65]),
            wait=[(dv, vpre)] + acq(("snapP",)), inc=dv)
        for c in range(NCH):
            vsp = p.op("vector", lambda vv, c=c, hp=hp: vv.scalar_tensor_tensor(
                snapP[:, (c + 1) * 65:(c + 2) * 65], snapo[:, c * 65:(c + 1) * 65],
                1.0, pref[:, hp * 65:(hp + 1) * 65], ALU.mult, ALU.add),
                wait=[lsn] if c == 0 else (), inc=dv)
        snp_done = p.at(dv)
        last[("pref",)] = (dv, snp_done)
        last[("snapo",)] = (dv, snp_done)
        for hl in range(2):
            h = 2 * hp + hl
            hb = slice(hl * 64, hl * 64 + 64)
            den = den2[hl]
            for c in range(NCH):
                parA = c % 2
                am = p.op("tensor", lambda t, c=c, hb=hb, parA=parA: t.matmul(
                    ps_A[parA][:, 0:128], skc[hb, c * 128:(c + 1) * 128],
                    sqc[hb, c * 128:(c + 1) * 128], start=True, stop=True),
                    wait=(hp_pairs + [(dv, snp_done)] if c == 0 and hl == 0 else []) +
                         acq(("pA", parA)), inc=mm)
                va = p.op("vector", lambda vv, parA=parA: vv.tensor_mul(
                    Asb[:, :], ps_A[parA][:, 0:128], ltri[:, :]),
                    wait=[(mm, am)] + acq(("Asb",)), inc=dv)
                last[("pA", parA)] = (dv, va)
                paro = c % 2
                m1 = p.op("tensor", lambda t, c=c, hl=hl, paro=paro: t.matmul(
                    ps_o[paro][0:65, 0:128], vaug[:, c * 130 + hl * 65:c * 130 + hl * 65 + 65],
                    Asb[:, :], start=True, stop=False),
                    wait=[(dv, va)] + acq(("po", paro)))
                m3 = p.op("tensor", lambda t, c=c, hb=hb, paro=paro: t.matmul(
                    ps_o[paro][0:65, 0:128], snapP[hb, c * 65:(c + 1) * 65],
                    sqc[hb, c * 128:(c + 1) * 128], start=False, stop=True), inc=mm)
                last[("Asb",)] = (mm, m3)
                vo = p.op("vector", lambda vv, c=c, hl=hl, paro=paro: vv.tensor_copy(
                    locasm[hl][:, c * 128:(c + 1) * 128], ps_o[paro][0:64, 0:128]),
                    wait=[(mm, m3)] + (acq(("locasm", hl)) if c == 0 else []), inc=dv)
                vd = p.op("vector", lambda vv, c=c, den=den, paro=paro: vv.tensor_copy(
                    den[:, c * 128:(c + 1) * 128], ps_o[paro][64:65, 0:128]),
                    wait=acq(("den", hl)) if c == 0 else (), inc=dv)
                last[("po", paro)] = (dv, vd)
            vc1 = p.op("vector", lambda vv, den=den: vv.tensor_scalar_max(
                den[:, :], den[:, :], 1e-6), inc=dv)
            p.op("scalar", lambda s, den=den: s.activation(den[:, :], den[:, :], AF.Ln),
                 wait=[(dv, vc1)], inc=ac)
            vc2 = p.op("scalar", lambda s, den=den: s.activation(den[:, :], den[:, :],
                                                                 AF.Exp, scale=-1.0), inc=ac)
            for nt in range(4):
                cs = slice(nt * 512, nt * 512 + 512)
                rm = p.op("tensor", lambda t, cs=cs, den=den: t.matmul(
                    ps_r[0:64, :], onesr[:, 0:64], den[:, cs], start=True, stop=True),
                    wait=[(ac, vc2)] + acq(("pr",)), inc=mm)
                vn = p.op("vector", lambda vv, cs=cs, hl=hl: vv.tensor_mul(
                    locn[hl][:, cs], locasm[hl][:, cs], ps_r[0:64, :]),
                    wait=[(mm, rm)] + (acq(("locn", hl)) if nt == 0 else []), inc=dv)
                last[("pr",)] = (dv, vn)
            last[("locasm", hl)] = (dv, p.at(dv))
            last[("den", hl)] = (mm, p.at(mm))
            for nt in range(4):
                cs = slice(nt * 512, nt * 512 + 512)
                rm = p.op("tensor", lambda t, cs=cs, hb=hb, h=h: t.matmul(
                    ps_t4[0:4, 0:512], mnT[hb, 4 * h:4 * h + 4], sqc[hb, cs],
                    start=True, stop=True),
                    wait=acq(("pt4",)) + [(dv, p.at(dv))] if nt == 0 else acq(("pt4",)),
                    inc=mm)
                vs1 = p.op("vector", lambda vv: vv.tensor_scalar_max(
                    srows[:, :], ps_t4[0:4, 0:512], 1e-6),
                    wait=[(mm, rm)] + acq(("srows",)), inc=dv)
                last[("pt4",)] = (dv, vs1)
                p.op("scalar", lambda s: s.activation(srows[:, :], srows[:, :], AF.Ln),
                     wait=[(dv, vs1)], inc=ac)
                vs2a = p.op("scalar", lambda s: s.activation(srows[:, :], srows[:, :],
                                                             AF.Exp, scale=-1.0), inc=ac)
                vs3 = p.op("vector", lambda vv, h=h: vv.tensor_scalar(
                    srows[:, :], srows[:, :], w4[:, h:h + 1], None, ALU.mult),
                    wait=[d4, (ac, vs2a)], inc=dv)
                mlast = None
                for m in range(4):
                    sm = p.op("tensor", lambda t, m=m: t.matmul(
                        ps_r[:, :], sel4[:, m * 128:(m + 1) * 128], srows[:, :],
                        start=True, stop=True),
                        wait=[(dv, vs3)] + acq(("pr",)), inc=mm)
                    vq = p.op("vector", lambda vv, cs=cs, hb=hb: vv.tensor_mul(
                        sqm[hb, :], sqc[hb, cs], ps_r[hb, :]),
                        wait=[(mm, sm)] + acq(("sqm",)), inc=dv)
                    last[("pr",)] = (dv, vq)
                    mcol = (hp * 4 + m) * 64
                    mlast = p.op("tensor", lambda t, m=m, hb=hb, mcol=mcol: t.matmul(
                        ps_m[hb, :], memd[hb, mcol:mcol + 64],
                        sqm[hb, :], start=(m == 0), stop=(m == 3)),
                        wait=[(dv, vq)] + (acq(("pm2",)) if m == 0 else []), inc=mm)
                    last[("sqm",)] = (mm, mlast)
                last[("srows",)] = (mm, p.at(mm))
                vt = p.op("vector", lambda vv, cs=cs, hl=hl, hb=hb: vv.tensor_sub(
                    tmpc[:, :], ps_m[hb, :], locn[hl][:, cs]),
                    wait=[(mm, mlast), (dv, gc_done[1])] + acq(("tmpc",)), inc=dv)
                last[("pm2",)] = (dv, vt)
                vcb = p.op("vector", lambda vv, h=h, hl=hl, cs=cs: vv.scalar_tensor_tensor(
                    combt[:, :], tmpc[:, :], gcols[:, h:h + 1], locn[hl][:, cs],
                    ALU.mult, ALU.add), wait=acq(("combt",)), inc=dv)
                last[("tmpc",)] = (dv, vcb)
                sv = p.dma("scalar", comb_d.ap()[h * 64:(h + 1) * 64, cs], combt[:, :],
                           wait=[(dv, vcb)])
                last[("combt",)] = sv
            last[("locn", hl)] = (dv, p.at(dv))
        last[("bufs",)] = [(mm, p.at(mm)), (dv, p.at(dv))]
        last[("snapP",)] = (mm, p.at(mm))
        last[("vaug",)] = (mm, p.at(mm))
    comb_done = p.dma_all()
    st.close()
    emit_linear_fm(p, "wo1", comb_d.ap(), woT_d.ap(), s2_d.ap(), D, D, T,
                   start_wait=comb_done)


def build_s13():
    p = P("s13")
    emit_attn1(p)
    return p.finalize()



def emit_moe(p):
    xnb_d = p.dram("xn2b", [D, T], BF16, kind="ExternalInput")
    x2_d = p.dram("x2", [D, T], F32, kind="ExternalInput")
    s2_d = p.dram("s2", [D, T], F32, kind="ExternalInput")
    gWT_d = p.dram("gWT", [D, 8], F32, kind="ExternalInput")
    W0T_d = p.dram("W0T", [D, D], F32, kind="ExternalInput")
    W1T_d = p.dram("W1T", [D, D], F32, kind="ExternalInput")
    oWT_d = p.dram("oWT", [D, D], F32, kind="ExternalInput")
    eye_d = p.dram("eyeF", [128, 128], F32, kind="ExternalInput")
    sel2_d = p.dram("sel2", [2, 256], F32, kind="ExternalInput")
    x3_d = p.dram("x3", [D, T], F32, kind="ExternalOutput")
    f_d = p.dram("fscr", [D, T], BF16)
    g2_d = p.dram("g2scr", [D, T], BF16)
    mm, dv, ac = p.mm, p.dv, p.ac
    st = ExitStack()
    xnb = [p.sb(f"m_x{i}", [128, T], BF16, st) for i in range(NDC)]
    gwt = p.sb("m_gwt", [128, 8 * 8], BF16, st)
    eye1 = p.sb("m_eye", [128, 128], F32, st)
    lgt = p.sb("m_lgt", [128, 8], F32, st)
    col = p.sb("m_col", [128, 4], F32, st)
    s01 = p.sb("m_s01", [128, 2], F32, st)
    s01row = p.sb("m_s01r", [2, T], F32, st)
    cnt = p.sb("m_cnt", [128, 8], F32, st)
    w0 = [p.sb(f"m_w0{i}", [128, 128], BF16, st) for i in range(NDC)]
    w1 = [p.sb(f"m_w1{i}", [128, 128], BF16, st) for i in range(NDC)]
    h0b = p.sb("m_h0", [128, 512], BF16, st)
    fb = p.sb("m_f", [128, 512], F32, st)
    fbb = p.sb("m_fb", [128, 512], BF16, st)
    onesr = p.sb("m_onesr", [1, 128], F32, st)
    sel2 = p.sb("m_sel2", [2, 256], F32, st)
    ps_g = [p.ps(f"m_pg{i}", [128, 512], F32, st) for i in range(2)]
    ps_t = p.ps("m_pt", [128, 512], F32, st)
    ps_r = p.ps("m_pr", [128, 512], F32, st)
    last = {}

    def acq(key):
        v = last.get(key)
        return [] if v is None else (list(v) if isinstance(v, list) else [v])

    p.op("gpsimd", lambda g: g.memset(onesr[:, :], 1.0))
    gp0 = p.at(p.gp)
    for i in range(NDC):
        p.dma("sync", xnb[i][:, :], xnb_d.ap()[i * 128:(i + 1) * 128, :])
        p.dma("gpsimd", gwt[:, i * 8:(i + 1) * 8], gWT_d.ap()[i * 128:(i + 1) * 128, :])
    p.dma("scalar", eye1[:, :], eye_d.ap()[:, :])
    p.dma("scalar", sel2[:, :], sel2_d.ap()[:, :])
    ld0 = p.dma_all()
    for tt in range(NCH):
        trows = slice(tt * 128, (tt + 1) * 128)
        m = mm_accum(p, ps_g[tt % 2][:, 0:8],
                     [(xnb[k][:, trows], gwt[:, k * 8:(k + 1) * 8]) for k in range(NDC)],
                     wait_first=ld0 + acq(("pg", tt % 2)), inc=mm)
        v = p.op("vector", lambda vv, par=tt % 2: vv.tensor_copy(lgt[:, :], ps_g[par][:, 0:8]),
                 wait=[(mm, m)] + acq(("lgt",)), inc=dv)
        last[("pg", tt % 2)] = (dv, v)
        v = p.op("vector", lambda vv: vv.reduce_max(col[:, 0:1], lgt[:, :], axis=AX.X),
                 inc=dv)
        v = p.op("vector", lambda vv: vv.tensor_scalar_mul(col[:, 0:1], col[:, 0:1], -1.0),
                 inc=dv)
        a = p.op("scalar", lambda s: s.activation(lgt[:, :], lgt[:, :], AF.Exp,
                                                  bias=col[:, 0:1]),
                 wait=[(dv, v)], inc=ac)
        v = p.op("vector", lambda vv: vv.reduce_sum(col[:, 0:1], lgt[:, :], axis=AX.X),
                 wait=[(ac, a)], inc=dv)
        v = p.op("vector", lambda vv: vv.reciprocal(col[:, 0:1], col[:, 0:1]), inc=dv)
        v = p.op("vector", lambda vv: vv.tensor_scalar(
            lgt[:, :], lgt[:, :], col[:, 0:1], None, ALU.mult), inc=dv)
        v = p.op("vector", lambda vv: vv.tensor_single_scalar(
            cnt[:, :], lgt[:, :], 0.3, ALU.is_gt), inc=dv)
        v = p.op("vector", lambda vv: vv.reduce_sum(col[:, 1:2], cnt[:, :], axis=AX.X),
                 inc=dv)
        v = p.op("vector", lambda vv: vv.tensor_scalar(
            col[:, 2:3], col[:, 1:2], -1.0, 8.0, ALU.mult, ALU.add), inc=dv)
        v = p.op("vector", lambda vv: vv.tensor_mul(s01[:, 0:1], col[:, 2:3], lgt[:, 0:1]),
                 inc=dv)
        v = p.op("vector", lambda vv: vv.tensor_mul(s01[:, 1:2], col[:, 1:2], lgt[:, 1:2]),
                 inc=dv)
        tmv = p.op("tensor", lambda t: t.transpose(
            ps_t[0:2, 0:128], s01[:, :], eye1[:, :]),
            wait=[(dv, v)] + acq(("pt",)), inc=mm)
        v = p.op("vector", lambda vv, tt=tt: vv.tensor_copy(
            s01row[:, tt * 128:(tt + 1) * 128], ps_t[0:2, 0:128]),
            wait=[(mm, tmv)], inc=dv)
        last[("pt",)] = (dv, v)
        last[("lgt",)] = (dv, v)
    s01_done = p.at(dv)
    for oc in range(NDC):
        for i in range(NDC):
            p.dma("gpsimd", w0[i][:, :], W0T_d.ap()[i * 128:(i + 1) * 128, oc * 128:(oc + 1) * 128],
                  wait=acq(("w0", i)))
            p.dma("gpsimd", w1[i][:, :], W1T_d.ap()[i * 128:(i + 1) * 128, oc * 128:(oc + 1) * 128],
                  wait=acq(("w1", i)))
        wp = p.dma_all()
        for nt in range(4):
            cs = bass.ts(nt, 512)
            m0 = mm_accum(p, ps_g[0][:, :],
                          [(w0[k][:, :], xnb[k][:, cs]) for k in range(NDC)],
                          wait_first=wp + acq(("pg", 0)), inc=mm)
            m1 = mm_accum(p, ps_g[1][:, :],
                          [(w1[k][:, :], xnb[k][:, cs]) for k in range(NDC)],
                          wait_first=acq(("pg", 1)), inc=mm)
            r0 = p.op("tensor", lambda t, cs=cs: t.matmul(
                ps_r[:, :], sel2[:, 0:128], s01row[0:2, cs], start=True, stop=True),
                wait=[(dv, s01_done), (p.gp, gp0)] + acq(("prr",)), inc=mm)
            v = p.op("vector", lambda vv: vv.tensor_copy(h0b[:, :], ps_g[0][:, :]),
                     wait=[(mm, m0)] + acq(("h0b",)), inc=dv)
            last[("pg", 0)] = (dv, v)
            v2 = p.op("vector", lambda vv: vv.tensor_mul(fb[:, :], h0b[:, :], ps_r[:, :]),
                      wait=[(mm, r0)] + acq(("fb",)), inc=dv)
            last[("h0b",)] = (dv, v2)
            r1 = p.op("tensor", lambda t, cs=cs: t.matmul(
                ps_r[:, :], sel2[:, 128:256], s01row[0:2, cs], start=True, stop=True),
                wait=[(dv, v2)], inc=mm)
            last[("prr",)] = (mm, r1)
            v3 = p.op("vector", lambda vv: vv.tensor_copy(h0b[:, :], ps_g[1][:, :]),
                      wait=[(mm, m1)], inc=dv)
            last[("pg", 1)] = (dv, v3)
            v4 = p.op("vector", lambda vv: vv.tensor_mul(h0b[:, :], h0b[:, :], ps_r[:, :]),
                      wait=[(mm, r1)], inc=dv)
            last[("prr",)] = (dv, v4)
            v5 = p.op("vector", lambda vv: vv.tensor_add(fbb[:, :], fb[:, :], h0b[:, :]),
                      wait=acq(("fbb",)), inc=dv)
            last[("fb",)] = (dv, v5)
            sv = p.dma("scalar", f_d.ap()[oc * 128:(oc + 1) * 128, cs], fbb[:, :],
                       wait=[(dv, v5)])
            last[("fbb",)] = sv
        for i in range(NDC):
            last[("w0", i)] = (mm, p.at(mm))
            last[("w1", i)] = (mm, p.at(mm))
    f_done = p.dma_all()
    st.close()
    st2 = ExitStack()
    fsb = [p.sb(f"m_f2{i}", [128, T], BF16, st2) for i in range(NDC)]
    wsb = [p.sb(f"m_wo{i}", [128, D], BF16, st2) for i in range(NDC)]
    g2s = p.sb("m_g2", [128, 512], BF16, st2)
    x2s = p.sb("m_x2s", [128, T], F32, st2)
    s2s = p.sb("m_s2", [128, T], F32, st2)
    g2f = p.sb("m_g2f", [128, T], BF16, st2)
    x3s = p.sb("m_x3s", [128, 512], F32, st2)
    ps_o = [p.ps(f"m_po{i}", [128, 512], F32, st2) for i in range(2)]
    for i in range(NDC):
        p.dma("sync", fsb[i][:, :], f_d.ap()[i * 128:(i + 1) * 128, :],
              wait=f_done if i == 0 else ())
        p.dma("gpsimd", wsb[i][:, :], oWT_d.ap()[i * 128:(i + 1) * 128, :],
              wait=f_done if i == 0 else ())
    f2_pairs = p.dma_all()
    for oc in range(NDC):
        for nt in range(4):
            cs = bass.ts(nt, 512)
            par = (oc * 4 + nt) % 2
            m = mm_accum(p, ps_o[par][:, :],
                         [(wsb[k][:, oc * 128:(oc + 1) * 128], fsb[k][:, cs])
                          for k in range(NDC)],
                         wait_first=f2_pairs + acq(("po2", par)), inc=mm)
            a = p.op("scalar", lambda s, par=par: s.activation(
                g2s[:, :], ps_o[par][:, :], AF.Silu),
                wait=[(mm, m)] + acq(("g2s",)), inc=ac)
            last[("po2", par)] = (ac, a)
            sv = p.dma("scalar", g2_d.ap()[oc * 128:(oc + 1) * 128, cs], g2s[:, :],
                       wait=[(ac, a)])
            last[("g2s",)] = sv
    g2_done = p.dma_all()
    for oc in range(NDC):
        l1 = p.dma("sync", x2s[:, :], x2_d.ap()[oc * 128:(oc + 1) * 128, :],
                   wait=acq(("x2s",)) + (g2_done if oc == 0 else []))
        l2 = p.dma("sync", s2s[:, :], s2_d.ap()[oc * 128:(oc + 1) * 128, :])
        l3 = p.dma("sync", g2f[:, :], g2_d.ap()[oc * 128:(oc + 1) * 128, :])
        for nt in range(4):
            cs = bass.ts(nt, 512)
            v = p.op("vector", lambda vv, cs=cs: vv.tensor_mul(
                x3s[:, :], s2s[:, cs], g2f[:, cs]),
                wait=([l1, l2, l3] if nt == 0 else []) + acq(("x3s",)), inc=dv)
            v = p.op("vector", lambda vv, cs=cs: vv.tensor_add(
                x3s[:, :], x3s[:, :], x2s[:, cs]), inc=dv)
            sv = p.dma("scalar", x3_d.ap()[oc * 128:(oc + 1) * 128, cs], x3s[:, :],
                       wait=[(dv, v)])
            last[("x3s",)] = sv
        last[("x2s",)] = [sv]
    p.op("sync", lambda e: None, wait=p.dma_all())
    st2.close()


def build_s14():
    p = P("s14")
    emit_moe(p)
    return p.finalize()




def _rope_tables(rope_cos, rope_sin, tok0):
    cf = np.ones((128, T), np.float32)
    sf = np.zeros((128, T), np.float32)
    P = rope_cos.shape[0]
    r0 = S - P
    g = np.arange(tok0, tok0 + T)
    sel = g >= r0
    if sel.any():
        rows = g[sel] - r0
        pi = (np.arange(128) % 64) // 2
        cf[:, sel] = rope_cos[rows][:, pi].T
        sf[:, sel] = rope_sin[rows][:, pi].T
    return cf, sf


def _qkv0_consts():
    hs2 = np.zeros((128, 2), BF)
    hs2[:64, 0] = 1; hs2[64:, 1] = 1
    rep2 = np.zeros((2, 128), np.float32)
    rep2[0, :64] = 1; rep2[1, 64:] = 1
    psign = np.zeros((128, 128), BF)
    i = np.arange(0, 128, 2)
    psign[i + 1, i] = -1
    psign[i, i + 1] = 1
    return hs2, rep2, psign


def run_stage1(x):
    xTs = shard_tokens(x)
    nc1 = build_s1()
    return xTs, launch(nc1, [{"xT": xTs[c]} for c in range(NCORES)], ["xn", "xnb"])


def run_stage2(inputs, r1):
    wqT = np.ascontiguousarray(np.asarray(inputs["L0_attn_wq"], np.float32).T)
    wkT = np.ascontiguousarray(np.asarray(inputs["L0_attn_wk"], np.float32).T)
    wvT = np.ascontiguousarray(np.asarray(inputs["L0_attn_wv"], np.float32).T)
    qnw = np.ascontiguousarray(np.tile(np.asarray(inputs["L0_attn_q_norm_w"], np.float32), 2)[:, None])
    knw = np.ascontiguousarray(np.tile(np.asarray(inputs["L0_attn_k_norm_w"], np.float32), 2)[:, None])
    hs2, rep2, psign = _qkv0_consts()
    nc2 = build_s2()
    maps = []
    for c in range(NCORES):
        j = c % 4
        cf, sf = _rope_tables(np.asarray(inputs["L0_attn_rope_cos"], np.float32),
                              np.asarray(inputs["L0_attn_rope_sin"], np.float32), j * T)
        maps.append({"xnb": r1[c]["xnb"], "wqT": wqT, "wkT": wkT, "wvT": wvT,
                     "qnw": qnw, "knw": knw, "cfm": cf, "sfm": sf,
                     "hs2": hs2, "rep2": rep2, "psign": psign})
    return launch(nc2, maps, ["Qp", "Kp", "V"])


def run_stage3(r2):
    nc3 = build_s3()
    maps = []
    for c in range(NCORES):
        b, j = divmod(c, 4)
        Kag = np.ascontiguousarray(np.concatenate([r2[4 * b + r]["Kp"] for r in range(4)], axis=0))
        Vag = np.ascontiguousarray(np.concatenate([r2[4 * b + r]["V"] for r in range(4)], axis=0))
        maps.append({"Qp2": r2[c]["Qp"], "Kag": Kag, "Vag": Vag})
    return launch(nc3, maps, ["AVt"])


def run_stage4(inputs, r3):
    nc4 = build_s4()
    woT = np.ascontiguousarray(np.asarray(inputs["L0_attn_wo"], np.float32).T)
    maps = [{"AVt": r3[c]["AVt"], "woT": woT} for c in range(NCORES)]
    return launch(nc4, maps, ["s1"])


def _ffn_tables(tok0):
    f = (np.arange(128) % 64) % 32
    inv_freq = (1.0 / 10000.0 ** (2.0 * f / 64.0)).astype(np.float64)
    base = ((2.0 * f + 25.6) / 89.6).astype(np.float64)

    def tab(lo):
        g = np.arange(tok0 + lo, tok0 + T, dtype=np.float64)
        ang = g[None, :] * inv_freq[:, None]
        ex = np.mod(g, 256.0) / 128.0
        sc_q = base[:, None] ** ex[None, :]
        return (np.cos(ang) * sc_q).astype(np.float32), \
               (np.sin(ang) * sc_q).astype(np.float32), \
               (np.cos(ang) / sc_q).astype(np.float32), \
               (np.sin(ang) / sc_q).astype(np.float32)

    cq, sq, _, _ = tab(-QLO)
    _, _, ck, sk = tab(-KLO)
    B2 = (base ** 2).astype(np.float32)[:, None]
    return cq, sq, ck, sk, B2


def _ffn_consts():
    prh = np.zeros((128, 128), BF)
    for blk in (0, 64):
        i = np.arange(32)
        prh[blk + i + 32, blk + i] = -1
        prh[blk + i, blk + i + 32] = 1
    eye = np.eye(128, dtype=BF)

    def mk_mask(nq, nks):
        m = np.zeros((128, nq * nks), BF)
        for sub in range(nks):
            kk = np.arange(128)[:, None]
            qq = np.arange(nq)[None, :]
            ko = sub * 128 + kk
            m[:, sub * nq:(sub + 1) * nq] = ((ko - 256 <= qq) & (qq <= ko)).astype(BF)
        return m

    maskA = mk_mask(256, 4)
    maskB = mk_mask(128, 3)
    return prh, eye, maskA, maskB


def run_ffn_layers(inputs, r1):
    hs2, rep2, _ = _qkv0_consts()
    prh, eye, maskA, maskB = _ffn_consts()
    maskA0_core0 = maskA.copy()
    maskA0_core0[:, 0:512] = 0
    ICS = tiles(2730, 128)
    nc = build_ffn_layer()
    ys = [r1[c]["xn"] for c in range(NCORES)]
    Wq = np.asarray(inputs["L0_ffn_Wqkv"], np.float32)
    QS = np.asarray(inputs["L0_ffn_q_scale"], np.float32)
    KS = np.asarray(inputs["L0_ffn_k_scale"], np.float32)
    WO = np.asarray(inputs["L0_ffn_Wo"], np.float32)
    LG = np.asarray(inputs["L0_ffn_ln_g"], np.float32)
    LB = np.asarray(inputs["L0_ffn_ln_b"], np.float32)
    W1 = np.asarray(inputs["L0_ffn_W1"], np.float32)
    B1 = np.asarray(inputs["L0_ffn_b1"], np.float32)
    W2 = np.asarray(inputs["L0_ffn_W2"], np.float32)
    B2b = np.asarray(inputs["L0_ffn_b2"], np.float32)
    ylist = None
    for l in range(6):
        maps = []
        for c in range(NCORES):
            b, j = divmod(c, 4)
            if j == 0:
                halo = np.zeros((D, YLO), np.float32)
            else:
                halo = np.ascontiguousarray(ys[c - 1][:, T - YLO:])
            mp = ffn_layer_map(inputs, l, c)
            mp["y"] = ys[c]
            mp["yh"] = halo
            maps.append(mp)
        res = launch(nc, maps, ["y_out"])
        ys = [res[c]["y_out"] for c in range(NCORES)]
        ylist = ys
    return ylist


def ffn_layer_map(inputs, l, c):
    hs2, rep2, _ = _qkv0_consts()
    prh, eye, maskA, maskB = _ffn_consts()
    maskA0_core0 = maskA.copy()
    maskA0_core0[:, 0:512] = 0
    ICS = tiles(2730, 128)
    b, j = divmod(c, 4)
    Wq = np.asarray(inputs["L0_ffn_Wqkv"], np.float32)
    QS = np.asarray(inputs["L0_ffn_q_scale"], np.float32)
    KS = np.asarray(inputs["L0_ffn_k_scale"], np.float32)
    WO = np.asarray(inputs["L0_ffn_Wo"], np.float32)
    LG = np.asarray(inputs["L0_ffn_ln_g"], np.float32)
    LB = np.asarray(inputs["L0_ffn_ln_b"], np.float32)
    W1 = np.asarray(inputs["L0_ffn_W1"], np.float32)
    B1 = np.asarray(inputs["L0_ffn_b1"], np.float32)
    W2 = np.asarray(inputs["L0_ffn_W2"], np.float32)
    B2b = np.asarray(inputs["L0_ffn_b2"], np.float32)
    lngb = np.zeros((128, 16), np.float32)
    lngb[:, 0:8] = LG[l].reshape(8, 128).T
    lngb[:, 8:16] = LB[l].reshape(8, 128).T
    b1a = np.zeros((128, 22), np.float32)
    b1g = np.zeros((128, 22), np.float32)
    for ici, (ic0, sz) in enumerate(ICS):
        b1a[0:sz, ici] = B1[l][ic0:ic0 + sz]
        b1g[0:sz, ici] = B1[l][2730 + ic0:2730 + ic0 + sz]
    b2c = np.ascontiguousarray(B2b[l].reshape(8, 128).T)
    cq, sq, ck, sk, B2col = _ffn_tables(j * T)
    return {"Wqkv": Wq[l], "qsc": np.tile(QS[l], 2)[:, None],
            "ksc": np.tile(KS[l], 2)[:, None], "Wo": WO[l], "lngb": lngb,
            "W1": W1[l], "b1a": b1a, "b1g": b1g, "W2": W2[l], "b2c": b2c,
            "cq": cq, "sq": sq, "ck": ck, "sk": sk, "B2": B2col,
            "prh": prh, "hs2": hs2, "rep2": rep2, "eye": eye,
            "maskA": maskA, "maskB": maskB,
            "maskA0": maskA0_core0 if j == 0 else maskA}



def run_stage11(inputs, xTs, r4, ys):
    nc = build_s11()
    maps = [{"xT": xTs[c], "s1": r4[c]["s1"], "yf": ys[c]} for c in range(NCORES)]
    return launch(nc, maps, ["x2", "xn2", "xn2b"])


def run_stage12(inputs, r11):
    nc = build_s12()
    wqT = np.ascontiguousarray(np.asarray(inputs["L1_attn_w_q"], np.float32).T)
    wkT = np.ascontiguousarray(np.asarray(inputs["L1_attn_w_k"], np.float32).T)
    wvT = np.ascontiguousarray(np.asarray(inputs["L1_attn_w_v"], np.float32).T)
    eye = np.eye(128, dtype=BF)
    maps = [{"xn2b": r11[c]["xn2b"], "wqT": wqT, "wkT": wkT, "wvT": wvT, "eye": eye}
            for c in range(NCORES)]
    return launch(nc, maps, ["sq", "sk", "v", "snap", "sqs"])


def run_stage13(inputs, r12):
    nc = build_s13()
    mem = np.asarray(inputs["L1_attn_memories"], np.float32)
    mn = np.asarray(inputs["L1_attn_memory_norms"], np.float32)
    mnT = np.zeros((128, 64), np.float32)
    memd = np.zeros((128, 2048), np.float32)
    for h in range(16):
        hl = h % 2
        hp = h // 2
        for m in range(4):
            mnT[hl * 64:(hl + 1) * 64, 4 * h + m] = mn[m, h]
            memd[hl * 64:(hl + 1) * 64, (hp * 4 + m) * 64:(hp * 4 + m) * 64 + 64] = mem[m, h]
    sel4 = np.zeros((4, 512), np.float32)
    for m in range(4):
        sel4[m, m * 128:(m + 1) * 128] = 1
    ltri = (np.arange(128)[:, None] <= np.arange(128)[None, :]).astype(BF)
    gate = np.asarray(inputs["L1_attn_gate"], np.float32)[None, :]
    woT = np.ascontiguousarray(np.asarray(inputs["L1_attn_w_o"], np.float32).T)
    tots = []
    for c in range(NCORES):
        s = r12[c]["snap"].reshape(128, 8, NCH, 65)
        tots.append(np.ascontiguousarray(s[:, :, NCH - 1, :].reshape(128, 8 * 65)))
    maps = []
    for c in range(NCORES):
        b, j = divmod(c, 4)
        pm = np.zeros((128, 4), np.float32)
        pm[:, :j] = 1.0
        mp = {"sq": r12[c]["sq"], "sk": r12[c]["sk"], "v": r12[c]["v"],
              "snap": r12[c]["snap"], "pmask": pm, "mnT": mnT, "memd": memd,
              "sel4": sel4, "ltri": ltri, "gate": gate, "woT": woT,
              "totA": np.ascontiguousarray(np.concatenate([tots[4 * b + jj] for jj in range(4)], axis=0)),
              "sqsA": np.ascontiguousarray(np.concatenate([r12[4 * b + jj]["sqs"] for jj in range(4)], axis=0))}
        maps.append(mp)
    return launch(nc, maps, ["s2"])


def run_stage14(inputs, r11, r13):
    nc = build_s14()
    gWT = np.ascontiguousarray(np.asarray(inputs["L1_ffn_gate_W"], np.float32).T)
    W0T = np.ascontiguousarray(np.asarray(inputs["L1_ffn_expert_W"], np.float32)[0].T)
    W1T = np.ascontiguousarray(np.asarray(inputs["L1_ffn_expert_W"], np.float32)[1].T)
    oWT = np.ascontiguousarray(np.asarray(inputs["L1_ffn_out_W"], np.float32).T)
    eyeF = np.eye(128, dtype=np.float32)
    sel2 = np.zeros((2, 256), np.float32)
    sel2[0, 0:128] = 1
    sel2[1, 128:256] = 1
    maps = [{"xn2b": r11[c]["xn2b"], "x2": r11[c]["x2"], "s2": r13[c]["s2"],
             "gWT": gWT, "W0T": W0T, "W1T": W1T, "oWT": oWT, "eyeF": eyeF,
             "sel2": sel2} for c in range(NCORES)]
    return launch(nc, maps, ["x3"])




def emit_halo_prep(p, tlag_d, yh_d, hsel_d, wait=()):
    dv = p.dv
    st = ExitStack()
    hb2 = [p.sb(f"hp_b{i}", [128, YLO], F32, st) for i in range(2)]
    acc2 = [p.sb(f"hp_acc{i}", [128, YLO], F32, st) for i in range(2)]
    hsel = p.sb("hp_sel", [128, 4], F32, st)
    ls = p.dma("sync", hsel[:, :], hsel_d.ap()[:, :])
    hlast = {}
    stlast = {}
    for i in range(NDC):
        acc = acc2[i % 2]
        vv_last = None
        for r in range(4):
            par = (i * 4 + r) % 2
            hb = hb2[par]
            lb = p.dma("sync", hb[:, :], tlag_d.ap()[r * D + i * 128:r * D + (i + 1) * 128, :],
                       wait=(list(wait) if i == 0 and r == 0 else []) +
                            ([hlast[par]] if par in hlast else []))
            if r == 0:
                vv_last = p.op("vector", lambda v, r=r, hb=hb, acc=acc: v.tensor_scalar(
                    acc[:, :], hb[:, :], hsel[:, r:r + 1], None, ALU.mult),
                    wait=[lb, ls] + ([stlast[i % 2]] if i % 2 in stlast else []), inc=dv)
            else:
                vv_last = p.op("vector", lambda v, r=r, hb=hb, acc=acc: v.scalar_tensor_tensor(
                    acc[:, :], hb[:, :], hsel[:, r:r + 1], acc[:, :],
                    ALU.mult, ALU.add), wait=[lb], inc=dv)
            hlast[par] = (dv, vv_last)
        stlast[i % 2] = p.dma("scalar", yh_d.ap()[i * 128:(i + 1) * 128, :], acc[:, :],
                              wait=[(dv, vv_last)])
    st.close()


def build_fused():
    p = P("fused")
    ccs = None
    p.prefix = "s1."
    xT = p.dram("xT", [D, T], F32, kind="ExternalInput")
    xn = p.dram("xn", [D, T], F32)
    xnb = p.dram("xnb", [D, T], BF16)
    emit_ln_fm(p, xT.ap(), xn.ap(), xnb.ap(), tag="ln1f")
    p.barrier()
    p.prefix = "q0."
    p.alias["q0.xnb"] = xnb
    emit_qkv0(p)
    p.barrier()
    p.prefix = "x0."
    kag = p.dram("kag", [4 * D, T], BF16)
    vag = p.dram("vag", [4 * 128, (T // 128) * D], BF16)
    p.collective("AllGather", ALU.bypass, p.alias["q0.Kp"], kag)
    p.collective("AllGather", ALU.bypass, p.alias["q0.V"], vag)
    p.prefix = "g."
    hsel = p.dram("hsel", [128, 4], F32, kind="ExternalInput")
    p.prefix = "x0."

    halos = {}

    def make_halo(l):
        old_pref = p.prefix
        p.prefix = f"f{l}."
        tl = p.dram("tl", [D, YLO], F32)
        tlag = p.dram("tlag", [4 * D, YLO], F32)
        yh = p.dram("yhp", [D, YLO], F32)
        p.prefix = old_pref
        return (tl, tlag, yh)

    def emit_halo_start(l, src_ap, wait_list):
        tl, tlag, yh = halos[l]
        stores = []
        for i in range(NDC):
            w = [wait_list[i]] if wait_list else []
            stores.append(p.dma("sync", tl.ap()[i * 128:(i + 1) * 128, :],
                                src_ap[i * 128:(i + 1) * 128, T - YLO:], wait=w))
        cc = p.collective("AllGather", ALU.bypass, tl, tlag, wait=stores)

        def fin():
            emit_halo_prep(p, tlag, yh, hsel, wait=[(p.gp, cc)])
        return fin

    halos[0] = make_halo(0)
    fin0 = emit_halo_start(0, xn.ap(), None)
    fin0()
    p.barrier()
    p.prefix = "a0."
    p.alias["a0.Qp2"] = p.alias["q0.Qp"]
    p.alias["a0.Kag"] = kag
    p.alias["a0.Vag"] = vag
    emit_attn0(p)
    p.barrier()
    p.prefix = "s4."
    woT0 = p.dram("woT", [D, D], F32, kind="ExternalInput")
    s1b = p.dram("s1b", [D, T], F32)
    emit_linear_fm(p, "wo0f", p.alias["a0.AVt"].ap(), woT0.ap(), s1b.ap(), D, D, T)
    p.barrier()
    p.prefix = "g."
    yprev = xn
    for l in range(6):
        p.prefix = f"f{l}."
        p.alias[f"f{l}.y"] = yprev
        p.alias[f"f{l}.yh"] = halos[l][2]
        if l < 5:
            halos[l + 1] = make_halo(l + 1)

            def cb(pairs, yout_d, l=l):
                return emit_halo_start(l + 1, yout_d.ap(), pairs)
        else:
            cb = None
        emit_ffn_layer(p, tail_cb=cb)
        p.barrier()
        yprev = p.alias[f"f{l}.y_out"]
    p.prefix = "c1."
    p.alias["c1.xT"] = xT
    p.alias["c1.s1"] = s1b
    p.alias["c1.yf"] = yprev
    emit_comb1(p)
    p.barrier()
    p.prefix = "q1."
    p.alias["q1.xn2b"] = p.alias["c1.xn2b"]
    emit_qkv1(p)
    p.barrier()
    p.prefix = "x1."
    snap = p.alias["q1.snap"]
    totl = p.dram("totl", [128, 8 * 65], BF16)
    totA = p.dram("totA", [4 * 128, 8 * 65], BF16)
    sqsA = p.dram("sqsA", [4 * 128, 8], F32)
    p.dma("sync", totl.ap()[:, :],
          bass.AP(snap, NCH * 65 - 65, [[8 * NCH * 65, 128], [NCH * 65, 8], [1, 65]]))
    p.barrier()
    p.collective("AllGather", ALU.bypass, totl, totA)
    p.collective("AllGather", ALU.bypass, p.alias["q1.sqs"], sqsA)
    p.barrier()
    p.prefix = "a1."
    for nm in ("sq", "sk", "v", "snap"):
        p.alias["a1." + nm] = p.alias["q1." + nm]
    p.alias["a1.totA"] = totA
    p.alias["a1.sqsA"] = sqsA
    emit_attn1(p)
    p.barrier()
    p.prefix = "m."
    p.keep_outputs = {"m.x3"}
    p.alias["m.xn2b"] = p.alias["c1.xn2b"]
    p.alias["m.x2"] = p.alias["c1.x2"]
    p.alias["m.s2"] = p.alias["a1.s2"]
    emit_moe(p)
    return p.finalize()


def fused_maps(inputs):
    x = np.asarray(inputs["L0_attn_x"], np.float32)
    xTs = shard_tokens(x)
    wqT = np.ascontiguousarray(np.asarray(inputs["L0_attn_wq"], np.float32).T)
    wkT = np.ascontiguousarray(np.asarray(inputs["L0_attn_wk"], np.float32).T)
    wvT = np.ascontiguousarray(np.asarray(inputs["L0_attn_wv"], np.float32).T)
    qnw = np.ascontiguousarray(np.tile(np.asarray(inputs["L0_attn_q_norm_w"], np.float32), 2)[:, None])
    knw = np.ascontiguousarray(np.tile(np.asarray(inputs["L0_attn_k_norm_w"], np.float32), 2)[:, None])
    hs2, rep2, psign = _qkv0_consts()
    woT0 = np.ascontiguousarray(np.asarray(inputs["L0_attn_wo"], np.float32).T)
    wqT1 = np.ascontiguousarray(np.asarray(inputs["L1_attn_w_q"], np.float32).T)
    wkT1 = np.ascontiguousarray(np.asarray(inputs["L1_attn_w_k"], np.float32).T)
    wvT1 = np.ascontiguousarray(np.asarray(inputs["L1_attn_w_v"], np.float32).T)
    eye = np.eye(128, dtype=BF)
    mem = np.asarray(inputs["L1_attn_memories"], np.float32)
    mn = np.asarray(inputs["L1_attn_memory_norms"], np.float32)
    mnT = np.zeros((128, 64), np.float32)
    memd = np.zeros((128, 2048), np.float32)
    for h in range(16):
        hl = h % 2
        hp = h // 2
        for m in range(4):
            mnT[hl * 64:(hl + 1) * 64, 4 * h + m] = mn[m, h]
            memd[hl * 64:(hl + 1) * 64, (hp * 4 + m) * 64:(hp * 4 + m) * 64 + 64] = mem[m, h]
    sel4 = np.zeros((4, 512), np.float32)
    for m in range(4):
        sel4[m, m * 128:(m + 1) * 128] = 1
    ltri = (np.arange(128)[:, None] <= np.arange(128)[None, :]).astype(BF)
    gate = np.asarray(inputs["L1_attn_gate"], np.float32)[None, :]
    woT1 = np.ascontiguousarray(np.asarray(inputs["L1_attn_w_o"], np.float32).T)
    gWT = np.ascontiguousarray(np.asarray(inputs["L1_ffn_gate_W"], np.float32).T)
    W0T = np.ascontiguousarray(np.asarray(inputs["L1_ffn_expert_W"], np.float32)[0].T)
    W1T = np.ascontiguousarray(np.asarray(inputs["L1_ffn_expert_W"], np.float32)[1].T)
    oWT = np.ascontiguousarray(np.asarray(inputs["L1_ffn_out_W"], np.float32).T)
    eyeF = np.eye(128, dtype=np.float32)
    sel2 = np.zeros((2, 256), np.float32)
    sel2[0, 0:128] = 1
    sel2[1, 128:256] = 1
    maps = []
    for c in range(NCORES):
        b, j = divmod(c, 4)
        cf, sf = _rope_tables(np.asarray(inputs["L0_attn_rope_cos"], np.float32),
                              np.asarray(inputs["L0_attn_rope_sin"], np.float32), j * T)
        hsel = np.zeros((128, 4), np.float32)
        if j > 0:
            hsel[:, j - 1] = 1.0
        pm = np.zeros((128, 4), np.float32)
        pm[:, :j] = 1.0
        mp = {"s1.xT": xTs[c],
              "q0.wqT": wqT, "q0.wkT": wkT, "q0.wvT": wvT, "q0.qnw": qnw,
              "q0.knw": knw, "q0.cfm": cf, "q0.sfm": sf, "q0.hs2": hs2,
              "q0.rep2": rep2, "q0.psign": psign,
              "s4.woT": woT0, "g.hsel": hsel,
              "q1.wqT": wqT1, "q1.wkT": wkT1, "q1.wvT": wvT1, "q1.eye": eye,
              "a1.pmask": pm, "a1.mnT": mnT, "a1.memd": memd, "a1.sel4": sel4,
              "a1.ltri": ltri, "a1.gate": gate, "a1.woT": woT1,
              "m.gWT": gWT, "m.W0T": W0T, "m.W1T": W1T, "m.oWT": oWT,
              "m.eyeF": eyeF, "m.sel2": sel2}
        for l in range(6):
            for k, v in ffn_layer_map(inputs, l, c).items():
                mp[f"f{l}.{k}"] = v
        maps.append(mp)
    return maps


def kernel_fused(**inputs):
    nc = build_fused()
    maps = fused_maps(inputs)
    res = launch(nc, maps, ["m.x3"])
    out = np.zeros((B, S, D), np.float32)
    for c in range(NCORES):
        b, j = divmod(c, 4)
        out[b, j * T:(j + 1) * T, :] = res[c]["m.x3"].T
    return out


def kernel(**inputs):
    return kernel_fused(**inputs)


def kernel_multilaunch(**inputs):
    x = np.asarray(inputs["L0_attn_x"], np.float32)
    xTs, r1 = run_stage1(x)
    r2 = run_stage2(inputs, r1)
    r3 = run_stage3(r2)
    r4 = run_stage4(inputs, r3)
    ys = run_ffn_layers(inputs, r1)
    r11 = run_stage11(inputs, xTs, r4, ys)
    r12 = run_stage12(inputs, r11)
    r13 = run_stage13(inputs, r12)
    r14 = run_stage14(inputs, r11, r13)
    out = np.zeros((B, S, D), np.float32)
    for c in range(NCORES):
        b, j = divmod(c, 4)
        out[b, j * T:(j + 1) * T, :] = r14[c]["x3"].T
    return out


# version 11
# speedup vs baseline: 1.2523x; 1.0643x over previous
import math
from collections import defaultdict
from contextlib import ExitStack

import numpy as np
import ml_dtypes

import concourse.bass as bass
import concourse.mybir as mybir
from concourse.bass_utils import run_bass_kernel_spmd

F32 = mybir.dt.float32
BF16 = mybir.dt.bfloat16
AF = mybir.ActivationFunctionType
ALU = mybir.AluOpType
AX = mybir.AxisListType
BF = ml_dtypes.bfloat16

NCORES = 8
B, S, D, H, Dh = 2, 8192, 1024, 16, 64
T = S // 4
NDC = D // 128
ENGINES = ("sync", "scalar", "vector", "tensor", "gpsimd")

_LAUNCHES = []


class P:

    SERIAL = {"vector": "dv", "scalar": "ac", "gpsimd": "gp"}

    def __init__(self, name):
        self.nc = bass.Bass("TRN2", target_bir_lowering=False)
        self.name = name
        self.prefix = ""
        self.alias = {}
        self.stack = ExitStack()
        self.ops = {e: [] for e in ENGINES}
        self.cnt = defaultdict(int)
        self.dmas = [self.sem(f"g_dma{i}") for i in range(8)]
        self.dma_rr = 0
        self.sdmas = [self.sem(f"g_sdma{i}") for i in range(6)]
        self.sdma_rr = 0
        self.mm = self.sem("g_mm")
        self.dv = self.sem("g_dv")
        self.ac = self.sem("g_ac")
        self.gp = self.sem("g_gp")
        self._esem = {"vector": self.dv, "scalar": self.ac, "gpsimd": self.gp}
        self._consts = {}
        self._const_tile = self.sb("constcols", [128, 16], F32)
        self.ccsem = None

    def collective(self, kind, op, ins, outs, groups=((0, 1, 2, 3), (4, 5, 6, 7)), wait=()):
        if self.ccsem is None:
            self.ccsem = self.sem("g_cc")
        return self.op("gpsimd", lambda g: g.collective_compute(
            kind, op, replica_groups=[list(g_) for g_ in groups],
            ins=[ins.ap().opt()], outs=[outs.ap().opt()]), wait=list(wait), inc=self.gp)

    def barrier(self, tag=""):
        pairs = [(self.mm, self.at(self.mm)), (self.dv, self.at(self.dv)),
                 (self.ac, self.at(self.ac)), (self.gp, self.at(self.gp))]
        pairs += self.dma_all()
        for e in ENGINES:
            self.op(e, lambda eng: None, wait=pairs)

    def const_col(self, value, dtype=F32):
        assert dtype == F32
        key = float(value)
        if key not in self._consts:
            idx = len(self._consts)
            assert idx < 16
            ap = self._const_tile[:, idx:idx + 1]
            post = self.op("gpsimd", lambda g, ap=ap, value=value: g.memset(ap, value))
            self._consts[key] = (ap, (self.gp, post))
        return self._consts[key]

    def sem(self, name):
        return self.stack.enter_context(self.nc.semaphore(name))

    def sb(self, name, shape, dtype, stack=None):
        return (stack or self.stack).enter_context(
            self.nc.sbuf_tensor(self.prefix + name, shape, dtype))

    def ps(self, name, shape, dtype=F32, stack=None):
        return (stack or self.stack).enter_context(
            self.nc.psum_tensor(self.prefix + name, shape, dtype))

    def dram(self, name, shape, dtype, kind=None):
        full = self.prefix + name
        if full in self.alias:
            return self.alias[full]
        if self.prefix and kind == "ExternalOutput" and \
                full not in getattr(self, "keep_outputs", ()):
            kind = None
        if kind is None:
            t = self.nc.dram_tensor(full, shape, dtype)
        else:
            t = self.nc.dram_tensor(full, shape, dtype, kind=kind)
        self.alias[full] = t
        return t

    def op(self, eng, fn, wait=(), inc=None, inc_n=1, _dma=False):
        wait = [w for w in wait if w is not None]
        if not _dma and eng in self._esem:
            es = self._esem[eng]
            assert inc is None or inc is es, (self.name, eng, inc and inc.name)
            wait.append((es, self.cnt[es.name]))
            inc = es
            inc_n = 1
        for s, v in wait:
            assert self.cnt[s.name] >= v, (self.name, eng, s.name, v, self.cnt[s.name])
        post = None
        if inc is not None:
            self.cnt[inc.name] += inc_n
            post = self.cnt[inc.name]

        def run(engobj, wait=wait, inc=inc, inc_n=inc_n, fn=fn):
            for s, v in wait:
                engobj.wait_ge(s, v)
            inst = fn(engobj)
            if inc is not None:
                if inst is None:
                    inst = engobj.nop()
                inst.then_inc(inc, inc_n)

        self.ops[eng].append(run)
        return post

    def dma(self, eng, out, in_, wait=(), inc=None, **kw):
        if eng == "gpsimd":
            s = self.sdmas[self.sdma_rr % len(self.sdmas)]
            self.sdma_rr += 1
        else:
            s = self.dmas[self.dma_rr % len(self.dmas)]
            self.dma_rr += 1
        wait = list(wait) + [(s, self.cnt[s.name])]
        post = self.op(eng, lambda e: e.dma_start(out=out, in_=in_, **kw), wait=wait,
                       inc=s, inc_n=16, _dma=True)
        return (s, post)

    def dma_all(self):
        return [(s, self.cnt[s.name]) for s in self.dmas + self.sdmas]

    def at(self, s):
        return self.cnt[s.name]

    def finalize(self):
        nc = self.nc
        with nc.Block() as block:
            for e in ENGINES:
                def mk(e):
                    def body(engobj):
                        for fn in self.ops[e]:
                            fn(engobj)
                    return body
                getattr(block, e)(mk(e))
        self.stack.close()
        return nc


def launch(nc, in_maps, outs):
    res = run_bass_kernel_spmd(nc, in_maps, core_ids=list(range(NCORES)))
    return [{k: r[k] for k in outs} for r in res.results]



def host_consts():
    c = {}
    c["ones_col_bf"] = np.ones((128, 1), BF)
    c["ones_row_f32"] = np.ones((1, 128), np.float32)
    c["eye_bf"] = np.eye(128, dtype=BF)
    return c



def emit_ln_fm(p, src, dst_f32, dst_bf=None, Tn=T, tag="ln", start_wait=()):
    nc = p.nc
    st = ExitStack()
    x = [p.sb(f"{tag}_x{i}", [128, Tn], F32, st) for i in range(NDC)]
    xb = [p.sb(f"{tag}_xb{i}", [128, Tn], BF16, st) for i in range(NDC)]
    ones_col = p.sb(f"{tag}_ones", [128, 1], BF16, st)
    ones_row = p.sb(f"{tag}_onesr", [1, 128], F32, st)
    mean_r = p.sb(f"{tag}_mean", [1, Tn], F32, st)
    msq_r = p.sb(f"{tag}_msq", [1, Tn], F32, st)
    var_r = p.sb(f"{tag}_var", [1, Tn], F32, st)
    rstd_r = p.sb(f"{tag}_rstd", [1, Tn], F32, st)
    mb_r = p.sb(f"{tag}_mb", [1, Tn], F32, st)
    cv, mm, ev = p.dv, p.mm, p.dv
    gp = p.gp

    p.op("gpsimd", lambda g: g.memset(ones_col[:, :], 1.0))
    p.op("gpsimd", lambda g: g.memset(ones_row[:, :], 1.0))
    gp_ones = p.at(gp)
    for i in range(NDC):
        p.dma("sync", x[i][:, :], src[i * 128:(i + 1) * 128, :],
              wait=start_wait if i == 0 else ())
        p.dma("gpsimd", xb[i][:, :], src[i * 128:(i + 1) * 128, :],
              wait=start_wait if i == 0 else ())
    ld_pairs = p.dma_all()

    with st:
        st2 = ExitStack()
        sq = [p.sb(f"{tag}_sq{i}", [128, Tn], BF16, st2) for i in range(NDC)]
        ps_sum = p.ps(f"{tag}_psum", [1, Tn], F32, st2)
        ps_sq = p.ps(f"{tag}_psq", [1, Tn], F32, st2)
        with st2:
            for i in range(NDC):
                p.op("scalar", lambda s, i=i: s.activation(sq[i][:, :], x[i][:, :], AF.Square),
                     wait=ld_pairs if i == 0 else ())
            sq_ac = p.at(p.ac)
            NT = Tn // 512
            for i in range(NDC):
                for n in range(NT):
                    cs = bass.ts(n, 512)
                    p.op("tensor", lambda t, i=i, n=n, cs=cs: t.matmul(
                        ps_sum[:, cs], ones_col[:, :], xb[i][:, cs],
                        start=(i == 0), stop=(i == NDC - 1)),
                        wait=(ld_pairs + [(gp, gp_ones)]) if i == 0 and n == 0 else ())
                    p.op("tensor", lambda t, i=i, n=n, cs=cs: t.matmul(
                        ps_sq[:, cs], ones_col[:, :], sq[i][:, cs],
                        start=(i == 0), stop=(i == NDC - 1)),
                        wait=[(p.ac, sq_ac)] if i == 0 and n == 0 else (),
                        inc=mm if i == NDC - 1 and n == NT - 1 else None)
            mm_done = p.at(mm)
            p.op("vector", lambda v: v.tensor_scalar_mul(mean_r[:, :], ps_sum[:, :], 1.0 / D),
                 wait=[(mm, mm_done)])
            p.op("vector", lambda v: v.tensor_mul(msq_r[:, :], mean_r[:, :], mean_r[:, :]))
            p.op("vector", lambda v: v.scalar_tensor_tensor(
                var_r[:, :], ps_sq[:, :], 1.0 / D, msq_r[:, :], ALU.mult, ALU.subtract),
                inc=ev)
            ev_rows = p.at(ev)
            epsc, epsw = p.const_col(1e-5)
            p.op("scalar", lambda s: s.activation(var_r[:, :], var_r[:, :], AF.Ln,
                                                  bias=epsc[0:1]),
                 wait=[(ev, ev_rows), epsw])
            sq_post = p.op("scalar", lambda s: s.activation(rstd_r[:, :], var_r[:, :],
                                                            AF.Exp, scale=-0.5))
            p.op("vector", lambda v: v.tensor_mul(mb_r[:, :], mean_r[:, :], rstd_r[:, :]),
                 wait=[(p.ac, sq_post)], inc=ev)
            ev_done = p.at(ev)
        ps_rstd = p.ps(f"{tag}_prstd", [128, Tn], F32, st)
        ps_mb = p.ps(f"{tag}_pmb", [128, Tn], F32, st)
        NT = Tn // 512
        for n in range(NT):
            cs = bass.ts(n, 512)
            p.op("tensor", lambda t, cs=cs: t.matmul(
                ps_rstd[:, cs], ones_row[:, :], rstd_r[:, cs], start=True, stop=True),
                wait=[(ev, ev_done)] if n == 0 else ())
            p.op("tensor", lambda t, cs=cs: t.matmul(
                ps_mb[:, cs], ones_row[:, :], mb_r[:, cs], start=True, stop=True),
                inc=mm if n == NT - 1 else None)
        rep_done = p.at(mm)
        for i in range(NDC):
            p.op("vector", lambda v, i=i: v.tensor_mul(x[i][:, :], x[i][:, :], ps_rstd[:, :]),
                 wait=[(mm, rep_done)] if i == 0 else ())
            p.op("vector", lambda v, i=i: v.tensor_sub(x[i][:, :], x[i][:, :], ps_mb[:, :]),
                 inc=cv)
            if dst_bf is not None:
                p.op("scalar", lambda s, i=i: s.copy(xb[i][:, :], x[i][:, :]), wait=[(cv, p.at(cv))])
        wr_done = p.at(cv)
        ac_done = p.at(p.ac)
        for i in range(NDC):
            p.dma("sync", dst_f32[i * 128:(i + 1) * 128, :], x[i][:, :],
                  wait=[(cv, wr_done)] if i == 0 else ())
            if dst_bf is not None:
                p.dma("scalar", dst_bf[i * 128:(i + 1) * 128, :], xb[i][:, :],
                      wait=[(p.ac, ac_done)] if i == 0 else ())
        st_pairs = p.dma_all()
        p.op("sync", lambda e: None, wait=st_pairs)
    return st_pairs


def build_s1():
    p = P("s1")
    xT = p.dram("xT", [D, T], F32, kind="ExternalInput")
    xn = p.dram("xn", [D, T], F32, kind="ExternalOutput")
    xnb = p.dram("xnb", [D, T], BF16, kind="ExternalOutput")
    emit_ln_fm(p, xT.ap(), xn.ap(), xnb.ap())
    return p.finalize()


def mm_accum(p, ps, pairs, wait_first=(), inc=None):
    n = len(pairs)
    post = None
    for idx, (l, r) in enumerate(pairs):
        post = p.op("tensor", lambda t, l=l, r=r, idx=idx, n=n: t.matmul(
            ps, l, r, start=(idx == 0), stop=(idx == n - 1)),
            wait=wait_first if idx == 0 else (),
            inc=inc if idx == n - 1 else None)
    return post



def emit_qkv0(p):
    nc = p.nc
    xnb_d = p.dram("xnb", [D, T], BF16, kind="ExternalInput")
    wqT_d = p.dram("wqT", [D, D], F32, kind="ExternalInput")
    wkT_d = p.dram("wkT", [D, D], F32, kind="ExternalInput")
    wvT_d = p.dram("wvT", [D, D], F32, kind="ExternalInput")
    qnw_d = p.dram("qnw", [128, 1], F32, kind="ExternalInput")
    knw_d = p.dram("knw", [128, 1], F32, kind="ExternalInput")
    cfm_d = p.dram("cfm", [128, T], F32, kind="ExternalInput")
    sfm_d = p.dram("sfm", [128, T], F32, kind="ExternalInput")
    hs2_d = p.dram("hs2", [128, 2], BF16, kind="ExternalInput")
    rep2_d = p.dram("rep2", [2, 128], F32, kind="ExternalInput")
    psign_d = p.dram("psign", [128, 128], BF16, kind="ExternalInput")
    Qp_d = p.dram("Qp", [D, T], BF16, kind="ExternalOutput")
    Kp_d = p.dram("Kp", [D, T], BF16, kind="ExternalOutput")
    V_d = p.dram("V", [128, (T // 128) * D], BF16, kind="ExternalOutput")

    st = ExitStack()
    xnb = [p.sb(f"q0_x{i}", [128, T], BF16, st) for i in range(NDC)]
    w_sb = [p.sb(f"q0_w{i}", [128, D], BF16, st) for i in range(NDC)]
    cfm = p.sb("q0_cfm", [128, T], F32, st)
    sfm = p.sb("q0_sfm", [128, T], F32, st)
    hs2 = p.sb("q0_hs2", [128, 2], BF16, st)
    rep2 = p.sb("q0_rep2", [2, 128], F32, st)
    psign = p.sb("q0_psign", [128, 128], BF16, st)
    nw = p.sb("q0_nw", [128, 1], F32, st)
    qraw = p.sb("q0_qraw", [128, T], F32, st)
    sqb = p.sb("q0_sqb", [128, T], BF16, st)
    ssrow = p.sb("q0_ss", [2, T], F32, st)
    qhat = p.sb("q0_qhat", [128, T], F32, st)
    qhatb = p.sb("q0_qhatb", [128, T], BF16, st)
    arow = p.sb("q0_arow", [128, T], F32, st)
    qfin = p.sb("q0_qfin", [128, T], BF16, st)
    ps_acc = [p.ps(f"q0_pa{i}", [128, 512], F32, st) for i in range(2)]
    ps_ss = p.ps("q0_pss", [2, 512], F32, st)
    ps_rep = [p.ps(f"q0_pr{i}", [128, 512], F32, st) for i in range(2)]
    ps_p = [p.ps(f"q0_pp{i}", [128, 512], F32, st) for i in range(2)]

    mm, dv, ac = p.mm, p.dv, p.ac

    eps64, w64 = p.const_col(64e-6)
    eps1, w1 = p.const_col(1e-6)
    sc64, wsc = p.const_col(1.0 / 64.0)

    for i in range(NDC):
        p.dma("sync", xnb[i][:, :], xnb_d.ap()[i * 128:(i + 1) * 128, :])
    p.dma("sync", cfm[:, :], cfm_d.ap()[:, :])
    p.dma("sync", sfm[:, :], sfm_d.ap()[:, :])
    p.dma("gpsimd", hs2[:, :], hs2_d.ap()[:, :])
    p.dma("sync", rep2[:, :], rep2_d.ap()[:, :])
    p.dma("gpsimd", psign[:, :], psign_d.ap()[:, :])
    base_pairs = p.dma_all()

    NT = T // 512
    last = {}

    def acq(key):
        v = last.get(key)
        if v is None:
            return []
        return list(v) if isinstance(v, list) else [v]

    for which, wT_d, nw_d, out_d in (("q", wqT_d, qnw_d, Qp_d), ("k", wkT_d, knw_d, Kp_d)):
        for i in range(NDC):
            p.dma("gpsimd", w_sb[i][:, :], wT_d.ap()[i * 128:(i + 1) * 128, :],
                  wait=acq(("w", i)))
        p.dma("sync", nw[:, :], nw_d.ap()[:, :], wait=acq(("nw",)))
        w_pairs = p.dma_all()

        for i in range(NDC):
            ochunk = slice(i * 128, (i + 1) * 128)
            for n in range(NT):
                cs = bass.ts(n, 512)
                par = n % 2
                mm_accum(p, ps_acc[par][:, :],
                         [(w_sb[k][:, ochunk], xnb[k][:, cs]) for k in range(NDC)],
                         wait_first=w_pairs + acq(("pa", par)), inc=mm)
                v = p.op("vector", lambda vv, n=n, par=par, cs=cs: vv.tensor_copy(
                    qraw[:, cs], ps_acc[par][:, :]), wait=[(mm, p.at(mm))], inc=dv)
                last[("pa", par)] = (dv, v)
            a = p.op("scalar", lambda s: s.activation(sqb[:, :], qraw[:, :], AF.Square),
                     wait=[(dv, p.at(dv))], inc=ac)
            for n in range(NT):
                cs = bass.ts(n, 512)
                m = mm_accum(p, ps_ss[:, :], [(hs2[:, :], sqb[:, cs])],
                             wait_first=[(ac, a)] + acq(("pss",)), inc=mm)
                v = p.op("vector", lambda vv, cs=cs: vv.tensor_copy(ssrow[:, cs], ps_ss[:, :]),
                         wait=[(mm, m)], inc=dv)
                last[("pss",)] = (dv, v)
            if which == "q":
                a = p.op("scalar", lambda s: s.activation(
                    ssrow[:, :], ssrow[:, :], AF.Ln, bias=eps64[0:2]),
                    wait=[(dv, p.at(dv)), w64], inc=ac)
            else:
                a = p.op("scalar", lambda s: s.activation(
                    ssrow[:, :], ssrow[:, :], AF.Ln, bias=eps1[0:2], scale=sc64[0:2]),
                    wait=[(dv, p.at(dv)), w1, wsc], inc=ac)
            rst_done = p.op("scalar", lambda s: s.activation(ssrow[:, :], ssrow[:, :],
                                                              AF.Exp, scale=-0.5), inc=ac)
            for n in range(NT):
                cs = bass.ts(n, 512)
                par = n % 2
                m = mm_accum(p, ps_rep[par][:, :], [(rep2[:, :], ssrow[:, cs])],
                             wait_first=[(ac, rst_done)] + acq(("pr", par)), inc=mm)
                v = p.op("vector", lambda vv, cs=cs, par=par: vv.scalar_tensor_tensor(
                    qhat[:, cs], qraw[:, cs], nw[:, :], ps_rep[par][:, :],
                    ALU.mult, ALU.mult), wait=[(mm, m)], inc=dv)
                last[("pr", par)] = (dv, v)
            cb = p.op("scalar", lambda s: s.copy(qhatb[:, :], qhat[:, :]),
                      wait=[(dv, p.at(dv))], inc=ac)
            p.op("vector", lambda vv: vv.tensor_mul(arow[:, :], qhat[:, :], cfm[:, :]),
                 wait=base_pairs, inc=dv)
            for n in range(NT):
                cs = bass.ts(n, 512)
                par = n % 2
                m = mm_accum(p, ps_p[par][:, :], [(psign[:, :], qhatb[:, cs])],
                             wait_first=[(ac, cb)] + acq(("pp", par)), inc=mm)
                v = p.op("vector", lambda vv, cs=cs, par=par: vv.tensor_mul(
                    qfin[:, cs], ps_p[par][:, :], sfm[:, cs]), wait=[(mm, m)], inc=dv)
                last[("pp", par)] = (dv, v)
            v = p.op("vector", lambda vv: vv.tensor_add(qfin[:, :], qfin[:, :], arow[:, :]),
                     inc=dv)
            sv = p.dma("scalar", out_d.ap()[ochunk, :], qfin[:, :], wait=[(dv, v)])
            last[("pa", 0)] = sv
            last[("pa", 1)] = sv
            last[("nw",)] = (dv, v)
        last[("w", 0)] = p.dma_all()
        for i in range(1, NDC):
            last[("w", i)] = last[("w", 0)]

    for i in range(NDC):
        p.dma("gpsimd", w_sb[i][:, :], wvT_d.ap()[i * 128:(i + 1) * 128, :],
              wait=acq(("w", i)))
    for tt in range(T // 128):
        trows = slice(tt * 128, (tt + 1) * 128)
        for o in range(2):
            cs = bass.ts(o, 512)
            par = (tt * 2 + o) % 2
            m = mm_accum(p, ps_acc[par][:, :],
                         [(xnb[k][:, trows], w_sb[k][:, cs]) for k in range(NDC)],
                         wait_first=p.dma_all() + acq(("pa", par)), inc=mm)
            v = p.op("vector", lambda vv, par=par: vv.tensor_copy(sqb[:, 0:512], ps_acc[par][:, :]),
                     wait=[(mm, m)] + acq(("vout",)), inc=dv)
            last[("pa", par)] = (dv, v)
            sv = p.dma("scalar", V_d.ap()[0:128, tt * D + o * 512:tt * D + o * 512 + 512],
                       sqb[:, 0:512], wait=[(dv, v)])
            last[("vout",)] = sv
    p.op("sync", lambda e: None, wait=p.dma_all())
    st.close()


def build_s2():
    p = P("s2")
    emit_qkv0(p)
    return p.finalize()



def emit_attn0(p):
    NKC0 = S // 128
    Qp_d = p.dram("Qp2", [D, T], BF16, kind="ExternalInput")
    Kag_d = p.dram("Kag", [4 * D, T], BF16, kind="ExternalInput")
    Vag_d = p.dram("Vag", [4 * 128, (T // 128) * D], BF16, kind="ExternalInput")
    avt_d = p.dram("AVt", [D, T], BF16, kind="ExternalOutput")
    mm, dv, ac = p.mm, p.dv, p.ac
    st = ExitStack()
    qsb = [p.sb(f"a0_q{i}", [64, T], BF16, st) for i in range(2)]
    ksb = [p.sb(f"a0_k{i}", [64, S], BF16, st) for i in range(2)]
    vraw = [p.sb(f"a0_vr{i}", [128, NKC0 * 64], BF16, st) for i in range(2)]
    vah = [p.sb(f"a0_va{i}", [128, NKC0 * 65], BF16, st) for i in range(2)]
    pex = p.sb("a0_p", [128, 2048], BF16, st)
    onesr = p.sb("a0_ones", [1, 64], F32, st)
    denr = p.sb("a0_den", [1, 512], F32, st)
    avf = p.sb("a0_avf", [64, 512], F32, st)
    avn = p.sb("a0_avn", [64, 512], BF16, st)
    ps_s = p.ps("a0_ps", [128, 2048], F32, st)
    ps_av = [p.ps(f"a0_pav{i}", [65, 512], F32, st) for i in range(2)]
    ps_rep = p.ps("a0_prep", [64, 512], F32, st)
    last = {}

    def acq(key):
        v = last.get(key)
        return [] if v is None else (list(v) if isinstance(v, list) else [v])

    p.op("gpsimd", lambda g: g.memset(onesr[:, :], 1.0))
    gp0 = p.at(p.gp)

    def load_head(h, par):
        p.dma("sync", qsb[par][:, :], Qp_d.ap()[h * 64:(h + 1) * 64, :],
              wait=acq(("hd", par)))
        for r in range(4):
            p.dma("sync", ksb[par][:, r * T:(r + 1) * T],
                  Kag_d.ap()[r * D + h * 64:r * D + h * 64 + 64, :])
        for r in range(4):
            p.dma("sync",
                  bass.AP(vraw[par], r * 16 * 64, [[NKC0 * 64, 128], [64, 16], [1, 64]]),
                  bass.AP(Vag_d.handle if hasattr(Vag_d, "handle") else Vag_d,
                          r * 128 * 16 * D + h * 64, [[16 * D, 128], [D, 16], [1, 64]]))
        prs = p.dma_all()
        g = p.op("gpsimd", lambda gg: gg.memset(vah[par][:, :], 1.0),
                 wait=acq(("vah", par)))
        v = p.op("vector", lambda vv: vv.tensor_copy(
            bass.AP(vah[par], 0, [[NKC0 * 65, 128], [65, NKC0], [1, 64]]),
            bass.AP(vraw[par], 0, [[NKC0 * 64, 128], [64, NKC0], [1, 64]])),
            wait=prs + [(p.gp, p.at(p.gp))], inc=dv)
        return prs, (dv, v)

    ld = {}
    ld[0] = load_head(0, 0)
    for h in range(16):
        par = h % 2
        if h + 1 < 16:
            ld[h + 1] = load_head(h + 1, (h + 1) % 2)
        prs, vdone = ld[h]
        for qg in range(4):
            qs = slice(qg * 512, (qg + 1) * 512)
            avm = None
            exp_of = {}

            def emit_av(kc):
                sp = kc % 4
                half = sp // 2
                a_ = p.op("tensor", lambda t, sp=sp, kc=kc, par=par, qg=qg: t.matmul(
                    ps_av[qg % 2][:, :], vah[par][:, kc * 65:(kc + 1) * 65],
                    pex[:, sp * 512:(sp + 1) * 512], start=(kc == 0), stop=(kc == NKC0 - 1)),
                    wait=[(ac, exp_of[kc]), vdone] + (acq(("pav", qg % 2)) if kc == 0 else []),
                    inc=mm)
                last[("pex", half)] = (mm, a_)
                return a_

            for kc in range(NKC0):
                sp = kc % 4
                half = sp // 2
                hs = slice(half * 1024, half * 1024 + 1024)
                sm = p.op("tensor", lambda t, sp=sp, kc=kc, qs=qs, par=par: t.matmul(
                    ps_s[:, sp * 512:(sp + 1) * 512], ksb[par][:, kc * 128:(kc + 1) * 128],
                    qsb[par][:, qs], start=True, stop=True),
                    wait=(prs if qg == 0 and kc == 0 else []) +
                         (acq(("ps", half)) if sp % 2 == 0 else []),
                    inc=mm)
                if sp % 2 == 1:
                    a = p.op("scalar", lambda s, hs=hs: s.activation(
                        pex[:, hs], ps_s[:, hs], AF.Exp),
                        wait=[(mm, sm)] + acq(("pex", half)), inc=ac)
                    last[("ps", half)] = (ac, a)
                    exp_of[kc - 1] = a
                    exp_of[kc] = a
                if kc >= 2:
                    avm = emit_av(kc - 2)
            avm = emit_av(NKC0 - 2)
            avm = emit_av(NKC0 - 1)
            d1 = p.op("vector", lambda v, qg=qg: v.tensor_copy(denr[:, :], ps_av[qg % 2][64:65, :]),
                      wait=[(mm, avm)] + acq(("denr",)), inc=dv)
            p.op("scalar", lambda s: s.activation(denr[:, :], denr[:, :], AF.Ln),
                 wait=[(dv, d1)], inc=ac)
            d2 = p.op("scalar", lambda s: s.activation(denr[:, :], denr[:, :],
                                                       AF.Exp, scale=-1.0), inc=ac)
            rm = p.op("tensor", lambda t: t.matmul(
                ps_rep[:, :], onesr[:, :], denr[:, :], start=True, stop=True),
                wait=[(ac, d2), (p.gp, gp0)] + acq(("prep",)), inc=mm)
            e1 = p.op("vector", lambda v, qg=qg: v.tensor_copy(avf[:, :], ps_av[qg % 2][0:64, :]),
                      wait=acq(("avf",)), inc=dv)
            last[("pav", qg % 2)] = (dv, e1)
            e2 = p.op("vector", lambda v: v.tensor_mul(avn[:, :], avf[:, :], ps_rep[:, :]),
                      wait=[(mm, rm)] + acq(("avn",)), inc=dv)
            last[("prep",)] = (dv, e2)
            last[("avf",)] = (dv, e2)
            last[("denr",)] = (dv, e2)
            sv = p.dma("scalar", avt_d.ap()[h * 64:(h + 1) * 64, qs], avn[:, :],
                       wait=[(dv, e2)])
            last[("avn",)] = sv
        last[("hd", par)] = [(mm, p.at(mm))]
        last[("vah", par)] = (mm, p.at(mm))
    p.op("sync", lambda e: None, wait=p.dma_all())
    st.close()


def build_s3():
    p = P("s3")
    emit_attn0(p)
    return p.finalize()



def emit_linear_fm(p, tag, xb_d, wT_d, out_d, Din, Dout, Tn, out_f32=True, start_wait=()):
    st = ExitStack()
    nin = Din // 128
    xb = [p.sb(f"{tag}_x{i}", [128, Tn], BF16, st) for i in range(nin)]
    w_sb = [p.sb(f"{tag}_w{i}", [128, Dout], BF16, st) for i in range(nin)]
    ev = p.sb(f"{tag}_ev", [128, 512], F32 if out_f32 else BF16, st)
    ps = [p.ps(f"{tag}_ps{i}", [128, 512], F32, st) for i in range(2)]
    mm, dv = p.mm, p.dv
    for i in range(nin):
        p.dma("sync", xb[i][:, :], xb_d[i * 128:(i + 1) * 128, :],
              wait=start_wait if i == 0 else ())
        p.dma("gpsimd", w_sb[i][:, :], wT_d[i * 128:(i + 1) * 128, :],
              wait=start_wait if i == 0 else ())
    ld_pairs = p.dma_all()
    last = {}

    def acq(key):
        v = last.get(key)
        if v is None:
            return []
        return list(v) if isinstance(v, list) else [v]

    NT = Tn // 512
    for o in range(Dout // 128):
        for n in range(NT):
            cs = bass.ts(n, 512)
            par = (o * NT + n) % 2
            m = mm_accum(p, ps[par][:, :],
                         [(w_sb[k][:, o * 128:(o + 1) * 128], xb[k][:, cs]) for k in range(nin)],
                         wait_first=ld_pairs + acq(("ps", par)), inc=mm)
            v = p.op("vector", lambda vv, par=par: vv.tensor_copy(ev[:, :], ps[par][:, :]),
                     wait=[(mm, m)] + acq(("ev",)), inc=dv)
            last[("ps", par)] = (dv, v)
            sv = p.dma("scalar", out_d[o * 128:(o + 1) * 128, cs], ev[:, :],
                       wait=[(dv, v)])
            last[("ev",)] = sv
    p.op("sync", lambda e: None, wait=p.dma_all())
    st.close()


def build_s4():
    p = P("s4")
    avt = p.dram("AVt", [D, T], BF16, kind="ExternalInput")
    woT = p.dram("woT", [D, D], F32, kind="ExternalInput")
    s1 = p.dram("s1", [D, T], F32, kind="ExternalOutput")
    emit_linear_fm(p, "wo0", avt.ap(), woT.ap(), s1.ap(), D, D, T)
    return p.finalize()



YLO, KLO, QLO = 258, 384, 128
NYC = T + YLO
NKC_F = T + KLO
NQC_F = T + QLO


def tiles(total, pref=512):
    out = []
    off = 0
    while off < total:
        sz = min(pref, total - off)
        out.append((off, sz))
        off += sz
    return out


class View:

    def __init__(self, t, c0, w, rows=None):
        self.t, self.c0, self.w, self.rows = t, c0, w, rows

    def __getitem__(self, idx):
        r, c = idx
        if self.rows is not None:
            assert r == slice(None, None, None)
            r = self.rows
        start = 0 if c.start is None else c.start
        stop = self.w if c.stop is None else c.stop
        return self.t[r, self.c0 + start:self.c0 + stop]


def emit_ffn_layer(p, debug=False, stop=None, hc_count=NDC, no_attn=False, no_v=False, no_proj=False, proj_upto=99, tail_cb=None):
    nc = p.nc
    yh_d = p.dram("yh", [D, YLO], F32, kind="ExternalInput")
    if debug:
        dbg_q = p.dram("dbg_q", [128, NQC_F], BF16, kind="ExternalOutput")
        dbg_k = p.dram("dbg_k", [128, NKC_F], BF16, kind="ExternalOutput")
        dbg_v = p.dram("dbg_v", [128, 19 * 130], BF16, kind="ExternalOutput")
    y_d = p.dram("y", [D, T], F32, kind="ExternalInput")
    Wqkv_d = p.dram("Wqkv", [D, 3 * D], F32, kind="ExternalInput")
    qs_d = p.dram("qsc", [128, 1], F32, kind="ExternalInput")
    ks_d = p.dram("ksc", [128, 1], F32, kind="ExternalInput")
    Wo_d = p.dram("Wo", [D, D], F32, kind="ExternalInput")
    lngb_d = p.dram("lngb", [128, 16], F32, kind="ExternalInput")
    W1_d = p.dram("W1", [D, 5460], F32, kind="ExternalInput")
    b1a_d = p.dram("b1a", [128, 22], F32, kind="ExternalInput")
    b1g_d = p.dram("b1g", [128, 22], F32, kind="ExternalInput")
    W2_d = p.dram("W2", [2730, D], F32, kind="ExternalInput")
    b2_d = p.dram("b2c", [128, 8], F32, kind="ExternalInput")
    cq_d = p.dram("cq", [128, NQC_F], F32, kind="ExternalInput")
    sq_d = p.dram("sq", [128, NQC_F], F32, kind="ExternalInput")
    ck_d = p.dram("ck", [128, NKC_F], F32, kind="ExternalInput")
    sk_d = p.dram("sk", [128, NKC_F], F32, kind="ExternalInput")
    B2_d = p.dram("B2", [128, 1], F32, kind="ExternalInput")
    prh_d = p.dram("prh", [128, 128], BF16, kind="ExternalInput")
    hs2_d = p.dram("hs2", [128, 2], BF16, kind="ExternalInput")
    rep2_d = p.dram("rep2", [2, 128], F32, kind="ExternalInput")
    eye_d = p.dram("eye", [128, 128], BF16, kind="ExternalInput")
    mA_d = p.dram("maskA", [128, 1024], BF16, kind="ExternalInput")
    mB_d = p.dram("maskB", [128, 384], BF16, kind="ExternalInput")
    mA0_d = p.dram("maskA0", [128, 1024], BF16, kind="ExternalInput")
    avn_d = p.dram("avn_s", [D, NQC_F], BF16)
    yout_d = p.dram("y_out", [D, T], F32, kind="ExternalOutput")

    mm, dv, ac, gp = p.mm, p.dv, p.ac, p.gp
    last = {}

    def acq(key):
        v = last.get(key)
        if v is None:
            return []
        return list(v) if isinstance(v, list) else [v]

    st = ExitStack()
    YB = 64
    ybuf = [p.sb(f"f_y{i}", [128, T + YB], F32, st) for i in range(NDC)]
    onesr = p.sb("f_onesr", [1, 128], F32, st)

    for i in range(NDC):
        p.dma("sync", ybuf[i][:, 0:YB], yh_d.ap()[i * 128:(i + 1) * 128, YLO - YB:])
        p.dma("sync", ybuf[i][:, YB:], y_d.ap()[i * 128:(i + 1) * 128, :])
    ones_gp = p.op("gpsimd", lambda g: g.memset(onesr[:, :], 1.0))

    def coly(t):
        return t + YB

    def colk(t):
        return t + KLO

    def colq(t):
        return t + QLO

    blocks = [(-128, 128, 3)] + [(k * 256, 256, 4) for k in range(8)]
    c8, w8 = p.const_col(8.0)

    st_a = ExitStack()
    ybf = [p.sb(f"f_yb{i}", [128, NKC_F + 1], BF16, st_a) for i in range(NDC)]
    hs2 = p.sb("f_hs2", [128, 2], BF16, st_a)
    rep2 = p.sb("f_rep2", [2, 128], F32, st_a)
    eye = p.sb("f_eye", [128, 128], BF16, st_a)
    prh = p.sb("f_prh", [128, 128], BF16, st_a)
    qs_c = p.sb("f_qs", [128, 1], F32, st_a)
    ks_c = p.sb("f_ks", [128, 1], F32, st_a)
    B2c = p.sb("f_B2", [128, 1], F32, st_a)
    mA = p.sb("f_mA", [128, 1024], BF16, st_a)
    mA0 = p.sb("f_mA0", [128, 1024], BF16, st_a)
    mB = p.sb("f_mB", [128, 384], BF16, st_a)
    cq = p.sb("f_cq", [128, NQC_F], F32, st_a)
    sq = p.sb("f_sq", [128, NQC_F], F32, st_a)
    ck = p.sb("f_ck", [128, NKC_F], F32, st_a)
    sk = p.sb("f_sk", [128, NKC_F], F32, st_a)
    wq_all3 = [p.sb(f"f_wqa{t}", [128, NDC * 128], BF16, st_a) for t in range(3)]

    class _WqK:
        def __init__(self, k):
            self.k = k

        def __getitem__(self, idx):
            r, c = idx
            start = 0 if c.start is None else c.start
            part = start // 128
            off = start % 128
            w = c.stop - start
            return wq_all3[part][:, self.k * 128 + off:self.k * 128 + off + w]

    class _WqView:
        def __getitem__(self, k):
            return _WqK(k)
    wq_all = _WqView()
    scr = p.sb("f_scr", [128, NKC_F], BF16, st_a)
    sqb2 = [p.sb(f"f_sqb{i}", [128, 512], BF16, st_a) for i in range(2)]
    rows = p.sb("f_rows", [2, 2 * NQC_F], F32, st_a)
    qhat = p.sb("f_qhat", [128, NKC_F], BF16, st_a)
    tsc0 = p.sb("f_tsc", [128, 512], F32, st_a)
    qfin = p.sb("f_qfin", [128, NQC_F], BF16, st_a)
    qfinL = p.sb("f_qfinL", [128, NQC_F], BF16, st_a)
    kfin = p.sb("f_kfin", [128, NKC_F], BF16, st_a)
    vfm = [p.sb(f"f_vfm{i}", [128, 512], BF16, st_a) for i in range(2)]
    vsb = p.sb("f_vsb", [128, 19 * 130], BF16, st_a)
    pmk1 = p.sb("f_pmk1", [128, 1024], BF16, st_a)
    pex_t2 = p.sb("f_pex2", [128, 1024], BF16, st_a)
    qraw = scr
    pex_t = View(scr, 0, 1024)
    pmk = [View(scr, 1024, 2048), pmk1]
    ssrow = View(rows, 0, 2 * NQC_F, rows=slice(0, 2))
    den2 = [View(rows, 0, NQC_F, rows=slice(0, 1)),
            View(rows, NQC_F, NQC_F, rows=slice(0, 1))]
    tsc = [tsc0, tsc0]
    avasm = qhat
    avnb = qfinL

    for dst, src in ((hs2, hs2_d), (rep2, rep2_d), (eye, eye_d), (prh, prh_d),
                     (qs_c, qs_d), (ks_c, ks_d), (B2c, B2_d),
                     (mA, mA_d), (mA0, mA0_d), (mB, mB_d),
                     (cq, cq_d), (sq, sq_d), (ck, ck_d), (sk, sk_d)):
        p.dma("scalar", dst[:, :], src.ap()[:, :])
    for i in range(NDC):
        p.op("gpsimd", lambda g, i=i: g.memset(ybf[i][:, 0:KLO - YLO + 1], 0.0), inc=gp)
        p.dma("gpsimd", ybf[i][:, KLO - YLO + 1:KLO - YB + 1],
              yh_d.ap()[i * 128:(i + 1) * 128, 0:YLO - YB])
        p.op("vector", lambda v, i=i: v.tensor_copy(ybf[i][:, KLO - YB + 1:], ybuf[i][:, :]),
             wait=p.dma_all() if i == 0 else (), inc=dv)
    ybf_done = p.at(dv)
    gp0 = p.at(gp)
    pb = [p.ps(f"f_pb{i}", [128, 512], F32, st_a) for i in range(4)]
    ps_a = [pb[0], pb[1]]
    ps_ss = lambda sz: pb[2][0:2, 0:sz]
    ps_rep = [pb[2], pb[3]]
    ps_pp = [pb[0], pb[1]]
    ps_s = [p.ps(f"f_pscr{i}", [128, 512], F32, st_a) for i in range(2)]
    ps_av = p.ps("f_pav", [65, 512], F32, st_a)
    ps_t = p.ps("f_pst", [128, 1024], BF16, st_a)

    QT = tiles(NQC_F)
    KT = tiles(NKC_F)

    def proj_norm_rope(hc, which, upto=99):
        isq = which == "q"
        rng = QT if isq else KT
        base_col = (KLO - QLO) if isq else 0
        wcol = slice(0, 128) if isq else slice(128, 256)
        nw_c = qs_c if isq else ks_c
        ctab, stab = (cq, sq) if isq else (ck, sk)
        dst = qfin if isq else kfin
        w = NQC_F if isq else NKC_F
        def flush_sq(pend):
            off, sz, par_i, a = pend
            m2 = mm_accum(p, ps_ss(sz), [(hs2[:, :], sqb2[par_i][:, 0:sz])],
                          wait_first=[(ac, a)] + acq(("pb", 2)) + acq(("pb2", 2)), inc=mm)
            last[("sqb", par_i)] = (mm, m2)
            v2 = p.op("vector", lambda vv, off=off, sz=sz: vv.tensor_copy(
                ssrow[:, off:off + sz], ps_ss(sz)), wait=[(mm, m2)], inc=dv)
            last[("pb", 2)] = (dv, v2)
            last[("pb2", 2)] = (dv, v2)

        pend_sq = None
        for ti, (off, sz) in enumerate(rng):
            par_i = ti % 2
            src_off = off + base_col
            m = mm_accum(p, ps_a[par_i][:, 0:sz],
                         [(wq_all[k][:, wcol],
                           ybf[k][:, src_off + 1:src_off + 1 + sz] if k < 4 else
                           ybf[k][:, src_off:src_off + sz])
                          for k in range(NDC)],
                         wait_first=[(dv, ybf_done), (gp, gp0)]
                         + p.dma_all() + acq(("pb", par_i)) + acq(("pb2", par_i)), inc=mm)
            v = p.op("vector", lambda vv, off=off, sz=sz, par_i=par_i: vv.tensor_copy(
                qraw[:, off:off + sz], ps_a[par_i][:, 0:sz]), wait=[(mm, m)], inc=dv)
            last[("pb", par_i)] = (dv, v)
            last[("pb2", par_i)] = (dv, v)
            if upto <= 0:
                continue
            a = p.op("scalar", lambda s, off=off, sz=sz, par_i=par_i: s.activation(
                sqb2[par_i][:, 0:sz], qraw[:, off:off + sz], AF.Square),
                wait=[(dv, v)] + acq(("sqb", par_i)), inc=ac)
            if pend_sq is not None:
                flush_sq(pend_sq)
            pend_sq = (off, sz, par_i, a)
        if pend_sq is not None:
            flush_sq(pend_sq)
        if upto <= 1:
            return p.at(dv)
        e24, w24 = p.const_col(1e-24)
        a = p.op("scalar", lambda s: s.activation(ssrow[:, 0:w], ssrow[:, 0:w], AF.Ln,
                                                  bias=e24[0:2]),
                 wait=[(dv, p.at(dv)), w24], inc=ac)
        rst = p.op("scalar", lambda s: s.activation(ssrow[:, 0:w], ssrow[:, 0:w], AF.Exp,
                                                    scale=-0.5), inc=ac)
        for ti, (off, sz) in enumerate(rng):
            par_i = ti % 2
            m = mm_accum(p, ps_rep[par_i][:, 0:sz], [(rep2[:, :], ssrow[:, off:off + sz])],
                         wait_first=[(ac, rst)] + acq(("pb", 2 + par_i)) + acq(("pb2", 2 + par_i)), inc=mm)
            v = p.op("vector", lambda vv, off=off, sz=sz, par_i=par_i: vv.scalar_tensor_tensor(
                qhat[:, off:off + sz], qraw[:, off:off + sz], nw_c[:, :],
                ps_rep[par_i][:, 0:sz], ALU.mult, ALU.mult), wait=[(mm, m)], inc=dv)
            last[("pb", 2 + par_i)] = (dv, v)
            last[("pb2", 2 + par_i)] = (dv, v)
        qhat_done = p.at(dv)
        if upto <= 2:
            return qhat_done
        for ti, (off, sz) in enumerate(rng):
            par_i = ti % 2
            m = mm_accum(p, ps_pp[par_i][:, 0:sz], [(prh[:, :], qhat[:, off:off + sz])],
                         wait_first=[(dv, qhat_done)] + acq(("pb", par_i)) + acq(("pb2", par_i)), inc=mm)
            v = p.op("vector", lambda vv, off=off, sz=sz, par_i=par_i: vv.tensor_mul(
                tsc[par_i][:, 0:sz], qhat[:, off:off + sz], ctab[:, off:off + sz]),
                wait=acq(("tsc", par_i)), inc=dv)
            v = p.op("vector", lambda vv, off=off, sz=sz, par_i=par_i: vv.tensor_mul(
                dst[:, off:off + sz], ps_pp[par_i][:, 0:sz], stab[:, off:off + sz]),
                wait=[(mm, m)], inc=dv)
            v = p.op("vector", lambda vv, off=off, sz=sz, par_i=par_i: vv.tensor_add(
                dst[:, off:off + sz], dst[:, off:off + sz], tsc[par_i][:, 0:sz]), inc=dv)
            last[("pb", par_i)] = (dv, v)
            last[("pb2", par_i)] = (dv, v)
            last[("tsc", par_i)] = (dv, v)
            if isq:
                v = p.op("vector", lambda vv, off=off, sz=sz: vv.tensor_scalar_mul(
                    qfinL[:, off:off + sz], dst[:, off:off + sz], B2c[:, :]),
                    wait=acq(("avnb",)) if ti == 0 else (), inc=dv)
        return p.at(dv)

    for hc in range(hc_count):
        Wqv = Wqkv_d.ap().rearrange("(k p) c -> p k c", p=128)
        wl = None
        for part, c0 in enumerate((hc * 128, D + hc * 128, 2 * D + hc * 128)):
            wl = p.dma("gpsimd",
                       bass.AP(wq_all3[part], 0, [[NDC * 128, 128], [128, NDC], [1, 128]]),
                       Wqv[:, :, c0:c0 + 128], wait=acq(("wq", 0, part)))
        if not no_proj:
            qv = proj_norm_rope(hc, "q", upto=proj_upto)
            kv = proj_norm_rope(hc, "k", upto=proj_upto)
        if no_v:
            continue
        gm = p.op("gpsimd", lambda g: g.memset(vsb[:, :], 1.0),
                  wait=acq(("vsb",)) + acq(("vsbdbg",)))
        gm_v = p.at(gp)
        def flush_tp(pend):
            off, sz, par_i, v = pend
            for tl in range(sz // 128):
                tt = (off + tl * 128) // 128
                tm = p.op("tensor", lambda t, tl=tl, par_i=par_i: t.transpose(
                    ps_t[:, 0:128], vfm[par_i][:, tl * 128:(tl + 1) * 128], eye[:, :]),
                    wait=[(dv, v)] + acq(("pst",)), inc=mm)
                dst_ap = bass.AP(vsb, tt * 130, [[vsb.shape[1], 128], [65, 2], [1, 64]])
                src_ap = bass.AP(ps_t, 0, [[1024, 128], [64, 2], [1, 64]])
                v2 = p.op("vector", lambda vv, dst_ap=dst_ap, src_ap=src_ap: vv.tensor_copy(
                    dst_ap, src_ap), wait=[(mm, tm), (gp, gm_v)], inc=dv)
                last[("pst",)] = (dv, v2)
                last[("vfm", par_i)] = (mm, tm)

        pend_tp = None
        for ti, (off, sz) in enumerate(KT):
            par_i = ti % 2
            m = mm_accum(p, ps_a[par_i][:, 0:sz],
                         [(wq_all[k][:, 256:384],
                           ybf[k][:, off + 1:off + 1 + sz] if k < 4 else
                           ybf[k][:, off:off + sz])
                          for k in range(NDC)],
                         wait_first=acq(("pb", par_i)) + acq(("pb2", par_i)), inc=mm)
            v = p.op("vector", lambda vv, sz=sz, par_i=par_i: vv.tensor_copy(
                vfm[par_i][:, 0:sz], ps_a[par_i][:, 0:sz]),
                wait=[(mm, m)] + acq(("vfm", par_i)), inc=dv)
            last[("pb2", par_i)] = (dv, v)
            last[("pb", par_i)] = (dv, v)
            if pend_tp is not None:
                flush_tp(pend_tp)
            pend_tp = (off, sz, par_i, v)
        if pend_tp is not None:
            flush_tp(pend_tp)
        vsb_done = p.at(dv)

        if debug and hc == 0:
            p.dma("scalar", dbg_q.ap()[:, :], qfin[:, :], wait=[(dv, p.at(dv))])
            p.dma("scalar", dbg_k.ap()[:, :], kfin[:, :])
            dvv = p.dma("scalar", dbg_v.ap()[:, :], vsb[:, :], wait=[(dv, vsb_done)])
            last[("vsbdbg",)] = dvv
        if no_attn:
            continue
        if no_attn:
            continue
        sset = [ps_s, [pb[2], pb[3]]]
        pexs = [pex_t, pex_t2]
        for hl in range(2):
            hb = slice(hl * 64, hl * 64 + 64)
            den = den2[hl]
            pend = None
            for bi in range(len(blocks) + 1):
                if bi < len(blocks):
                    tb, nq, nks = blocks[bi]
                    par = bi % 2
                    qcol = colq(tb)
                    banks = sset[par]
                    mypex = pexs[par]
                    a = None
                    for ks in range(nks):
                        bank = ks // 2
                        coloff = (ks % 2) * nq
                        kcol = colk(tb - 256 + ks * 128)
                        if bi == 0:
                            qsrc = qfinL if ks == 0 else qfin
                        else:
                            qsrc = qfinL if ks < 2 else qfin
                        bkey = ("pss2", par, bank)
                        sm = p.op("tensor", lambda t, banks=banks, bank=bank, coloff=coloff, kcol=kcol, qcol=qcol, nq=nq, qsrc=qsrc, hb=hb: t.matmul(
                            banks[bank][:, coloff:coloff + nq],
                            kfin[hb, kcol:kcol + 128], qsrc[hb, qcol:qcol + nq],
                            start=True, stop=True),
                            wait=(acq(bkey) if ks % 2 == 0 else []) +
                                 ([(dv, vsb_done)] if bi == 0 and hl == 0 and ks == 0 else []),
                            inc=mm if (ks % 2 == 1 or ks == nks - 1) else None)
                        if ks % 2 == 1 or ks == nks - 1:
                            wexp = (ks % 2 + 1) * nq
                            a = p.op("scalar", lambda s, banks=banks, bank=bank, wexp=wexp, nq=nq, mypex=mypex: s.activation(
                                mypex[:, bank * 2 * nq:bank * 2 * nq + wexp],
                                banks[bank][:, 0:wexp], AF.Exp, scale=c8[:, :]),
                                wait=[(mm, sm), w8] + (acq(("pexk", par)) if bank == 0 else []), inc=ac)
                            last[bkey] = (ac, a)
                    msk = mB if bi == 0 else (mA0 if bi == 1 else mA)
                    g = p.op("gpsimd", lambda gg, par=par, nq=nq, nks=nks, msk=msk, mypex=mypex: gg.tensor_mul(
                        pmk[par][:, 0:nq * nks], mypex[:, 0:nq * nks], msk[:, 0:nq * nks]),
                        wait=[(ac, a)] + acq(("pmk", par)), inc=gp)
                    last[("pexk", par)] = (gp, g)
                    cur = (bi, g, nq, nks, qcol, par)
                else:
                    cur = None
                if pend is not None:
                    bi0, g0, nq0, nks0, qcol0, par0 = pend
                    tb0 = blocks[bi0][0]
                    avm = None
                    for ks in range(nks0):
                        tt = (colk(tb0 - 256) + ks * 128) // 128
                        avm = p.op("tensor", lambda t, ks=ks, tt=tt, nq0=nq0, hl=hl, par0=par0, nks0=nks0: t.matmul(
                            ps_av[:, 0:nq0], vsb[:, tt * 130 + hl * 65: tt * 130 + hl * 65 + 65],
                            pmk[par0][:, ks * nq0:(ks + 1) * nq0],
                            start=(ks == 0), stop=(ks == nks0 - 1)),
                            wait=[(gp, g0)] + (acq(("pav",)) if ks == 0 else []),
                            inc=mm if ks == nks0 - 1 else None)
                    last[("pmk", par0)] = (mm, avm)
                    v1 = p.op("vector", lambda vv, qcol0=qcol0, nq0=nq0, hb=hb: vv.tensor_copy(
                        avasm[hb, qcol0:qcol0 + nq0], ps_av[0:64, 0:nq0]),
                        wait=[(mm, avm)] + acq(("avasm", hl)), inc=dv)
                    v2 = p.op("vector", lambda vv, qcol0=qcol0, nq0=nq0, den=den: vv.tensor_copy(
                        den[:, qcol0:qcol0 + nq0], ps_av[64:65, 0:nq0]),
                        wait=acq(("den", hl)) if bi0 == 0 else (), inc=dv)
                    last[("pav",)] = (dv, v2)
                pend = cur
        for hl in range(2):
            hb = slice(hl * 64, hl * 64 + 64)
            den = den2[hl]
            rv0 = p.op("scalar", lambda s, den=den: s.activation(den[:, :], den[:, :], AF.Ln),
                       wait=[(dv, p.at(dv))], inc=ac)
            rv = p.op("scalar", lambda s, den=den: s.activation(den[:, :], den[:, :],
                                                                AF.Exp, scale=-1.0), inc=ac)
            for (off, sz) in QT:
                par_i = (off // 512) % 2
                m = p.op("tensor", lambda t, off=off, sz=sz, par_i=par_i, den=den, hl=hl: t.matmul(
                    ps_rep[par_i][hl * 64:hl * 64 + 64, 0:sz], onesr[:, 0:64],
                    den[:, off:off + sz], start=True, stop=True),
                    wait=[(ac, rv), (gp, ones_gp)] + acq(("pb", 2 + par_i)) + acq(("pb2", 2 + par_i)), inc=mm)
                v = p.op("vector", lambda vv, off=off, sz=sz, par_i=par_i, hb=hb, hl=hl: vv.tensor_mul(
                    avnb[hb, off:off + sz], avasm[hb, off:off + sz],
                    ps_rep[par_i][hl * 64:hl * 64 + 64, 0:sz]),
                    wait=[(mm, m)], inc=dv)
                last[("pb", 2 + par_i)] = (dv, v)
                last[("pb2", 2 + par_i)] = (dv, v)
            last[("avasm", hl)] = (dv, p.at(dv))
            last[("den", hl)] = (mm, p.at(mm))
        sv = p.dma("scalar", avn_d.ap()[hc * 128:(hc + 1) * 128, :], avnb[:, :],
                   wait=[(dv, p.at(dv))])
        last[("avnb",)] = sv
        last[("vsb",)] = (mm, p.at(mm))
        for part in range(3):
            last[("wq", 0, part)] = (mm, p.at(mm))
    st_a.close()

    if stop == "A":
        fin = []
        for oc in range(NDC):
            fin.append(p.dma("sync", yout_d.ap()[oc * 128:(oc + 1) * 128, :],
                             ybuf[oc][:, YB:], wait=p.dma_all() if oc == 0 else ()))
        p.op("sync", lambda e: None, wait=p.dma_all())
        st_a.close()
        st.close()
        return
    st_b = ExitStack()
    avn_sb = [p.sb(f"f_av{i}", [128, NQC_F], BF16, st_b) for i in range(NDC)]
    wo_sb = [p.sb(f"f_wo{i}", [128, D], BF16, st_b) for i in range(NDC)]
    ps_o = [p.ps(f"f_pso{i}", [128, 512], F32, st_b) for i in range(2)]
    aw = p.dma_all()
    for i in range(NDC):
        p.dma("sync", avn_sb[i][:, :], avn_d.ap()[i * 128:(i + 1) * 128, :],
              wait=aw if i == 0 else ())
        p.dma("gpsimd", wo_sb[i][:, :], Wo_d.ap()[i * 128:(i + 1) * 128, :],
              wait=aw if i == 0 else ())
    a2_pairs = p.dma_all()
    MT = tiles(NQC_F - 64)
    for oc in range(NDC):
        for (off, sz) in MT:
            qcol = 64 + off
            ycol = coly(-64) + off
            par = (oc * len(MT) + (off // 512)) % 2
            m = mm_accum(p, ps_o[par][:, 0:sz],
                         [(wo_sb[k][:, oc * 128:(oc + 1) * 128], avn_sb[k][:, qcol:qcol + sz])
                          for k in range(NDC)],
                         wait_first=a2_pairs + acq(("pso", par)), inc=mm)
            v = p.op("vector", lambda vv, oc=oc, ycol=ycol, sz=sz, par=par: vv.scalar_tensor_tensor(
                ybuf[oc][:, ycol:ycol + sz], ps_o[par][:, 0:sz], 1.0,
                ybuf[oc][:, ycol:ycol + sz], ALU.mult, ALU.add),
                wait=[(mm, m)], inc=dv)
            last[("pso", par)] = (dv, v)
    ymid_done = p.at(dv)
    st_b.close()

    if stop == "A2":
        for oc in range(NDC):
            p.dma("sync", yout_d.ap()[oc * 128:(oc + 1) * 128, :], ybuf[oc][:, YB:],
                  wait=[(dv, ymid_done)] if oc == 0 else ())
        p.op("sync", lambda e: None, wait=p.dma_all())
        st.close()
        return
    st_c = ExitStack()
    xnb = [p.sb(f"f_xnb{i}", [128, T], BF16, st_c) for i in range(NDC)]
    lngb = p.sb("f_lngb", [128, 16], F32, st_c)
    ones_cb = p.sb("f_onescb", [128, 1], BF16, st_c)
    mean_r = p.sb("f_mean", [1, T], F32, st_c)
    msq_r = p.sb("f_msq", [1, T], F32, st_c)
    var_r = p.sb("f_var", [1, T], F32, st_c)
    mb_r = p.sb("f_mb", [1, T], F32, st_c)
    lngb_pair = p.dma("scalar", lngb[:, :], lngb_d.ap()[:, :], wait=[(dv, ymid_done)])
    p.op("gpsimd", lambda g: g.memset(ones_cb[:, :], 1.0), wait=[(dv, ymid_done)])
    ln_gp = p.at(gp)

    def xin_ap(buf_list, i, off, sz, lead):
        o = lead + off if i < 4 else lead + off - 1
        return buf_list[i][:, o:o + sz]

    NT = T // 512
    st_stats = ExitStack()
    ymb = [p.sb(f"f_ymb{i}", [128, T + 4], BF16, st_stats) for i in range(NDC)]
    sqf = [p.sb(f"f_sqf{i}", [128, T + 4], BF16, st_stats) for i in range(NDC)]
    ps_sum = p.ps("f_psum", [1, T], F32, st_stats)
    ps_sq = p.ps("f_psq", [1, T], F32, st_stats)
    for i in range(NDC):
        p.op("vector", lambda v, i=i: v.tensor_copy(ymb[i][:, :], ybuf[i][:, coly(-4):]),
             wait=[(dv, ymid_done)] if i == 0 else (), inc=dv)
        p.op("scalar", lambda s, i=i: s.activation(sqf[i][:, :], ybuf[i][:, coly(-4):],
                                                   AF.Square),
             wait=[(dv, ymid_done)], inc=ac)
    cast_done = p.at(dv)
    sq_done = p.at(ac)
    for i in range(NDC):
        for n in range(NT):
            cs = bass.ts(n, 512)
            p.op("tensor", lambda t, i=i, n=n, cs=cs: t.matmul(
                ps_sum[:, cs], ones_cb[:, :], xin_ap(ymb, i, n * 512, 512, 4),
                start=(i == 0), stop=(i == NDC - 1)),
                wait=[(dv, cast_done), (gp, ln_gp)] if i == 0 and n == 0 else ())
            p.op("tensor", lambda t, i=i, n=n, cs=cs: t.matmul(
                ps_sq[:, cs], ones_cb[:, :], xin_ap(sqf, i, n * 512, 512, 4),
                start=(i == 0), stop=(i == NDC - 1)),
                wait=[(ac, sq_done)] if i == 0 and n == 0 else (),
                inc=mm if i == NDC - 1 and n == NT - 1 else None)
    stats_mm = p.at(mm)
    p.op("vector", lambda v: v.tensor_scalar_mul(mean_r[:, :], ps_sum[:, :], 1.0 / D),
         wait=[(mm, stats_mm)])
    p.op("vector", lambda v: v.tensor_mul(msq_r[:, :], mean_r[:, :], mean_r[:, :]))
    p.op("vector", lambda v: v.scalar_tensor_tensor(
        var_r[:, :], ps_sq[:, :], 1.0 / D, msq_r[:, :], ALU.mult, ALU.subtract), inc=dv)
    epsc, epsw = p.const_col(1e-5)
    p.op("scalar", lambda s: s.activation(var_r[:, :], var_r[:, :], AF.Ln,
                                          bias=epsc[0:1]),
         wait=[(dv, p.at(dv)), epsw], inc=ac)
    a = p.op("scalar", lambda s: s.activation(var_r[:, :], var_r[:, :], AF.Exp,
                                              scale=-0.5), inc=ac)
    p.op("vector", lambda v: v.tensor_mul(mb_r[:, :], mean_r[:, :], var_r[:, :]),
         wait=[(ac, a)], inc=dv)
    rows_done = p.at(dv)
    st_stats.close()
    st_rep = ExitStack()
    ps_mrep = p.ps("f_pmrep", [128, T], F32, st_rep)
    ps_rrep = p.ps("f_prrep", [128, T], F32, st_rep)
    for n in range(NT):
        cs = bass.ts(n, 512)
        p.op("tensor", lambda t, cs=cs: t.matmul(
            ps_mrep[:, cs], onesr[:, :], mean_r[:, cs], start=True, stop=True),
            wait=[(dv, rows_done)] if n == 0 else ())
        p.op("tensor", lambda t, cs=cs: t.matmul(
            ps_rrep[:, cs], onesr[:, :], var_r[:, cs], start=True, stop=True),
            inc=mm if n == NT - 1 else None)
    rep_mm = p.at(mm)
    tmpf = p.sb("f_tmpf", [128, T], F32, st_rep)
    for i in range(NDC):
        p.op("vector", lambda v, i=i: v.tensor_sub(
            tmpf[:, :], xin_ap(ybuf, i, 0, T, YB), ps_mrep[:, :]),
            wait=[(mm, rep_mm), lngb_pair] if i == 0 else ())
        p.op("vector", lambda v, i=i: v.tensor_mul(tmpf[:, :], tmpf[:, :], ps_rrep[:, :]))
        p.op("vector", lambda v, i=i: v.tensor_scalar(
            xnb[i][:, :], tmpf[:, :], lngb[:, i:i + 1], lngb[:, 8 + i:9 + i],
            ALU.mult, ALU.add), inc=dv)
    xnb_done = p.at(dv)
    st_rep.close()

    ICS = tiles(2730, 128)
    st_f = ExitStack()
    w1a = [p.sb(f"f_w1a{i}", [128, 8 * 128], BF16, st_f) for i in range(2)]
    w1g = [p.sb(f"f_w1g{i}", [128, 8 * 128], BF16, st_f) for i in range(2)]
    w2sb2 = [p.sb(f"f_w2sb{i}", [128, 22 * 128], BF16, st_f) for i in range(2)]
    b1a = p.sb("f_b1a", [128, 22], F32, st_f)
    b1g = p.sb("f_b1g", [128, 22], F32, st_f)
    b2c = p.sb("f_b2c", [128, 8], F32, st_f)
    gel = p.sb("f_gel", [128, 512], BF16, st_f)
    prod = [p.sb(f"f_prod{i}", [128, 1024], BF16, st_f) for i in range(22)]
    yout_sb = p.sb("f_yout", [128, 512], F32, st_f)
    ps_1 = [p.ps(f"f_ps1{i}", [128, 512], F32, st_f) for i in range(4)]
    ps_o2 = [p.ps(f"f_pso2{i}", [128, 512], F32, st_f) for i in range(2)]
    p.dma("scalar", b1a[:, :], b1a_d.ap()[:, :], wait=[(dv, xnb_done)])
    p.dma("scalar", b1g[:, :], b1g_d.ap()[:, :])
    p.dma("scalar", b2c[:, :], b2_d.ap()[:, :])
    fb_pairs = p.dma_all()
    W1v = W1_d.ap().rearrange("(k p) c -> p k c", p=128)
    W2v = W2_d.ap()
    tail_stores = []
    halo_fin = None
    for nh in (1, 0):
        t0 = nh * 1024
        for ici, (ic0, sz) in enumerate(ICS):
            par_w = ici % 2
            wla = p.dma("gpsimd", bass.AP(w1a[par_w], 0, [[8 * 128, 128], [128, NDC], [1, sz]]),
                        W1v[:, :, ic0:ic0 + sz],
                        wait=acq(("w1a", par_w)) + ([(dv, xnb_done)] if nh == 1 and ici == 0 else []))
            wlg = p.dma("gpsimd", bass.AP(w1g[par_w], 0, [[8 * 128, 128], [128, NDC], [1, sz]]),
                        W1v[:, :, 2730 + ic0:2730 + ic0 + sz], wait=acq(("w1g", par_w)))
            pr1 = 2 * (ici % 2)
            for n2 in range(2):
                cs = slice(t0 + n2 * 512, t0 + n2 * 512 + 512)
                ma = mm_accum(p, ps_1[pr1][0:sz, :],
                              [(w1a[par_w][:, k * 128:k * 128 + sz], xnb[k][:, cs])
                               for k in range(NDC)],
                              wait_first=[wla, wlg, (dv, xnb_done)] + acq(("ps1", pr1)), inc=mm)
                mg = mm_accum(p, ps_1[pr1 + 1][0:sz, :],
                              [(w1g[par_w][:, k * 128:k * 128 + sz], xnb[k][:, cs])
                               for k in range(NDC)],
                              wait_first=acq(("ps1", pr1 + 1)), inc=mm)
                ga = p.op("scalar", lambda s, sz=sz, ici=ici, pr1=pr1: s.activation(
                    gel[0:sz, :], ps_1[pr1 + 1][0:sz, :], AF.Gelu, bias=b1g[0:sz, ici:ici + 1]),
                    wait=[(mm, mg)] + fb_pairs + acq(("gel",)), inc=ac)
                last[("ps1", pr1 + 1)] = (ac, ga)
                pv = p.op("vector", lambda v, sz=sz, ici=ici, n2=n2, pr1=pr1: v.scalar_tensor_tensor(
                    prod[ici][0:sz, n2 * 512:(n2 + 1) * 512], ps_1[pr1][0:sz, :],
                    b1a[0:sz, ici:ici + 1], gel[0:sz, :], ALU.add, ALU.mult),
                    wait=[(mm, ma), (ac, ga)] + acq(("prod", ici)), inc=dv)
                last[("ps1", pr1)] = (dv, pv)
                last[("gel",)] = (dv, pv)
            last[("w1a", par_w)] = (mm, p.at(mm))
            last[("w1g", par_w)] = (mm, p.at(mm))
        prod_done = p.at(dv)
        for oc in range(NDC):
            po = oc % 2
            w2sb = w2sb2[po]
            W2r = W2_d.ap()[0:2688, :].rearrange("(ic p) c -> p ic c", p=128)
            wl2a = p.dma("gpsimd",
                         bass.AP(w2sb, 0, [[22 * 128, 128], [128, 11], [1, 128]]),
                         W2r[:, 0:11, oc * 128:(oc + 1) * 128],
                         wait=acq(("w2", po)) + ([(dv, xnb_done)] if nh == 1 and oc == 0 else []))
            wl2a2 = p.dma("gpsimd",
                          bass.AP(w2sb, 11 * 128, [[22 * 128, 128], [128, 10], [1, 128]]),
                          W2r[:, 11:21, oc * 128:(oc + 1) * 128],
                          wait=acq(("w2", po)))
            wl2b = p.dma("gpsimd", w2sb[0:42, 21 * 128:21 * 128 + 128],
                         W2_d.ap()[2688:2730, oc * 128:(oc + 1) * 128],
                         wait=acq(("w2", po)))
            w2pairs = [wl2a, wl2a2, wl2b]
            for n2 in (1, 0):
                cs = slice(t0 + n2 * 512, t0 + n2 * 512 + 512)
                par = n2
                m = mm_accum(p, ps_o2[par][:, :],
                             [(w2sb[0:sz, ici * 128:ici * 128 + 128],
                               prod[ici][0:sz, n2 * 512:(n2 + 1) * 512])
                              for ici, (ic0, sz) in enumerate(ICS)],
                             wait_first=w2pairs + [(dv, prod_done)] + acq(("pso2", par)),
                             inc=mm)
                v = p.op("vector", lambda vv, oc=oc, cs=cs, par=par: vv.scalar_tensor_tensor(
                    yout_sb[:, :], ps_o2[par][:, :], b2c[:, oc:oc + 1],
                    ybuf[oc][:, YB + cs.start:YB + cs.stop], ALU.add, ALU.add),
                    wait=[(mm, m)] + acq(("yout",)), inc=dv)
                last[("pso2", par)] = (dv, v)
                sv = p.dma("sync", yout_d.ap()[oc * 128:(oc + 1) * 128, cs], yout_sb[:, :],
                           wait=[(dv, v)])
                last[("yout",)] = sv
                if nh == 1 and n2 == 1:
                    tail_stores.append(sv)
            last[("w2", po)] = (mm, p.at(mm))
        for ici in range(22):
            last[("prod", ici)] = (mm, p.at(mm))
        if nh == 1 and tail_cb is not None:
            halo_fin = tail_cb(tail_stores, yout_d)
    if halo_fin is not None:
        halo_fin()
    p.op("sync", lambda e: None, wait=p.dma_all())
    st_f.close()
    st_c.close()
    st.close()


def build_ffn_layer(debug=False, stop=None, **kw):
    p = P("ffn")
    emit_ffn_layer(p, debug=debug, stop=stop, **kw)
    return p.finalize()



def shard_tokens(x):
    out = []
    for c in range(NCORES):
        b, j = divmod(c, 4)
        sl = x[b, j * T:(j + 1) * T, :]
        out.append(np.ascontiguousarray(sl.T))
    return out




def emit_comb1(p):
    xT_d = p.dram("xT", [D, T], F32, kind="ExternalInput")
    s1_d = p.dram("s1", [D, T], F32, kind="ExternalInput")
    y_d = p.dram("yf", [D, T], F32, kind="ExternalInput")
    x2_d = p.dram("x2", [D, T], F32, kind="ExternalOutput")
    xn2_d = p.dram("xn2", [D, T], F32, kind="ExternalOutput")
    xn2b_d = p.dram("xn2b", [D, T], BF16, kind="ExternalOutput")
    mm, dv, ac = p.mm, p.dv, p.ac
    st = ExitStack()
    xa = p.sb("c1_x", [128, T], F32, st)
    sa = p.sb("c1_s", [128, T], F32, st)
    ya = p.sb("c1_y", [128, T], F32, st)
    ga = p.sb("c1_g", [128, T], F32, st)
    last = {}

    def acq(key):
        v = last.get(key)
        return [] if v is None else (list(v) if isinstance(v, list) else [v])

    for i in range(NDC):
        lx = p.dma("sync", xa[:, :], xT_d.ap()[i * 128:(i + 1) * 128, :],
                   wait=acq(("bufs",)))
        ls = p.dma("sync", sa[:, :], s1_d.ap()[i * 128:(i + 1) * 128, :])
        ly = p.dma("sync", ya[:, :], y_d.ap()[i * 128:(i + 1) * 128, :])
        a = p.op("scalar", lambda s: s.activation(ga[:, :], ya[:, :], AF.Silu),
                 wait=[ly] + acq(("ga",)), inc=ac)
        v = p.op("vector", lambda vv: vv.tensor_mul(ga[:, :], ga[:, :], sa[:, :]),
                 wait=[(ac, a), ls], inc=dv)
        v = p.op("vector", lambda vv: vv.tensor_add(xa[:, :], xa[:, :], ga[:, :]),
                 wait=[lx], inc=dv)
        sv = p.dma("scalar", x2_d.ap()[i * 128:(i + 1) * 128, :], xa[:, :],
                   wait=[(dv, v)])
        last[("bufs",)] = sv
        last[("ga",)] = (dv, v)
    x2_done = p.dma_all()
    st.close()
    emit_ln_fm(p, x2_d.ap(), xn2_d.ap(), xn2b_d.ap(), tag="ln2", start_wait=x2_done)


def build_s11():
    p = P("s11")
    emit_comb1(p)
    return p.finalize()



NCH = T // 128


def emit_qkv1(p):
    xnb_d = p.dram("xn2b", [D, T], BF16, kind="ExternalInput")
    wqT_d = p.dram("wqT", [D, D], F32, kind="ExternalInput")
    wkT_d = p.dram("wkT", [D, D], F32, kind="ExternalInput")
    wvT_d = p.dram("wvT", [D, D], F32, kind="ExternalInput")
    eye_d = p.dram("eye", [128, 128], BF16, kind="ExternalInput")
    sq_d = p.dram("sq", [D, T], BF16, kind="ExternalOutput")
    sk_d = p.dram("sk", [D, T], BF16, kind="ExternalOutput")
    v_d = p.dram("v", [T, D], BF16, kind="ExternalOutput")
    snap_d = p.dram("snap", [128, 8 * NCH * 65], BF16, kind="ExternalOutput")
    sqs_d = p.dram("sqs", [128, 8], F32, kind="ExternalOutput")
    mm, dv, ac = p.mm, p.dv, p.ac
    st = ExitStack()
    xnb = [p.sb(f"q1_x{i}", [128, T], BF16, st) for i in range(NDC)]
    w_sb = [p.sb(f"q1_w{i}", [128, 128], BF16, st) for i in range(NDC)]
    eye1 = p.sb("q1_eye", [128, 128], BF16, st)
    raw = p.sb("q1_raw", [128, T], F32, st)
    e1b = p.sb("q1_e1", [128, T], F32, st)
    fout = p.sb("q1_f", [128, T], BF16, st)
    ktm = p.sb("q1_ktm", [128, T], BF16, st)
    vtm = p.sb("q1_vtm", [128, T], BF16, st)
    ones_c = p.sb("q1_ones", [128, 1], BF16, st)
    sqs = p.sb("q1_sqs", [128, 8], F32, st)
    snapb = p.sb("q1_snap", [128, NCH * 65], BF16, st)
    ps_a = [p.ps(f"q1_pa{i}", [128, 512], F32, st) for i in range(2)]
    ps_t = p.ps("q1_pt", [128, 1024], BF16, st)
    ps_st = [p.ps(f"q1_ps{i}", [128, 512], F32, st) for i in range(2)]
    last = {}

    def acq(key):
        v = last.get(key)
        return [] if v is None else (list(v) if isinstance(v, list) else [v])

    p.op("gpsimd", lambda g: g.memset(ones_c[:, :], 1.0))
    gp_ones = p.at(p.gp)
    for i in range(NDC):
        p.dma("sync", xnb[i][:, :], xnb_d.ap()[i * 128:(i + 1) * 128, :])
    p.dma("scalar", eye1[:, :], eye_d.ap()[:, :])
    x_pairs = p.dma_all()
    NT = T // 512

    def proj(hp, which, wT_d, store_d):
        for i in range(NDC):
            p.dma("gpsimd", w_sb[i][:, :],
                  wT_d.ap()[i * 128:(i + 1) * 128, hp * 128:(hp + 1) * 128],
                  wait=acq(("w", i)))
        wp = p.dma_all()
        for n in range(NT):
            cs = bass.ts(n, 512)
            par = n % 2
            m = mm_accum(p, ps_a[par][:, :],
                         [(w_sb[k][:, :], xnb[k][:, cs]) for k in range(NDC)],
                         wait_first=x_pairs + wp + acq(("pa", par)), inc=mm)
            v = p.op("vector", lambda vv, cs=cs, par=par: vv.tensor_copy(
                raw[:, cs], ps_a[par][:, :]), wait=[(mm, m)] + acq(("raw",)), inc=dv)
            last[("pa", par)] = (dv, v)
        raw_done = p.at(dv)
        if which == "v":
            cb = p.op("scalar", lambda s: s.copy(fout[:, :], raw[:, :]),
                      wait=[(dv, raw_done)] + acq(("fout",)), inc=ac)
            fin = (ac, cb)
        else:
            v = p.op("vector", lambda vv: vv.tensor_scalar_min(e1b[:, :], raw[:, :], 0.0),
                     wait=[(dv, raw_done)] + acq(("e1",)), inc=dv)
            a = p.op("scalar", lambda s: s.activation(e1b[:, :], e1b[:, :], AF.Exp),
                     wait=[(dv, v)], inc=ac)
            v = p.op("vector", lambda vv: vv.scalar_tensor_tensor(
                fout[:, :], raw[:, :], 1.0, e1b[:, :], ALU.add, ALU.max),
                wait=[(ac, a)] + acq(("fout",)), inc=dv)
            last[("e1",)] = (ac, a) if False else (dv, v)
            fin = (dv, v)
        sv = p.dma("scalar", store_d.ap()[hp * 128:(hp + 1) * 128, :], fout[:, :],
                   wait=[fin])
        last[("raw",)] = fin
        return fin, sv

    for hp in range(NDC):
        fin, sv = proj(hp, "q", wqT_d, sq_d)
        r = p.op("vector", lambda vv, hp=hp: vv.reduce_sum(
            sqs[:, hp:hp + 1], fout[:, :], axis=AX.X), inc=dv)
        last[("fout",)] = [sv, (dv, r)]
        for i in range(NDC):
            last[("w", i)] = (mm, p.at(mm))
    p.dma("scalar", sqs_d.ap()[:, :], sqs[:, :], wait=[(dv, p.at(dv))])

    for hp in range(NDC):
        fin, sv = proj(hp, "k", wkT_d, sk_d)
        last[("fout",)] = [sv]
        for c in range(NCH):
            tmv = p.op("tensor", lambda t, c=c: t.transpose(
                ps_t[:, 0:128], fout[:, c * 128:(c + 1) * 128], eye1[:, :]),
                wait=acq(("pt",)) + ([fin] if c == 0 else []), inc=mm)
            v = p.op("vector", lambda vv, c=c: vv.tensor_copy(
                ktm[:, c * 128:(c + 1) * 128], ps_t[:, 0:128]),
                wait=[(mm, tmv)] + (acq(("ktm",)) if c == 0 else []), inc=dv)
            last[("pt",)] = (dv, v)
        last[("fout",)] = [(mm, p.at(mm))]
        for i in range(NDC):
            last[("w", i)] = (mm, p.at(mm))
        ktm_done = p.at(dv)
        for c in range(NCH):
            par = c % 2
            m = mm_accum(p, ps_a[par][:, 0:128],
                         [(xnb[k][:, c * 128:(c + 1) * 128],
                           w_sb2_cols(w_sb, k)) for k in range(NDC)],
                         wait_first=acq(("pa", par)) + (vw_pairs(p, wvT_d, w_sb, hp, acq) if c == 0 else []),
                         inc=mm)
            v = p.op("vector", lambda vv, c=c, par=par: vv.tensor_copy(
                vtm[:, c * 128:(c + 1) * 128], ps_a[par][:, 0:128]),
                wait=[(mm, m)] + (acq(("vtm",)) if c == 0 else []), inc=dv)
            last[("pa", par)] = (dv, v)
            sv = p.dma("scalar", v_d.ap()[c * 128:(c + 1) * 128, hp * 128:(hp + 1) * 128],
                       vtm[:, c * 128:(c + 1) * 128], wait=[(dv, v)])
            last[("vst", c)] = sv
        vtm_done = p.at(dv)
        for hl in range(2):
            hb = slice(hl * 64, hl * 64 + 64)
            for c in range(NCH):
                kv = p.op("tensor", lambda t, c=c, hl=hl, hb=hb: t.matmul(
                    ps_st[hl][0:64, 0:64], ktm[:, c * 128 + hl * 64:c * 128 + hl * 64 + 64],
                    vtm[:, c * 128 + hl * 64:c * 128 + hl * 64 + 64],
                    start=(c == 0), stop=False),
                    wait=([(dv, vtm_done), (p.gp, gp_ones)] if c == 0 else []) +
                         acq(("pst", hl)))
                ks = p.op("tensor", lambda t, c=c, hl=hl: t.matmul(
                    ps_st[hl][0:64, 64:65], ktm[:, c * 128 + hl * 64:c * 128 + hl * 64 + 64],
                    ones_c[:, :], start=False, stop=False), inc=mm)
                v = p.op("vector", lambda vv, c=c, hl=hl, hb=hb: vv.tensor_copy(
                    snapb[hb, c * 65:(c + 1) * 65], ps_st[hl][0:64, 0:65]),
                    wait=[(mm, p.at(mm))] + (acq(("snapb",)) if c == 0 and hl == 0 else []),
                    inc=dv)
                last[("pst", hl)] = (dv, v)
        sv = p.dma("scalar", snap_d.ap()[:, hp * NCH * 65:(hp + 1) * NCH * 65],
                   snapb[:, :], wait=[(dv, p.at(dv))])
        last[("snapb",)] = sv
        last[("ktm",)] = (mm, p.at(mm))
        last[("vtm",)] = (mm, p.at(mm))
        for i in range(NDC):
            last[("w", i)] = (mm, p.at(mm))
    p.op("sync", lambda e: None, wait=p.dma_all())
    st.close()


def w_sb2_cols(w_sb, k):
    return w_sb[k][:, :]


def vw_pairs(p, wvT_d, w_sb, hp, acq):
    for i in range(NDC):
        p.dma("gpsimd", w_sb[i][:, :],
              wvT_d.ap()[i * 128:(i + 1) * 128, hp * 128:(hp + 1) * 128],
              wait=acq(("w", i)))
    return p.dma_all()


def build_s12():
    p = P("s12")
    emit_qkv1(p)
    return p.finalize()




def emit_attn1(p):
    sq_dd = p.dram("sq", [D, T], BF16, kind="ExternalInput")
    sk_dd = p.dram("sk", [D, T], BF16, kind="ExternalInput")
    v_dd = p.dram("v", [T, D], BF16, kind="ExternalInput")
    snap_dd = p.dram("snap", [128, 8 * NCH * 65], BF16, kind="ExternalInput")
    totA_d = p.dram("totA", [4 * 128, 8 * 65], BF16, kind="ExternalInput")
    sqsA_d = p.dram("sqsA", [4 * 128, 8], F32, kind="ExternalInput")
    pm_d = p.dram("pmask", [128, 4], F32, kind="ExternalInput")
    mnT_d = p.dram("mnT", [128, 64], F32, kind="ExternalInput")
    memd_d = p.dram("memd", [128, 2048], F32, kind="ExternalInput")
    sel4_d = p.dram("sel4", [4, 512], F32, kind="ExternalInput")
    ltri_d = p.dram("ltri", [128, 128], BF16, kind="ExternalInput")
    gate_d = p.dram("gate", [1, 16], F32, kind="ExternalInput")
    woT_d = p.dram("woT", [D, D], F32, kind="ExternalInput")
    comb_d = p.dram("comb", [D, T], BF16)
    s2_d = p.dram("s2", [D, T], F32, kind="ExternalOutput")
    mm, dv, ac = p.mm, p.dv, p.ac
    st = ExitStack()
    sqc = p.sb("a1_sq", [128, T], BF16, st)
    skc = p.sb("a1_sk", [128, T], BF16, st)
    vtm = p.sb("a1_v", [128, T], BF16, st)
    snapP = p.sb("a1_snapP", [128, (NCH + 1) * 65], BF16, st)
    snapo = p.sb("a1_snapo", [128, NCH * 65], F32, st)
    pref = p.sb("a1_pref", [128, 8 * 65], F32, st)
    totb = p.sb("a1_tot", [128, 8 * 65], F32, st)
    pmc = p.sb("a1_pm", [128, 4], F32, st)
    mnT = p.sb("a1_mnT", [128, 64], BF16, st)
    memd = p.sb("a1_memd", [128, 2048], BF16, st)
    sel4 = p.sb("a1_sel4", [4, 512], F32, st)
    ltri = p.sb("a1_ltri", [128, 128], BF16, st)
    grow = p.sb("a1_grow", [1, 16], F32, st)
    gcols = p.sb("a1_gcols", [64, 16], F32, st)
    onesr = p.sb("a1_onesr", [1, 128], F32, st)
    ones_c = p.sb("a1_ones", [128, 1], BF16, st)
    tsqs = p.sb("a1_tsqs", [128, 8], F32, st)
    tsqsb = p.sb("a1_tsqsb", [128, 8], BF16, st)
    rel16 = p.sb("a1_rel16", [4, 16], F32, st)
    w16 = p.sb("a1_w16", [16, 4], F32, st)
    w4 = p.sb("a1_w4", [4, 16], F32, st)
    Asb = p.sb("a1_A", [128, 128], BF16, st)
    vaug = p.sb("a1_vaug", [128, NCH * 130], BF16, st)
    locasm = [p.sb(f"a1_loc{i}", [64, T], BF16, st) for i in range(2)]
    locn = [p.sb(f"a1_locn{i}", [64, T], BF16, st) for i in range(2)]
    rows = p.sb("a1_rows", [2, 2 * T], F32, st)
    den2 = [View(rows, 0, T, rows=slice(0, 1)), View(rows, T, T, rows=slice(0, 1))]
    srows = p.sb("a1_srows", [4, 512], F32, st)
    sqm = p.sb("a1_sqm", [128, 512], BF16, st)
    tmpc = p.sb("a1_tmpc", [64, 512], F32, st)
    combt = p.sb("a1_combt", [64, 512], BF16, st)
    ps_A = [p.ps(f"a1_pA{i}", [128, 512], F32, st) for i in range(2)]
    ps_o = [p.ps(f"a1_po{i}", [65, 512], F32, st) for i in range(2)]
    ps_r = p.ps("a1_pr", [128, 512], F32, st)
    ps_m = p.ps("a1_pm2", [128, 512], F32, st)
    ps_t4 = p.ps("a1_pt4", [16, 512], F32, st)
    last = {}

    def acq(key):
        v = last.get(key)
        return [] if v is None else (list(v) if isinstance(v, list) else [v])

    p.op("gpsimd", lambda g: g.memset(onesr[:, :], 1.0))
    p.op("gpsimd", lambda g: g.memset(ones_c[:, :], 1.0))
    gp0 = p.at(p.gp)
    for dst, sd in ((pmc, pm_d), (mnT, mnT_d), (memd, memd_d), (sel4, sel4_d),
                    (ltri, ltri_d), (grow, gate_d)):
        p.dma("scalar" if dst.dtype == F32 else "gpsimd", dst[:, :], sd.ap()[:, :])
    c_pairs = p.dma_all()

    scr_d = p.dram("a1scr", [16, 4], F32)
    l0 = p.dma("sync", tsqs[:, :], sqsA_d.ap()[0:128, :])
    for j in range(1, 4):
        lj = p.dma("sync", totb[:, 0:8], sqsA_d.ap()[j * 128:(j + 1) * 128, :], wait=acq(("totb",)))
        acc = p.op("vector", lambda vv: vv.tensor_add(tsqs[:, :], tsqs[:, :], totb[:, 0:8]),
                   wait=[l0, lj], inc=dv)
        last[("totb",)] = (dv, acc)
    vtb = p.op("vector", lambda vv: vv.tensor_copy(tsqsb[:, :], tsqs[:, :]),
               wait=[(dv, p.at(dv))], inc=dv)
    relmm = None
    for h in range(16):
        hl, hp = h % 2, h // 2
        hb = slice(hl * 64, hl * 64 + 64)
        relmm = p.op("tensor", lambda t, h=h, hb=hb, hp=hp: t.matmul(
            ps_t4[0:4, h:h + 1], mnT[hb, 4 * h:4 * h + 4], tsqsb[hb, hp:hp + 1],
            start=True, stop=True),
            wait=([(dv, p.at(dv))] + c_pairs) if h == 0 else (), inc=mm)
    v1 = p.op("vector", lambda vv: vv.tensor_copy(rel16[:, :], ps_t4[0:4, 0:16]),
              wait=[(mm, relmm)], inc=dv)
    d1 = p.dma("scalar", scr_d.ap().rearrange("h m -> m h"), rel16[:, :], wait=[(dv, v1)],
               allow_slow_non_contiguous=True)
    d2 = p.dma("sync", w16[:, :], scr_d.ap()[:, :], wait=[d1])
    csc, wsc = p.const_col(1.0 / 8192.0)
    vmx = p.op("vector", lambda vv: vv.reduce_max(w4[0:1, 0:1] if False else rows[0:1, 0:16].t if False else tsqs[0:16, 0:1] if False else w16[:, 0:4].t if False else w16[:, :], w16[:, :], axis=AX.X), wait=[d2]) if False else None
    mxc = p.sb("a1_mx", [16, 1], F32, st)
    vmx = p.op("vector", lambda vv: vv.reduce_max(mxc[:, :], w16[:, :], axis=AX.X),
               wait=[d2], inc=dv)
    vsb_ = p.op("vector", lambda vv: vv.tensor_scalar(
        w16[:, :], w16[:, :], mxc[:, :], None, ALU.subtract), inc=dv)
    aex = p.op("scalar", lambda s: s.activation(w16[:, :], w16[:, :], AF.Exp,
                                                scale=csc[0:16]),
               wait=[(dv, vsb_), wsc], inc=ac)
    vsm = p.op("vector", lambda vv: vv.reduce_sum(mxc[:, :], w16[:, :], axis=AX.X),
               wait=[(ac, aex)], inc=dv)
    vrc = p.op("vector", lambda vv: vv.reciprocal(mxc[:, :], mxc[:, :]), inc=dv)
    vw = p.op("vector", lambda vv: vv.tensor_scalar(
        w16[:, :], w16[:, :], mxc[:, :], None, ALU.mult), inc=dv)
    d3 = p.dma("scalar", scr_d.ap()[:, :], w16[:, :], wait=[(dv, vw)])
    d4 = p.dma("sync", w4[:, :], scr_d.ap().rearrange("h m -> m h"), wait=[d3],
               allow_slow_non_contiguous=True)
    ag = p.op("scalar", lambda s: s.activation(grow[:, :], grow[:, :], AF.Sigmoid),
              wait=c_pairs, inc=ac)
    gmm = None
    for h in range(16):
        gmm = p.op("tensor", lambda t, h=h: t.matmul(
            ps_m[0:64, h:h + 1], onesr[:, 0:64], grow[:, h:h + 1],
            start=True, stop=True),
            wait=[(ac, ag), (p.gp, gp0)] if h == 0 else (), inc=mm)
    vg = p.op("vector", lambda vv: vv.tensor_copy(gcols[:, :], ps_m[0:64, 0:16]),
              wait=[(mm, gmm)], inc=dv)
    gc_done = (dv, vg)

    for hp in range(NDC):
        lq = p.dma("sync", sqc[:, :], sq_dd.ap()[hp * 128:(hp + 1) * 128, :],
                   wait=acq(("bufs",)))
        lk = p.dma("sync", skc[:, :], sk_dd.ap()[hp * 128:(hp + 1) * 128, :])
        lvs = [p.dma("sync", vtm[:, c * 128:(c + 1) * 128],
                     v_dd.ap()[c * 128:(c + 1) * 128, hp * 128:(hp + 1) * 128])
               for c in range(NCH)]
        lsn = p.dma("gpsimd", snapo[:, :],
                    snap_dd.ap()[:, hp * NCH * 65:(hp + 1) * NCH * 65],
                    wait=acq(("snapo",)))
        vpre = None
        for j in range(4):
            lj = p.dma("gpsimd", totb[:, :], totA_d.ap()[j * 128:(j + 1) * 128, :],
                       wait=acq(("totb",)))
            if j == 0:
                vpre = p.op("vector", lambda vv: vv.tensor_scalar(
                    pref[:, :], totb[:, :], pmc[:, 0:1], None, ALU.mult),
                    wait=[lj] + c_pairs + acq(("pref",)), inc=dv)
            else:
                vpre = p.op("vector", lambda vv, j=j: vv.scalar_tensor_tensor(
                    pref[:, :], totb[:, :], pmc[:, j:j + 1], pref[:, :],
                    ALU.mult, ALU.add), wait=[lj], inc=dv)
            last[("totb",)] = (dv, vpre)
        hp_pairs = p.dma_all()
        gva = p.op("gpsimd", lambda g: g.memset(vaug[:, :], 1.0),
                   wait=acq(("vaug",)))
        vva = p.op("vector", lambda vv: vv.tensor_copy(
            bass.AP(vaug, 0, [[NCH * 130, 128], [130, NCH], [65, 2], [1, 64]]),
            bass.AP(vtm.handle if hasattr(vtm, 'handle') else vtm, 0,
                    [[T, 128], [128, NCH], [64, 2], [1, 64]])),
            wait=hp_pairs + [(p.gp, p.at(p.gp))], inc=dv)
        vsp = p.op("vector", lambda vv, hp=hp: vv.tensor_copy(
            snapP[:, 0:65], pref[:, hp * 65:(hp + 1) * 65]),
            wait=[(dv, vpre)] + acq(("snapP",)), inc=dv)
        for c in range(NCH):
            vsp = p.op("vector", lambda vv, c=c, hp=hp: vv.scalar_tensor_tensor(
                snapP[:, (c + 1) * 65:(c + 2) * 65], snapo[:, c * 65:(c + 1) * 65],
                1.0, pref[:, hp * 65:(hp + 1) * 65], ALU.mult, ALU.add),
                wait=[lsn] if c == 0 else (), inc=dv)
        snp_done = p.at(dv)
        last[("pref",)] = (dv, snp_done)
        last[("snapo",)] = (dv, snp_done)
        for hl in range(2):
            h = 2 * hp + hl
            hb = slice(hl * 64, hl * 64 + 64)
            den = den2[hl]
            for c in range(NCH):
                parA = c % 2
                am = p.op("tensor", lambda t, c=c, hb=hb, parA=parA: t.matmul(
                    ps_A[parA][:, 0:128], skc[hb, c * 128:(c + 1) * 128],
                    sqc[hb, c * 128:(c + 1) * 128], start=True, stop=True),
                    wait=(hp_pairs + [(dv, snp_done)] if c == 0 and hl == 0 else []) +
                         acq(("pA", parA)), inc=mm)
                va = p.op("vector", lambda vv, parA=parA: vv.tensor_mul(
                    Asb[:, :], ps_A[parA][:, 0:128], ltri[:, :]),
                    wait=[(mm, am)] + acq(("Asb",)), inc=dv)
                last[("pA", parA)] = (dv, va)
                paro = c % 2
                m1 = p.op("tensor", lambda t, c=c, hl=hl, paro=paro: t.matmul(
                    ps_o[paro][0:65, 0:128], vaug[:, c * 130 + hl * 65:c * 130 + hl * 65 + 65],
                    Asb[:, :], start=True, stop=False),
                    wait=[(dv, va)] + acq(("po", paro)))
                m3 = p.op("tensor", lambda t, c=c, hb=hb, paro=paro: t.matmul(
                    ps_o[paro][0:65, 0:128], snapP[hb, c * 65:(c + 1) * 65],
                    sqc[hb, c * 128:(c + 1) * 128], start=False, stop=True), inc=mm)
                last[("Asb",)] = (mm, m3)
                vo = p.op("vector", lambda vv, c=c, hl=hl, paro=paro: vv.tensor_copy(
                    locasm[hl][:, c * 128:(c + 1) * 128], ps_o[paro][0:64, 0:128]),
                    wait=[(mm, m3)] + (acq(("locasm", hl)) if c == 0 else []), inc=dv)
                vd = p.op("vector", lambda vv, c=c, den=den, paro=paro: vv.tensor_copy(
                    den[:, c * 128:(c + 1) * 128], ps_o[paro][64:65, 0:128]),
                    wait=acq(("den", hl)) if c == 0 else (), inc=dv)
                last[("po", paro)] = (dv, vd)
            vc1 = p.op("vector", lambda vv, den=den: vv.tensor_scalar_max(
                den[:, :], den[:, :], 1e-6), inc=dv)
            p.op("scalar", lambda s, den=den: s.activation(den[:, :], den[:, :], AF.Ln),
                 wait=[(dv, vc1)], inc=ac)
            vc2 = p.op("scalar", lambda s, den=den: s.activation(den[:, :], den[:, :],
                                                                 AF.Exp, scale=-1.0), inc=ac)
            for nt in range(4):
                cs = slice(nt * 512, nt * 512 + 512)
                rm = p.op("tensor", lambda t, cs=cs, den=den: t.matmul(
                    ps_r[0:64, :], onesr[:, 0:64], den[:, cs], start=True, stop=True),
                    wait=[(ac, vc2)] + acq(("pr",)), inc=mm)
                vn = p.op("vector", lambda vv, cs=cs, hl=hl: vv.tensor_mul(
                    locn[hl][:, cs], locasm[hl][:, cs], ps_r[0:64, :]),
                    wait=[(mm, rm)] + (acq(("locn", hl)) if nt == 0 else []), inc=dv)
                last[("pr",)] = (dv, vn)
            last[("locasm", hl)] = (dv, p.at(dv))
            last[("den", hl)] = (mm, p.at(mm))
            for nt in range(4):
                cs = slice(nt * 512, nt * 512 + 512)
                rm = p.op("tensor", lambda t, cs=cs, hb=hb, h=h: t.matmul(
                    ps_t4[0:4, 0:512], mnT[hb, 4 * h:4 * h + 4], sqc[hb, cs],
                    start=True, stop=True),
                    wait=acq(("pt4",)) + [(dv, p.at(dv))] if nt == 0 else acq(("pt4",)),
                    inc=mm)
                vs1 = p.op("vector", lambda vv: vv.tensor_scalar_max(
                    srows[:, :], ps_t4[0:4, 0:512], 1e-6),
                    wait=[(mm, rm)] + acq(("srows",)), inc=dv)
                last[("pt4",)] = (dv, vs1)
                p.op("scalar", lambda s: s.activation(srows[:, :], srows[:, :], AF.Ln),
                     wait=[(dv, vs1)], inc=ac)
                vs2a = p.op("scalar", lambda s: s.activation(srows[:, :], srows[:, :],
                                                             AF.Exp, scale=-1.0), inc=ac)
                vs3 = p.op("vector", lambda vv, h=h: vv.tensor_scalar(
                    srows[:, :], srows[:, :], w4[:, h:h + 1], None, ALU.mult),
                    wait=[d4, (ac, vs2a)], inc=dv)
                mlast = None
                for m in range(4):
                    sm = p.op("tensor", lambda t, m=m: t.matmul(
                        ps_r[:, :], sel4[:, m * 128:(m + 1) * 128], srows[:, :],
                        start=True, stop=True),
                        wait=[(dv, vs3)] + acq(("pr",)), inc=mm)
                    vq = p.op("vector", lambda vv, cs=cs, hb=hb: vv.tensor_mul(
                        sqm[hb, :], sqc[hb, cs], ps_r[hb, :]),
                        wait=[(mm, sm)] + acq(("sqm",)), inc=dv)
                    last[("pr",)] = (dv, vq)
                    mcol = (hp * 4 + m) * 64
                    mlast = p.op("tensor", lambda t, m=m, hb=hb, mcol=mcol: t.matmul(
                        ps_m[hb, :], memd[hb, mcol:mcol + 64],
                        sqm[hb, :], start=(m == 0), stop=(m == 3)),
                        wait=[(dv, vq)] + (acq(("pm2",)) if m == 0 else []), inc=mm)
                    last[("sqm",)] = (mm, mlast)
                last[("srows",)] = (mm, p.at(mm))
                vt = p.op("vector", lambda vv, cs=cs, hl=hl, hb=hb: vv.tensor_sub(
                    tmpc[:, :], ps_m[hb, :], locn[hl][:, cs]),
                    wait=[(mm, mlast), (dv, gc_done[1])] + acq(("tmpc",)), inc=dv)
                last[("pm2",)] = (dv, vt)
                vcb = p.op("vector", lambda vv, h=h, hl=hl, cs=cs: vv.scalar_tensor_tensor(
                    combt[:, :], tmpc[:, :], gcols[:, h:h + 1], locn[hl][:, cs],
                    ALU.mult, ALU.add), wait=acq(("combt",)), inc=dv)
                last[("tmpc",)] = (dv, vcb)
                sv = p.dma("scalar", comb_d.ap()[h * 64:(h + 1) * 64, cs], combt[:, :],
                           wait=[(dv, vcb)])
                last[("combt",)] = sv
            last[("locn", hl)] = (dv, p.at(dv))
        last[("bufs",)] = [(mm, p.at(mm)), (dv, p.at(dv))]
        last[("snapP",)] = (mm, p.at(mm))
        last[("vaug",)] = (mm, p.at(mm))
    comb_done = p.dma_all()
    st.close()
    emit_linear_fm(p, "wo1", comb_d.ap(), woT_d.ap(), s2_d.ap(), D, D, T,
                   start_wait=comb_done)


def build_s13():
    p = P("s13")
    emit_attn1(p)
    return p.finalize()



def emit_moe(p):
    xnb_d = p.dram("xn2b", [D, T], BF16, kind="ExternalInput")
    x2_d = p.dram("x2", [D, T], F32, kind="ExternalInput")
    s2_d = p.dram("s2", [D, T], F32, kind="ExternalInput")
    gWT_d = p.dram("gWT", [D, 8], F32, kind="ExternalInput")
    W0T_d = p.dram("W0T", [D, D], F32, kind="ExternalInput")
    W1T_d = p.dram("W1T", [D, D], F32, kind="ExternalInput")
    oWT_d = p.dram("oWT", [D, D], F32, kind="ExternalInput")
    eye_d = p.dram("eyeF", [128, 128], F32, kind="ExternalInput")
    sel2_d = p.dram("sel2", [2, 256], F32, kind="ExternalInput")
    x3_d = p.dram("x3", [D, T], F32, kind="ExternalOutput")
    f_d = p.dram("fscr", [D, T], BF16)
    g2_d = p.dram("g2scr", [D, T], BF16)
    mm, dv, ac = p.mm, p.dv, p.ac
    st = ExitStack()
    xnb = [p.sb(f"m_x{i}", [128, T], BF16, st) for i in range(NDC)]
    gwt = p.sb("m_gwt", [128, 8 * 8], BF16, st)
    eye1 = p.sb("m_eye", [128, 128], F32, st)
    lgt = p.sb("m_lgt", [128, 8], F32, st)
    col = p.sb("m_col", [128, 4], F32, st)
    s01 = p.sb("m_s01", [128, 2], F32, st)
    s01row = p.sb("m_s01r", [2, T], F32, st)
    cnt = p.sb("m_cnt", [128, 8], F32, st)
    w0 = [p.sb(f"m_w0{i}", [128, 128], BF16, st) for i in range(NDC)]
    w1 = [p.sb(f"m_w1{i}", [128, 128], BF16, st) for i in range(NDC)]
    h0b = p.sb("m_h0", [128, 512], BF16, st)
    fb = p.sb("m_f", [128, 512], F32, st)
    fbb = p.sb("m_fb", [128, 512], BF16, st)
    onesr = p.sb("m_onesr", [1, 128], F32, st)
    sel2 = p.sb("m_sel2", [2, 256], F32, st)
    ps_g = [p.ps(f"m_pg{i}", [128, 512], F32, st) for i in range(2)]
    ps_t = p.ps("m_pt", [128, 512], F32, st)
    ps_r = p.ps("m_pr", [128, 512], F32, st)
    last = {}

    def acq(key):
        v = last.get(key)
        return [] if v is None else (list(v) if isinstance(v, list) else [v])

    p.op("gpsimd", lambda g: g.memset(onesr[:, :], 1.0))
    gp0 = p.at(p.gp)
    for i in range(NDC):
        p.dma("sync", xnb[i][:, :], xnb_d.ap()[i * 128:(i + 1) * 128, :])
        p.dma("gpsimd", gwt[:, i * 8:(i + 1) * 8], gWT_d.ap()[i * 128:(i + 1) * 128, :])
    p.dma("scalar", eye1[:, :], eye_d.ap()[:, :])
    p.dma("scalar", sel2[:, :], sel2_d.ap()[:, :])
    ld0 = p.dma_all()
    for tt in range(NCH):
        trows = slice(tt * 128, (tt + 1) * 128)
        m = mm_accum(p, ps_g[tt % 2][:, 0:8],
                     [(xnb[k][:, trows], gwt[:, k * 8:(k + 1) * 8]) for k in range(NDC)],
                     wait_first=ld0 + acq(("pg", tt % 2)), inc=mm)
        v = p.op("vector", lambda vv, par=tt % 2: vv.tensor_copy(lgt[:, :], ps_g[par][:, 0:8]),
                 wait=[(mm, m)] + acq(("lgt",)), inc=dv)
        last[("pg", tt % 2)] = (dv, v)
        v = p.op("vector", lambda vv: vv.reduce_max(col[:, 0:1], lgt[:, :], axis=AX.X),
                 inc=dv)
        v = p.op("vector", lambda vv: vv.tensor_scalar_mul(col[:, 0:1], col[:, 0:1], -1.0),
                 inc=dv)
        a = p.op("scalar", lambda s: s.activation(lgt[:, :], lgt[:, :], AF.Exp,
                                                  bias=col[:, 0:1]),
                 wait=[(dv, v)], inc=ac)
        v = p.op("vector", lambda vv: vv.reduce_sum(col[:, 0:1], lgt[:, :], axis=AX.X),
                 wait=[(ac, a)], inc=dv)
        v = p.op("vector", lambda vv: vv.reciprocal(col[:, 0:1], col[:, 0:1]), inc=dv)
        v = p.op("vector", lambda vv: vv.tensor_scalar(
            lgt[:, :], lgt[:, :], col[:, 0:1], None, ALU.mult), inc=dv)
        v = p.op("vector", lambda vv: vv.tensor_single_scalar(
            cnt[:, :], lgt[:, :], 0.3, ALU.is_gt), inc=dv)
        v = p.op("vector", lambda vv: vv.reduce_sum(col[:, 1:2], cnt[:, :], axis=AX.X),
                 inc=dv)
        v = p.op("vector", lambda vv: vv.tensor_scalar(
            col[:, 2:3], col[:, 1:2], -1.0, 8.0, ALU.mult, ALU.add), inc=dv)
        v = p.op("vector", lambda vv: vv.tensor_mul(s01[:, 0:1], col[:, 2:3], lgt[:, 0:1]),
                 inc=dv)
        v = p.op("vector", lambda vv: vv.tensor_mul(s01[:, 1:2], col[:, 1:2], lgt[:, 1:2]),
                 inc=dv)
        tmv = p.op("tensor", lambda t: t.transpose(
            ps_t[0:2, 0:128], s01[:, :], eye1[:, :]),
            wait=[(dv, v)] + acq(("pt",)), inc=mm)
        v = p.op("vector", lambda vv, tt=tt: vv.tensor_copy(
            s01row[:, tt * 128:(tt + 1) * 128], ps_t[0:2, 0:128]),
            wait=[(mm, tmv)], inc=dv)
        last[("pt",)] = (dv, v)
        last[("lgt",)] = (dv, v)
    s01_done = p.at(dv)
    for oc in range(NDC):
        for i in range(NDC):
            p.dma("gpsimd", w0[i][:, :], W0T_d.ap()[i * 128:(i + 1) * 128, oc * 128:(oc + 1) * 128],
                  wait=acq(("w0", i)))
            p.dma("gpsimd", w1[i][:, :], W1T_d.ap()[i * 128:(i + 1) * 128, oc * 128:(oc + 1) * 128],
                  wait=acq(("w1", i)))
        wp = p.dma_all()
        for nt in range(4):
            cs = bass.ts(nt, 512)
            m0 = mm_accum(p, ps_g[0][:, :],
                          [(w0[k][:, :], xnb[k][:, cs]) for k in range(NDC)],
                          wait_first=wp + acq(("pg", 0)), inc=mm)
            m1 = mm_accum(p, ps_g[1][:, :],
                          [(w1[k][:, :], xnb[k][:, cs]) for k in range(NDC)],
                          wait_first=acq(("pg", 1)), inc=mm)
            r0 = p.op("tensor", lambda t, cs=cs: t.matmul(
                ps_r[:, :], sel2[:, 0:128], s01row[0:2, cs], start=True, stop=True),
                wait=[(dv, s01_done), (p.gp, gp0)] + acq(("prr",)), inc=mm)
            v = p.op("vector", lambda vv: vv.tensor_copy(h0b[:, :], ps_g[0][:, :]),
                     wait=[(mm, m0)] + acq(("h0b",)), inc=dv)
            last[("pg", 0)] = (dv, v)
            v2 = p.op("vector", lambda vv: vv.tensor_mul(fb[:, :], h0b[:, :], ps_r[:, :]),
                      wait=[(mm, r0)] + acq(("fb",)), inc=dv)
            last[("h0b",)] = (dv, v2)
            r1 = p.op("tensor", lambda t, cs=cs: t.matmul(
                ps_r[:, :], sel2[:, 128:256], s01row[0:2, cs], start=True, stop=True),
                wait=[(dv, v2)], inc=mm)
            last[("prr",)] = (mm, r1)
            v3 = p.op("vector", lambda vv: vv.tensor_copy(h0b[:, :], ps_g[1][:, :]),
                      wait=[(mm, m1)], inc=dv)
            last[("pg", 1)] = (dv, v3)
            v4 = p.op("vector", lambda vv: vv.tensor_mul(h0b[:, :], h0b[:, :], ps_r[:, :]),
                      wait=[(mm, r1)], inc=dv)
            last[("prr",)] = (dv, v4)
            v5 = p.op("vector", lambda vv: vv.tensor_add(fbb[:, :], fb[:, :], h0b[:, :]),
                      wait=acq(("fbb",)), inc=dv)
            last[("fb",)] = (dv, v5)
            sv = p.dma("scalar", f_d.ap()[oc * 128:(oc + 1) * 128, cs], fbb[:, :],
                       wait=[(dv, v5)])
            last[("fbb",)] = sv
        for i in range(NDC):
            last[("w0", i)] = (mm, p.at(mm))
            last[("w1", i)] = (mm, p.at(mm))
    f_done = p.dma_all()
    st.close()
    st2 = ExitStack()
    fsb = [p.sb(f"m_f2{i}", [128, T], BF16, st2) for i in range(NDC)]
    wsb = [p.sb(f"m_wo{i}", [128, D], BF16, st2) for i in range(NDC)]
    g2s = p.sb("m_g2", [128, 512], BF16, st2)
    x2s = p.sb("m_x2s", [128, T], F32, st2)
    s2s = p.sb("m_s2", [128, T], F32, st2)
    g2f = p.sb("m_g2f", [128, T], BF16, st2)
    x3s = p.sb("m_x3s", [128, 512], F32, st2)
    ps_o = [p.ps(f"m_po{i}", [128, 512], F32, st2) for i in range(2)]
    for i in range(NDC):
        p.dma("sync", fsb[i][:, :], f_d.ap()[i * 128:(i + 1) * 128, :],
              wait=f_done if i == 0 else ())
        p.dma("gpsimd", wsb[i][:, :], oWT_d.ap()[i * 128:(i + 1) * 128, :],
              wait=f_done if i == 0 else ())
    f2_pairs = p.dma_all()
    for oc in range(NDC):
        for nt in range(4):
            cs = bass.ts(nt, 512)
            par = (oc * 4 + nt) % 2
            m = mm_accum(p, ps_o[par][:, :],
                         [(wsb[k][:, oc * 128:(oc + 1) * 128], fsb[k][:, cs])
                          for k in range(NDC)],
                         wait_first=f2_pairs + acq(("po2", par)), inc=mm)
            a = p.op("scalar", lambda s, par=par: s.activation(
                g2s[:, :], ps_o[par][:, :], AF.Silu),
                wait=[(mm, m)] + acq(("g2s",)), inc=ac)
            last[("po2", par)] = (ac, a)
            sv = p.dma("scalar", g2_d.ap()[oc * 128:(oc + 1) * 128, cs], g2s[:, :],
                       wait=[(ac, a)])
            last[("g2s",)] = sv
    g2_done = p.dma_all()
    for oc in range(NDC):
        l1 = p.dma("sync", x2s[:, :], x2_d.ap()[oc * 128:(oc + 1) * 128, :],
                   wait=acq(("x2s",)) + (g2_done if oc == 0 else []))
        l2 = p.dma("sync", s2s[:, :], s2_d.ap()[oc * 128:(oc + 1) * 128, :])
        l3 = p.dma("sync", g2f[:, :], g2_d.ap()[oc * 128:(oc + 1) * 128, :])
        for nt in range(4):
            cs = bass.ts(nt, 512)
            v = p.op("vector", lambda vv, cs=cs: vv.tensor_mul(
                x3s[:, :], s2s[:, cs], g2f[:, cs]),
                wait=([l1, l2, l3] if nt == 0 else []) + acq(("x3s",)), inc=dv)
            v = p.op("vector", lambda vv, cs=cs: vv.tensor_add(
                x3s[:, :], x3s[:, :], x2s[:, cs]), inc=dv)
            sv = p.dma("scalar", x3_d.ap()[oc * 128:(oc + 1) * 128, cs], x3s[:, :],
                       wait=[(dv, v)])
            last[("x3s",)] = sv
        last[("x2s",)] = [sv]
    p.op("sync", lambda e: None, wait=p.dma_all())
    st2.close()


def build_s14():
    p = P("s14")
    emit_moe(p)
    return p.finalize()




def _rope_tables(rope_cos, rope_sin, tok0):
    cf = np.ones((128, T), np.float32)
    sf = np.zeros((128, T), np.float32)
    P = rope_cos.shape[0]
    r0 = S - P
    g = np.arange(tok0, tok0 + T)
    sel = g >= r0
    if sel.any():
        rows = g[sel] - r0
        pi = (np.arange(128) % 64) // 2
        cf[:, sel] = rope_cos[rows][:, pi].T
        sf[:, sel] = rope_sin[rows][:, pi].T
    return cf, sf


def _qkv0_consts():
    hs2 = np.zeros((128, 2), BF)
    hs2[:64, 0] = 1; hs2[64:, 1] = 1
    rep2 = np.zeros((2, 128), np.float32)
    rep2[0, :64] = 1; rep2[1, 64:] = 1
    psign = np.zeros((128, 128), BF)
    i = np.arange(0, 128, 2)
    psign[i + 1, i] = -1
    psign[i, i + 1] = 1
    return hs2, rep2, psign


def run_stage1(x):
    xTs = shard_tokens(x)
    nc1 = build_s1()
    return xTs, launch(nc1, [{"xT": xTs[c]} for c in range(NCORES)], ["xn", "xnb"])


def run_stage2(inputs, r1):
    wqT = np.ascontiguousarray(np.asarray(inputs["L0_attn_wq"], np.float32).T)
    wkT = np.ascontiguousarray(np.asarray(inputs["L0_attn_wk"], np.float32).T)
    wvT = np.ascontiguousarray(np.asarray(inputs["L0_attn_wv"], np.float32).T)
    qnw = np.ascontiguousarray(np.tile(np.asarray(inputs["L0_attn_q_norm_w"], np.float32), 2)[:, None])
    knw = np.ascontiguousarray(np.tile(np.asarray(inputs["L0_attn_k_norm_w"], np.float32), 2)[:, None])
    hs2, rep2, psign = _qkv0_consts()
    nc2 = build_s2()
    maps = []
    for c in range(NCORES):
        j = c % 4
        cf, sf = _rope_tables(np.asarray(inputs["L0_attn_rope_cos"], np.float32),
                              np.asarray(inputs["L0_attn_rope_sin"], np.float32), j * T)
        maps.append({"xnb": r1[c]["xnb"], "wqT": wqT, "wkT": wkT, "wvT": wvT,
                     "qnw": qnw, "knw": knw, "cfm": cf, "sfm": sf,
                     "hs2": hs2, "rep2": rep2, "psign": psign})
    return launch(nc2, maps, ["Qp", "Kp", "V"])


def run_stage3(r2):
    nc3 = build_s3()
    maps = []
    for c in range(NCORES):
        b, j = divmod(c, 4)
        Kag = np.ascontiguousarray(np.concatenate([r2[4 * b + r]["Kp"] for r in range(4)], axis=0))
        Vag = np.ascontiguousarray(np.concatenate([r2[4 * b + r]["V"] for r in range(4)], axis=0))
        maps.append({"Qp2": r2[c]["Qp"], "Kag": Kag, "Vag": Vag})
    return launch(nc3, maps, ["AVt"])


def run_stage4(inputs, r3):
    nc4 = build_s4()
    woT = np.ascontiguousarray(np.asarray(inputs["L0_attn_wo"], np.float32).T)
    maps = [{"AVt": r3[c]["AVt"], "woT": woT} for c in range(NCORES)]
    return launch(nc4, maps, ["s1"])


def _ffn_tables(tok0):
    f = (np.arange(128) % 64) % 32
    inv_freq = (1.0 / 10000.0 ** (2.0 * f / 64.0)).astype(np.float64)
    base = ((2.0 * f + 25.6) / 89.6).astype(np.float64)

    def tab(lo):
        g = np.arange(tok0 + lo, tok0 + T, dtype=np.float64)
        ang = g[None, :] * inv_freq[:, None]
        ex = np.mod(g, 256.0) / 128.0
        sc_q = base[:, None] ** ex[None, :]
        return (np.cos(ang) * sc_q).astype(np.float32), \
               (np.sin(ang) * sc_q).astype(np.float32), \
               (np.cos(ang) / sc_q).astype(np.float32), \
               (np.sin(ang) / sc_q).astype(np.float32)

    cq, sq, _, _ = tab(-QLO)
    _, _, ck, sk = tab(-KLO)
    B2 = (base ** 2).astype(np.float32)[:, None]
    return cq, sq, ck, sk, B2


def _ffn_consts():
    prh = np.zeros((128, 128), BF)
    for blk in (0, 64):
        i = np.arange(32)
        prh[blk + i + 32, blk + i] = -1
        prh[blk + i, blk + i + 32] = 1
    eye = np.eye(128, dtype=BF)

    def mk_mask(nq, nks):
        m = np.zeros((128, nq * nks), BF)
        for sub in range(nks):
            kk = np.arange(128)[:, None]
            qq = np.arange(nq)[None, :]
            ko = sub * 128 + kk
            m[:, sub * nq:(sub + 1) * nq] = ((ko - 256 <= qq) & (qq <= ko)).astype(BF)
        return m

    maskA = mk_mask(256, 4)
    maskB = mk_mask(128, 3)
    return prh, eye, maskA, maskB


def run_ffn_layers(inputs, r1):
    hs2, rep2, _ = _qkv0_consts()
    prh, eye, maskA, maskB = _ffn_consts()
    maskA0_core0 = maskA.copy()
    maskA0_core0[:, 0:512] = 0
    ICS = tiles(2730, 128)
    nc = build_ffn_layer()
    ys = [r1[c]["xn"] for c in range(NCORES)]
    Wq = np.asarray(inputs["L0_ffn_Wqkv"], np.float32)
    QS = np.asarray(inputs["L0_ffn_q_scale"], np.float32)
    KS = np.asarray(inputs["L0_ffn_k_scale"], np.float32)
    WO = np.asarray(inputs["L0_ffn_Wo"], np.float32)
    LG = np.asarray(inputs["L0_ffn_ln_g"], np.float32)
    LB = np.asarray(inputs["L0_ffn_ln_b"], np.float32)
    W1 = np.asarray(inputs["L0_ffn_W1"], np.float32)
    B1 = np.asarray(inputs["L0_ffn_b1"], np.float32)
    W2 = np.asarray(inputs["L0_ffn_W2"], np.float32)
    B2b = np.asarray(inputs["L0_ffn_b2"], np.float32)
    ylist = None
    for l in range(6):
        maps = []
        for c in range(NCORES):
            b, j = divmod(c, 4)
            if j == 0:
                halo = np.zeros((D, YLO), np.float32)
            else:
                halo = np.ascontiguousarray(ys[c - 1][:, T - YLO:])
            mp = ffn_layer_map(inputs, l, c)
            mp["y"] = ys[c]
            mp["yh"] = halo
            maps.append(mp)
        res = launch(nc, maps, ["y_out"])
        ys = [res[c]["y_out"] for c in range(NCORES)]
        ylist = ys
    return ylist


def ffn_layer_map(inputs, l, c):
    hs2, rep2, _ = _qkv0_consts()
    prh, eye, maskA, maskB = _ffn_consts()
    maskA0_core0 = maskA.copy()
    maskA0_core0[:, 0:512] = 0
    ICS = tiles(2730, 128)
    b, j = divmod(c, 4)
    Wq = np.asarray(inputs["L0_ffn_Wqkv"], np.float32)
    QS = np.asarray(inputs["L0_ffn_q_scale"], np.float32)
    KS = np.asarray(inputs["L0_ffn_k_scale"], np.float32)
    WO = np.asarray(inputs["L0_ffn_Wo"], np.float32)
    LG = np.asarray(inputs["L0_ffn_ln_g"], np.float32)
    LB = np.asarray(inputs["L0_ffn_ln_b"], np.float32)
    W1 = np.asarray(inputs["L0_ffn_W1"], np.float32)
    B1 = np.asarray(inputs["L0_ffn_b1"], np.float32)
    W2 = np.asarray(inputs["L0_ffn_W2"], np.float32)
    B2b = np.asarray(inputs["L0_ffn_b2"], np.float32)
    lngb = np.zeros((128, 16), np.float32)
    lngb[:, 0:8] = LG[l].reshape(8, 128).T
    lngb[:, 8:16] = LB[l].reshape(8, 128).T
    b1a = np.zeros((128, 22), np.float32)
    b1g = np.zeros((128, 22), np.float32)
    for ici, (ic0, sz) in enumerate(ICS):
        b1a[0:sz, ici] = B1[l][ic0:ic0 + sz]
        b1g[0:sz, ici] = B1[l][2730 + ic0:2730 + ic0 + sz]
    b2c = np.ascontiguousarray(B2b[l].reshape(8, 128).T)
    cq, sq, ck, sk, B2col = _ffn_tables(j * T)
    return {"Wqkv": Wq[l], "qsc": np.tile(QS[l], 2)[:, None],
            "ksc": np.tile(KS[l], 2)[:, None], "Wo": WO[l], "lngb": lngb,
            "W1": W1[l], "b1a": b1a, "b1g": b1g, "W2": W2[l], "b2c": b2c,
            "cq": cq, "sq": sq, "ck": ck, "sk": sk, "B2": B2col,
            "prh": prh, "hs2": hs2, "rep2": rep2, "eye": eye,
            "maskA": maskA, "maskB": maskB,
            "maskA0": maskA0_core0 if j == 0 else maskA}



def run_stage11(inputs, xTs, r4, ys):
    nc = build_s11()
    maps = [{"xT": xTs[c], "s1": r4[c]["s1"], "yf": ys[c]} for c in range(NCORES)]
    return launch(nc, maps, ["x2", "xn2", "xn2b"])


def run_stage12(inputs, r11):
    nc = build_s12()
    wqT = np.ascontiguousarray(np.asarray(inputs["L1_attn_w_q"], np.float32).T)
    wkT = np.ascontiguousarray(np.asarray(inputs["L1_attn_w_k"], np.float32).T)
    wvT = np.ascontiguousarray(np.asarray(inputs["L1_attn_w_v"], np.float32).T)
    eye = np.eye(128, dtype=BF)
    maps = [{"xn2b": r11[c]["xn2b"], "wqT": wqT, "wkT": wkT, "wvT": wvT, "eye": eye}
            for c in range(NCORES)]
    return launch(nc, maps, ["sq", "sk", "v", "snap", "sqs"])


def run_stage13(inputs, r12):
    nc = build_s13()
    mem = np.asarray(inputs["L1_attn_memories"], np.float32)
    mn = np.asarray(inputs["L1_attn_memory_norms"], np.float32)
    mnT = np.zeros((128, 64), np.float32)
    memd = np.zeros((128, 2048), np.float32)
    for h in range(16):
        hl = h % 2
        hp = h // 2
        for m in range(4):
            mnT[hl * 64:(hl + 1) * 64, 4 * h + m] = mn[m, h]
            memd[hl * 64:(hl + 1) * 64, (hp * 4 + m) * 64:(hp * 4 + m) * 64 + 64] = mem[m, h]
    sel4 = np.zeros((4, 512), np.float32)
    for m in range(4):
        sel4[m, m * 128:(m + 1) * 128] = 1
    ltri = (np.arange(128)[:, None] <= np.arange(128)[None, :]).astype(BF)
    gate = np.asarray(inputs["L1_attn_gate"], np.float32)[None, :]
    woT = np.ascontiguousarray(np.asarray(inputs["L1_attn_w_o"], np.float32).T)
    tots = []
    for c in range(NCORES):
        s = r12[c]["snap"].reshape(128, 8, NCH, 65)
        tots.append(np.ascontiguousarray(s[:, :, NCH - 1, :].reshape(128, 8 * 65)))
    maps = []
    for c in range(NCORES):
        b, j = divmod(c, 4)
        pm = np.zeros((128, 4), np.float32)
        pm[:, :j] = 1.0
        mp = {"sq": r12[c]["sq"], "sk": r12[c]["sk"], "v": r12[c]["v"],
              "snap": r12[c]["snap"], "pmask": pm, "mnT": mnT, "memd": memd,
              "sel4": sel4, "ltri": ltri, "gate": gate, "woT": woT,
              "totA": np.ascontiguousarray(np.concatenate([tots[4 * b + jj] for jj in range(4)], axis=0)),
              "sqsA": np.ascontiguousarray(np.concatenate([r12[4 * b + jj]["sqs"] for jj in range(4)], axis=0))}
        maps.append(mp)
    return launch(nc, maps, ["s2"])


def run_stage14(inputs, r11, r13):
    nc = build_s14()
    gWT = np.ascontiguousarray(np.asarray(inputs["L1_ffn_gate_W"], np.float32).T)
    W0T = np.ascontiguousarray(np.asarray(inputs["L1_ffn_expert_W"], np.float32)[0].T)
    W1T = np.ascontiguousarray(np.asarray(inputs["L1_ffn_expert_W"], np.float32)[1].T)
    oWT = np.ascontiguousarray(np.asarray(inputs["L1_ffn_out_W"], np.float32).T)
    eyeF = np.eye(128, dtype=np.float32)
    sel2 = np.zeros((2, 256), np.float32)
    sel2[0, 0:128] = 1
    sel2[1, 128:256] = 1
    maps = [{"xn2b": r11[c]["xn2b"], "x2": r11[c]["x2"], "s2": r13[c]["s2"],
             "gWT": gWT, "W0T": W0T, "W1T": W1T, "oWT": oWT, "eyeF": eyeF,
             "sel2": sel2} for c in range(NCORES)]
    return launch(nc, maps, ["x3"])




def emit_halo_prep(p, tlag_d, yh_d, hsel_d, wait=()):
    dv = p.dv
    st = ExitStack()
    hb2 = [p.sb(f"hp_b{i}", [128, YLO], F32, st) for i in range(2)]
    acc2 = [p.sb(f"hp_acc{i}", [128, YLO], F32, st) for i in range(2)]
    hsel = p.sb("hp_sel", [128, 4], F32, st)
    ls = p.dma("sync", hsel[:, :], hsel_d.ap()[:, :])
    hlast = {}
    stlast = {}
    for i in range(NDC):
        acc = acc2[i % 2]
        vv_last = None
        for r in range(4):
            par = (i * 4 + r) % 2
            hb = hb2[par]
            lb = p.dma("sync", hb[:, :], tlag_d.ap()[r * D + i * 128:r * D + (i + 1) * 128, :],
                       wait=(list(wait) if i == 0 and r == 0 else []) +
                            ([hlast[par]] if par in hlast else []))
            if r == 0:
                vv_last = p.op("vector", lambda v, r=r, hb=hb, acc=acc: v.tensor_scalar(
                    acc[:, :], hb[:, :], hsel[:, r:r + 1], None, ALU.mult),
                    wait=[lb, ls] + ([stlast[i % 2]] if i % 2 in stlast else []), inc=dv)
            else:
                vv_last = p.op("vector", lambda v, r=r, hb=hb, acc=acc: v.scalar_tensor_tensor(
                    acc[:, :], hb[:, :], hsel[:, r:r + 1], acc[:, :],
                    ALU.mult, ALU.add), wait=[lb], inc=dv)
            hlast[par] = (dv, vv_last)
        stlast[i % 2] = p.dma("scalar", yh_d.ap()[i * 128:(i + 1) * 128, :], acc[:, :],
                              wait=[(dv, vv_last)])
    st.close()


def build_fused():
    p = P("fused")
    ccs = None
    p.prefix = "s1."
    xT = p.dram("xT", [D, T], F32, kind="ExternalInput")
    xn = p.dram("xn", [D, T], F32)
    xnb = p.dram("xnb", [D, T], BF16)
    emit_ln_fm(p, xT.ap(), xn.ap(), xnb.ap(), tag="ln1f")
    p.barrier()
    p.prefix = "q0."
    p.alias["q0.xnb"] = xnb
    emit_qkv0(p)
    p.barrier()
    p.prefix = "x0."
    kag = p.dram("kag", [4 * D, T], BF16)
    vag = p.dram("vag", [4 * 128, (T // 128) * D], BF16)
    p.collective("AllGather", ALU.bypass, p.alias["q0.Kp"], kag)
    p.collective("AllGather", ALU.bypass, p.alias["q0.V"], vag)
    p.prefix = "g."
    hsel = p.dram("hsel", [128, 4], F32, kind="ExternalInput")
    p.prefix = "x0."

    halos = {}

    def make_halo(l):
        old_pref = p.prefix
        p.prefix = f"f{l}."
        tl = p.dram("tl", [D, YLO], F32)
        tlag = p.dram("tlag", [4 * D, YLO], F32)
        yh = p.dram("yhp", [D, YLO], F32)
        p.prefix = old_pref
        return (tl, tlag, yh)

    def emit_halo_start(l, src_ap, wait_list):
        tl, tlag, yh = halos[l]
        stores = []
        for i in range(NDC):
            w = [wait_list[i]] if wait_list else []
            stores.append(p.dma("sync", tl.ap()[i * 128:(i + 1) * 128, :],
                                src_ap[i * 128:(i + 1) * 128, T - YLO:], wait=w))
        cc = p.collective("AllGather", ALU.bypass, tl, tlag, wait=stores)

        def fin():
            emit_halo_prep(p, tlag, yh, hsel, wait=[(p.gp, cc)])
        return fin

    halos[0] = make_halo(0)
    fin0 = emit_halo_start(0, xn.ap(), None)
    fin0()
    p.barrier()
    p.prefix = "a0."
    p.alias["a0.Qp2"] = p.alias["q0.Qp"]
    p.alias["a0.Kag"] = kag
    p.alias["a0.Vag"] = vag
    emit_attn0(p)
    p.barrier()
    p.prefix = "s4."
    woT0 = p.dram("woT", [D, D], F32, kind="ExternalInput")
    s1b = p.dram("s1b", [D, T], F32)
    emit_linear_fm(p, "wo0f", p.alias["a0.AVt"].ap(), woT0.ap(), s1b.ap(), D, D, T)
    p.barrier()
    p.prefix = "g."
    yprev = xn
    for l in range(6):
        p.prefix = f"f{l}."
        p.alias[f"f{l}.y"] = yprev
        p.alias[f"f{l}.yh"] = halos[l][2]
        if l < 5:
            halos[l + 1] = make_halo(l + 1)

            def cb(pairs, yout_d, l=l):
                return emit_halo_start(l + 1, yout_d.ap(), pairs)
        else:
            cb = None
        emit_ffn_layer(p, tail_cb=cb)
        p.barrier()
        yprev = p.alias[f"f{l}.y_out"]
    p.prefix = "c1."
    p.alias["c1.xT"] = xT
    p.alias["c1.s1"] = s1b
    p.alias["c1.yf"] = yprev
    emit_comb1(p)
    p.barrier()
    p.prefix = "q1."
    p.alias["q1.xn2b"] = p.alias["c1.xn2b"]
    emit_qkv1(p)
    p.barrier()
    p.prefix = "x1."
    snap = p.alias["q1.snap"]
    totl = p.dram("totl", [128, 8 * 65], BF16)
    totA = p.dram("totA", [4 * 128, 8 * 65], BF16)
    sqsA = p.dram("sqsA", [4 * 128, 8], F32)
    p.dma("sync", totl.ap()[:, :],
          bass.AP(snap, NCH * 65 - 65, [[8 * NCH * 65, 128], [NCH * 65, 8], [1, 65]]))
    p.barrier()
    p.collective("AllGather", ALU.bypass, totl, totA)
    p.collective("AllGather", ALU.bypass, p.alias["q1.sqs"], sqsA)
    p.barrier()
    p.prefix = "a1."
    for nm in ("sq", "sk", "v", "snap"):
        p.alias["a1." + nm] = p.alias["q1." + nm]
    p.alias["a1.totA"] = totA
    p.alias["a1.sqsA"] = sqsA
    emit_attn1(p)
    p.barrier()
    p.prefix = "m."
    p.keep_outputs = {"m.x3"}
    p.alias["m.xn2b"] = p.alias["c1.xn2b"]
    p.alias["m.x2"] = p.alias["c1.x2"]
    p.alias["m.s2"] = p.alias["a1.s2"]
    emit_moe(p)
    return p.finalize()


def fused_maps(inputs):
    x = np.asarray(inputs["L0_attn_x"], np.float32)
    xTs = shard_tokens(x)
    wqT = np.ascontiguousarray(np.asarray(inputs["L0_attn_wq"], np.float32).T)
    wkT = np.ascontiguousarray(np.asarray(inputs["L0_attn_wk"], np.float32).T)
    wvT = np.ascontiguousarray(np.asarray(inputs["L0_attn_wv"], np.float32).T)
    qnw = np.ascontiguousarray(np.tile(np.asarray(inputs["L0_attn_q_norm_w"], np.float32), 2)[:, None])
    knw = np.ascontiguousarray(np.tile(np.asarray(inputs["L0_attn_k_norm_w"], np.float32), 2)[:, None])
    hs2, rep2, psign = _qkv0_consts()
    woT0 = np.ascontiguousarray(np.asarray(inputs["L0_attn_wo"], np.float32).T)
    wqT1 = np.ascontiguousarray(np.asarray(inputs["L1_attn_w_q"], np.float32).T)
    wkT1 = np.ascontiguousarray(np.asarray(inputs["L1_attn_w_k"], np.float32).T)
    wvT1 = np.ascontiguousarray(np.asarray(inputs["L1_attn_w_v"], np.float32).T)
    eye = np.eye(128, dtype=BF)
    mem = np.asarray(inputs["L1_attn_memories"], np.float32)
    mn = np.asarray(inputs["L1_attn_memory_norms"], np.float32)
    mnT = np.zeros((128, 64), np.float32)
    memd = np.zeros((128, 2048), np.float32)
    for h in range(16):
        hl = h % 2
        hp = h // 2
        for m in range(4):
            mnT[hl * 64:(hl + 1) * 64, 4 * h + m] = mn[m, h]
            memd[hl * 64:(hl + 1) * 64, (hp * 4 + m) * 64:(hp * 4 + m) * 64 + 64] = mem[m, h]
    sel4 = np.zeros((4, 512), np.float32)
    for m in range(4):
        sel4[m, m * 128:(m + 1) * 128] = 1
    ltri = (np.arange(128)[:, None] <= np.arange(128)[None, :]).astype(BF)
    gate = np.asarray(inputs["L1_attn_gate"], np.float32)[None, :]
    woT1 = np.ascontiguousarray(np.asarray(inputs["L1_attn_w_o"], np.float32).T)
    gWT = np.ascontiguousarray(np.asarray(inputs["L1_ffn_gate_W"], np.float32).T)
    W0T = np.ascontiguousarray(np.asarray(inputs["L1_ffn_expert_W"], np.float32)[0].T)
    W1T = np.ascontiguousarray(np.asarray(inputs["L1_ffn_expert_W"], np.float32)[1].T)
    oWT = np.ascontiguousarray(np.asarray(inputs["L1_ffn_out_W"], np.float32).T)
    eyeF = np.eye(128, dtype=np.float32)
    sel2 = np.zeros((2, 256), np.float32)
    sel2[0, 0:128] = 1
    sel2[1, 128:256] = 1
    maps = []
    for c in range(NCORES):
        b, j = divmod(c, 4)
        cf, sf = _rope_tables(np.asarray(inputs["L0_attn_rope_cos"], np.float32),
                              np.asarray(inputs["L0_attn_rope_sin"], np.float32), j * T)
        hsel = np.zeros((128, 4), np.float32)
        if j > 0:
            hsel[:, j - 1] = 1.0
        pm = np.zeros((128, 4), np.float32)
        pm[:, :j] = 1.0
        mp = {"s1.xT": xTs[c],
              "q0.wqT": wqT, "q0.wkT": wkT, "q0.wvT": wvT, "q0.qnw": qnw,
              "q0.knw": knw, "q0.cfm": cf, "q0.sfm": sf, "q0.hs2": hs2,
              "q0.rep2": rep2, "q0.psign": psign,
              "s4.woT": woT0, "g.hsel": hsel,
              "q1.wqT": wqT1, "q1.wkT": wkT1, "q1.wvT": wvT1, "q1.eye": eye,
              "a1.pmask": pm, "a1.mnT": mnT, "a1.memd": memd, "a1.sel4": sel4,
              "a1.ltri": ltri, "a1.gate": gate, "a1.woT": woT1,
              "m.gWT": gWT, "m.W0T": W0T, "m.W1T": W1T, "m.oWT": oWT,
              "m.eyeF": eyeF, "m.sel2": sel2}
        for l in range(6):
            for k, v in ffn_layer_map(inputs, l, c).items():
                mp[f"f{l}.{k}"] = v
        maps.append(mp)
    return maps


def kernel_fused(**inputs):
    nc = build_fused()
    maps = fused_maps(inputs)
    res = launch(nc, maps, ["m.x3"])
    out = np.zeros((B, S, D), np.float32)
    for c in range(NCORES):
        b, j = divmod(c, 4)
        out[b, j * T:(j + 1) * T, :] = res[c]["m.x3"].T
    return out


def kernel(**inputs):
    return kernel_fused(**inputs)


def kernel_multilaunch(**inputs):
    x = np.asarray(inputs["L0_attn_x"], np.float32)
    xTs, r1 = run_stage1(x)
    r2 = run_stage2(inputs, r1)
    r3 = run_stage3(r2)
    r4 = run_stage4(inputs, r3)
    ys = run_ffn_layers(inputs, r1)
    r11 = run_stage11(inputs, xTs, r4, ys)
    r12 = run_stage12(inputs, r11)
    r13 = run_stage13(inputs, r12)
    r14 = run_stage14(inputs, r11, r13)
    out = np.zeros((B, S, D), np.float32)
    for c in range(NCORES):
        b, j = divmod(c, 4)
        out[b, j * T:(j + 1) * T, :] = r14[c]["x3"].T
    return out
